# version 01
# baseline (speedup 1.0000x reference)
from contextlib import ExitStack

import numpy as np
import ml_dtypes

import concourse.bass as bass
import concourse.mybir as mybir
from concourse.alu_op_type import AluOpType
from concourse.bass_utils import run_bass_kernel_spmd

F32 = mybir.dt.float32
BF16 = mybir.dt.bfloat16
I32 = mybir.dt.int32
U32 = mybir.dt.uint32
AF = mybir.ActivationFunctionType
OP = AluOpType

BF = ml_dtypes.bfloat16

D_MODEL = 1024
NB = 2
T_FULL = 8192
N_CORES = 8
TPC = 2048

K_CONV = 4
MEPS = 1e-6
LNEPS = 1e-5


class P:

    def __init__(self, nc, stack, names):
        self.nc = nc
        self.sems = {}
        self.cnt = {}
        for name in names:
            self.sems[name] = stack.enter_context(nc.semaphore(name))
            self.cnt[name] = 0

    def inc(self, instr, name, n=1):
        instr.then_inc(self.sems[name], n)
        self.cnt[name] += n
        return instr

    def wait(self, eng, *names):
        for name in names:
            eng.wait_ge(self.sems[name], self.cnt[name])

    def acx(self, eng, instr):
        self.inc(instr, "ac")
        eng.drain()
        return instr

    def dvx(self, eng, instr):
        self.inc(instr, "dv")
        eng.drain()
        return instr


def _consts():
    return dict(
        onesr=np.ones((1, 128), np.float32),
        trilc=np.triu(np.ones((128, 128), np.float32)),
        eyeb=np.eye(128, dtype=BF),
        eyef=np.eye(128, dtype=np.float32),
    )


def _add_const_tensors(nc):
    onesr = nc.dram_tensor("onesr", [1, 128], F32, kind="ExternalInput").ap()
    trilc = nc.dram_tensor("trilc", [128, 128], F32, kind="ExternalInput").ap()
    eyeb = nc.dram_tensor("eyeb", [128, 128], BF16, kind="ExternalInput").ap()
    eyef = nc.dram_tensor("eyef", [128, 128], F32, kind="ExternalInput").ap()
    return onesr, trilc, eyeb, eyef



def build_launch_A(ntile=None, dbg=False):
    nc = bass.Bass("TRN2", target_bir_lowering=False)
    TT = 512
    NTILE = ntile or (T_FULL // TT)
    NCH = 4
    HALO = 3
    NCK = TT // 128

    xb = nc.dram_tensor("xb", [D_MODEL, T_FULL], F32, kind="ExternalInput").ap()
    wzx = nc.dram_tensor("wzx", [D_MODEL, 1024], F32, kind="ExternalInput").ap()
    wbc = nc.dram_tensor("wbc", [D_MODEL, 256], F32, kind="ExternalInput").ap()
    wdt = nc.dram_tensor("wdt", [D_MODEL, 8], F32, kind="ExternalInput").ap()
    cw = nc.dram_tensor("cw", [512, K_CONV], F32, kind="ExternalInput").ap()
    cb = nc.dram_tensor("cb", [128, 4], F32, kind="ExternalInput").ap()
    alog = nc.dram_tensor("alog", [8, 1], F32, kind="ExternalInput").ap()
    dcol = nc.dram_tensor("dcol", [128, 4], F32, kind="ExternalInput").ap()
    dtb = nc.dram_tensor("dtb", [8, 1], F32, kind="ExternalInput").ap()
    nw = nc.dram_tensor("nw", [128, 8], F32, kind="ExternalInput").ap()
    wout = nc.dram_tensor("wout", [512, D_MODEL], F32, kind="ExternalInput").ap()
    onesr, trilc, eyeb, eyef = _add_const_tensors(nc)
    pout = nc.dram_tensor("pout", [D_MODEL, T_FULL], F32, kind="ExternalOutput").ap()
    dbg_names = ["sH", "sHbf", "sDrow", "sRow8", "sDAc", "sCdb", "sCp", "sMaskT", "sU", "sUp",
                 "sColT", "sY", "sClrel", "sWs", "sDtsp", "sXhT", "sBsm", "sRow", "sMi", "sEm"]
    dbg_out = {}
    if dbg:
        shapes = {"sH": [128, 512], "sHbf": [128, 512], "sDrow": [1, 512], "sRow8": [1, 8],
                  "sDAc": [8, 1], "sCdb": [128, 128], "sCp": [128, 128], "sMaskT": [128, 128],
                  "sU": [128, 64], "sUp": [128, 64], "sColT": [128, 24], "sY": [128, 4 * 512],
                  "sClrel": [8, 512], "sWs": [8, 512], "sDtsp": [8, 512], "sXhT": [128, 512],
                  "sBsm": [128, 128], "sRow": [1, 128], "sMi": [128, 128], "sEm": [128, 128]}
        for n in dbg_names:
            dbg_out[n] = nc.dram_tensor("dbg_" + n, shapes[n], F32, kind="ExternalOutput").ap()

    xb_r = xb.rearrange("(c p) t -> p c t", p=128)
    pout_r = pout.rearrange("(c p) t -> p c t", p=128)

    with ExitStack() as st:
        sb = lambda name, shape, dt=F32: st.enter_context(nc.sbuf_tensor(name, shape, dt))
        sWzx = sb("sWzx", [128, 8 * 1024], BF16)
        sWbc = sb("sWbc", [128, 8 * 256], BF16)
        sWdt = sb("sWdt", [128, 8 * 8], BF16)
        sWout = sb("sWout", [128, 4 * 1024], BF16)
        sCw = sb("sCw", [128, NCH * K_CONV])
        sCb = sb("sCb", [128, NCH])
        sAlog = sb("sAlog", [8, 1])
        snegA = sb("snegA", [8, 1])
        sDcol = sb("sDcol", [128, 4])
        sDtb = sb("sDtb", [8, 1])
        sNw = sb("sNw", [128, 8])
        sOnes = sb("sOnes", [1, 128])
        sOnesCol = sb("sOnesCol", [128, 1])
        sTril = sb("sTril", [128, 128])
        sEyeB = sb("sEyeB", [128, 128], BF16)
        sEyeF = sb("sEyeF", [128, 128])
        sEpsR = sb("sEpsR", [1, 1])
        sH = sb("sH", [128, 512])
        sHbf = sb("sHbf", [128, 512], BF16)
        sClCar = sb("sClCar", [8, 1])
        sXraw = sb("sXraw", [128, 8 * TT])
        sSq = sb("sSq", [128, TT])
        sRrms = sb("sRrms", [1, TT])
        sRrms2 = sb("sRrms2", [1, TT])
        sTmpR3 = sb("sTmpR3", [8, TT])
        sXn = sb("sXn", [128, 8 * TT], BF16)
        sZ = sb("sZ", [128, NCH * TT], BF16)
        sSiluZ = sb("sSiluZ", [128, NCH * TT], BF16)
        sXs = sb("sXs", [128, NCH * (TT + HALO)], BF16)
        sCacc0 = sb("sCacc0", [128, TT])
        sCacc1 = sb("sCacc1", [128, TT])
        sXh = sb("sXh", [128, NCH * TT], BF16)
        sXhT = sb("sXhT", [128, NCH * 128], BF16)
        sB = sb("sB", [128, TT], BF16)
        sC = sb("sC", [128, TT], BF16)
        sBsm = sb("sBsm", [128, 128], BF16)
        sDtr = sb("sDtr", [8, TT])
        sTmpR = sb("sTmpR", [8, TT])
        sTmpR2 = sb("sTmpR2", [8, TT])
        sClrel = sb("sClrel", [8, TT])
        sWs = sb("sWs", [8, TT])
        sDtsp = sb("sDtsp", [8, TT])
        sCl = sb("sCl", [8, TT])
        sColT = sb("sColT", [128, 24])
        sRow = sb("sRow", [1, 128])
        sRow8 = sb("sRow8", [1, 8])
        sDrow = sb("sDrow", [1, 512])
        sDAc = sb("sDAc", [8, 1])
        sU = sb("sU", [128, 64], BF16)
        sUp = sb("sUp", [128, 64], BF16)
        sMi = sb("sMi", [128, 128])
        sEm = sb("sEm", [128, 128], BF16)
        sTrilG = sb("sTrilG", [128, 128], BF16)
        sMaskT = sb("sMaskT", [128, 128], BF16)
        sCdb = sb("sCdb", [128, 128], BF16)
        sCp = sb("sCp", [128, 128], BF16)
        sHtmp = sb("sHtmp", [128, 512])
        sY = sb("sY", [128, NCH * TT], BF16)
        sYevT = sb("sYevT", [128, 128])
        sPoutT = sb("sPoutT", [128, 512])

        ppool = st.enter_context(nc.psum_tensor("ppool", [128, 4096], F32))

        def bank(i, w=512, off=0):
            return ppool[:, i * 512 + off: i * 512 + off + w]

        def bankp(i, p0, prows, w, off=0):
            return ppool[p0:p0 + prows, i * 512 + off: i * 512 + off + w]

        p = P(nc, st, ["ld", "gld", "pe", "dv", "ac", "st"])
        sy, gp, pe, dv, ac = nc.sync, nc.gpsimd, nc.tensor, nc.vector, nc.scalar

        p.inc(sy.dma_start(out=sCw[:, :].rearrange("p (c k) -> p c k", c=NCH),
                           in_=cw.rearrange("(c p) k -> p c k", p=128)), "ld", 16)
        p.inc(sy.dma_start(out=sCb[:, :], in_=cb[:, :]), "ld", 16)
        p.inc(sy.dma_start(out=sAlog[:, :], in_=alog[:, :]), "ld", 16)
        p.inc(sy.dma_start(out=sDcol[:, :], in_=dcol[:, :]), "ld", 16)
        p.inc(sy.dma_start(out=sDtb[:, :], in_=dtb[:, :]), "ld", 16)
        p.inc(sy.dma_start(out=sNw[:, :], in_=nw[:, :]), "ld", 16)
        p.inc(sy.dma_start(out=sOnes[:, :], in_=onesr[:, :]), "ld", 16)
        p.inc(sy.dma_start(out=sTril[:, :], in_=trilc[:, :]), "ld", 16)
        p.inc(sy.dma_start(out=sEyeB[:, :], in_=eyeb[:, :]), "ld", 16)
        p.inc(sy.dma_start(out=sEyeF[:, :], in_=eyef[:, :]), "ld", 16)
        p.inc(gp.dma_start(out=sWzx[:, :].rearrange("p (c m) -> p c m", c=8),
                           in_=wzx.rearrange("(c p) m -> p c m", p=128)), "gld", 16)
        p.inc(gp.dma_start(out=sWbc[:, :].rearrange("p (c m) -> p c m", c=8),
                           in_=wbc.rearrange("(c p) m -> p c m", p=128)), "gld", 16)
        p.inc(gp.dma_start(out=sWdt[:, :].rearrange("p (c m) -> p c m", c=8),
                           in_=wdt.rearrange("(c p) m -> p c m", p=128)), "gld", 16)
        p.inc(gp.dma_start(out=sWout[:, :].rearrange("p (c m) -> p c m", c=4),
                           in_=wout.rearrange("(c p) m -> p c m", p=128)), "gld", 16)

        p.wait(dv, "ld", "gld")
        p.dvx(dv, dv.memset(sH[:, :], 0.0))
        p.dvx(dv, dv.memset(sHbf[:, :], 0.0))
        p.dvx(dv, dv.memset(sClCar[:, :], 0.0))
        p.dvx(dv, dv.memset(sXs[:, :], 0.0))
        p.dvx(dv, dv.memset(sOnesCol[:, :], 1.0))
        p.dvx(dv, dv.memset(sEpsR[:, :], MEPS))
        p.wait(ac, "dv")
        p.acx(ac, ac.activation(out=sDAc[:, :], in_=sAlog[:, :], func=AF.Exp))
        p.wait(dv, "ac")
        p.dvx(dv, dv.tensor_scalar(snegA[:, :], sDAc[:, :], -1.0, None, OP.mult))

        for tt in range(NTILE):
            tsl = slice(tt * TT, (tt + 1) * TT)
            p.wait(sy, "pe", "ac", "dv")
            p.inc(sy.dma_start(out=sXraw[:, :].rearrange("p (c t) -> p c t", c=8),
                               in_=xb_r[:, :, tsl]), "ld", 16)
            if tt > 0:
                for m in range(NCH):
                    base = m * (TT + HALO)
                    p.dvx(dv, dv.tensor_copy(sXs[:, base: base + HALO],
                                         sXs[:, base + TT: base + TT + HALO]))
            p.wait(ac, "ld")
            for c in range(8):
                if c > 0:
                    p.wait(ac, "pe")
                p.acx(ac, ac.activation(out=sSq[:, :], in_=sXraw[:, c * TT:(c + 1) * TT],
                                    func=AF.Square))
                p.wait(pe, "ac")
                p.inc(pe.matmul(bankp(0, 0, 1, TT), sOnesCol[:, :], sSq[:, :],
                                start=(c == 0), stop=(c == 7)), "pe")
            p.wait(ac, "pe")
            p.acx(ac, ac.activation(out=sRrms2[:, :], in_=bankp(0, 0, 1, TT), func=AF.Ln,
                                scale=1.0 / D_MODEL, bias=sEpsR[:, :]))
            p.acx(ac, ac.activation(out=sRrms[:, :], in_=sRrms2[:, :], func=AF.Exp,
                                scale=-0.5))
            p.wait(pe, "ac")
            p.inc(pe.matmul(bank(1, TT), sOnes[:, :], sRrms[:, :], start=True, stop=True), "pe")
            p.wait(dv, "pe", "ld")
            for c in range(8):
                p.dvx(dv, dv.scalar_tensor_tensor(sXn[:, c * TT:(c + 1) * TT],
                                              sXraw[:, c * TT:(c + 1) * TT],
                                              sNw[:, c:c + 1], bank(1, TT),
                                              OP.mult, OP.mult))
            p.wait(pe, "dv", "ac")
            for m in range(4):
                for k in range(8):
                    p.inc(pe.matmul(bank(m), sWzx[:, k * 1024 + m * 128: k * 1024 + (m + 1) * 128],
                                    sXn[:, k * TT:(k + 1) * TT],
                                    start=(k == 0), stop=(k == 7)), "pe")
            for m in range(4):
                for k in range(8):
                    p.inc(pe.matmul(bank(4 + m),
                                    sWzx[:, k * 1024 + 512 + m * 128: k * 1024 + 512 + (m + 1) * 128],
                                    sXn[:, k * TT:(k + 1) * TT],
                                    start=(k == 0), stop=(k == 7)), "pe")
            p.wait(dv, "pe")
            for m in range(4):
                p.dvx(dv, dv.tensor_copy(sZ[:, m * TT:(m + 1) * TT], bank(m)))
            for m in range(4):
                p.dvx(dv, dv.tensor_copy(sXs[:, m * (TT + HALO) + HALO: (m + 1) * (TT + HALO)],
                                     bank(4 + m)))
            p.wait(ac, "dv")
            for m in range(4):
                p.acx(ac, ac.activation(out=sSiluZ[:, m * TT:(m + 1) * TT],
                                    in_=sZ[:, m * TT:(m + 1) * TT], func=AF.Silu))
            p.wait(pe, "dv")
            for k in range(8):
                p.inc(pe.matmul(bank(0), sWbc[:, k * 256: k * 256 + 128],
                                sXn[:, k * TT:(k + 1) * TT], start=(k == 0), stop=(k == 7)), "pe")
            for k in range(8):
                p.inc(pe.matmul(bank(1), sWbc[:, k * 256 + 128: (k + 1) * 256],
                                sXn[:, k * TT:(k + 1) * TT], start=(k == 0), stop=(k == 7)), "pe")
            for k in range(8):
                p.inc(pe.matmul(bankp(2, 0, 8, TT), sWdt[:, k * 8:(k + 1) * 8],
                                sXn[:, k * TT:(k + 1) * TT], start=(k == 0), stop=(k == 7)), "pe")
            p.wait(dv, "pe")
            p.dvx(dv, dv.tensor_copy(sB[:, :], bank(0)))
            p.dvx(dv, dv.tensor_copy(sC[:, :], bank(1)))
            p.dvx(dv, dv.tensor_copy(sDtr[:, :], bankp(2, 0, 8, TT)))

            for m in range(NCH):
                base = m * (TT + HALO)
                p.wait(dv, "ac")
                p.dvx(dv, dv.tensor_scalar(sCacc0[:, :], sXs[:, base: base + TT],
                                       sCw[:, m * K_CONV: m * K_CONV + 1],
                                       sCb[:, m:m + 1], OP.mult, OP.add))
                accs = [sCacc0, sCacc1]
                for k in range(1, K_CONV):
                    src, dst = accs[(k - 1) % 2], accs[k % 2]
                    p.dvx(dv, dv.scalar_tensor_tensor(dst[:, :], sXs[:, base + k: base + k + TT],
                                                  sCw[:, m * K_CONV + k: m * K_CONV + k + 1],
                                                  src[:, :], OP.mult, OP.add))
                final = accs[(K_CONV - 1) % 2]
                p.wait(ac, "dv")
                p.acx(ac, ac.activation(out=sXh[:, m * TT:(m + 1) * TT], in_=final[:, :],
                                    func=AF.Silu))
            p.dvx(dv, dv.tensor_scalar(sTmpR[:, :], sDtr[:, :], sDtb[:, :], None, OP.add))
            p.wait(ac, "dv")
            p.acx(ac, ac.activation(out=sTmpR2[:, :], in_=sTmpR[:, :], func=AF.Abs))
            p.acx(ac, ac.activation(out=sTmpR3[:, :], in_=sTmpR2[:, :], func=AF.Exp, scale=-1.0))
            p.acx(ac, ac.activation(out=sTmpR2[:, :], in_=sTmpR3[:, :], func=AF.Ln, bias=1.0))
            p.wait(dv, "ac")
            p.dvx(dv, dv.tensor_scalar(sTmpR3[:, :], sTmpR[:, :], 0.0, None, OP.max))
            p.dvx(dv, dv.tensor_tensor(sDtsp[:, :], sTmpR3[:, :], sTmpR2[:, :], OP.add))
            p.dvx(dv, dv.tensor_scalar(sTmpR[:, :], sDtsp[:, :], snegA[:, :], None, OP.mult))
            p.dvx(dv, dv.tensor_tensor_scan(sCl[:, :], sTmpR[:, :], sTmpR[:, :], sClCar[:, :],
                                        OP.add, OP.bypass))
            for c in range(NCK):
                prevc = sClCar[:, :] if c == 0 else sCl[:, c * 128 - 1: c * 128]
                csl = slice(c * 128, (c + 1) * 128)
                p.dvx(dv, dv.tensor_scalar(sClrel[:, csl], sCl[:, csl], prevc, None,
                                       OP.subtract))
                endc = sClrel[:, c * 128 + 127: c * 128 + 128]
                p.dvx(dv, dv.tensor_scalar(sTmpR2[:, csl], sClrel[:, csl], endc, None,
                                       OP.subtract))
            p.dvx(dv, dv.tensor_copy(sClCar[:, :], sCl[:, TT - 1: TT]))
            p.wait(ac, "dv")
            p.acx(ac, ac.activation(out=sTmpR3[:, :], in_=sTmpR2[:, :], func=AF.Exp, scale=-1.0))
            p.wait(dv, "ac")
            p.dvx(dv, dv.tensor_tensor(sWs[:, :], sTmpR3[:, :], sDtsp[:, :], OP.mult))

            for c in range(NCK):
                csl = slice(c * 128, (c + 1) * 128)
                p.wait(pe, "dv")
                p.inc(pe.transpose(bankp(3, 0, 128, 8, off=0), sClrel[:, csl], sEyeF[0:8, 0:8]), "pe")
                p.inc(pe.transpose(bankp(3, 0, 128, 8, off=8), sWs[:, csl], sEyeF[0:8, 0:8]), "pe")
                p.inc(pe.transpose(bankp(3, 0, 128, 8, off=16), sDtsp[:, csl], sEyeF[0:8, 0:8]), "pe")
                p.wait(dv, "pe")
                p.dvx(dv, dv.tensor_copy(sColT[:, :], bankp(3, 0, 128, 24)))
                p.wait(ac, "dv")
                p.acx(ac, ac.activation(out=sDAc[:, :], in_=sClrel[:, c * 128 + 127: c * 128 + 128],
                                    func=AF.Exp))
                p.inc(pe.matmul(bank(5, 128), sB[:, csl], sC[:, csl], start=True, stop=True), "pe")
                p.wait(dv, "pe")
                p.dvx(dv, dv.tensor_tensor(sTrilG[:, :], bank(5, 128), sTril[:, :], OP.mult))
                p.wait(pe, "dv")
                p.inc(pe.transpose(bank(5, 256).bitcast(BF16)[:, 0:128], sB[:, csl], sEyeB[:, :]), "pe")
                p.wait(dv, "pe")
                p.dvx(dv, dv.tensor_copy(sBsm[:, :], bank(5, 256).bitcast(BF16)[:, 0:128]))
                for m in range(NCH):
                    p.wait(pe, "dv")
                    p.inc(pe.transpose(bank(4, 256).bitcast(BF16)[:, 0:128],
                                       sXh[:, m * TT + c * 128: m * TT + (c + 1) * 128],
                                       sEyeB[:, :]), "pe")
                    p.wait(dv, "pe")
                    p.dvx(dv, dv.tensor_copy(sXhT[:, m * 128:(m + 1) * 128],
                                         bank(4, 256).bitcast(BF16)[:, 0:128]))

                for h in range(8):
                    pair = h % 2
                    hsl = slice((h // 2) * 128 + (h % 2) * 64, (h // 2) * 128 + (h % 2) * 64 + 64)
                    p.wait(dv, "pe", "ac")
                    p.dvx(dv, dv.tensor_scalar(sU[:, :], sXhT[:, hsl], sColT[:, 16 + h: 17 + h],
                                           None, OP.mult))
                    p.dvx(dv, dv.tensor_scalar(sUp[:, :], sXhT[:, hsl], sColT[:, 8 + h: 9 + h],
                                           None, OP.mult))
                    p.wait(pe, "dv", "ac")
                    p.inc(pe.transpose(bankp(3, 0, 1, 128, off=256), sColT[:, h:h + 1],
                                       sEyeF[:, :]), "pe")
                    p.wait(dv, "pe")
                    p.dvx(dv, dv.tensor_copy(sRow[:, :], bankp(3, 0, 1, 128, off=256)))
                    p.wait(pe, "dv", "ac")
                    p.inc(pe.matmul(bank(6, 128), sOnes[:, :], sRow[:, :],
                                    start=True, stop=True), "pe")
                    p.wait(dv, "pe")
                    p.dvx(dv, dv.tensor_scalar(sMi[:, :], bank(6, 128), sColT[:, h:h + 1], 0.0,
                                           OP.subtract, OP.min))
                    p.wait(ac, "dv", "pe")
                    p.acx(ac, ac.activation(out=sEm[:, :], in_=sMi[:, :], func=AF.Exp))
                    p.acx(ac, ac.activation(out=sCdb[:, :], in_=bank(6, 128), func=AF.Exp))
                    p.wait(dv, "ac")
                    p.dvx(dv, dv.tensor_tensor(sMaskT[:, :], sEm[:, :], sTrilG[:, :], OP.mult))
                    p.dvx(dv, dv.tensor_tensor(sCp[:, :], sC[:, csl], sCdb[:, :], OP.mult))
                    p.wait(pe, "dv")
                    ybank = 7 if pair == 0 else 2
                    yps = ppool[pair * 64: pair * 64 + 64, ybank * 512: ybank * 512 + 128]
                    p.inc(pe.matmul(yps, sU[:, :], sMaskT[:, :],
                                    start=True, stop=False), "pe")
                    p.inc(pe.matmul(yps, sHbf[:, h * 64:(h + 1) * 64], sCp[:, :],
                                    start=False, stop=True), "pe")
                    p.inc(pe.matmul(bank(0)[:, h * 64:(h + 1) * 64], sBsm[:, :], sUp[:, :],
                                    start=(h == 0), stop=(h == 7)), "pe")
                    if pair == 1:
                        m = h // 2
                        csl128 = slice(m * TT + c * 128, m * TT + (c + 1) * 128)
                        p.wait(dv, "pe")
                        for hb, yb in ((0, 7), (1, 2)):
                            rsl = slice(hb * 64, hb * 64 + 64)
                            yv = ppool[rsl, yb * 512: yb * 512 + 128]
                            p.dvx(dv, dv.scalar_tensor_tensor(sYevT[rsl, :],
                                                          sXh[rsl, csl128],
                                                          sDcol[rsl.start:rsl.start + 64, m:m + 1] if False else sDcol[rsl, m:m + 1],
                                                          yv, OP.mult, OP.add))
                            p.dvx(dv, dv.tensor_tensor(sY[rsl, csl128], sYevT[rsl, :],
                                                   sSiluZ[rsl, csl128], OP.mult))
                p.wait(pe, "ac", "dv")
                p.inc(pe.transpose(bankp(2, 0, 1, 8), sDAc[:, :], sEyeF[0:8, 0:8]), "pe")
                p.wait(dv, "pe")
                p.dvx(dv, dv.tensor_copy(sRow8[:, :], bankp(2, 0, 1, 8)))
                p.dvx(dv, dv.tensor_copy(sDrow[:, :].rearrange("q (a b) -> q a b", a=8),
                                     sRow8[:, :].unsqueeze(2).broadcast_to([1, 8, 64])))
                p.wait(pe, "dv")
                p.inc(pe.matmul(bank(1), sOnes[:, :], sDrow[:, :], start=True, stop=True), "pe")
                p.wait(dv, "pe")
                p.dvx(dv, dv.tensor_tensor(sHtmp[:, :], sH[:, :], bank(1), OP.mult))
                p.dvx(dv, dv.tensor_tensor(sH[:, :], sHtmp[:, :], bank(0), OP.add))
                p.dvx(dv, dv.tensor_copy(sHbf[:, :], sH[:, :]))

            p.wait(pe, "dv")
            for m in range(8):
                for k in range(4):
                    p.inc(pe.matmul(bank(m), sWout[:, k * 1024 + m * 128: k * 1024 + (m + 1) * 128],
                                    sY[:, k * TT:(k + 1) * TT],
                                    start=(k == 0), stop=(k == 3)), "pe")
            p.wait(dv, "pe")
            for m in range(8):
                p.dvx(dv, dv.tensor_copy(sPoutT[:, :], bank(m)))
                p.wait(sy, "dv")
                p.inc(sy.dma_start(out=pout_r[:, m, tsl], in_=sPoutT[:, :]), "st", 16)
                p.wait(dv, "st")

        if dbg:
            local = dict(sH=sH, sHbf=sHbf, sDrow=sDrow, sRow8=sRow8, sDAc=sDAc, sCdb=sCdb,
                         sCp=sCp, sMaskT=sMaskT, sU=sU, sUp=sUp, sColT=sColT, sY=sY,
                         sClrel=sClrel, sWs=sWs, sDtsp=sDtsp, sXhT=sXhT, sBsm=sBsm, sRow=sRow,
                         sMi=sMi, sEm=sEm)
            p.wait(gp, "dv", "ac", "pe")
            for n in dbg_names:
                p.inc(gp.dma_start(out=dbg_out[n][:, :], in_=local[n][:, :]), "st", 16)
        p.wait(gp, "st")
        p.wait(gp, "ld")
    return nc


def prep_A_inputs(L0_attn_x, norm_w, in_proj_w, conv_w, conv_b, A_log, D, dt_bias, out_proj_w):
    consts = _consts()
    in_maps = []
    for cid in range(N_CORES):
        b, g = cid // 4, cid % 4
        hsl = slice(8 * g, 8 * g + 8)
        chsl = slice(512 * g, 512 * g + 512)
        wz = in_proj_w[chsl]
        wx = in_proj_w[2048 + 512 * g: 2048 + 512 * g + 512]
        wbc = in_proj_w[4096:4352]
        wdtr = in_proj_w[4352 + 8 * g: 4352 + 8 * g + 8]
        m = dict(
            xb=f32(L0_attn_x[b].T),
            wzx=f32(np.concatenate([wz, wx], 0).T),
            wbc=f32(wbc.T),
            wdt=f32(wdtr.T),
            cw=f32(conv_w[chsl]),
            cb=f32(conv_b[chsl].reshape(4, 128).T),
            alog=f32(A_log[hsl][:, None]),
            dcol=f32(np.repeat(np.asarray(D)[hsl], 64).reshape(4, 128).T),
            dtb=f32(dt_bias[hsl][:, None]),
            nw=f32(np.asarray(norm_w).reshape(8, 128).T),
            wout=f32(out_proj_w[:, chsl].T),
            **consts,
        )
        in_maps.append(m)
    return in_maps


def f32(x):
    return np.ascontiguousarray(np.asarray(x), dtype=np.float32)






def build_launch_B(dbg=False):
    nc = bass.Bass("TRN2", target_bir_lowering=False)
    TT = 512
    NTILE = TPC // TT

    xo = nc.dram_tensor("xo", [D_MODEL, TPC], F32, kind="ExternalInput").ap()
    pps = [nc.dram_tensor(f"pp{j}", [D_MODEL, TPC], F32, kind="ExternalInput").ap()
           for j in range(4)]
    wg = nc.dram_tensor("wg", [D_MODEL, 8], F32, kind="ExternalInput").ap()
    onesr, trilc, eyeb, eyef = _add_const_tensors(nc)
    x1f = nc.dram_tensor("x1f", [D_MODEL, TPC], F32, kind="ExternalOutput").ap()
    x1b = nc.dram_tensor("x1b", [D_MODEL, TPC], BF16, kind="ExternalOutput").ap()
    sct = nc.dram_tensor("sct", [8, TPC], F32, kind="ExternalOutput").ap()
    dbgB = {}
    if dbg:
        for n, shp in [("sRaw", [128, 4096]), ("sR1", [1, 512]), ("sR2", [1, 512]),
                       ("sR3", [1, 512]), ("sAcc", [128, 4096]), ("sXo", [128, 4096]),
                       ("sSq", [128, 512])]:
            dbgB[n] = nc.dram_tensor("dbg_" + n, shp, F32, kind="ExternalOutput").ap()

    xo_r = xo.rearrange("(c p) t -> p c t", p=128)
    pp_r = [q.rearrange("(c p) t -> p c t", p=128) for q in pps]
    x1f_r = x1f.rearrange("(c p) t -> p c t", p=128)
    x1b_r = x1b.rearrange("(c p) t -> p c t", p=128)

    with ExitStack() as st:
        sb = lambda name, shape, dt=F32: st.enter_context(nc.sbuf_tensor(name, shape, dt))
        sAcc = sb("sAcc", [128, 8 * TT])
        sP1 = sb("sP1", [128, 8 * TT])
        sXo = sb("sXo", [128, 8 * TT])
        sRaw = sb("sRaw", [128, 8 * TT])
        sSq = sb("sSq", [128, TT])
        sR1 = sb("sR1", [1, TT])
        sR2 = sb("sR2", [1, TT])
        sR3 = sb("sR3", [1, TT])
        sX1 = sb("sX1", [128, 8 * TT])
        sX1b = sb("sX1b", [128, 8 * TT], BF16)
        sWg = sb("sWg", [128, 8 * 8])
        sOnes = sb("sOnes", [1, 128])
        sOnesCol = sb("sOnesCol", [128, 1])
        sEyeF = sb("sEyeF", [128, 128])
        sEpsR = sb("sEpsR", [1, 1])
        sMx = sb("sMx", [128, 1])
        sSc = sb("sSc", [128, 8])
        sScE = sb("sScE", [128, 8])
        sScT = sb("sScT", [8, TT])
        sRec = sb("sRec", [128, 1])

        ppool = st.enter_context(nc.psum_tensor("ppool", [128, 4096], F32))

        def bank(i, w=512, off=0):
            return ppool[:, i * 512 + off: i * 512 + off + w]

        def bankp(i, p0, prows, w, off=0):
            return ppool[p0:p0 + prows, i * 512 + off: i * 512 + off + w]

        p = P(nc, st, ["ld", "pe", "dv", "ac", "st"])
        sy, gp, pe, dv, ac = nc.sync, nc.gpsimd, nc.tensor, nc.vector, nc.scalar

        p.inc(sy.dma_start(out=sOnes[:, :], in_=onesr[:, :]), "ld", 16)
        p.inc(sy.dma_start(out=sEyeF[:, :], in_=eyef[:, :]), "ld", 16)
        p.inc(sy.dma_start(out=sWg[:, :].rearrange("p (c m) -> p c m", c=8),
                           in_=wg.rearrange("(c p) m -> p c m", p=128)), "ld", 16)
        p.wait(dv, "ld")
        p.dvx(dv, dv.memset(sOnesCol[:, :], 1.0))
        p.dvx(dv, dv.memset(sEpsR[:, :], LNEPS))

        for tt in range(NTILE):
            tsl = slice(tt * TT, (tt + 1) * TT)
            p.wait(sy, "pe", "dv", "ac")
            p.inc(sy.dma_start(out=sXo[:, :].rearrange("p (c t) -> p c t", c=8),
                               in_=xo_r[:, :, tsl]), "ld", 16)
            p.inc(sy.dma_start(out=sAcc[:, :].rearrange("p (c t) -> p c t", c=8),
                               in_=pp_r[0][:, :, tsl]), "ld", 16)
            for j in range(1, 4):
                if j > 1:
                    p.wait(sy, "dv")
                p.inc(sy.dma_start(out=sP1[:, :].rearrange("p (c t) -> p c t", c=8),
                                   in_=pp_r[j][:, :, tsl]), "ld", 16)
                p.wait(dv, "ld")
                p.dvx(dv, dv.tensor_tensor(sAcc[:, :], sAcc[:, :], sP1[:, :], OP.add))
            p.wait(dv, "ld")
            p.dvx(dv, dv.scalar_tensor_tensor(sRaw[:, :], sXo[:, :], 2.0, sAcc[:, :],
                                              OP.mult, OP.add))
            p.wait(pe, "dv")
            for c in range(8):
                p.inc(pe.matmul(bankp(0, 0, 1, TT), sOnesCol[:, :],
                                sRaw[:, c * TT:(c + 1) * TT],
                                start=(c == 0), stop=(c == 7)), "pe")
            p.wait(ac, "dv")
            for c in range(8):
                if c > 0:
                    p.wait(ac, "pe")
                p.acx(ac, ac.activation(out=sSq[:, :], in_=sRaw[:, c * TT:(c + 1) * TT],
                                        func=AF.Square))
                p.wait(pe, "ac")
                p.inc(pe.matmul(bankp(1, 0, 1, TT), sOnesCol[:, :], sSq[:, :],
                                start=(c == 0), stop=(c == 7)), "pe")
            p.wait(dv, "pe")
            p.dvx(dv, dv.tensor_scalar(sR1[:, :], bankp(0, 0, 1, TT), 1.0 / D_MODEL,
                                       None, OP.mult))
            p.dvx(dv, dv.tensor_scalar(sR2[:, :], bankp(1, 0, 1, TT), 1.0 / D_MODEL,
                                       None, OP.mult))
            p.dvx(dv, dv.tensor_tensor(sR3[:, :], sR1[:, :], sR1[:, :], OP.mult))
            p.dvx(dv, dv.tensor_tensor(sR2[:, :], sR2[:, :], sR3[:, :], OP.subtract))
            p.wait(ac, "dv")
            p.acx(ac, ac.activation(out=sR3[:, :], in_=sR2[:, :], func=AF.Ln, bias=sEpsR[:, :]))
            p.acx(ac, ac.activation(out=sR2[:, :], in_=sR3[:, :], func=AF.Exp, scale=-0.5))
            p.wait(dv, "ac")
            p.dvx(dv, dv.scalar_tensor_tensor(sR3[:, :], sR1[:, :], -1.0, sR2[:, :],
                                              OP.mult, OP.mult))
            p.wait(pe, "dv")
            p.inc(pe.matmul(bank(2, TT), sOnes[:, :], sR2[:, :], start=True, stop=True), "pe")
            p.inc(pe.matmul(bank(3, TT), sOnes[:, :], sR3[:, :], start=True, stop=True), "pe")
            p.wait(dv, "pe", "st")
            for c in range(8):
                csl = slice(c * TT, (c + 1) * TT)
                p.dvx(dv, dv.tensor_tensor(sX1[:, csl], sRaw[:, csl], bank(2, TT), OP.mult))
                p.dvx(dv, dv.tensor_tensor(sX1[:, csl], sX1[:, csl], bank(3, TT), OP.add))
                p.dvx(dv, dv.tensor_copy(sX1b[:, csl], sX1[:, csl]))
            p.wait(sy, "dv")
            p.inc(sy.dma_start(out=x1f_r[:, :, tsl],
                               in_=sX1[:, :].rearrange("p (c t) -> p c t", c=8)), "st", 16)
            p.inc(sy.dma_start(out=x1b_r[:, :, tsl],
                               in_=sX1b[:, :].rearrange("p (c t) -> p c t", c=8)), "st", 16)
            for tg in range(TT // 128):
                p.wait(pe, "dv", "ac")
                for k in range(8):
                    p.inc(pe.matmul(bankp(4, 0, 128, 8), sX1[:, k * TT + tg * 128: k * TT + (tg + 1) * 128],
                                    sWg[:, k * 8:(k + 1) * 8],
                                    start=(k == 0), stop=(k == 7)), "pe")
                p.wait(dv, "pe")
                p.dvx(dv, dv.reduce_max(sMx[:, :], bankp(4, 0, 128, 8), axis=mybir.AxisListType.X))
                p.dvx(dv, dv.tensor_scalar(sMx[:, :], sMx[:, :], -1.0, None, OP.mult))
                p.wait(ac, "dv")
                p.acx(ac, ac.activation(out=sScE[:, :], in_=bankp(4, 0, 128, 8), func=AF.Exp,
                                        bias=sMx[:, :]))
                p.wait(dv, "ac")
                p.dvx(dv, dv.reduce_sum(sRec[:, :], sScE[:, :], axis=mybir.AxisListType.X))
                p.dvx(dv, dv.reciprocal(sRec[:, :], sRec[:, :]))
                p.dvx(dv, dv.tensor_scalar(sSc[:, :], sScE[:, :], sRec[:, :], None, OP.mult))
                p.wait(pe, "dv")
                p.inc(pe.transpose(bankp(5, 0, 8, 128), sSc[:, :], sEyeF[:, :]), "pe")
                p.wait(dv, "pe")
                if tg == 0:
                    p.wait(dv, "st")
                p.dvx(dv, dv.tensor_copy(sScT[0:8, tg * 128:(tg + 1) * 128], bankp(5, 0, 8, 128)))
            p.wait(sy, "dv")
            p.inc(sy.dma_start(out=sct[:, tsl], in_=sScT[0:8, :]), "st", 16)
        if dbg:
            local = dict(sRaw=sRaw, sR1=sR1, sR2=sR2, sR3=sR3, sAcc=sAcc, sXo=sXo, sSq=sSq)
            p.wait(gp, "dv", "ac", "pe")
            for n, ap in dbgB.items():
                p.inc(gp.dma_start(out=ap[:, :], in_=local[n][:, :]), "st", 16)
        p.wait(gp, "st")
        p.wait(gp, "ld")
    return nc


def prep_B_inputs(x_full, poutsA, wg):
    consts = _consts()
    in_maps = []
    for cid in range(N_CORES):
        b, q = cid // 4, cid % 4
        tsl = slice(2048 * q, 2048 * q + 2048)
        m = dict(
            xo=f32(x_full[b].T[:, tsl]),
            wg=f32(wg),
            **consts,
        )
        for j in range(4):
            m[f"pp{j}"] = f32(poutsA[4 * b + j][:, tsl])
        in_maps.append(m)
    return in_maps



def build_launch_C():
    nc = bass.Bass("TRN2", target_bir_lowering=False)
    TT = 512
    NTOK = 16384
    NTILE = NTOK // TT
    NITER = 42
    CAP = 4096.0

    x1all = nc.dram_tensor("x1all", [D_MODEL, NTOK], BF16, kind="ExternalInput").ap()
    srow = nc.dram_tensor("srow", [128, 128], F32, kind="ExternalInput").ap()
    w1 = nc.dram_tensor("w1", [D_MODEL, 2048], F32, kind="ExternalInput").ap()
    w3 = nc.dram_tensor("w3", [D_MODEL, 2048], F32, kind="ExternalInput").ap()
    w2 = nc.dram_tensor("w2", [2048, D_MODEL], F32, kind="ExternalInput").ap()
    triu1 = nc.dram_tensor("triu1", [128, 128], F32, kind="ExternalInput").ap()
    onesr, trilc, eyeb, eyef = _add_const_tensors(nc)
    eo = nc.dram_tensor("eo", [D_MODEL, NTOK], BF16, kind="ExternalOutput").ap()
    gateD = nc.dram_tensor("gateD", [128, 128], F32, kind="Internal").ap()

    x1_r = x1all.rearrange("(c p) t -> p c t", p=128)
    eo_r = eo.rearrange("(c p) t -> p c t", p=128)
    gateD_f = gateD.rearrange("p f -> (p f)")

    with ExitStack() as st:
        sb = lambda name, shape, dt=F32: st.enter_context(nc.sbuf_tensor(name, shape, dt))
        sW1 = sb("sW1", [128, 8 * 2048], BF16)
        sW3 = sb("sW3", [128, 8 * 2048], BF16)
        sW2 = sb("sW2", [128, 16 * 1024], BF16)
        sGrid = sb("sGrid", [128, 128])
        sLo = sb("sLo", [128, 1])
        sHi = sb("sHi", [128, 1])
        sMid = sb("sMid", [128, 1])
        sCnt = sb("sCnt", [128, 1])
        sSel = sb("sSel", [128, 1])
        sT1 = sb("sT1", [128, 1])
        sT2 = sb("sT2", [128, 1])
        sR11 = sb("sR11", [1, 1])
        sTrash = sb("sTrash", [128, 128], BF16)
        sEq = sb("sEq", [128, 128])
        sRk = sb("sRk", [128, 128])
        sRk2 = sb("sRk2", [128, 128])
        sMask = sb("sMask", [128, 128])
        sGate = sb("sGate", [128, 128])
        sTriu1 = sb("sTriu1", [128, 128])
        sOnes = sb("sOnes", [1, 128])
        sOnesCol = sb("sOnesCol", [128, 1])
        sGrow = sb("sGrow", [1, TT])
        sGb = sb("sGb", [128, TT], BF16)
        sXt = sb("sXt", [128, 8 * TT], BF16)
        sRt = sb("sRt", [128, 8 * TT], BF16)
        sGs = sb("sGs", [128, TT], BF16)
        sH1 = sb("sH1", [128, 16 * TT], BF16)
        sEo = sb("sEo", [128, 8 * TT], BF16)

        ppool = st.enter_context(nc.psum_tensor("ppool", [128, 4096], F32))

        def bank(i, w=512, off=0):
            return ppool[:, i * 512 + off: i * 512 + off + w]

        def bankp(i, p0, prows, w, off=0):
            return ppool[p0:p0 + prows, i * 512 + off: i * 512 + off + w]

        p = P(nc, st, ["ld", "gld", "pe", "dv", "ac", "st"])
        sy, gp, pe, dv, ac = nc.sync, nc.gpsimd, nc.tensor, nc.vector, nc.scalar

        p.inc(sy.dma_start(out=sGrid[:, :], in_=srow[:, :]), "ld", 16)
        p.inc(sy.dma_start(out=sTriu1[:, :], in_=triu1[:, :]), "ld", 16)
        p.inc(sy.dma_start(out=sOnes[:, :], in_=onesr[:, :]), "ld", 16)
        p.inc(gp.dma_start(out=sW1[:, :].rearrange("p (c m) -> p c m", c=8),
                           in_=w1.rearrange("(c p) m -> p c m", p=128)), "gld", 16)
        p.inc(gp.dma_start(out=sW3[:, :].rearrange("p (c m) -> p c m", c=8),
                           in_=w3.rearrange("(c p) m -> p c m", p=128)), "gld", 16)
        p.inc(gp.dma_start(out=sW2[:, :].rearrange("p (c m) -> p c m", c=16),
                           in_=w2.rearrange("(c p) m -> p c m", p=128)), "gld", 16)

        p.wait(dv, "ld")
        p.dvx(dv, dv.memset(sOnesCol[:, :], 1.0))
        p.dvx(dv, dv.memset(sLo[:, :], 0.0))
        p.dvx(dv, dv.memset(sHi[:, :], 2.0))

        for it in range(NITER):
            p.dvx(dv, dv.tensor_scalar(sMid[:, :], sLo[:, :], sHi[:, :], 0.5,
                                       OP.add, OP.mult))
            p.dvx(dv, dv.tensor_scalar(sTrash[:, :], sGrid[:, :], sMid[:, :], 0.0,
                                       OP.is_ge, OP.add, accum_out=sCnt[:, :]))
            p.wait(pe, "dv")
            p.inc(pe.matmul(bankp(0, 0, 1, 1), sCnt[:, :], sOnesCol[:, :],
                            start=True, stop=True), "pe")
            p.wait(dv, "pe")
            p.dvx(dv, dv.tensor_copy(sR11[:, :], bankp(0, 0, 1, 1)))
            p.wait(pe, "dv")
            p.inc(pe.matmul(bankp(1, 0, 128, 1), sOnes[:, :], sR11[:, :],
                            start=True, stop=True), "pe")
            p.wait(dv, "pe")
            p.dvx(dv, dv.tensor_scalar(sSel[:, :], bankp(1, 0, 128, 1), CAP, None, OP.is_ge))
            p.dvx(dv, dv.tensor_tensor(sT1[:, :], sMid[:, :], sLo[:, :], OP.subtract))
            p.dvx(dv, dv.tensor_tensor(sT1[:, :], sT1[:, :], sSel[:, :], OP.mult))
            p.dvx(dv, dv.tensor_tensor(sLo[:, :], sLo[:, :], sT1[:, :], OP.add))
            p.dvx(dv, dv.tensor_tensor(sT2[:, :], sHi[:, :], sMid[:, :], OP.subtract))
            p.dvx(dv, dv.tensor_tensor(sT2[:, :], sT2[:, :], sSel[:, :], OP.mult))
            p.dvx(dv, dv.tensor_tensor(sHi[:, :], sMid[:, :], sT2[:, :], OP.add))
        p.dvx(dv, dv.tensor_scalar(sMask[:, :], sGrid[:, :], sLo[:, :], 0.0,
                                   OP.is_gt, OP.add, accum_out=sCnt[:, :]))
        p.wait(pe, "dv")
        p.inc(pe.matmul(bankp(0, 0, 1, 1), sCnt[:, :], sOnesCol[:, :],
                        start=True, stop=True), "pe")
        p.wait(dv, "pe")
        p.dvx(dv, dv.tensor_scalar(sR11[:, :], bankp(0, 0, 1, 1), -1.0, CAP,
                                   OP.mult, OP.add))
        p.wait(pe, "dv")
        p.inc(pe.matmul(bankp(1, 0, 128, 1), sOnes[:, :], sR11[:, :],
                        start=True, stop=True), "pe")
        p.wait(dv, "pe")
        p.dvx(dv, dv.tensor_copy(sT1[:, :], bankp(1, 0, 128, 1)))
        p.dvx(dv, dv.tensor_scalar(sEq[:, :], sGrid[:, :], sLo[:, :], 0.0,
                                   OP.is_equal, OP.add, accum_out=sCnt[:, :]))
        p.dvx(dv, dv.tensor_tensor_scan(sRk[:, :], sEq[:, :], sEq[:, :], 0.0,
                                        OP.add, OP.bypass))
        p.dvx(dv, dv.tensor_tensor(sRk[:, :], sRk[:, :], sEq[:, :], OP.subtract))
        p.wait(pe, "dv")
        p.inc(pe.matmul(bankp(2, 0, 128, 1), sTriu1[:, :], sCnt[:, :],
                        start=True, stop=True), "pe")
        p.wait(dv, "pe")
        p.dvx(dv, dv.tensor_scalar(sRk2[:, :], sRk[:, :], bankp(2, 0, 128, 1), None, OP.add))
        p.dvx(dv, dv.tensor_scalar(sRk2[:, :], sRk2[:, :], sT1[:, :], None, OP.is_lt))
        p.dvx(dv, dv.tensor_tensor(sRk2[:, :], sRk2[:, :], sEq[:, :], OP.mult))
        p.dvx(dv, dv.tensor_tensor(sMask[:, :], sMask[:, :], sRk2[:, :], OP.add))
        p.dvx(dv, dv.tensor_tensor(sGate[:, :], sMask[:, :], sGrid[:, :], OP.mult))
        p.wait(sy, "dv")
        p.inc(sy.dma_start(out=gateD[:, :], in_=sGate[:, :]), "st", 16)
        p.wait(sy, "st")

        p.wait(dv, "gld")
        for tt in range(NTILE):
            tsl = slice(tt * TT, (tt + 1) * TT)
            p.wait(sy, "pe", "dv", "ac")
            p.inc(sy.dma_start(out=sXt[:, :].rearrange("p (c t) -> p c t", c=8),
                               in_=x1_r[:, :, tsl]), "ld", 16)
            p.inc(sy.dma_start(out=sGrow[:, :], in_=gateD_f[tt * TT:(tt + 1) * TT].unsqueeze(0)),
                  "ld", 16)
            p.wait(pe, "ld", "dv")
            p.inc(pe.matmul(bank(3, TT), sOnes[:, :], sGrow[:, :], start=True, stop=True), "pe")
            p.wait(dv, "pe", "ld")
            p.dvx(dv, dv.tensor_copy(sGb[:, :], bank(3, TT)))
            for c in range(8):
                csl = slice(c * TT, (c + 1) * TT)
                p.dvx(dv, dv.tensor_tensor(sRt[:, csl], sXt[:, csl], sGb[:, :], OP.mult))
            for m in range(16):
                p.wait(pe, "dv", "ac")
                for k in range(8):
                    p.inc(pe.matmul(bank(0), sW1[:, k * 2048 + m * 128: k * 2048 + (m + 1) * 128],
                                    sRt[:, k * TT:(k + 1) * TT],
                                    start=(k == 0), stop=(k == 7)), "pe")
                for k in range(8):
                    p.inc(pe.matmul(bank(1), sW3[:, k * 2048 + m * 128: k * 2048 + (m + 1) * 128],
                                    sRt[:, k * TT:(k + 1) * TT],
                                    start=(k == 0), stop=(k == 7)), "pe")
                p.wait(ac, "pe", "dv")
                p.acx(ac, ac.activation(out=sGs[:, :], in_=bank(0), func=AF.Silu))
                p.wait(dv, "ac")
                p.dvx(dv, dv.tensor_tensor(sH1[:, m * TT:(m + 1) * TT], sGs[:, :],
                                           bank(1), OP.mult))
            for dm in range(8):
                p.wait(pe, "dv")
                for k in range(16):
                    p.inc(pe.matmul(bank(2), sW2[:, k * 1024 + dm * 128: k * 1024 + (dm + 1) * 128],
                                    sH1[:, k * TT:(k + 1) * TT],
                                    start=(k == 0), stop=(k == 15)), "pe")
                p.wait(dv, "pe")
                if dm == 0:
                    p.wait(dv, "st")
                p.dvx(dv, dv.tensor_copy(sEo[:, dm * TT:(dm + 1) * TT], bank(2)))
            p.wait(sy, "dv")
            p.inc(sy.dma_start(out=eo_r[:, :, tsl],
                               in_=sEo[:, :].rearrange("p (c t) -> p c t", c=8)), "st", 16)
        p.wait(gp, "st")
        p.wait(gp, "ld")
    return nc


def prep_C_inputs(x1bs, scores, w1, w3, w2):
    consts = _consts()
    x1all = np.ascontiguousarray(np.concatenate(x1bs, axis=1))
    in_maps = []
    for e in range(N_CORES):
        m = dict(
            x1all=x1all,
            srow=f32(np.ascontiguousarray(scores[:, e]).reshape(128, 128)),
            w1=f32(w1[e]),
            w3=f32(w3[e]),
            w2=f32(w2[e]),
            triu1=np.triu(np.ones((128, 128), np.float32), 1),
            **consts,
        )
        in_maps.append(m)
    return in_maps



def build_launch_D():
    nc = bass.Bass("TRN2", target_bir_lowering=False)
    TT = 512
    NTILE = TPC // TT

    x1o = nc.dram_tensor("x1o", [D_MODEL, TPC], F32, kind="ExternalInput").ap()
    eos = [nc.dram_tensor(f"eo{j}", [D_MODEL, TPC], BF16, kind="ExternalInput").ap()
           for j in range(8)]
    onesr, trilc, eyeb, eyef = _add_const_tensors(nc)
    x2f = nc.dram_tensor("x2f", [D_MODEL, TPC], F32, kind="ExternalOutput").ap()
    x2b = nc.dram_tensor("x2b", [D_MODEL, TPC], BF16, kind="ExternalOutput").ap()

    x1_r = x1o.rearrange("(c p) t -> p c t", p=128)
    eo_r = [q.rearrange("(c p) t -> p c t", p=128) for q in eos]
    x2f_r = x2f.rearrange("(c p) t -> p c t", p=128)
    x2b_r = x2b.rearrange("(c p) t -> p c t", p=128)

    with ExitStack() as st:
        sb = lambda name, shape, dt=F32: st.enter_context(nc.sbuf_tensor(name, shape, dt))
        sAcc = sb("sAcc", [128, 8 * TT])
        sP1 = sb("sP1", [128, 8 * TT], BF16)
        sRaw = sb("sRaw", [128, 8 * TT])
        sSq = sb("sSq", [128, TT])
        sR1 = sb("sR1", [1, TT])
        sR2 = sb("sR2", [1, TT])
        sR3 = sb("sR3", [1, TT])
        sX2 = sb("sX2", [128, 8 * TT])
        sX2b = sb("sX2b", [128, 8 * TT], BF16)
        sOnes = sb("sOnes", [1, 128])
        sOnesCol = sb("sOnesCol", [128, 1])
        sEpsR = sb("sEpsR", [1, 1])

        ppool = st.enter_context(nc.psum_tensor("ppool", [128, 4096], F32))

        def bank(i, w=512, off=0):
            return ppool[:, i * 512 + off: i * 512 + off + w]

        def bankp(i, p0, prows, w, off=0):
            return ppool[p0:p0 + prows, i * 512 + off: i * 512 + off + w]

        p = P(nc, st, ["ld", "pe", "dv", "ac", "st"])
        sy, gp, pe, dv, ac = nc.sync, nc.gpsimd, nc.tensor, nc.vector, nc.scalar

        p.inc(sy.dma_start(out=sOnes[:, :], in_=onesr[:, :]), "ld", 16)
        p.wait(dv, "ld")
        p.dvx(dv, dv.memset(sOnesCol[:, :], 1.0))
        p.dvx(dv, dv.memset(sEpsR[:, :], LNEPS))

        for tt in range(NTILE):
            tsl = slice(tt * TT, (tt + 1) * TT)
            p.wait(sy, "pe", "dv", "ac")
            p.inc(sy.dma_start(out=sAcc[:, :].rearrange("p (c t) -> p c t", c=8),
                               in_=x1_r[:, :, tsl]), "ld", 16)
            for j in range(8):
                if j > 0:
                    p.wait(sy, "dv")
                p.inc(sy.dma_start(out=sP1[:, :].rearrange("p (c t) -> p c t", c=8),
                                   in_=eo_r[j][:, :, tsl]), "ld", 16)
                p.wait(dv, "ld")
                p.dvx(dv, dv.tensor_tensor(sAcc[:, :], sAcc[:, :], sP1[:, :], OP.add))
            p.dvx(dv, dv.tensor_copy(sRaw[:, :], sAcc[:, :]))
            p.wait(pe, "dv")
            for c in range(8):
                p.inc(pe.matmul(bankp(0, 0, 1, TT), sOnesCol[:, :],
                                sRaw[:, c * TT:(c + 1) * TT],
                                start=(c == 0), stop=(c == 7)), "pe")
            p.wait(ac, "dv")
            for c in range(8):
                if c > 0:
                    p.wait(ac, "pe")
                p.acx(ac, ac.activation(out=sSq[:, :], in_=sRaw[:, c * TT:(c + 1) * TT],
                                        func=AF.Square))
                p.wait(pe, "ac")
                p.inc(pe.matmul(bankp(1, 0, 1, TT), sOnesCol[:, :], sSq[:, :],
                                start=(c == 0), stop=(c == 7)), "pe")
            p.wait(dv, "pe")
            p.dvx(dv, dv.tensor_scalar(sR1[:, :], bankp(0, 0, 1, TT), 1.0 / D_MODEL,
                                       None, OP.mult))
            p.dvx(dv, dv.tensor_scalar(sR2[:, :], bankp(1, 0, 1, TT), 1.0 / D_MODEL,
                                       None, OP.mult))
            p.dvx(dv, dv.tensor_tensor(sR3[:, :], sR1[:, :], sR1[:, :], OP.mult))
            p.dvx(dv, dv.tensor_tensor(sR2[:, :], sR2[:, :], sR3[:, :], OP.subtract))
            p.wait(ac, "dv")
            p.acx(ac, ac.activation(out=sR3[:, :], in_=sR2[:, :], func=AF.Ln, bias=sEpsR[:, :]))
            p.acx(ac, ac.activation(out=sR2[:, :], in_=sR3[:, :], func=AF.Exp, scale=-0.5))
            p.wait(dv, "ac")
            p.dvx(dv, dv.scalar_tensor_tensor(sR3[:, :], sR1[:, :], -1.0, sR2[:, :],
                                              OP.mult, OP.mult))
            p.wait(pe, "dv")
            p.inc(pe.matmul(bank(2, TT), sOnes[:, :], sR2[:, :], start=True, stop=True), "pe")
            p.inc(pe.matmul(bank(3, TT), sOnes[:, :], sR3[:, :], start=True, stop=True), "pe")
            p.wait(dv, "pe", "st")
            for c in range(8):
                csl = slice(c * TT, (c + 1) * TT)
                p.dvx(dv, dv.tensor_tensor(sX2[:, csl], sRaw[:, csl], bank(2, TT), OP.mult))
                p.dvx(dv, dv.tensor_tensor(sX2[:, csl], sX2[:, csl], bank(3, TT), OP.add))
                p.dvx(dv, dv.tensor_copy(sX2b[:, csl], sX2[:, csl]))
            p.wait(sy, "dv")
            p.inc(sy.dma_start(out=x2f_r[:, :, tsl],
                               in_=sX2[:, :].rearrange("p (c t) -> p c t", c=8)), "st", 16)
            p.inc(sy.dma_start(out=x2b_r[:, :, tsl],
                               in_=sX2b[:, :].rearrange("p (c t) -> p c t", c=8)), "st", 16)
        p.wait(gp, "st")
        p.wait(gp, "ld")
    return nc


def prep_D_inputs(x1fs, eos_full):
    consts = _consts()
    in_maps = []
    for cid in range(N_CORES):
        tsl = slice(2048 * cid, 2048 * cid + 2048)
        m = dict(x1o=f32(x1fs[cid]), **consts)
        for j in range(8):
            m[f"eo{j}"] = np.ascontiguousarray(eos_full[j][:, tsl])
        in_maps.append(m)
    return in_maps



LNC16 = 2.772588722239781
EPS16 = 1e-6 / 16.0


def build_launch_E():
    nc = bass.Bass("TRN2", target_bir_lowering=False)
    T = TPC
    NH = 16

    x2bi = nc.dram_tensor("x2bi", [D_MODEL, T], BF16, kind="ExternalInput").ap()
    wqT = nc.dram_tensor("wqT", [D_MODEL, D_MODEL], F32, kind="ExternalInput").ap()
    wkT = nc.dram_tensor("wkT", [D_MODEL, D_MODEL], F32, kind="ExternalInput").ap()
    wvT = nc.dram_tensor("wvT", [D_MODEL, D_MODEL], F32, kind="ExternalInput").ap()
    projT = nc.dram_tensor("projT", [NH, 64, 256], F32, kind="ExternalInput").ap()
    onesr, trilc, eyeb, eyef = _add_const_tensors(nc)
    phiq = nc.dram_tensor("phiq", [T, NH * 256], BF16, kind="ExternalOutput").ap()
    kvp = nc.dram_tensor("kvp", [NH, 2, 128, 65], F32, kind="ExternalOutput").ap()

    x2_r = x2bi.rearrange("(c p) t -> p c t", p=128)
    phiq_r = phiq.rearrange("(g p) m -> p g m", p=128)

    with ExitStack() as st:
        sb = lambda name, shape, dt=F32: st.enter_context(nc.sbuf_tensor(name, shape, dt))
        sX2 = sb("sX2", [128, 8 * T], BF16)
        sVt = sb("sVt", [128, 16 * 1024], BF16)
        sWq = sb("sWq", [128, 8 * 1024], BF16)
        sWk = sb("sWk", [128, 8 * 1024], BF16)
        sProj = sb("sProj", [64, NH * 256], BF16)
        sQh = sb("sQh", [64, T], BF16)
        sKh = sb("sKh", [64, T], BF16)
        sQsq = sb("sQsq", [64, T])
        sPhiQh = sb("sPhiQh", [128, 16 * 256], BF16)
        sPhi = sb("sPhi", [128, 256], BF16)
        sMx = sb("sMx", [128, 1])
        sB1 = sb("sB1", [128, 1])
        sKVo = sb("sKVo", [128, 130])
        sOnesCol = sb("sOnesCol", [128, 1])
        sOnesColB = sb("sOnesColB", [128, 1], BF16)

        ppool = st.enter_context(nc.psum_tensor("ppool", [128, 4096], F32))

        def bank(i, w=512, off=0):
            return ppool[:, i * 512 + off: i * 512 + off + w]

        def bankp(i, p0, prows, w, off=0):
            return ppool[p0:p0 + prows, i * 512 + off: i * 512 + off + w]

        p = P(nc, st, ["ld", "gld", "pe", "dv", "ac", "st"])
        sy, gp, pe, dv, ac = nc.sync, nc.gpsimd, nc.tensor, nc.vector, nc.scalar

        p.inc(sy.dma_start(out=sX2[:, :].rearrange("p (c t) -> p c t", c=8),
                           in_=x2_r[:, :, :]), "ld", 16)
        p.inc(gp.dma_start(out=sProj[:, :].rearrange("p (h m) -> p h m", h=NH),
                           in_=projT.rearrange("h d m -> d h m")), "gld", 16)
        p.inc(gp.dma_start(out=sWq[:, :].rearrange("p (c m) -> p c m", c=8),
                           in_=wvT.rearrange("(c p) m -> p c m", p=128)), "gld", 16)
        p.wait(dv, "ld")
        p.dvx(dv, dv.memset(sOnesCol[:, :], 1.0))
        p.dvx(dv, dv.memset(sOnesColB[:, :], 1.0))
        p.wait(pe, "ld", "gld", "dv")
        for tg in range(16):
            p.wait(pe, "dv")
            for n in range(2):
                for k in range(8):
                    p.inc(pe.matmul(bank(n), sX2[:, k * T + tg * 128: k * T + (tg + 1) * 128],
                                    sWq[:, k * 1024 + n * 512:(k * 1024) + (n + 1) * 512],
                                    start=(k == 0), stop=(k == 7)), "pe")
            p.wait(dv, "pe")
            for n in range(2):
                p.dvx(dv, dv.tensor_copy(sVt[:, tg * 1024 + n * 512: tg * 1024 + (n + 1) * 512],
                                         bank(n)))
        p.wait(gp, "dv")
        p.inc(gp.dma_start(out=sWq[:, :].rearrange("p (c m) -> p c m", c=8),
                           in_=wqT.rearrange("(c p) m -> p c m", p=128)), "gld", 16)
        p.inc(gp.dma_start(out=sWk[:, :].rearrange("p (c m) -> p c m", c=8),
                           in_=wkT.rearrange("(c p) m -> p c m", p=128)), "gld", 16)
        p.wait(pe, "gld")

        for h in range(NH):
            hof = h * 64
            for (buf, w) in ((sQh, sWq), (sKh, sWk)):
                for n in range(4):
                    p.wait(pe, "dv", "ac")
                    for k in range(8):
                        p.inc(pe.matmul(bankp(0, 0, 64, 512), w[:, k * 1024 + hof: k * 1024 + hof + 64],
                                        sX2[:, k * T + n * 512: k * T + (n + 1) * 512],
                                        start=(k == 0), stop=(k == 7)), "pe")
                    p.wait(dv, "pe")
                    p.dvx(dv, dv.tensor_scalar(buf[0:64, n * 512:(n + 1) * 512],
                                               bankp(0, 0, 64, 512), 0.125, None, OP.mult))
            p.wait(ac, "dv")
            p.acx(ac, ac.activation(out=sQsq[0:64, :], in_=sQh[0:64, :], func=AF.Square))
            p.wait(dv, "st")
            for tg in range(16):
                p.wait(pe, "ac", "dv")
                p.inc(pe.matmul(bankp(4, 0, 128, 256), sQh[0:64, tg * 128:(tg + 1) * 128],
                                sProj[0:64, h * 256:(h + 1) * 256],
                                start=True, stop=True), "pe")
                p.inc(pe.matmul(bankp(5, 0, 128, 1), sQsq[0:64, tg * 128:(tg + 1) * 128],
                                sOnesCol[0:64, :], start=True, stop=True), "pe")
                p.wait(dv, "pe")
                p.dvx(dv, dv.reduce_max(sMx[:, :], bankp(4, 0, 128, 256),
                                        axis=mybir.AxisListType.X))
                p.dvx(dv, dv.scalar_tensor_tensor(sB1[:, :], bankp(5, 0, 128, 1), 0.5,
                                                  sMx[:, :], OP.mult, OP.add))
                p.dvx(dv, dv.tensor_scalar(sB1[:, :], sB1[:, :], LNC16, -1.0,
                                           OP.add, OP.mult))
                p.wait(ac, "dv")
                p.acx(ac, ac.activation(out=sPhi[:, :], in_=bankp(4, 0, 128, 256),
                                        func=AF.Exp, bias=sB1[:, :]))
                p.wait(dv, "ac")
                p.dvx(dv, dv.tensor_scalar(sPhiQh[:, tg * 256:(tg + 1) * 256],
                                           sPhi[:, :], EPS16, None, OP.add))
            p.wait(sy, "dv", "st")
            p.inc(sy.dma_start(out=phiq_r[:, :, h * 256:(h + 1) * 256],
                               in_=sPhiQh[:, :].rearrange("p (g m) -> p g m", g=16)), "st", 16)
            p.wait(ac, "dv", "pe")
            p.acx(ac, ac.activation(out=sQsq[0:64, :], in_=sKh[0:64, :], func=AF.Square))
            for tg in range(16):
                p.wait(pe, "ac", "dv")
                p.inc(pe.matmul(bankp(4, 0, 128, 256), sKh[0:64, tg * 128:(tg + 1) * 128],
                                sProj[0:64, h * 256:(h + 1) * 256],
                                start=True, stop=True), "pe")
                p.inc(pe.matmul(bankp(5, 0, 128, 1), sQsq[0:64, tg * 128:(tg + 1) * 128],
                                sOnesCol[0:64, :], start=True, stop=True), "pe")
                p.wait(dv, "pe")
                p.dvx(dv, dv.reduce_max(sMx[:, :], bankp(4, 0, 128, 256),
                                        axis=mybir.AxisListType.X))
                p.dvx(dv, dv.scalar_tensor_tensor(sB1[:, :], bankp(5, 0, 128, 1), 0.5,
                                                  sMx[:, :], OP.mult, OP.add))
                p.dvx(dv, dv.tensor_scalar(sB1[:, :], sB1[:, :], LNC16, -1.0,
                                           OP.add, OP.mult))
                p.wait(ac, "dv")
                p.acx(ac, ac.activation(out=sPhi[:, :], in_=bankp(4, 0, 128, 256),
                                        func=AF.Exp, bias=sB1[:, :]))
                p.wait(dv, "ac")
                p.dvx(dv, dv.tensor_scalar(sPhi[:, :], sPhi[:, :], EPS16, None, OP.add))
                p.wait(pe, "dv")
                for mc in range(2):
                    p.inc(pe.matmul(bankp(mc, 0, 128, 64),
                                    sPhi[:, mc * 128:(mc + 1) * 128],
                                    sVt[:, tg * 1024 + hof: tg * 1024 + hof + 64],
                                    start=(tg == 0), stop=(tg == 15)), "pe")
                    p.inc(pe.matmul(bankp(2 + mc, 0, 128, 1),
                                    sPhi[:, mc * 128:(mc + 1) * 128],
                                    sOnesColB[:, :], start=(tg == 0), stop=(tg == 15)), "pe")
            p.wait(dv, "pe", "st")
            p.dvx(dv, dv.tensor_copy(sKVo[:, 0:64], bankp(0, 0, 128, 64)))
            p.dvx(dv, dv.tensor_copy(sKVo[:, 64:65], bankp(2, 0, 128, 1)))
            p.dvx(dv, dv.tensor_copy(sKVo[:, 65:129], bankp(1, 0, 128, 64)))
            p.dvx(dv, dv.tensor_copy(sKVo[:, 129:130], bankp(3, 0, 128, 1)))
            p.wait(sy, "dv")
            p.inc(sy.dma_start(out=kvp[h].rearrange("k p m -> p k m"),
                               in_=sKVo[:, :].rearrange("p (k m) -> p k m", k=2)), "st", 16)
        p.wait(gp, "st")
        p.wait(gp, "ld")
        p.wait(gp, "gld")
    return nc


def prep_E_inputs(x2bs, Wq, Wk, Wv, proj):
    consts = _consts()
    in_maps = []
    for cid in range(N_CORES):
        m = dict(
            x2bi=np.ascontiguousarray(x2bs[cid]),
            wqT=f32(np.asarray(Wq).T),
            wkT=f32(np.asarray(Wk).T),
            wvT=f32(np.asarray(Wv).T),
            projT=f32(np.transpose(np.asarray(proj), (0, 2, 1))),
            **consts,
        )
        in_maps.append(m)
    return in_maps



def build_launch_F():
    nc = bass.Bass("TRN2", target_bir_lowering=False)
    T = TPC
    NH = 16
    TT = 512

    x2o = nc.dram_tensor("x2o", [D_MODEL, T], F32, kind="ExternalInput").ap()
    phiqi = nc.dram_tensor("phiqi", [T, NH * 256], BF16, kind="ExternalInput").ap()
    kvps = [nc.dram_tensor(f"kvp{j}", [NH, 2, 128, 65], F32, kind="ExternalInput").ap()
            for j in range(4)]
    woT = nc.dram_tensor("woT", [D_MODEL, D_MODEL], F32, kind="ExternalInput").ap()
    bo = nc.dram_tensor("bo", [128, 8], F32, kind="ExternalInput").ap()
    onesr, trilc, eyeb, eyef = _add_const_tensors(nc)
    x3f = nc.dram_tensor("x3f", [D_MODEL, T], F32, kind="ExternalOutput").ap()
    x3b = nc.dram_tensor("x3b", [D_MODEL, T], BF16, kind="ExternalOutput").ap()

    x2_r = x2o.rearrange("(c p) t -> p c t", p=128)
    phiq_r = phiqi.rearrange("(g p) m -> p g m", p=128)
    x3f_r = x3f.rearrange("(c p) t -> p c t", p=128)
    x3b_r = x3b.rearrange("(c p) t -> p c t", p=128)

    with ExitStack() as st:
        sb = lambda name, shape, dt=F32: st.enter_context(nc.sbuf_tensor(name, shape, dt))
        sKV = sb("sKV", [128, NH * 2 * 65])
        sKV2 = sb("sKV2", [128, NH * 2 * 65])
        sKVb = sb("sKVb", [128, NH * 2 * 65], BF16)
        sKrow = sb("sKrow", [1, 256])
        sKb = sb("sKb", [128, 256])
        sPhi = sb("sPhi", [128, 256], BF16)
        sPq = sb("sPq", [128, 256], BF16)
        sPqm = sb("sPqm", [128, 256], BF16)
        sT1 = sb("sT1", [128, 256])
        sDen = sb("sDen", [128, 1])
        sMerged = sb("sMerged", [128, 8 * T], BF16)
        sWo = sb("sWo", [128, 8 * 1024], BF16)
        sBo = sb("sBo", [128, 8])
        sXo = sb("sXo", [128, 8 * TT])
        sRaw = sb("sRaw", [128, 8 * TT])
        sSq = sb("sSq", [128, TT])
        sR1 = sb("sR1", [1, TT])
        sR2 = sb("sR2", [1, TT])
        sR3 = sb("sR3", [1, TT])
        sX3 = sb("sX3", [128, 8 * TT])
        sX3b = sb("sX3b", [128, 8 * TT], BF16)
        sOnes = sb("sOnes", [1, 128])
        sOnesCol = sb("sOnesCol", [128, 1])
        sEyeF = sb("sEyeF", [128, 128])
        sEyeB = sb("sEyeB", [128, 128], BF16)
        sEpsR = sb("sEpsR", [1, 1])

        ppool = st.enter_context(nc.psum_tensor("ppool", [128, 4096], F32))

        def bank(i, w=512, off=0):
            return ppool[:, i * 512 + off: i * 512 + off + w]

        def bankp(i, p0, prows, w, off=0):
            return ppool[p0:p0 + prows, i * 512 + off: i * 512 + off + w]

        p = P(nc, st, ["ld", "gld", "pe", "dv", "ac", "st"])
        sy, gp, pe, dv, ac = nc.sync, nc.gpsimd, nc.tensor, nc.vector, nc.scalar

        p.inc(sy.dma_start(out=sOnes[:, :], in_=onesr[:, :]), "ld", 16)
        p.inc(sy.dma_start(out=sEyeF[:, :], in_=eyef[:, :]), "ld", 16)
        p.inc(sy.dma_start(out=sEyeB[:, :], in_=eyeb[:, :]), "ld", 16)
        p.inc(sy.dma_start(out=sBo[:, :], in_=bo[:, :]), "ld", 16)
        p.inc(sy.dma_start(out=sKV[:, :].rearrange("p (h k m) -> p h k m", h=NH, k=2),
                           in_=kvps[0].rearrange("h k p m -> p h k m")), "ld", 16)
        p.inc(gp.dma_start(out=sWo[:, :].rearrange("p (c m) -> p c m", c=8),
                           in_=woT.rearrange("(c p) m -> p c m", p=128)), "gld", 16)
        p.wait(dv, "ld")
        p.dvx(dv, dv.memset(sOnesCol[:, :], 1.0))
        p.dvx(dv, dv.memset(sEpsR[:, :], LNEPS))
        for j in range(1, 4):
            if j > 1:
                p.wait(sy, "dv")
            p.inc(sy.dma_start(out=sKV2[:, :].rearrange("p (h k m) -> p h k m", h=NH, k=2),
                               in_=kvps[j].rearrange("h k p m -> p h k m")), "ld", 16)
            p.wait(dv, "ld")
            p.dvx(dv, dv.tensor_tensor(sKV[:, :], sKV[:, :], sKV2[:, :], OP.add))
        p.dvx(dv, dv.tensor_copy(sKVb[:, :], sKV[:, :]))

        for h in range(NH):
            hb = h * 130
            p.wait(pe, "dv", "ac")
            p.inc(pe.transpose(bankp(5, 0, 1, 128), sKV[:, hb + 64: hb + 65], sEyeF[:, :]), "pe")
            p.inc(pe.transpose(bankp(5, 0, 1, 128, off=128), sKV[:, hb + 129: hb + 130],
                               sEyeF[:, :]), "pe")
            p.wait(dv, "pe")
            p.dvx(dv, dv.tensor_copy(sKrow[:, 0:128], bankp(5, 0, 1, 128)))
            p.dvx(dv, dv.tensor_copy(sKrow[:, 128:256], bankp(5, 0, 1, 128, off=128)))
            p.wait(pe, "dv")
            p.inc(pe.matmul(bankp(6, 0, 128, 256), sOnes[:, :], sKrow[:, :],
                            start=True, stop=True), "pe")
            p.wait(dv, "pe")
            p.dvx(dv, dv.tensor_copy(sKb[:, :], bankp(6, 0, 128, 256)))
            for tg in range(16):
                p.wait(sy, "dv", "pe")
                p.inc(sy.dma_start(out=sPhi[:, :],
                                   in_=phiq_r[:, tg, h * 256:(h + 1) * 256]), "ld", 16)
                p.wait(dv, "ld")
                p.dvx(dv, dv.tensor_tensor(sT1[:, :], sPhi[:, :], sKb[:, :], OP.mult))
                p.dvx(dv, dv.reduce_sum(sDen[:, :], sT1[:, :], axis=mybir.AxisListType.X))
                p.dvx(dv, dv.tensor_scalar(sDen[:, :], sDen[:, :], 1e-6, None, OP.add))
                p.dvx(dv, dv.reciprocal(sDen[:, :], sDen[:, :]))
                p.dvx(dv, dv.tensor_scalar(sPq[:, :], sPhi[:, :], sDen[:, :], None, OP.mult))
                p.wait(pe, "dv")
                for mc in range(2):
                    p.inc(pe.transpose(bankp(5, 0, 128, 128).bitcast(BF16)[:, 0:128],
                                       sPq[:, mc * 128:(mc + 1) * 128], sEyeB[:, :]), "pe")
                    p.wait(dv, "pe")
                    p.dvx(dv, dv.tensor_copy(sPqm[:, mc * 128:(mc + 1) * 128],
                                             bankp(5, 0, 128, 128).bitcast(BF16)[:, 0:128]))
                    p.wait(pe, "dv")
                p.wait(pe, "dv")
                for mc in range(2):
                    p.inc(pe.matmul(bankp(7, (h % 2) * 64, 64, 128),
                                    sKVb[:, hb + mc * 65: hb + mc * 65 + 64],
                                    sPqm[:, mc * 128:(mc + 1) * 128],
                                    start=(mc == 0), stop=(mc == 1)), "pe")
                p.wait(dv, "pe")
                p.dvx(dv, dv.tensor_copy(
                    sMerged[(h % 2) * 64:(h % 2) * 64 + 64,
                            (h // 2) * T + tg * 128:(h // 2) * T + (tg + 1) * 128],
                    bankp(7, (h % 2) * 64, 64, 128)))

        for tt in range(T // TT):
            tsl = slice(tt * TT, (tt + 1) * TT)
            p.wait(sy, "pe", "dv", "ac")
            p.inc(sy.dma_start(out=sXo[:, :].rearrange("p (c t) -> p c t", c=8),
                               in_=x2_r[:, :, tsl]), "ld", 16)
            p.wait(pe, "dv", "ac")
            for m in range(8):
                for k in range(8):
                    p.inc(pe.matmul(bank(m % 4), sWo[:, k * 1024 + m * 128:(k * 1024) + (m + 1) * 128],
                                    sMerged[:, k * T + tt * TT: k * T + (tt + 1) * TT],
                                    start=(k == 0), stop=(k == 7)), "pe")
                p.wait(dv, "pe", "ld")
                p.dvx(dv, dv.scalar_tensor_tensor(sRaw[:, m * TT:(m + 1) * TT],
                                                  bank(m % 4), sBo[:, m:m + 1],
                                                  sXo[:, m * TT:(m + 1) * TT],
                                                  OP.add, OP.add))
                p.wait(pe, "dv")
            p.wait(pe, "dv")
            for c in range(8):
                p.inc(pe.matmul(bankp(4, 0, 1, TT), sOnesCol[:, :],
                                sRaw[:, c * TT:(c + 1) * TT],
                                start=(c == 0), stop=(c == 7)), "pe")
            p.wait(ac, "dv")
            for c in range(8):
                if c > 0:
                    p.wait(ac, "pe")
                p.acx(ac, ac.activation(out=sSq[:, :], in_=sRaw[:, c * TT:(c + 1) * TT],
                                        func=AF.Square))
                p.wait(pe, "ac")
                p.inc(pe.matmul(bankp(5, 0, 1, TT), sOnesCol[:, :], sSq[:, :],
                                start=(c == 0), stop=(c == 7)), "pe")
            p.wait(dv, "pe")
            p.dvx(dv, dv.tensor_scalar(sR1[:, :], bankp(4, 0, 1, TT), 1.0 / D_MODEL,
                                       None, OP.mult))
            p.dvx(dv, dv.tensor_scalar(sR2[:, :], bankp(5, 0, 1, TT), 1.0 / D_MODEL,
                                       None, OP.mult))
            p.dvx(dv, dv.tensor_tensor(sR3[:, :], sR1[:, :], sR1[:, :], OP.mult))
            p.dvx(dv, dv.tensor_tensor(sR2[:, :], sR2[:, :], sR3[:, :], OP.subtract))
            p.wait(ac, "dv")
            p.acx(ac, ac.activation(out=sR3[:, :], in_=sR2[:, :], func=AF.Ln, bias=sEpsR[:, :]))
            p.acx(ac, ac.activation(out=sR2[:, :], in_=sR3[:, :], func=AF.Exp, scale=-0.5))
            p.wait(dv, "ac")
            p.dvx(dv, dv.scalar_tensor_tensor(sR3[:, :], sR1[:, :], -1.0, sR2[:, :],
                                              OP.mult, OP.mult))
            p.wait(pe, "dv")
            p.inc(pe.matmul(bank(6, TT), sOnes[:, :], sR2[:, :], start=True, stop=True), "pe")
            p.inc(pe.matmul(bank(7, TT), sOnes[:, :], sR3[:, :], start=True, stop=True), "pe")
            p.wait(dv, "pe", "st")
            for c in range(8):
                csl = slice(c * TT, (c + 1) * TT)
                p.dvx(dv, dv.tensor_tensor(sX3[:, csl], sRaw[:, csl], bank(6, TT), OP.mult))
                p.dvx(dv, dv.tensor_tensor(sX3[:, csl], sX3[:, csl], bank(7, TT), OP.add))
                p.dvx(dv, dv.tensor_copy(sX3b[:, csl], sX3[:, csl]))
            p.wait(sy, "dv")
            p.inc(sy.dma_start(out=x3f_r[:, :, tsl],
                               in_=sX3[:, :].rearrange("p (c t) -> p c t", c=8)), "st", 16)
            p.inc(sy.dma_start(out=x3b_r[:, :, tsl],
                               in_=sX3b[:, :].rearrange("p (c t) -> p c t", c=8)), "st", 16)
        p.wait(gp, "st")
        p.wait(gp, "ld")
        p.wait(gp, "gld")
    return nc


def prep_F_inputs(x2fs, phiqs, kvps_all, Wo, bo):
    consts = _consts()
    in_maps = []
    for cid in range(N_CORES):
        b = cid // 4
        m = dict(
            x2o=f32(x2fs[cid]),
            phiqi=np.ascontiguousarray(phiqs[cid]),
            woT=f32(np.asarray(Wo).T),
            bo=f32(np.asarray(bo).reshape(8, 128).T),
            **consts,
        )
        for j in range(4):
            m[f"kvp{j}"] = f32(kvps_all[4 * b + j])
        in_maps.append(m)
    return in_maps



QS = 1.0 / 127.0


def build_launch_G():
    nc = bass.Bass("TRN2", target_bir_lowering=False)
    T = TPC
    TT = 512
    NHID = 3072

    x3fi = nc.dram_tensor("x3fi", [D_MODEL, T], F32, kind="ExternalInput").ap()
    x3bi = nc.dram_tensor("x3bi", [D_MODEL, T], BF16, kind="ExternalInput").ap()
    wgT = nc.dram_tensor("wgT", [D_MODEL, NHID], I32, kind="ExternalInput").ap()
    wuT = nc.dram_tensor("wuT", [D_MODEL, NHID], I32, kind="ExternalInput").ap()
    wd3 = nc.dram_tensor("wd3", [8, NHID, 128], I32, kind="ExternalInput").ap()
    onesr, trilc, eyeb, eyef = _add_const_tensors(nc)
    outf = nc.dram_tensor("outf", [D_MODEL, T], F32, kind="ExternalOutput").ap()

    x3f_r = x3fi.rearrange("(c p) t -> p c t", p=128)
    x3b_r = x3bi.rearrange("(c p) t -> p c t", p=128)
    out_r = outf.rearrange("(c p) t -> p c t", p=128)

    with ExitStack() as st:
        sb = lambda name, shape, dt=F32: st.enter_context(nc.sbuf_tensor(name, shape, dt))
        sWg = sb("sWg", [128, 8 * NHID], BF16)
        sWu = sb("sWu", [128, 8 * NHID], BF16)
        sWdc = sb("sWdc", [128, 24 * 128], BF16)
        sXb = sb("sXb", [128, 8 * TT], BF16)
        sXf = sb("sXf", [128, 8 * TT])
        sGs = sb("sGs", [128, TT], BF16)
        sH1 = sb("sH1", [128, 24 * TT], BF16)
        sRaw = sb("sRaw", [128, 8 * TT])
        sSq = sb("sSq", [128, TT])
        sR1 = sb("sR1", [1, TT])
        sR2 = sb("sR2", [1, TT])
        sR3 = sb("sR3", [1, TT])
        sOut = sb("sOut", [128, 8 * TT])
        sOnes = sb("sOnes", [1, 128])
        sOnesCol = sb("sOnesCol", [128, 1])
        sEpsR = sb("sEpsR", [1, 1])

        ppool = st.enter_context(nc.psum_tensor("ppool", [128, 4096], F32))

        def bank(i, w=512, off=0):
            return ppool[:, i * 512 + off: i * 512 + off + w]

        def bankp(i, p0, prows, w, off=0):
            return ppool[p0:p0 + prows, i * 512 + off: i * 512 + off + w]

        p = P(nc, st, ["ld", "gld", "pe", "dv", "ac", "st"])
        sy, gp, pe, dv, ac = nc.sync, nc.gpsimd, nc.tensor, nc.vector, nc.scalar

        p.inc(sy.dma_start(out=sOnes[:, :], in_=onesr[:, :]), "ld", 16)
        p.inc(gp.dma_start(out=sWg[:, :].rearrange("p (c m) -> p c m", c=8),
                           in_=wgT.rearrange("(c p) m -> p c m", p=128)), "gld", 16)
        p.inc(gp.dma_start(out=sWu[:, :].rearrange("p (c m) -> p c m", c=8),
                           in_=wuT.rearrange("(c p) m -> p c m", p=128)), "gld", 16)
        p.wait(dv, "ld")
        p.dvx(dv, dv.memset(sOnesCol[:, :], 1.0))
        p.dvx(dv, dv.memset(sEpsR[:, :], LNEPS))
        p.wait(pe, "gld", "dv")

        for tt in range(T // TT):
            tsl = slice(tt * TT, (tt + 1) * TT)
            p.wait(sy, "pe", "dv", "ac")
            p.inc(sy.dma_start(out=sXb[:, :].rearrange("p (c t) -> p c t", c=8),
                               in_=x3b_r[:, :, tsl]), "ld", 16)
            p.inc(sy.dma_start(out=sXf[:, :].rearrange("p (c t) -> p c t", c=8),
                               in_=x3f_r[:, :, tsl]), "ld", 16)
            p.wait(pe, "ld")
            for m in range(24):
                p.wait(pe, "dv", "ac")
                for k in range(8):
                    p.inc(pe.matmul(bank(0), sWg[:, k * NHID + m * 128: k * NHID + (m + 1) * 128],
                                    sXb[:, k * TT:(k + 1) * TT],
                                    start=(k == 0), stop=(k == 7)), "pe")
                for k in range(8):
                    p.inc(pe.matmul(bank(1), sWu[:, k * NHID + m * 128: k * NHID + (m + 1) * 128],
                                    sXb[:, k * TT:(k + 1) * TT],
                                    start=(k == 0), stop=(k == 7)), "pe")
                p.wait(ac, "pe", "dv")
                p.acx(ac, ac.activation(out=sGs[:, :], in_=bank(0), func=AF.Silu, scale=QS))
                p.wait(dv, "ac")
                p.dvx(dv, dv.scalar_tensor_tensor(sH1[:, m * TT:(m + 1) * TT],
                                                  bank(1), QS, sGs[:, :],
                                                  OP.mult, OP.mult))
            for dm in range(8):
                p.wait(gp, "pe")
                p.inc(gp.dma_start(out=sWdc[:, :].rearrange("p (c m) -> p c m", c=24),
                                   in_=wd3[dm].rearrange("(c p) m -> p c m", p=128)), "gld", 16)
                p.wait(pe, "gld", "dv")
                for k in range(24):
                    p.inc(pe.matmul(bank(2), sWdc[:, k * 128:(k + 1) * 128],
                                    sH1[:, k * TT:(k + 1) * TT],
                                    start=(k == 0), stop=(k == 23)), "pe")
                p.wait(dv, "pe", "ld")
                p.dvx(dv, dv.scalar_tensor_tensor(sRaw[:, dm * TT:(dm + 1) * TT],
                                                  bank(2), QS, sXf[:, dm * TT:(dm + 1) * TT],
                                                  OP.mult, OP.add))
            p.wait(pe, "dv")
            for c in range(8):
                p.inc(pe.matmul(bankp(4, 0, 1, TT), sOnesCol[:, :],
                                sRaw[:, c * TT:(c + 1) * TT],
                                start=(c == 0), stop=(c == 7)), "pe")
            p.wait(ac, "dv")
            for c in range(8):
                if c > 0:
                    p.wait(ac, "pe")
                p.acx(ac, ac.activation(out=sSq[:, :], in_=sRaw[:, c * TT:(c + 1) * TT],
                                        func=AF.Square))
                p.wait(pe, "ac")
                p.inc(pe.matmul(bankp(5, 0, 1, TT), sOnesCol[:, :], sSq[:, :],
                                start=(c == 0), stop=(c == 7)), "pe")
            p.wait(dv, "pe")
            p.dvx(dv, dv.tensor_scalar(sR1[:, :], bankp(4, 0, 1, TT), 1.0 / D_MODEL,
                                       None, OP.mult))
            p.dvx(dv, dv.tensor_scalar(sR2[:, :], bankp(5, 0, 1, TT), 1.0 / D_MODEL,
                                       None, OP.mult))
            p.dvx(dv, dv.tensor_tensor(sR3[:, :], sR1[:, :], sR1[:, :], OP.mult))
            p.dvx(dv, dv.tensor_tensor(sR2[:, :], sR2[:, :], sR3[:, :], OP.subtract))
            p.wait(ac, "dv")
            p.acx(ac, ac.activation(out=sR3[:, :], in_=sR2[:, :], func=AF.Ln, bias=sEpsR[:, :]))
            p.acx(ac, ac.activation(out=sR2[:, :], in_=sR3[:, :], func=AF.Exp, scale=-0.5))
            p.wait(dv, "ac")
            p.dvx(dv, dv.scalar_tensor_tensor(sR3[:, :], sR1[:, :], -1.0, sR2[:, :],
                                              OP.mult, OP.mult))
            p.wait(pe, "dv")
            p.inc(pe.matmul(bank(6, TT), sOnes[:, :], sR2[:, :], start=True, stop=True), "pe")
            p.inc(pe.matmul(bank(7, TT), sOnes[:, :], sR3[:, :], start=True, stop=True), "pe")
            p.wait(dv, "pe", "st")
            for c in range(8):
                csl = slice(c * TT, (c + 1) * TT)
                p.dvx(dv, dv.tensor_tensor(sOut[:, csl], sRaw[:, csl], bank(6, TT), OP.mult))
                p.dvx(dv, dv.tensor_tensor(sOut[:, csl], sOut[:, csl], bank(7, TT), OP.add))
            p.wait(sy, "dv")
            p.inc(sy.dma_start(out=out_r[:, :, tsl],
                               in_=sOut[:, :].rearrange("p (c t) -> p c t", c=8)), "st", 16)
        p.wait(gp, "st")
        p.wait(gp, "ld")
    return nc


def prep_G_inputs(x3fs, x3bs, w_gate, w_up, w_down):
    consts = _consts()
    wgT = np.ascontiguousarray(np.asarray(w_gate).T.astype(np.int32))
    wuT = np.ascontiguousarray(np.asarray(w_up).T.astype(np.int32))
    wd = np.asarray(w_down)
    wd3 = np.ascontiguousarray(wd.reshape(8, 128, 3072).transpose(0, 2, 1).astype(np.int32))
    in_maps = []
    for cid in range(N_CORES):
        m = dict(
            x3fi=f32(x3fs[cid]),
            x3bi=np.ascontiguousarray(x3bs[cid]),
            wgT=wgT, wuT=wuT, wd3=wd3,
            **consts,
        )
        in_maps.append(m)
    return in_maps



def kernel(**inputs):
    run = lambda nc, maps: run_bass_kernel_spmd(nc, maps, core_ids=list(range(N_CORES))).results

    resA = run(build_launch_A(),
               prep_A_inputs(inputs["L0_attn_x"], inputs["L0_attn_norm_w"],
                             inputs["L0_attn_in_proj_w"], inputs["L0_attn_conv_w"],
                             inputs["L0_attn_conv_b"], inputs["L0_attn_A_log"],
                             inputs["L0_attn_D"], inputs["L0_attn_dt_bias"],
                             inputs["L0_attn_out_proj_w"]))
    pouts = [resA[c]["pout"] for c in range(N_CORES)]

    resB = run(build_launch_B(), prep_B_inputs(inputs["L0_attn_x"], pouts, inputs["L0_ffn_wg"]))
    x1fs = [resB[c]["x1f"] for c in range(N_CORES)]
    x1bs = [resB[c]["x1b"] for c in range(N_CORES)]
    scores = np.concatenate([resB[c]["sct"] for c in range(N_CORES)], axis=1).T

    resC = run(build_launch_C(),
               prep_C_inputs(x1bs, scores, inputs["L0_ffn_w1"], inputs["L0_ffn_w3"],
                             inputs["L0_ffn_w2"]))
    eos_full = [resC[e]["eo"] for e in range(N_CORES)]

    resD = run(build_launch_D(), prep_D_inputs(x1fs, eos_full))
    x2fs = [resD[c]["x2f"] for c in range(N_CORES)]
    x2bs = [resD[c]["x2b"] for c in range(N_CORES)]

    resE = run(build_launch_E(),
               prep_E_inputs(x2bs, inputs["L1_attn_Wq"], inputs["L1_attn_Wk"],
                             inputs["L1_attn_Wv"], inputs["L1_attn_proj"]))
    phiqs = [resE[c]["phiq"] for c in range(N_CORES)]
    kvps = [resE[c]["kvp"] for c in range(N_CORES)]

    resF = run(build_launch_F(),
               prep_F_inputs(x2fs, phiqs, kvps, inputs["L1_attn_Wo"], inputs["L1_attn_bo"]))
    x3fs = [resF[c]["x3f"] for c in range(N_CORES)]
    x3bs = [resF[c]["x3b"] for c in range(N_CORES)]

    resG = run(build_launch_G(),
               prep_G_inputs(x3fs, x3bs, inputs["L1_ffn_w_gate"], inputs["L1_ffn_w_up"],
                             inputs["L1_ffn_w_down"]))
    out = np.zeros((2, 8192, 1024), np.float32)
    for cid in range(N_CORES):
        b, q = cid // 4, cid % 4
        out[b, 2048 * q: 2048 * q + 2048] = resG[cid]["outf"].T
    return out


# version 02
# speedup vs baseline: 1.4712x; 1.4712x over previous
from contextlib import ExitStack

import numpy as np
import ml_dtypes

import concourse.bass as bass
import concourse.mybir as mybir
from concourse.alu_op_type import AluOpType
from concourse.bass_utils import run_bass_kernel_spmd

F32 = mybir.dt.float32
BF16 = mybir.dt.bfloat16
I32 = mybir.dt.int32
U32 = mybir.dt.uint32
AF = mybir.ActivationFunctionType
OP = AluOpType

BF = ml_dtypes.bfloat16

D_MODEL = 1024
NB = 2
T_FULL = 8192
N_CORES = 8
TPC = 2048

K_CONV = 4
MEPS = 1e-6
LNEPS = 1e-5


class P:

    def __init__(self, nc, stack, names):
        self.nc = nc
        self.sems = {}
        self.cnt = {}
        for name in names:
            self.sems[name] = stack.enter_context(nc.semaphore(name))
            self.cnt[name] = 0

    def inc(self, instr, name, n=1):
        instr.then_inc(self.sems[name], n)
        self.cnt[name] += n
        return instr

    def wait(self, eng, *names):
        for name in names:
            eng.wait_ge(self.sems[name], self.cnt[name])

    def acx(self, eng, instr):
        self.inc(instr, "ac")
        eng.drain()
        return instr

    def dvx(self, eng, instr):
        self.inc(instr, "dv")
        eng.drain()
        return instr

    def at(self, name):
        return self.cnt[name]

    def wait_at(self, eng, name, val):
        if val > 0:
            eng.wait_ge(self.sems[name], val)


def _consts():
    return dict(
        onesr=np.ones((1, 128), np.float32),
        trilc=np.triu(np.ones((128, 128), np.float32)),
        eyeb=np.eye(128, dtype=BF),
        eyef=np.eye(128, dtype=np.float32),
    )


def _add_const_tensors(nc):
    onesr = nc.dram_tensor("onesr", [1, 128], F32, kind="ExternalInput").ap()
    trilc = nc.dram_tensor("trilc", [128, 128], F32, kind="ExternalInput").ap()
    eyeb = nc.dram_tensor("eyeb", [128, 128], BF16, kind="ExternalInput").ap()
    eyef = nc.dram_tensor("eyef", [128, 128], F32, kind="ExternalInput").ap()
    return onesr, trilc, eyeb, eyef



def build_launch_A(ntile=None, dbg=False):
    nc = bass.Bass("TRN2", target_bir_lowering=False)
    TT = 512
    NTILE = ntile or (T_FULL // TT)
    NCH = 4
    HALO = 3
    NCK = TT // 128

    xb = nc.dram_tensor("xb", [D_MODEL, T_FULL], F32, kind="ExternalInput").ap()
    wzx = nc.dram_tensor("wzx", [D_MODEL, 1024], F32, kind="ExternalInput").ap()
    wbc = nc.dram_tensor("wbc", [D_MODEL, 256], F32, kind="ExternalInput").ap()
    wdt = nc.dram_tensor("wdt", [D_MODEL, 8], F32, kind="ExternalInput").ap()
    cw = nc.dram_tensor("cw", [512, K_CONV], F32, kind="ExternalInput").ap()
    cb = nc.dram_tensor("cb", [128, 4], F32, kind="ExternalInput").ap()
    alog = nc.dram_tensor("alog", [8, 1], F32, kind="ExternalInput").ap()
    dcol = nc.dram_tensor("dcol", [128, 4], F32, kind="ExternalInput").ap()
    dtb = nc.dram_tensor("dtb", [8, 1], F32, kind="ExternalInput").ap()
    nw = nc.dram_tensor("nw", [128, 8], F32, kind="ExternalInput").ap()
    wout = nc.dram_tensor("wout", [512, D_MODEL], F32, kind="ExternalInput").ap()
    selc = nc.dram_tensor("selc", [8, 1024], F32, kind="ExternalInput").ap()
    onesr, trilc, eyeb, eyef = _add_const_tensors(nc)
    pout = nc.dram_tensor("pout", [D_MODEL, T_FULL], F32, kind="ExternalOutput").ap()
    dbg_names = ["sH", "sHbf", "sDrow", "sRow8", "sDAc", "sCdb", "sCp", "sMaskT", "sU", "sUp",
                 "sColT", "sY", "sClrel", "sWs", "sDtsp", "sXhT", "sBsm", "sRow", "sMi", "sEm"]
    dbg_out = {}
    if dbg:
        shapes = {"sH": [128, 512], "sHbf": [128, 512], "sDrow": [1, 512], "sRow8": [1, 8],
                  "sDAc": [8, 1], "sCdb": [128, 128], "sCp": [128, 128], "sMaskT": [128, 128],
                  "sU": [128, 64], "sUp": [128, 64], "sColT": [128, 24], "sY": [128, 4 * 512],
                  "sClrel": [8, 512], "sWs": [8, 512], "sDtsp": [8, 512], "sXhT": [128, 512],
                  "sBsm": [128, 128], "sRow": [1, 128], "sMi": [128, 128], "sEm": [128, 128]}
        for n in dbg_names:
            dbg_out[n] = nc.dram_tensor("dbg_" + n, shapes[n], F32, kind="ExternalOutput").ap()

    xb_r = xb.rearrange("(c p) t -> p c t", p=128)
    pout_r = pout.rearrange("(c p) t -> p c t", p=128)

    with ExitStack() as st:
        sb = lambda name, shape, dt=F32: st.enter_context(nc.sbuf_tensor(name, shape, dt))
        sWzx = sb("sWzx", [128, 8 * 1024], BF16)
        sWbc = sb("sWbc", [128, 8 * 256], BF16)
        sWdt = sb("sWdt", [128, 8 * 8], BF16)
        sWout = sb("sWout", [128, 4 * 1024], BF16)
        sCw = sb("sCw", [128, NCH * K_CONV])
        sCb = sb("sCb", [128, NCH])
        sAlog = sb("sAlog", [8, 1])
        snegA = sb("snegA", [8, 1])
        sDcol = sb("sDcol", [128, 4])
        sDtb = sb("sDtb", [8, 1])
        sNw = sb("sNw", [128, 8])
        sOnes = sb("sOnes", [1, 128])
        sOnesCol = sb("sOnesCol", [128, 1])
        sTril = sb("sTril", [128, 128])
        sEyeB = sb("sEyeB", [128, 128], BF16)
        sEyeF = sb("sEyeF", [128, 128])
        sEpsR = sb("sEpsR", [1, 1])
        sH = sb("sH", [128, 512])
        sHbf = sb("sHbf", [128, 512], BF16)
        sClCar = sb("sClCar", [8, 1])
        sXraw = sb("sXraw", [128, 8 * TT])
        sSq = sb("sSq", [128, TT])
        sRrms = sb("sRrms", [1, TT])
        sRrms2 = sb("sRrms2", [1, TT])
        sTmpR3 = sb("sTmpR3", [8, TT])
        sXn = sb("sXn", [128, 8 * TT], BF16)
        sZ = sb("sZ", [128, NCH * TT], BF16)
        sSiluZ = sb("sSiluZ", [128, NCH * TT], BF16)
        sXs = sb("sXs", [128, NCH * (TT + HALO)], BF16)
        sCacc0 = sb("sCacc0", [128, TT])
        sCacc1 = sb("sCacc1", [128, TT])
        sXh = sb("sXh", [128, NCH * TT], BF16)
        sXhT = sb("sXhT", [128, NCH * 128], BF16)
        sB = sb("sB", [128, TT], BF16)
        sC = sb("sC", [128, TT], BF16)
        sBsm = sb("sBsm", [128, 128], BF16)
        sDtr = sb("sDtr", [8, TT])
        sTmpR = sb("sTmpR", [8, TT])
        sTmpR2 = sb("sTmpR2", [8, TT])
        sClrel = sb("sClrel", [8, TT])
        sWs = sb("sWs", [8, TT])
        sDtsp = sb("sDtsp", [8, TT])
        sCl = sb("sCl", [8, TT])
        sColT = sb("sColT", [128, 24])
        sRow = sb("sRow", [1, 128])
        sRow8 = sb("sRow8", [1, 8])
        sDrow = sb("sDrow", [1, 512])
        sDAc = sb("sDAc", [8, 1])
        sSelc = sb("sSelc", [8, 1024])
        sTrilG = sb("sTrilG", [128, 128], BF16)
        sMiA = sb("sMiA", [128, 1024])
        sEmA = sb("sEmA", [128, 1024], BF16)
        sCdbA = sb("sCdbA", [128, 1024], BF16)
        sMaskA = sb("sMaskA", [128, 1024], BF16)
        sCpA = sb("sCpA", [128, 1024], BF16)
        sUA = sb("sUA", [128, 512], BF16)
        sUpA = sb("sUpA", [128, 512], BF16)
        sDxh = sb("sDxh", [128, NCH * TT], BF16)
        sHtmp = sb("sHtmp", [128, 512])
        sY = sb("sY", [128, NCH * TT], BF16)
        sYevT = sb("sYevT", [128, 512])
        sPoutT = sb("sPoutT", [128, 512])

        ppool = st.enter_context(nc.psum_tensor("ppool", [128, 4096], F32))

        def bank(i, w=512, off=0):
            return ppool[:, i * 512 + off: i * 512 + off + w]

        def bankp(i, p0, prows, w, off=0):
            return ppool[p0:p0 + prows, i * 512 + off: i * 512 + off + w]

        p = P(nc, st, ["ld", "gld", "pe", "dv", "ac", "st"])
        sy, gp, pe, dv, ac = nc.sync, nc.gpsimd, nc.tensor, nc.vector, nc.scalar

        p.inc(sy.dma_start(out=sCw[:, :].rearrange("p (c k) -> p c k", c=NCH),
                           in_=cw.rearrange("(c p) k -> p c k", p=128)), "ld", 16)
        p.inc(sy.dma_start(out=sCb[:, :], in_=cb[:, :]), "ld", 16)
        p.inc(sy.dma_start(out=sAlog[:, :], in_=alog[:, :]), "ld", 16)
        p.inc(sy.dma_start(out=sDcol[:, :], in_=dcol[:, :]), "ld", 16)
        p.inc(sy.dma_start(out=sDtb[:, :], in_=dtb[:, :]), "ld", 16)
        p.inc(sy.dma_start(out=sNw[:, :], in_=nw[:, :]), "ld", 16)
        p.inc(sy.dma_start(out=sOnes[:, :], in_=onesr[:, :]), "ld", 16)
        p.inc(sy.dma_start(out=sTril[:, :], in_=trilc[:, :]), "ld", 16)
        p.inc(sy.dma_start(out=sEyeB[:, :], in_=eyeb[:, :]), "ld", 16)
        p.inc(sy.dma_start(out=sEyeF[:, :], in_=eyef[:, :]), "ld", 16)
        p.inc(sy.dma_start(out=sSelc[:, :], in_=selc[:, :]), "ld", 16)
        p.inc(gp.dma_start(out=sWzx[:, :].rearrange("p (c m) -> p c m", c=8),
                           in_=wzx.rearrange("(c p) m -> p c m", p=128)), "gld", 16)
        p.inc(gp.dma_start(out=sWbc[:, :].rearrange("p (c m) -> p c m", c=8),
                           in_=wbc.rearrange("(c p) m -> p c m", p=128)), "gld", 16)
        p.inc(gp.dma_start(out=sWdt[:, :].rearrange("p (c m) -> p c m", c=8),
                           in_=wdt.rearrange("(c p) m -> p c m", p=128)), "gld", 16)
        p.inc(gp.dma_start(out=sWout[:, :].rearrange("p (c m) -> p c m", c=4),
                           in_=wout.rearrange("(c p) m -> p c m", p=128)), "gld", 16)

        p.wait(dv, "ld", "gld")
        p.dvx(dv, dv.memset(sH[:, :], 0.0))
        p.dvx(dv, dv.memset(sHbf[:, :], 0.0))
        p.dvx(dv, dv.memset(sClCar[:, :], 0.0))
        p.dvx(dv, dv.memset(sXs[:, :], 0.0))
        p.dvx(dv, dv.memset(sOnesCol[:, :], 1.0))
        p.dvx(dv, dv.memset(sEpsR[:, :], MEPS))
        p.wait(ac, "dv")
        p.acx(ac, ac.activation(out=sDAc[:, :], in_=sAlog[:, :], func=AF.Exp))
        p.wait(dv, "ac")
        p.dvx(dv, dv.tensor_scalar(snegA[:, :], sDAc[:, :], -1.0, None, OP.mult))

        for tt in range(NTILE):
            tsl = slice(tt * TT, (tt + 1) * TT)
            p.wait(sy, "pe", "ac", "dv")
            p.inc(sy.dma_start(out=sXraw[:, :].rearrange("p (c t) -> p c t", c=8),
                               in_=xb_r[:, :, tsl]), "ld", 16)
            if tt > 0:
                for m in range(NCH):
                    base = m * (TT + HALO)
                    p.dvx(dv, dv.tensor_copy(sXs[:, base: base + HALO],
                                         sXs[:, base + TT: base + TT + HALO]))
            p.wait(ac, "ld")
            for c in range(8):
                if c > 0:
                    p.wait(ac, "pe")
                p.acx(ac, ac.activation(out=sSq[:, :], in_=sXraw[:, c * TT:(c + 1) * TT],
                                    func=AF.Square))
                p.wait(pe, "ac")
                p.inc(pe.matmul(bankp(0, 0, 1, TT), sOnesCol[:, :], sSq[:, :],
                                start=(c == 0), stop=(c == 7)), "pe")
            p.wait(ac, "pe")
            p.acx(ac, ac.activation(out=sRrms2[:, :], in_=bankp(0, 0, 1, TT), func=AF.Ln,
                                scale=1.0 / D_MODEL, bias=sEpsR[:, :]))
            p.acx(ac, ac.activation(out=sRrms[:, :], in_=sRrms2[:, :], func=AF.Exp,
                                scale=-0.5))
            p.wait(pe, "ac")
            p.inc(pe.matmul(bank(1, TT), sOnes[:, :], sRrms[:, :], start=True, stop=True), "pe")
            p.wait(dv, "pe", "ld")
            for c in range(8):
                p.dvx(dv, dv.scalar_tensor_tensor(sXn[:, c * TT:(c + 1) * TT],
                                              sXraw[:, c * TT:(c + 1) * TT],
                                              sNw[:, c:c + 1], bank(1, TT),
                                              OP.mult, OP.mult))
            p.wait(pe, "dv", "ac")
            for m in range(4):
                for k in range(8):
                    p.inc(pe.matmul(bank(m), sWzx[:, k * 1024 + m * 128: k * 1024 + (m + 1) * 128],
                                    sXn[:, k * TT:(k + 1) * TT],
                                    start=(k == 0), stop=(k == 7)), "pe")
            for m in range(4):
                for k in range(8):
                    p.inc(pe.matmul(bank(4 + m),
                                    sWzx[:, k * 1024 + 512 + m * 128: k * 1024 + 512 + (m + 1) * 128],
                                    sXn[:, k * TT:(k + 1) * TT],
                                    start=(k == 0), stop=(k == 7)), "pe")
            p.wait(dv, "pe")
            for m in range(4):
                p.dvx(dv, dv.tensor_copy(sZ[:, m * TT:(m + 1) * TT], bank(m)))
            for m in range(4):
                p.dvx(dv, dv.tensor_copy(sXs[:, m * (TT + HALO) + HALO: (m + 1) * (TT + HALO)],
                                     bank(4 + m)))
            p.wait(ac, "dv")
            for m in range(4):
                p.acx(ac, ac.activation(out=sSiluZ[:, m * TT:(m + 1) * TT],
                                    in_=sZ[:, m * TT:(m + 1) * TT], func=AF.Silu))
            p.wait(pe, "dv")
            for k in range(8):
                p.inc(pe.matmul(bank(0), sWbc[:, k * 256: k * 256 + 128],
                                sXn[:, k * TT:(k + 1) * TT], start=(k == 0), stop=(k == 7)), "pe")
            for k in range(8):
                p.inc(pe.matmul(bank(1), sWbc[:, k * 256 + 128: (k + 1) * 256],
                                sXn[:, k * TT:(k + 1) * TT], start=(k == 0), stop=(k == 7)), "pe")
            for k in range(8):
                p.inc(pe.matmul(bankp(2, 0, 8, TT), sWdt[:, k * 8:(k + 1) * 8],
                                sXn[:, k * TT:(k + 1) * TT], start=(k == 0), stop=(k == 7)), "pe")
            p.wait(dv, "pe")
            p.dvx(dv, dv.tensor_copy(sB[:, :], bank(0)))
            p.dvx(dv, dv.tensor_copy(sC[:, :], bank(1)))
            p.dvx(dv, dv.tensor_copy(sDtr[:, :], bankp(2, 0, 8, TT)))

            for m in range(NCH):
                base = m * (TT + HALO)
                p.wait(dv, "ac")
                p.dvx(dv, dv.tensor_scalar(sCacc0[:, :], sXs[:, base: base + TT],
                                       sCw[:, m * K_CONV: m * K_CONV + 1],
                                       sCb[:, m:m + 1], OP.mult, OP.add))
                accs = [sCacc0, sCacc1]
                for k in range(1, K_CONV):
                    src, dst = accs[(k - 1) % 2], accs[k % 2]
                    p.dvx(dv, dv.scalar_tensor_tensor(dst[:, :], sXs[:, base + k: base + k + TT],
                                                  sCw[:, m * K_CONV + k: m * K_CONV + k + 1],
                                                  src[:, :], OP.mult, OP.add))
                final = accs[(K_CONV - 1) % 2]
                p.wait(ac, "dv")
                p.acx(ac, ac.activation(out=sXh[:, m * TT:(m + 1) * TT], in_=final[:, :],
                                    func=AF.Silu))
            p.wait(dv, "ac")
            for m in range(NCH):
                p.dvx(dv, dv.tensor_scalar(sDxh[:, m * TT:(m + 1) * TT],
                                           sXh[:, m * TT:(m + 1) * TT],
                                           sDcol[:, m:m + 1], None, OP.mult))
            p.dvx(dv, dv.tensor_scalar(sTmpR[:, :], sDtr[:, :], sDtb[:, :], None, OP.add))
            p.wait(ac, "dv")
            p.acx(ac, ac.activation(out=sTmpR2[:, :], in_=sTmpR[:, :], func=AF.Abs))
            p.acx(ac, ac.activation(out=sTmpR3[:, :], in_=sTmpR2[:, :], func=AF.Exp, scale=-1.0))
            p.acx(ac, ac.activation(out=sTmpR2[:, :], in_=sTmpR3[:, :], func=AF.Ln, bias=1.0))
            p.wait(dv, "ac")
            p.dvx(dv, dv.tensor_scalar(sTmpR3[:, :], sTmpR[:, :], 0.0, None, OP.max))
            p.dvx(dv, dv.tensor_tensor(sDtsp[:, :], sTmpR3[:, :], sTmpR2[:, :], OP.add))
            p.dvx(dv, dv.tensor_scalar(sTmpR[:, :], sDtsp[:, :], snegA[:, :], None, OP.mult))
            p.dvx(dv, dv.tensor_tensor_scan(sCl[:, :], sTmpR[:, :], sTmpR[:, :], sClCar[:, :],
                                        OP.add, OP.bypass))
            for c in range(NCK):
                prevc = sClCar[:, :] if c == 0 else sCl[:, c * 128 - 1: c * 128]
                csl = slice(c * 128, (c + 1) * 128)
                p.dvx(dv, dv.tensor_scalar(sClrel[:, csl], sCl[:, csl], prevc, None,
                                       OP.subtract))
                endc = sClrel[:, c * 128 + 127: c * 128 + 128]
                p.dvx(dv, dv.tensor_scalar(sTmpR2[:, csl], sClrel[:, csl], endc, None,
                                       OP.subtract))
            p.dvx(dv, dv.tensor_copy(sClCar[:, :], sCl[:, TT - 1: TT]))
            p.wait(ac, "dv")
            p.acx(ac, ac.activation(out=sTmpR3[:, :], in_=sTmpR2[:, :], func=AF.Exp, scale=-1.0))
            p.wait(dv, "ac")
            p.dvx(dv, dv.tensor_tensor(sWs[:, :], sTmpR3[:, :], sDtsp[:, :], OP.mult))

            for c in range(NCK):
                csl = slice(c * 128, (c + 1) * 128)
                p.wait(pe, "dv")
                p.inc(pe.transpose(bankp(3, 0, 128, 8, off=0), sClrel[:, csl], sEyeF[0:8, 0:8]), "pe")
                p.inc(pe.transpose(bankp(3, 0, 128, 8, off=8), sWs[:, csl], sEyeF[0:8, 0:8]), "pe")
                p.inc(pe.transpose(bankp(3, 0, 128, 8, off=16), sDtsp[:, csl], sEyeF[0:8, 0:8]), "pe")
                p.wait(dv, "pe")
                p.dvx(dv, dv.tensor_copy(sColT[:, :], bankp(3, 0, 128, 24)))
                p.wait(ac, "dv")
                p.acx(ac, ac.activation(out=sDAc[:, :], in_=sClrel[:, c * 128 + 127: c * 128 + 128],
                                        func=AF.Exp))
                p.wait(pe, "dv")
                p.inc(pe.matmul(bankp(4, 0, 128, 128), sB[:, csl], sC[:, csl],
                                start=True, stop=True), "pe")
                p.inc(pe.transpose(bankp(4, 0, 128, 128, off=128).bitcast(BF16)[:, 0:128],
                                   sB[:, csl], sEyeB[:, :]), "pe")
                for m in range(NCH):
                    p.inc(pe.transpose(
                        bankp(4, 0, 128, 64, off=256 + m * 64).bitcast(BF16)[:, 0:128],
                        sXh[:, m * TT + c * 128: m * TT + (c + 1) * 128], sEyeB[:, :]), "pe")
                p.wait(dv, "pe")
                p.dvx(dv, dv.tensor_tensor(sTrilG[:, :], bankp(4, 0, 128, 128), sTril[:, :],
                                           OP.mult))
                p.dvx(dv, dv.tensor_copy(sBsm[:, :],
                                         bankp(4, 0, 128, 128, off=128).bitcast(BF16)[:, 0:128]))
                for m in range(NCH):
                    p.dvx(dv, dv.tensor_copy(
                        sXhT[:, m * 128:(m + 1) * 128],
                        bankp(4, 0, 128, 64, off=256 + m * 64).bitcast(BF16)[:, 0:128]))
                p.wait(pe, "dv", "ac")
                clb = ppool[:, 5 * 512: 5 * 512 + 1024]
                for h in range(8):
                    p.inc(pe.matmul(ppool[:, 5 * 512 + h * 128: 5 * 512 + (h + 1) * 128],
                                    sSelc[:, h * 128:(h + 1) * 128], sClrel[:, csl],
                                    start=True, stop=True), "pe")
                p.wait(dv, "pe")
                clcol_exp = sColT[:, 0:8].unsqueeze(2).broadcast_to([128, 8, 128])
                p.dvx(dv, dv.tensor_tensor(sMiA[:, :].rearrange("p (h t) -> p h t", h=8),
                                           clb.rearrange("p (h t) -> p h t", h=8),
                                           clcol_exp, OP.subtract))
                p.dvx(dv, dv.tensor_scalar(sMiA[:, :], sMiA[:, :], 0.0, None, OP.min))
                p.wait(ac, "dv")
                p.acx(ac, ac.activation(out=sEmA[:, :], in_=sMiA[:, :], func=AF.Exp))
                p.acx(ac, ac.activation(out=sCdbA[:, :], in_=clb, func=AF.Exp))
                p.wait(dv, "ac")
                trilg_exp = sTrilG[:, :].unsqueeze(1).broadcast_to([128, 8, 128])
                p.dvx(dv, dv.tensor_tensor(sMaskA[:, :].rearrange("p (h t) -> p h t", h=8),
                                           sEmA[:, :].rearrange("p (h t) -> p h t", h=8),
                                           trilg_exp, OP.mult))
                c_exp = sC[:, csl].unsqueeze(1).broadcast_to([128, 8, 128])
                p.dvx(dv, dv.tensor_tensor(sCpA[:, :].rearrange("p (h t) -> p h t", h=8),
                                           sCdbA[:, :].rearrange("p (h t) -> p h t", h=8),
                                           c_exp, OP.mult))
                dt_exp = sColT[:, 16:24].unsqueeze(2).broadcast_to([128, 8, 64])
                p.dvx(dv, dv.tensor_tensor(sUA[:, :].rearrange("p (h q) -> p h q", h=8),
                                           sXhT[:, :].rearrange("p (h q) -> p h q", h=8),
                                           dt_exp, OP.mult))
                ws_exp = sColT[:, 8:16].unsqueeze(2).broadcast_to([128, 8, 64])
                p.dvx(dv, dv.tensor_tensor(sUpA[:, :].rearrange("p (h q) -> p h q", h=8),
                                           sXhT[:, :].rearrange("p (h q) -> p h q", h=8),
                                           ws_exp, OP.mult))
                p.wait(pe, "dv")
                for h in range(8):
                    ybank = 7 if h % 2 == 0 else 2
                    rows = slice((h % 2) * 64, (h % 2) * 64 + 64)
                    yp = ppool[rows, ybank * 512 + (h // 2) * 128: ybank * 512 + (h // 2) * 128 + 128]
                    first = h in (0, 1)
                    last = h in (6, 7)
                    p.inc(pe.matmul(yp, sUA[:, h * 64:(h + 1) * 64],
                                    sMaskA[:, h * 128:(h + 1) * 128],
                                    start=first, stop=False), "pe")
                    p.inc(pe.matmul(yp, sHbf[:, h * 64:(h + 1) * 64],
                                    sCpA[:, h * 128:(h + 1) * 128],
                                    start=False, stop=last), "pe")
                    p.inc(pe.matmul(bank(0)[:, h * 64:(h + 1) * 64], sBsm[:, :],
                                    sUpA[:, h * 64:(h + 1) * 64],
                                    start=(h == 0), stop=(h == 7)), "pe")
                p.wait(dv, "pe")
                for par, ybank in ((0, 7), (1, 2)):
                    rows = slice(par * 64, par * 64 + 64)
                    ypsv = ppool[rows, ybank * 512: ybank * 512 + 512]
                    dxh_v = sDxh[:, :].rearrange("p (m t) -> p m t", m=NCH)[rows, :, csl]
                    silu_v = sSiluZ[:, :].rearrange("p (m t) -> p m t", m=NCH)[rows, :, csl]
                    y_v = sY[:, :].rearrange("p (m t) -> p m t", m=NCH)[rows, :, csl]
                    p.dvx(dv, dv.tensor_tensor(sYevT[rows, :].rearrange("p (m t) -> p m t", m=4),
                                               ypsv.rearrange("p (m t) -> p m t", m=4),
                                               dxh_v, OP.add))
                    p.dvx(dv, dv.tensor_tensor(y_v, sYevT[rows, :].rearrange("p (m t) -> p m t", m=4),
                                               silu_v, OP.mult))
                p.wait(pe, "ac", "dv")
                p.inc(pe.transpose(bankp(3, 0, 1, 8, off=256), sDAc[:, :], sEyeF[0:8, 0:8]), "pe")
                p.wait(dv, "pe")
                p.dvx(dv, dv.tensor_copy(sRow8[:, :], bankp(3, 0, 1, 8, off=256)))
                p.dvx(dv, dv.tensor_copy(sDrow[:, :].rearrange("q (a b) -> q a b", a=8),
                                         sRow8[:, :].unsqueeze(2).broadcast_to([1, 8, 64])))
                p.wait(pe, "dv")
                p.inc(pe.matmul(bank(1), sOnes[:, :], sDrow[:, :], start=True, stop=True), "pe")
                p.wait(dv, "pe")
                p.dvx(dv, dv.tensor_tensor(sHtmp[:, :], sH[:, :], bank(1), OP.mult))
                p.dvx(dv, dv.tensor_tensor(sH[:, :], sHtmp[:, :], bank(0), OP.add))
                p.dvx(dv, dv.tensor_copy(sHbf[:, :], sH[:, :]))

            p.wait(pe, "dv")
            for m in range(8):
                for k in range(4):
                    p.inc(pe.matmul(bank(m), sWout[:, k * 1024 + m * 128: k * 1024 + (m + 1) * 128],
                                    sY[:, k * TT:(k + 1) * TT],
                                    start=(k == 0), stop=(k == 3)), "pe")
            p.wait(dv, "pe")
            for m in range(8):
                p.dvx(dv, dv.tensor_copy(sPoutT[:, :], bank(m)))
                p.wait(sy, "dv")
                p.inc(sy.dma_start(out=pout_r[:, m, tsl], in_=sPoutT[:, :]), "st", 16)
                p.wait(dv, "st")

        if dbg:
            local = dict(sH=sH, sHbf=sHbf, sDrow=sDrow, sRow8=sRow8, sDAc=sDAc, sCdb=sCdb,
                         sCp=sCp, sMaskT=sMaskT, sU=sU, sUp=sUp, sColT=sColT, sY=sY,
                         sClrel=sClrel, sWs=sWs, sDtsp=sDtsp, sXhT=sXhT, sBsm=sBsm, sRow=sRow,
                         sMi=sMi, sEm=sEm)
            p.wait(gp, "dv", "ac", "pe")
            for n in dbg_names:
                p.inc(gp.dma_start(out=dbg_out[n][:, :], in_=local[n][:, :]), "st", 16)
        p.wait(gp, "st")
        p.wait(gp, "ld")
    return nc


def prep_A_inputs(L0_attn_x, norm_w, in_proj_w, conv_w, conv_b, A_log, D, dt_bias, out_proj_w):
    consts = _consts()
    in_maps = []
    for cid in range(N_CORES):
        b, g = cid // 4, cid % 4
        hsl = slice(8 * g, 8 * g + 8)
        chsl = slice(512 * g, 512 * g + 512)
        wz = in_proj_w[chsl]
        wx = in_proj_w[2048 + 512 * g: 2048 + 512 * g + 512]
        wbc = in_proj_w[4096:4352]
        wdtr = in_proj_w[4352 + 8 * g: 4352 + 8 * g + 8]
        m = dict(
            xb=f32(L0_attn_x[b].T),
            wzx=f32(np.concatenate([wz, wx], 0).T),
            wbc=f32(wbc.T),
            wdt=f32(wdtr.T),
            cw=f32(conv_w[chsl]),
            cb=f32(conv_b[chsl].reshape(4, 128).T),
            alog=f32(A_log[hsl][:, None]),
            dcol=f32(np.repeat(np.asarray(D)[hsl], 64).reshape(4, 128).T),
            dtb=f32(dt_bias[hsl][:, None]),
            nw=f32(np.asarray(norm_w).reshape(8, 128).T),
            wout=f32(out_proj_w[:, chsl].T),
            selc=f32(np.repeat(np.eye(8, dtype=np.float32), 128, axis=1)),
            **consts,
        )
        in_maps.append(m)
    return in_maps


def f32(x):
    return np.ascontiguousarray(np.asarray(x), dtype=np.float32)






def build_launch_B(dbg=False):
    nc = bass.Bass("TRN2", target_bir_lowering=False)
    TT = 512
    NTILE = TPC // TT

    xo = nc.dram_tensor("xo", [D_MODEL, TPC], F32, kind="ExternalInput").ap()
    pps = [nc.dram_tensor(f"pp{j}", [D_MODEL, TPC], F32, kind="ExternalInput").ap()
           for j in range(4)]
    wg = nc.dram_tensor("wg", [D_MODEL, 8], F32, kind="ExternalInput").ap()
    onesr, trilc, eyeb, eyef = _add_const_tensors(nc)
    x1f = nc.dram_tensor("x1f", [D_MODEL, TPC], F32, kind="ExternalOutput").ap()
    x1b = nc.dram_tensor("x1b", [D_MODEL, TPC], BF16, kind="ExternalOutput").ap()
    sct = nc.dram_tensor("sct", [8, TPC], F32, kind="ExternalOutput").ap()
    dbgB = {}
    if dbg:
        for n, shp in [("sRaw", [128, 4096]), ("sR1", [1, 512]), ("sR2", [1, 512]),
                       ("sR3", [1, 512]), ("sAcc", [128, 4096]), ("sXo", [128, 4096]),
                       ("sSq", [128, 512])]:
            dbgB[n] = nc.dram_tensor("dbg_" + n, shp, F32, kind="ExternalOutput").ap()

    xo_r = xo.rearrange("(c p) t -> p c t", p=128)
    pp_r = [q.rearrange("(c p) t -> p c t", p=128) for q in pps]
    x1f_r = x1f.rearrange("(c p) t -> p c t", p=128)
    x1b_r = x1b.rearrange("(c p) t -> p c t", p=128)

    with ExitStack() as st:
        sb = lambda name, shape, dt=F32: st.enter_context(nc.sbuf_tensor(name, shape, dt))
        sAcc = sb("sAcc", [128, 8 * TT])
        sP1 = sb("sP1", [128, 8 * TT])
        sXo = sb("sXo", [128, 8 * TT])
        sRaw = sb("sRaw", [128, 8 * TT])
        sSq = sb("sSq", [128, TT])
        sR1 = sb("sR1", [1, TT])
        sR2 = sb("sR2", [1, TT])
        sR3 = sb("sR3", [1, TT])
        sX1 = sb("sX1", [128, 8 * TT])
        sX1b = sb("sX1b", [128, 8 * TT], BF16)
        sWg = sb("sWg", [128, 8 * 8])
        sOnes = sb("sOnes", [1, 128])
        sOnesCol = sb("sOnesCol", [128, 1])
        sEyeF = sb("sEyeF", [128, 128])
        sEpsR = sb("sEpsR", [1, 1])
        sMx = sb("sMx", [128, 1])
        sSc = sb("sSc", [128, 8])
        sScE = sb("sScE", [128, 8])
        sScT = sb("sScT", [8, TT])
        sRec = sb("sRec", [128, 1])

        ppool = st.enter_context(nc.psum_tensor("ppool", [128, 4096], F32))

        def bank(i, w=512, off=0):
            return ppool[:, i * 512 + off: i * 512 + off + w]

        def bankp(i, p0, prows, w, off=0):
            return ppool[p0:p0 + prows, i * 512 + off: i * 512 + off + w]

        p = P(nc, st, ["ld", "pe", "dv", "ac", "st"])
        sy, gp, pe, dv, ac = nc.sync, nc.gpsimd, nc.tensor, nc.vector, nc.scalar

        p.inc(sy.dma_start(out=sOnes[:, :], in_=onesr[:, :]), "ld", 16)
        p.inc(sy.dma_start(out=sEyeF[:, :], in_=eyef[:, :]), "ld", 16)
        p.inc(sy.dma_start(out=sWg[:, :].rearrange("p (c m) -> p c m", c=8),
                           in_=wg.rearrange("(c p) m -> p c m", p=128)), "ld", 16)
        p.wait(dv, "ld")
        p.dvx(dv, dv.memset(sOnesCol[:, :], 1.0))
        p.dvx(dv, dv.memset(sEpsR[:, :], LNEPS))

        for tt in range(NTILE):
            tsl = slice(tt * TT, (tt + 1) * TT)
            p.wait(sy, "pe", "dv", "ac")
            p.inc(sy.dma_start(out=sXo[:, :].rearrange("p (c t) -> p c t", c=8),
                               in_=xo_r[:, :, tsl]), "ld", 16)
            p.inc(sy.dma_start(out=sAcc[:, :].rearrange("p (c t) -> p c t", c=8),
                               in_=pp_r[0][:, :, tsl]), "ld", 16)
            for j in range(1, 4):
                if j > 1:
                    p.wait(sy, "dv")
                p.inc(sy.dma_start(out=sP1[:, :].rearrange("p (c t) -> p c t", c=8),
                                   in_=pp_r[j][:, :, tsl]), "ld", 16)
                p.wait(dv, "ld")
                p.dvx(dv, dv.tensor_tensor(sAcc[:, :], sAcc[:, :], sP1[:, :], OP.add))
            p.wait(dv, "ld")
            p.dvx(dv, dv.scalar_tensor_tensor(sRaw[:, :], sXo[:, :], 2.0, sAcc[:, :],
                                              OP.mult, OP.add))
            p.wait(pe, "dv")
            for c in range(8):
                p.inc(pe.matmul(bankp(0, 0, 1, TT), sOnesCol[:, :],
                                sRaw[:, c * TT:(c + 1) * TT],
                                start=(c == 0), stop=(c == 7)), "pe")
            p.wait(ac, "dv")
            for c in range(8):
                if c > 0:
                    p.wait(ac, "pe")
                p.acx(ac, ac.activation(out=sSq[:, :], in_=sRaw[:, c * TT:(c + 1) * TT],
                                        func=AF.Square))
                p.wait(pe, "ac")
                p.inc(pe.matmul(bankp(1, 0, 1, TT), sOnesCol[:, :], sSq[:, :],
                                start=(c == 0), stop=(c == 7)), "pe")
            p.wait(dv, "pe")
            p.dvx(dv, dv.tensor_scalar(sR1[:, :], bankp(0, 0, 1, TT), 1.0 / D_MODEL,
                                       None, OP.mult))
            p.dvx(dv, dv.tensor_scalar(sR2[:, :], bankp(1, 0, 1, TT), 1.0 / D_MODEL,
                                       None, OP.mult))
            p.dvx(dv, dv.tensor_tensor(sR3[:, :], sR1[:, :], sR1[:, :], OP.mult))
            p.dvx(dv, dv.tensor_tensor(sR2[:, :], sR2[:, :], sR3[:, :], OP.subtract))
            p.wait(ac, "dv")
            p.acx(ac, ac.activation(out=sR3[:, :], in_=sR2[:, :], func=AF.Ln, bias=sEpsR[:, :]))
            p.acx(ac, ac.activation(out=sR2[:, :], in_=sR3[:, :], func=AF.Exp, scale=-0.5))
            p.wait(dv, "ac")
            p.dvx(dv, dv.scalar_tensor_tensor(sR3[:, :], sR1[:, :], -1.0, sR2[:, :],
                                              OP.mult, OP.mult))
            p.wait(pe, "dv")
            p.inc(pe.matmul(bank(2, TT), sOnes[:, :], sR2[:, :], start=True, stop=True), "pe")
            p.inc(pe.matmul(bank(3, TT), sOnes[:, :], sR3[:, :], start=True, stop=True), "pe")
            p.wait(dv, "pe", "st")
            for c in range(8):
                csl = slice(c * TT, (c + 1) * TT)
                p.dvx(dv, dv.tensor_tensor(sX1[:, csl], sRaw[:, csl], bank(2, TT), OP.mult))
                p.dvx(dv, dv.tensor_tensor(sX1[:, csl], sX1[:, csl], bank(3, TT), OP.add))
                p.dvx(dv, dv.tensor_copy(sX1b[:, csl], sX1[:, csl]))
            p.wait(sy, "dv")
            p.inc(sy.dma_start(out=x1f_r[:, :, tsl],
                               in_=sX1[:, :].rearrange("p (c t) -> p c t", c=8)), "st", 16)
            p.inc(sy.dma_start(out=x1b_r[:, :, tsl],
                               in_=sX1b[:, :].rearrange("p (c t) -> p c t", c=8)), "st", 16)
            for tg in range(TT // 128):
                p.wait(pe, "dv", "ac")
                for k in range(8):
                    p.inc(pe.matmul(bankp(4, 0, 128, 8), sX1[:, k * TT + tg * 128: k * TT + (tg + 1) * 128],
                                    sWg[:, k * 8:(k + 1) * 8],
                                    start=(k == 0), stop=(k == 7)), "pe")
                p.wait(dv, "pe")
                p.dvx(dv, dv.reduce_max(sMx[:, :], bankp(4, 0, 128, 8), axis=mybir.AxisListType.X))
                p.dvx(dv, dv.tensor_scalar(sMx[:, :], sMx[:, :], -1.0, None, OP.mult))
                p.wait(ac, "dv")
                p.acx(ac, ac.activation(out=sScE[:, :], in_=bankp(4, 0, 128, 8), func=AF.Exp,
                                        bias=sMx[:, :]))
                p.wait(dv, "ac")
                p.dvx(dv, dv.reduce_sum(sRec[:, :], sScE[:, :], axis=mybir.AxisListType.X))
                p.dvx(dv, dv.reciprocal(sRec[:, :], sRec[:, :]))
                p.dvx(dv, dv.tensor_scalar(sSc[:, :], sScE[:, :], sRec[:, :], None, OP.mult))
                p.wait(pe, "dv")
                p.inc(pe.transpose(bankp(5, 0, 8, 128), sSc[:, :], sEyeF[:, :]), "pe")
                p.wait(dv, "pe")
                if tg == 0:
                    p.wait(dv, "st")
                p.dvx(dv, dv.tensor_copy(sScT[0:8, tg * 128:(tg + 1) * 128], bankp(5, 0, 8, 128)))
            p.wait(sy, "dv")
            p.inc(sy.dma_start(out=sct[:, tsl], in_=sScT[0:8, :]), "st", 16)
        if dbg:
            local = dict(sRaw=sRaw, sR1=sR1, sR2=sR2, sR3=sR3, sAcc=sAcc, sXo=sXo, sSq=sSq)
            p.wait(gp, "dv", "ac", "pe")
            for n, ap in dbgB.items():
                p.inc(gp.dma_start(out=ap[:, :], in_=local[n][:, :]), "st", 16)
        p.wait(gp, "st")
        p.wait(gp, "ld")
    return nc


def prep_B_inputs(x_full, poutsA, wg):
    consts = _consts()
    in_maps = []
    for cid in range(N_CORES):
        b, q = cid // 4, cid % 4
        tsl = slice(2048 * q, 2048 * q + 2048)
        m = dict(
            xo=f32(x_full[b].T[:, tsl]),
            wg=f32(wg),
            **consts,
        )
        for j in range(4):
            m[f"pp{j}"] = f32(poutsA[4 * b + j][:, tsl])
        in_maps.append(m)
    return in_maps



def build_launch_C():
    nc = bass.Bass("TRN2", target_bir_lowering=False)
    TT = 512
    NTOK = 16384
    NTILE = NTOK // TT
    NITER = 42
    CAP = 4096.0

    x1all = nc.dram_tensor("x1all", [D_MODEL, NTOK], BF16, kind="ExternalInput").ap()
    srow = nc.dram_tensor("srow", [128, 128], F32, kind="ExternalInput").ap()
    w1 = nc.dram_tensor("w1", [D_MODEL, 2048], F32, kind="ExternalInput").ap()
    w3 = nc.dram_tensor("w3", [D_MODEL, 2048], F32, kind="ExternalInput").ap()
    w2 = nc.dram_tensor("w2", [2048, D_MODEL], F32, kind="ExternalInput").ap()
    triu1 = nc.dram_tensor("triu1", [128, 128], F32, kind="ExternalInput").ap()
    onesr, trilc, eyeb, eyef = _add_const_tensors(nc)
    eo = nc.dram_tensor("eo", [D_MODEL, NTOK], BF16, kind="ExternalOutput").ap()
    gateD = nc.dram_tensor("gateD", [128, 128], F32, kind="Internal").ap()

    x1_r = x1all.rearrange("(c p) t -> p c t", p=128)
    eo_r = eo.rearrange("(c p) t -> p c t", p=128)
    gateD_f = gateD.rearrange("p f -> (p f)")

    with ExitStack() as st:
        sb = lambda name, shape, dt=F32: st.enter_context(nc.sbuf_tensor(name, shape, dt))
        sW1 = sb("sW1", [128, 8 * 2048], BF16)
        sW3 = sb("sW3", [128, 8 * 2048], BF16)
        sW2 = sb("sW2", [128, 16 * 1024], BF16)
        sGrid = sb("sGrid", [128, 128])
        sLo = sb("sLo", [128, 1])
        sHi = sb("sHi", [128, 1])
        sMid = sb("sMid", [128, 1])
        sCnt = sb("sCnt", [128, 1])
        sSel = sb("sSel", [128, 1])
        sT1 = sb("sT1", [128, 1])
        sT2 = sb("sT2", [128, 1])
        sR11 = sb("sR11", [1, 1])
        sTrash = sb("sTrash", [128, 128], BF16)
        sEq = sb("sEq", [128, 128])
        sRk = sb("sRk", [128, 128])
        sRk2 = sb("sRk2", [128, 128])
        sMask = sb("sMask", [128, 128])
        sGate = sb("sGate", [128, 128])
        sTriu1 = sb("sTriu1", [128, 128])
        sOnes = sb("sOnes", [1, 128])
        sOnesCol = sb("sOnesCol", [128, 1])
        sGrow = sb("sGrow", [1, TT])
        sGb = sb("sGb", [128, TT], BF16)
        sXt = sb("sXt", [128, 8 * TT], BF16)
        sRt = sb("sRt", [128, 8 * TT], BF16)
        sGs = sb("sGs", [128, TT], BF16)
        sGs2 = sb("sGs2", [128, TT], BF16)
        sH1 = sb("sH1", [128, 16 * TT], BF16)
        sEo = sb("sEo", [128, 8 * TT], BF16)

        ppool = st.enter_context(nc.psum_tensor("ppool", [128, 4096], F32))

        def bank(i, w=512, off=0):
            return ppool[:, i * 512 + off: i * 512 + off + w]

        def bankp(i, p0, prows, w, off=0):
            return ppool[p0:p0 + prows, i * 512 + off: i * 512 + off + w]

        p = P(nc, st, ["ld", "gld", "pe", "dv", "ac", "st"])
        sy, gp, pe, dv, ac = nc.sync, nc.gpsimd, nc.tensor, nc.vector, nc.scalar

        p.inc(sy.dma_start(out=sGrid[:, :], in_=srow[:, :]), "ld", 16)
        p.inc(sy.dma_start(out=sTriu1[:, :], in_=triu1[:, :]), "ld", 16)
        p.inc(sy.dma_start(out=sOnes[:, :], in_=onesr[:, :]), "ld", 16)
        p.inc(gp.dma_start(out=sW1[:, :].rearrange("p (c m) -> p c m", c=8),
                           in_=w1.rearrange("(c p) m -> p c m", p=128)), "gld", 16)
        p.inc(gp.dma_start(out=sW3[:, :].rearrange("p (c m) -> p c m", c=8),
                           in_=w3.rearrange("(c p) m -> p c m", p=128)), "gld", 16)
        p.inc(gp.dma_start(out=sW2[:, :].rearrange("p (c m) -> p c m", c=16),
                           in_=w2.rearrange("(c p) m -> p c m", p=128)), "gld", 16)

        p.wait(dv, "ld")
        p.dvx(dv, dv.memset(sOnesCol[:, :], 1.0))
        p.dvx(dv, dv.memset(sLo[:, :], 0.0))
        p.dvx(dv, dv.memset(sHi[:, :], 2.0))

        for it in range(NITER):
            p.dvx(dv, dv.tensor_scalar(sMid[:, :], sLo[:, :], sHi[:, :], 0.5,
                                       OP.add, OP.mult))
            p.dvx(dv, dv.tensor_scalar(sTrash[:, :], sGrid[:, :], sMid[:, :], 0.0,
                                       OP.is_ge, OP.add, accum_out=sCnt[:, :]))
            p.wait(pe, "dv")
            p.inc(pe.matmul(bankp(0, 0, 1, 1), sCnt[:, :], sOnesCol[:, :],
                            start=True, stop=True), "pe")
            p.wait(dv, "pe")
            p.dvx(dv, dv.tensor_copy(sR11[:, :], bankp(0, 0, 1, 1)))
            p.wait(pe, "dv")
            p.inc(pe.matmul(bankp(1, 0, 128, 1), sOnes[:, :], sR11[:, :],
                            start=True, stop=True), "pe")
            p.wait(dv, "pe")
            p.dvx(dv, dv.tensor_scalar(sSel[:, :], bankp(1, 0, 128, 1), CAP, None, OP.is_ge))
            p.dvx(dv, dv.tensor_tensor(sT1[:, :], sMid[:, :], sLo[:, :], OP.subtract))
            p.dvx(dv, dv.tensor_tensor(sT1[:, :], sT1[:, :], sSel[:, :], OP.mult))
            p.dvx(dv, dv.tensor_tensor(sLo[:, :], sLo[:, :], sT1[:, :], OP.add))
            p.dvx(dv, dv.tensor_tensor(sT2[:, :], sHi[:, :], sMid[:, :], OP.subtract))
            p.dvx(dv, dv.tensor_tensor(sT2[:, :], sT2[:, :], sSel[:, :], OP.mult))
            p.dvx(dv, dv.tensor_tensor(sHi[:, :], sMid[:, :], sT2[:, :], OP.add))
        p.dvx(dv, dv.tensor_scalar(sMask[:, :], sGrid[:, :], sLo[:, :], 0.0,
                                   OP.is_gt, OP.add, accum_out=sCnt[:, :]))
        p.wait(pe, "dv")
        p.inc(pe.matmul(bankp(0, 0, 1, 1), sCnt[:, :], sOnesCol[:, :],
                        start=True, stop=True), "pe")
        p.wait(dv, "pe")
        p.dvx(dv, dv.tensor_scalar(sR11[:, :], bankp(0, 0, 1, 1), -1.0, CAP,
                                   OP.mult, OP.add))
        p.wait(pe, "dv")
        p.inc(pe.matmul(bankp(1, 0, 128, 1), sOnes[:, :], sR11[:, :],
                        start=True, stop=True), "pe")
        p.wait(dv, "pe")
        p.dvx(dv, dv.tensor_copy(sT1[:, :], bankp(1, 0, 128, 1)))
        p.dvx(dv, dv.tensor_scalar(sEq[:, :], sGrid[:, :], sLo[:, :], 0.0,
                                   OP.is_equal, OP.add, accum_out=sCnt[:, :]))
        p.dvx(dv, dv.tensor_tensor_scan(sRk[:, :], sEq[:, :], sEq[:, :], 0.0,
                                        OP.add, OP.bypass))
        p.dvx(dv, dv.tensor_tensor(sRk[:, :], sRk[:, :], sEq[:, :], OP.subtract))
        p.wait(pe, "dv")
        p.inc(pe.matmul(bankp(2, 0, 128, 1), sTriu1[:, :], sCnt[:, :],
                        start=True, stop=True), "pe")
        p.wait(dv, "pe")
        p.dvx(dv, dv.tensor_scalar(sRk2[:, :], sRk[:, :], bankp(2, 0, 128, 1), None, OP.add))
        p.dvx(dv, dv.tensor_scalar(sRk2[:, :], sRk2[:, :], sT1[:, :], None, OP.is_lt))
        p.dvx(dv, dv.tensor_tensor(sRk2[:, :], sRk2[:, :], sEq[:, :], OP.mult))
        p.dvx(dv, dv.tensor_tensor(sMask[:, :], sMask[:, :], sRk2[:, :], OP.add))
        p.dvx(dv, dv.tensor_tensor(sGate[:, :], sMask[:, :], sGrid[:, :], OP.mult))
        p.wait(sy, "dv")
        p.inc(sy.dma_start(out=gateD[:, :], in_=sGate[:, :]), "st", 16)
        p.wait(sy, "st")

        p.wait(dv, "gld")
        for tt in range(NTILE):
            tsl = slice(tt * TT, (tt + 1) * TT)
            p.wait(sy, "pe", "dv", "ac")
            p.inc(sy.dma_start(out=sXt[:, :].rearrange("p (c t) -> p c t", c=8),
                               in_=x1_r[:, :, tsl]), "ld", 16)
            p.inc(sy.dma_start(out=sGrow[:, :], in_=gateD_f[tt * TT:(tt + 1) * TT].unsqueeze(0)),
                  "ld", 16)
            p.wait(pe, "ld", "dv")
            p.inc(pe.matmul(bank(3, TT), sOnes[:, :], sGrow[:, :], start=True, stop=True), "pe")
            p.wait(dv, "pe", "ld")
            p.dvx(dv, dv.tensor_copy(sGb[:, :], bank(3, TT)))
            for c in range(8):
                csl = slice(c * TT, (c + 1) * TT)
                p.dvx(dv, dv.tensor_tensor(sRt[:, csl], sXt[:, csl], sGb[:, :], OP.mult))
            consumed_dv = [0] * 16
            consumed_ac = [0] * 16
            for m in range(16):
                gb, ub = (bank(0), bank(1)) if m % 2 == 0 else (bank(4), bank(5))
                if m >= 2:
                    p.wait_at(pe, "dv", consumed_dv[m - 2])
                    p.wait_at(pe, "ac", consumed_ac[m - 2])
                elif m == 0:
                    p.wait(pe, "dv", "ac")
                for k in range(8):
                    p.inc(pe.matmul(gb, sW1[:, k * 2048 + m * 128: k * 2048 + (m + 1) * 128],
                                    sRt[:, k * TT:(k + 1) * TT],
                                    start=(k == 0), stop=(k == 7)), "pe")
                for k in range(8):
                    p.inc(pe.matmul(ub, sW3[:, k * 2048 + m * 128: k * 2048 + (m + 1) * 128],
                                    sRt[:, k * TT:(k + 1) * TT],
                                    start=(k == 0), stop=(k == 7)), "pe")
                p.wait(ac, "pe")
                sgs = sGs if m % 2 == 0 else sGs2
                p.acx(ac, ac.activation(out=sgs[:, :], in_=gb, func=AF.Silu))
                consumed_ac[m] = p.at("ac")
                p.wait(dv, "ac")
                p.dvx(dv, dv.tensor_tensor(sH1[:, m * TT:(m + 1) * TT], sgs[:, :],
                                           ub, OP.mult))
                consumed_dv[m] = p.at("dv")
            dcon = [0] * 8
            for dm in range(8):
                db = bank(2) if dm % 2 == 0 else bank(6)
                if dm >= 2:
                    p.wait_at(pe, "dv", dcon[dm - 2])
                else:
                    p.wait(pe, "dv")
                for k in range(16):
                    p.inc(pe.matmul(db, sW2[:, k * 1024 + dm * 128: k * 1024 + (dm + 1) * 128],
                                    sH1[:, k * TT:(k + 1) * TT],
                                    start=(k == 0), stop=(k == 15)), "pe")
                p.wait(dv, "pe")
                if dm == 0:
                    p.wait(dv, "st")
                p.dvx(dv, dv.tensor_copy(sEo[:, dm * TT:(dm + 1) * TT], db))
                dcon[dm] = p.at("dv")
            p.wait(sy, "dv")
            p.inc(sy.dma_start(out=eo_r[:, :, tsl],
                               in_=sEo[:, :].rearrange("p (c t) -> p c t", c=8)), "st", 16)
        p.wait(gp, "st")
        p.wait(gp, "ld")
    return nc


def prep_C_inputs(x1bs, scores, w1, w3, w2):
    consts = _consts()
    x1all = np.ascontiguousarray(np.concatenate(x1bs, axis=1))
    in_maps = []
    for e in range(N_CORES):
        m = dict(
            x1all=x1all,
            srow=f32(np.ascontiguousarray(scores[:, e]).reshape(128, 128)),
            w1=f32(w1[e]),
            w3=f32(w3[e]),
            w2=f32(w2[e]),
            triu1=np.triu(np.ones((128, 128), np.float32), 1),
            **consts,
        )
        in_maps.append(m)
    return in_maps



def build_launch_D():
    nc = bass.Bass("TRN2", target_bir_lowering=False)
    TT = 512
    NTILE = TPC // TT

    x1o = nc.dram_tensor("x1o", [D_MODEL, TPC], F32, kind="ExternalInput").ap()
    eos = [nc.dram_tensor(f"eo{j}", [D_MODEL, TPC], BF16, kind="ExternalInput").ap()
           for j in range(8)]
    onesr, trilc, eyeb, eyef = _add_const_tensors(nc)
    x2f = nc.dram_tensor("x2f", [D_MODEL, TPC], F32, kind="ExternalOutput").ap()
    x2b = nc.dram_tensor("x2b", [D_MODEL, TPC], BF16, kind="ExternalOutput").ap()

    x1_r = x1o.rearrange("(c p) t -> p c t", p=128)
    eo_r = [q.rearrange("(c p) t -> p c t", p=128) for q in eos]
    x2f_r = x2f.rearrange("(c p) t -> p c t", p=128)
    x2b_r = x2b.rearrange("(c p) t -> p c t", p=128)

    with ExitStack() as st:
        sb = lambda name, shape, dt=F32: st.enter_context(nc.sbuf_tensor(name, shape, dt))
        sAcc = sb("sAcc", [128, 8 * TT])
        sP1 = sb("sP1", [128, 8 * TT], BF16)
        sRaw = sb("sRaw", [128, 8 * TT])
        sSq = sb("sSq", [128, TT])
        sR1 = sb("sR1", [1, TT])
        sR2 = sb("sR2", [1, TT])
        sR3 = sb("sR3", [1, TT])
        sX2 = sb("sX2", [128, 8 * TT])
        sX2b = sb("sX2b", [128, 8 * TT], BF16)
        sOnes = sb("sOnes", [1, 128])
        sOnesCol = sb("sOnesCol", [128, 1])
        sEpsR = sb("sEpsR", [1, 1])

        ppool = st.enter_context(nc.psum_tensor("ppool", [128, 4096], F32))

        def bank(i, w=512, off=0):
            return ppool[:, i * 512 + off: i * 512 + off + w]

        def bankp(i, p0, prows, w, off=0):
            return ppool[p0:p0 + prows, i * 512 + off: i * 512 + off + w]

        p = P(nc, st, ["ld", "pe", "dv", "ac", "st"])
        sy, gp, pe, dv, ac = nc.sync, nc.gpsimd, nc.tensor, nc.vector, nc.scalar

        p.inc(sy.dma_start(out=sOnes[:, :], in_=onesr[:, :]), "ld", 16)
        p.wait(dv, "ld")
        p.dvx(dv, dv.memset(sOnesCol[:, :], 1.0))
        p.dvx(dv, dv.memset(sEpsR[:, :], LNEPS))

        for tt in range(NTILE):
            tsl = slice(tt * TT, (tt + 1) * TT)
            p.wait(sy, "pe", "dv", "ac")
            p.inc(sy.dma_start(out=sAcc[:, :].rearrange("p (c t) -> p c t", c=8),
                               in_=x1_r[:, :, tsl]), "ld", 16)
            for j in range(8):
                if j > 0:
                    p.wait(sy, "dv")
                p.inc(sy.dma_start(out=sP1[:, :].rearrange("p (c t) -> p c t", c=8),
                                   in_=eo_r[j][:, :, tsl]), "ld", 16)
                p.wait(dv, "ld")
                p.dvx(dv, dv.tensor_tensor(sAcc[:, :], sAcc[:, :], sP1[:, :], OP.add))
            p.dvx(dv, dv.tensor_copy(sRaw[:, :], sAcc[:, :]))
            p.wait(pe, "dv")
            for c in range(8):
                p.inc(pe.matmul(bankp(0, 0, 1, TT), sOnesCol[:, :],
                                sRaw[:, c * TT:(c + 1) * TT],
                                start=(c == 0), stop=(c == 7)), "pe")
            p.wait(ac, "dv")
            for c in range(8):
                if c > 0:
                    p.wait(ac, "pe")
                p.acx(ac, ac.activation(out=sSq[:, :], in_=sRaw[:, c * TT:(c + 1) * TT],
                                        func=AF.Square))
                p.wait(pe, "ac")
                p.inc(pe.matmul(bankp(1, 0, 1, TT), sOnesCol[:, :], sSq[:, :],
                                start=(c == 0), stop=(c == 7)), "pe")
            p.wait(dv, "pe")
            p.dvx(dv, dv.tensor_scalar(sR1[:, :], bankp(0, 0, 1, TT), 1.0 / D_MODEL,
                                       None, OP.mult))
            p.dvx(dv, dv.tensor_scalar(sR2[:, :], bankp(1, 0, 1, TT), 1.0 / D_MODEL,
                                       None, OP.mult))
            p.dvx(dv, dv.tensor_tensor(sR3[:, :], sR1[:, :], sR1[:, :], OP.mult))
            p.dvx(dv, dv.tensor_tensor(sR2[:, :], sR2[:, :], sR3[:, :], OP.subtract))
            p.wait(ac, "dv")
            p.acx(ac, ac.activation(out=sR3[:, :], in_=sR2[:, :], func=AF.Ln, bias=sEpsR[:, :]))
            p.acx(ac, ac.activation(out=sR2[:, :], in_=sR3[:, :], func=AF.Exp, scale=-0.5))
            p.wait(dv, "ac")
            p.dvx(dv, dv.scalar_tensor_tensor(sR3[:, :], sR1[:, :], -1.0, sR2[:, :],
                                              OP.mult, OP.mult))
            p.wait(pe, "dv")
            p.inc(pe.matmul(bank(2, TT), sOnes[:, :], sR2[:, :], start=True, stop=True), "pe")
            p.inc(pe.matmul(bank(3, TT), sOnes[:, :], sR3[:, :], start=True, stop=True), "pe")
            p.wait(dv, "pe", "st")
            for c in range(8):
                csl = slice(c * TT, (c + 1) * TT)
                p.dvx(dv, dv.tensor_tensor(sX2[:, csl], sRaw[:, csl], bank(2, TT), OP.mult))
                p.dvx(dv, dv.tensor_tensor(sX2[:, csl], sX2[:, csl], bank(3, TT), OP.add))
                p.dvx(dv, dv.tensor_copy(sX2b[:, csl], sX2[:, csl]))
            p.wait(sy, "dv")
            p.inc(sy.dma_start(out=x2f_r[:, :, tsl],
                               in_=sX2[:, :].rearrange("p (c t) -> p c t", c=8)), "st", 16)
            p.inc(sy.dma_start(out=x2b_r[:, :, tsl],
                               in_=sX2b[:, :].rearrange("p (c t) -> p c t", c=8)), "st", 16)
        p.wait(gp, "st")
        p.wait(gp, "ld")
    return nc


def prep_D_inputs(x1fs, eos_full):
    consts = _consts()
    in_maps = []
    for cid in range(N_CORES):
        tsl = slice(2048 * cid, 2048 * cid + 2048)
        m = dict(x1o=f32(x1fs[cid]), **consts)
        for j in range(8):
            m[f"eo{j}"] = np.ascontiguousarray(eos_full[j][:, tsl])
        in_maps.append(m)
    return in_maps



LNC16 = 2.772588722239781
EPS16 = 1e-6 / 16.0


def build_launch_E():
    nc = bass.Bass("TRN2", target_bir_lowering=False)
    T = TPC
    NH = 16

    x2bi = nc.dram_tensor("x2bi", [D_MODEL, T], BF16, kind="ExternalInput").ap()
    wqT = nc.dram_tensor("wqT", [D_MODEL, D_MODEL], F32, kind="ExternalInput").ap()
    wkT = nc.dram_tensor("wkT", [D_MODEL, D_MODEL], F32, kind="ExternalInput").ap()
    wvT = nc.dram_tensor("wvT", [D_MODEL, D_MODEL], F32, kind="ExternalInput").ap()
    projT = nc.dram_tensor("projT", [128, NH * 256], F32, kind="ExternalInput").ap()
    onesr, trilc, eyeb, eyef = _add_const_tensors(nc)
    phiq = nc.dram_tensor("phiq", [T, NH * 256], BF16, kind="ExternalOutput").ap()
    kvp = nc.dram_tensor("kvp", [NH, 2, 128, 65], F32, kind="ExternalOutput").ap()

    x2_r = x2bi.rearrange("(c p) t -> p c t", p=128)
    phiq_r = phiq.rearrange("(g p) m -> p g m", p=128)

    with ExitStack() as st:
        sb = lambda name, shape, dt=F32: st.enter_context(nc.sbuf_tensor(name, shape, dt))
        sX2 = sb("sX2", [128, 8 * T], BF16)
        sVt = sb("sVt", [128, 16 * 1024], BF16)
        sWq = sb("sWq", [128, 8 * 1024], BF16)
        sWk = sb("sWk", [128, 8 * 1024], BF16)
        sProj = sb("sProj", [128, NH * 256], BF16)
        sQh = sb("sQh", [128, T], BF16)
        sKh = sb("sKh", [128, T], BF16)
        sQsq = sb("sQsq", [128, T])
        sPhiQh = sb("sPhiQh", [128, 16 * 256], BF16)
        sMx4 = sb("sMx4", [128, 4])
        sB4 = sb("sB4", [128, 4])
        sXwS = sb("sXwS", [128, 1024])
        sPhi4 = sb("sPhi4", [128, 1024], BF16)
        sKVo = sb("sKVo", [128, 130])
        sOnesCol = sb("sOnesCol", [128, 1])
        sOnesColB = sb("sOnesColB", [128, 1], BF16)

        ppool = st.enter_context(nc.psum_tensor("ppool", [128, 4096], F32))

        def bank(i, w=512, off=0):
            return ppool[:, i * 512 + off: i * 512 + off + w]

        def bankp(i, p0, prows, w, off=0):
            return ppool[p0:p0 + prows, i * 512 + off: i * 512 + off + w]

        p = P(nc, st, ["ld", "gld", "pe", "dv", "ac", "st"])
        sy, gp, pe, dv, ac = nc.sync, nc.gpsimd, nc.tensor, nc.vector, nc.scalar

        p.inc(sy.dma_start(out=sX2[:, :].rearrange("p (c t) -> p c t", c=8),
                           in_=x2_r[:, :, :]), "ld", 16)
        p.inc(gp.dma_start(out=sProj[:, :], in_=projT[:, :]), "gld", 16)
        p.inc(gp.dma_start(out=sWq[:, :].rearrange("p (c m) -> p c m", c=8),
                           in_=wvT.rearrange("(c p) m -> p c m", p=128)), "gld", 16)
        p.wait(dv, "ld")
        p.dvx(dv, dv.memset(sOnesCol[:, :], 1.0))
        p.dvx(dv, dv.memset(sOnesColB[:, :], 1.0))
        p.wait(pe, "ld", "gld", "dv")
        for tg in range(16):
            p.wait(pe, "dv")
            for n in range(2):
                for k in range(8):
                    p.inc(pe.matmul(bank(n), sX2[:, k * T + tg * 128: k * T + (tg + 1) * 128],
                                    sWq[:, k * 1024 + n * 512:(k * 1024) + (n + 1) * 512],
                                    start=(k == 0), stop=(k == 7)), "pe")
            p.wait(dv, "pe")
            for n in range(2):
                p.dvx(dv, dv.tensor_copy(sVt[:, tg * 1024 + n * 512: tg * 1024 + (n + 1) * 512],
                                         bank(n)))
        p.wait(gp, "dv")
        p.inc(gp.dma_start(out=sWq[:, :].rearrange("p (c m) -> p c m", c=8),
                           in_=wqT.rearrange("(c p) m -> p c m", p=128)), "gld", 16)
        p.inc(gp.dma_start(out=sWk[:, :].rearrange("p (c m) -> p c m", c=8),
                           in_=wkT.rearrange("(c p) m -> p c m", p=128)), "gld", 16)
        p.wait(pe, "gld")

        for hp in range(NH // 2):
            for (buf, w) in ((sQh, sWq), (sKh, sWk)):
                for n in range(4):
                    p.wait(pe, "dv", "ac")
                    for k in range(8):
                        p.inc(pe.matmul(bankp(0, 0, 128, 512),
                                        w[:, k * 1024 + hp * 128: k * 1024 + (hp + 1) * 128],
                                        sX2[:, k * T + n * 512: k * T + (n + 1) * 512],
                                        start=(k == 0), stop=(k == 7)), "pe")
                    p.wait(dv, "pe")
                    p.dvx(dv, dv.tensor_scalar(buf[:, n * 512:(n + 1) * 512],
                                               bankp(0, 0, 128, 512), 0.125, None, OP.mult))
            for sub in range(2):
                h = hp * 2 + sub
                hof = h * 64
                rsl = slice(sub * 64, sub * 64 + 64)
                for side in range(2):
                    qb = sQh if side == 0 else sKh
                    p.wait(ac, "dv", "pe")
                    p.acx(ac, ac.activation(out=sQsq[rsl, :], in_=qb[rsl, :], func=AF.Square))
                    if side == 0:
                        p.wait(dv, "st")
                    for g in range(4):
                        p.wait(pe, "ac", "dv")
                        for j in range(4):
                            tg = g * 4 + j
                            p.inc(pe.matmul(bankp(4, 0, 128, 256, off=j * 256),
                                            qb[rsl, tg * 128:(tg + 1) * 128],
                                            sProj[rsl, h * 256:(h + 1) * 256],
                                            start=True, stop=True), "pe")
                            p.inc(pe.matmul(bankp(6, 0, 128, 1, off=j),
                                            sQsq[rsl, tg * 128:(tg + 1) * 128],
                                            sOnesCol[rsl, :], start=True, stop=True), "pe")
                        p.wait(dv, "pe")
                        xwp = ppool[:, 4 * 512: 4 * 512 + 1024]
                        p.dvx(dv, dv.tensor_reduce(sMx4[:, :],
                                                   xwp.rearrange("p (j m) -> p j m", j=4),
                                                   axis=mybir.AxisListType.X, op=OP.max))
                        p.inc(dv.scalar_tensor_tensor(sB4[:, :], bankp(6, 0, 128, 4), 0.5,
                                                      sMx4[:, :], OP.mult, OP.add), "dv")
                        dv.drain()
                        p.dvx(dv, dv.tensor_scalar(sB4[:, :], sB4[:, :], LNC16, -1.0,
                                                   OP.add, OP.mult))
                        b4e = sB4[:, :].unsqueeze(2).broadcast_to([128, 4, 256])
                        p.dvx(dv, dv.tensor_tensor(sXwS[:, :].rearrange("p (j m) -> p j m", j=4),
                                                   xwp.rearrange("p (j m) -> p j m", j=4),
                                                   b4e, OP.add))
                        p.wait(ac, "dv")
                        p.acx(ac, ac.activation(out=sPhi4[:, :], in_=sXwS[:, :], func=AF.Exp))
                        p.wait(dv, "ac")
                        if side == 0:
                            p.dvx(dv, dv.tensor_scalar(sPhiQh[:, g * 1024:(g + 1) * 1024],
                                                       sPhi4[:, :], EPS16, None, OP.add))
                        else:
                            p.dvx(dv, dv.tensor_scalar(sPhi4[:, :], sPhi4[:, :], EPS16,
                                                       None, OP.add))
                            p.wait(pe, "dv")
                            for j in range(4):
                                tg = g * 4 + j
                                for mc in range(2):
                                    p.inc(pe.matmul(bankp(mc, 0, 128, 64),
                                                    sPhi4[:, j * 256 + mc * 128: j * 256 + (mc + 1) * 128],
                                                    sVt[:, tg * 1024 + hof: tg * 1024 + hof + 64],
                                                    start=(tg == 0), stop=(tg == 15)), "pe")
                                    p.inc(pe.matmul(bankp(2 + mc, 0, 128, 1),
                                                    sPhi4[:, j * 256 + mc * 128: j * 256 + (mc + 1) * 128],
                                                    sOnesColB[:, :], start=(tg == 0), stop=(tg == 15)), "pe")
                    if side == 0:
                        p.wait(sy, "dv", "st")
                        p.inc(sy.dma_start(out=phiq_r[:, :, h * 256:(h + 1) * 256],
                                           in_=sPhiQh[:, :].rearrange("p (g m) -> p g m", g=16)),
                              "st", 16)
                p.wait(dv, "pe", "st")
                p.dvx(dv, dv.tensor_copy(sKVo[:, 0:64], bankp(0, 0, 128, 64)))
                p.dvx(dv, dv.tensor_copy(sKVo[:, 64:65], bankp(2, 0, 128, 1)))
                p.dvx(dv, dv.tensor_copy(sKVo[:, 65:129], bankp(1, 0, 128, 64)))
                p.dvx(dv, dv.tensor_copy(sKVo[:, 129:130], bankp(3, 0, 128, 1)))
                p.wait(sy, "dv")
                p.inc(sy.dma_start(out=kvp[h].rearrange("k p m -> p k m"),
                                   in_=sKVo[:, :].rearrange("p (k m) -> p k m", k=2)), "st", 16)
        p.wait(gp, "st")
        p.wait(gp, "ld")
        p.wait(gp, "gld")
    return nc


def prep_E_inputs(x2bs, Wq, Wk, Wv, proj):
    consts = _consts()
    in_maps = []
    for cid in range(N_CORES):
        m = dict(
            x2bi=np.ascontiguousarray(x2bs[cid]),
            wqT=f32(np.asarray(Wq).T),
            wkT=f32(np.asarray(Wk).T),
            wvT=f32(np.asarray(Wv).T),
            projT=f32(np.tile(np.transpose(np.asarray(proj), (2, 0, 1)), (2, 1, 1)).reshape(128, -1)),
            **consts,
        )
        in_maps.append(m)
    return in_maps



def build_launch_F():
    nc = bass.Bass("TRN2", target_bir_lowering=False)
    T = TPC
    NH = 16
    TT = 512

    x2o = nc.dram_tensor("x2o", [D_MODEL, T], F32, kind="ExternalInput").ap()
    phiqi = nc.dram_tensor("phiqi", [T, NH * 256], BF16, kind="ExternalInput").ap()
    kvps = [nc.dram_tensor(f"kvp{j}", [NH, 2, 128, 65], F32, kind="ExternalInput").ap()
            for j in range(4)]
    woT = nc.dram_tensor("woT", [D_MODEL, D_MODEL], F32, kind="ExternalInput").ap()
    bo = nc.dram_tensor("bo", [128, 8], F32, kind="ExternalInput").ap()
    onesr, trilc, eyeb, eyef = _add_const_tensors(nc)
    x3f = nc.dram_tensor("x3f", [D_MODEL, T], F32, kind="ExternalOutput").ap()
    x3b = nc.dram_tensor("x3b", [D_MODEL, T], BF16, kind="ExternalOutput").ap()

    x2_r = x2o.rearrange("(c p) t -> p c t", p=128)
    phiq_r = phiqi.rearrange("(g p) m -> p g m", p=128)
    x3f_r = x3f.rearrange("(c p) t -> p c t", p=128)
    x3b_r = x3b.rearrange("(c p) t -> p c t", p=128)

    with ExitStack() as st:
        sb = lambda name, shape, dt=F32: st.enter_context(nc.sbuf_tensor(name, shape, dt))
        sKV = sb("sKV", [128, NH * 2 * 65])
        sKV2 = sb("sKV2", [128, NH * 2 * 65])
        sKVb = sb("sKVb", [128, NH * 2 * 65], BF16)
        sKrow = sb("sKrow", [1, 256])
        sKb = sb("sKb", [128, 256])
        sPhiF = sb("sPhiF", [128, 1024], BF16)
        sPq4 = sb("sPq4", [128, 1024], BF16)
        sPqm4 = sb("sPqm4", [128, 1024], BF16)
        sT4 = sb("sT4", [128, 1024])
        sDen4 = sb("sDen4", [128, 4])
        sMerged = sb("sMerged", [128, 8 * T], BF16)
        sWo = sb("sWo", [128, 8 * 1024], BF16)
        sBo = sb("sBo", [128, 8])
        sXo = sb("sXo", [128, 8 * TT])
        sRaw = sb("sRaw", [128, 8 * TT])
        sSq = sb("sSq", [128, TT])
        sR1 = sb("sR1", [1, TT])
        sR2 = sb("sR2", [1, TT])
        sR3 = sb("sR3", [1, TT])
        sX3 = sb("sX3", [128, 8 * TT])
        sX3b = sb("sX3b", [128, 8 * TT], BF16)
        sOnes = sb("sOnes", [1, 128])
        sOnesCol = sb("sOnesCol", [128, 1])
        sEyeF = sb("sEyeF", [128, 128])
        sEyeB = sb("sEyeB", [128, 128], BF16)
        sEpsR = sb("sEpsR", [1, 1])

        ppool = st.enter_context(nc.psum_tensor("ppool", [128, 4096], F32))

        def bank(i, w=512, off=0):
            return ppool[:, i * 512 + off: i * 512 + off + w]

        def bankp(i, p0, prows, w, off=0):
            return ppool[p0:p0 + prows, i * 512 + off: i * 512 + off + w]

        p = P(nc, st, ["ld", "gld", "pe", "dv", "ac", "st"])
        sy, gp, pe, dv, ac = nc.sync, nc.gpsimd, nc.tensor, nc.vector, nc.scalar

        p.inc(sy.dma_start(out=sOnes[:, :], in_=onesr[:, :]), "ld", 16)
        p.inc(sy.dma_start(out=sEyeF[:, :], in_=eyef[:, :]), "ld", 16)
        p.inc(sy.dma_start(out=sEyeB[:, :], in_=eyeb[:, :]), "ld", 16)
        p.inc(sy.dma_start(out=sBo[:, :], in_=bo[:, :]), "ld", 16)
        p.inc(sy.dma_start(out=sKV[:, :].rearrange("p (h k m) -> p h k m", h=NH, k=2),
                           in_=kvps[0].rearrange("h k p m -> p h k m")), "ld", 16)
        p.inc(gp.dma_start(out=sWo[:, :].rearrange("p (c m) -> p c m", c=8),
                           in_=woT.rearrange("(c p) m -> p c m", p=128)), "gld", 16)
        p.wait(dv, "ld")
        p.dvx(dv, dv.memset(sOnesCol[:, :], 1.0))
        p.dvx(dv, dv.memset(sEpsR[:, :], LNEPS))
        for j in range(1, 4):
            if j > 1:
                p.wait(sy, "dv")
            p.inc(sy.dma_start(out=sKV2[:, :].rearrange("p (h k m) -> p h k m", h=NH, k=2),
                               in_=kvps[j].rearrange("h k p m -> p h k m")), "ld", 16)
            p.wait(dv, "ld")
            p.dvx(dv, dv.tensor_tensor(sKV[:, :], sKV[:, :], sKV2[:, :], OP.add))
        p.dvx(dv, dv.tensor_copy(sKVb[:, :], sKV[:, :]))

        for h in range(NH):
            hb = h * 130
            p.wait(pe, "dv", "ac")
            p.inc(pe.transpose(bankp(5, 0, 1, 128), sKV[:, hb + 64: hb + 65], sEyeF[:, :]), "pe")
            p.inc(pe.transpose(bankp(5, 0, 1, 128, off=128), sKV[:, hb + 129: hb + 130],
                               sEyeF[:, :]), "pe")
            p.wait(dv, "pe")
            p.dvx(dv, dv.tensor_copy(sKrow[:, 0:128], bankp(5, 0, 1, 128)))
            p.dvx(dv, dv.tensor_copy(sKrow[:, 128:256], bankp(5, 0, 1, 128, off=128)))
            p.wait(pe, "dv")
            p.inc(pe.matmul(bankp(6, 0, 128, 256), sOnes[:, :], sKrow[:, :],
                            start=True, stop=True), "pe")
            p.wait(dv, "pe")
            p.dvx(dv, dv.tensor_copy(sKb[:, :], bankp(6, 0, 128, 256)))
            for g in range(4):
                p.wait(sy, "dv", "pe")
                p.inc(sy.dma_start(out=sPhiF[:, :].rearrange("p (j m) -> p j m", j=4),
                                   in_=phiq_r[:, g * 4:(g + 1) * 4, h * 256:(h + 1) * 256]),
                      "ld", 16)
                p.wait(dv, "ld")
                ksb_e = sKb[:, :].unsqueeze(1).broadcast_to([128, 4, 256])
                p.dvx(dv, dv.tensor_tensor(sT4[:, :].rearrange("p (j m) -> p j m", j=4),
                                           sPhiF[:, :].rearrange("p (j m) -> p j m", j=4),
                                           ksb_e, OP.mult))
                p.dvx(dv, dv.tensor_reduce(sDen4[:, :],
                                           sT4[:, :].rearrange("p (j m) -> p j m", j=4),
                                           axis=mybir.AxisListType.X, op=OP.add))
                p.dvx(dv, dv.tensor_scalar(sDen4[:, :], sDen4[:, :], 1e-6, None, OP.add))
                p.dvx(dv, dv.reciprocal(sDen4[:, :], sDen4[:, :]))
                rd_e = sDen4[:, :].unsqueeze(2).broadcast_to([128, 4, 256])
                p.dvx(dv, dv.tensor_tensor(sPq4[:, :].rearrange("p (j m) -> p j m", j=4),
                                           sPhiF[:, :].rearrange("p (j m) -> p j m", j=4),
                                           rd_e, OP.mult))
                p.wait(pe, "dv")
                for q in range(8):
                    p.inc(pe.transpose(
                        bankp(5, 0, 128, 64, off=q * 64).bitcast(BF16)[:, 0:128],
                        sPq4[:, q * 128:(q + 1) * 128], sEyeB[:, :]), "pe")
                p.wait(dv, "pe")
                p.dvx(dv, dv.tensor_copy(sPqm4[:, :],
                                         bankp(5, 0, 128, 512).bitcast(BF16)[:, 0:1024]))
                p.wait(pe, "dv")
                rows = slice((h % 2) * 64, (h % 2) * 64 + 64)
                for j in range(4):
                    for mc in range(2):
                        p.inc(pe.matmul(
                            ppool[rows, 7 * 512 + j * 128: 7 * 512 + (j + 1) * 128],
                            sKVb[:, hb + mc * 65: hb + mc * 65 + 64],
                            sPqm4[:, j * 256 + mc * 128: j * 256 + (mc + 1) * 128],
                            start=(mc == 0), stop=(mc == 1)), "pe")
                p.wait(dv, "pe")
                p.dvx(dv, dv.tensor_copy(
                    sMerged[rows, (h // 2) * T + g * 512:(h // 2) * T + (g + 1) * 512],
                    ppool[rows, 7 * 512: 7 * 512 + 512]))

        for tt in range(T // TT):
            tsl = slice(tt * TT, (tt + 1) * TT)
            p.wait(sy, "pe", "dv", "ac")
            p.inc(sy.dma_start(out=sXo[:, :].rearrange("p (c t) -> p c t", c=8),
                               in_=x2_r[:, :, tsl]), "ld", 16)
            p.wait(pe, "dv", "ac")
            for m in range(8):
                for k in range(8):
                    p.inc(pe.matmul(bank(m % 4), sWo[:, k * 1024 + m * 128:(k * 1024) + (m + 1) * 128],
                                    sMerged[:, k * T + tt * TT: k * T + (tt + 1) * TT],
                                    start=(k == 0), stop=(k == 7)), "pe")
                p.wait(dv, "pe", "ld")
                p.dvx(dv, dv.scalar_tensor_tensor(sRaw[:, m * TT:(m + 1) * TT],
                                                  bank(m % 4), sBo[:, m:m + 1],
                                                  sXo[:, m * TT:(m + 1) * TT],
                                                  OP.add, OP.add))
                p.wait(pe, "dv")
            p.wait(pe, "dv")
            for c in range(8):
                p.inc(pe.matmul(bankp(4, 0, 1, TT), sOnesCol[:, :],
                                sRaw[:, c * TT:(c + 1) * TT],
                                start=(c == 0), stop=(c == 7)), "pe")
            p.wait(ac, "dv")
            for c in range(8):
                if c > 0:
                    p.wait(ac, "pe")
                p.acx(ac, ac.activation(out=sSq[:, :], in_=sRaw[:, c * TT:(c + 1) * TT],
                                        func=AF.Square))
                p.wait(pe, "ac")
                p.inc(pe.matmul(bankp(5, 0, 1, TT), sOnesCol[:, :], sSq[:, :],
                                start=(c == 0), stop=(c == 7)), "pe")
            p.wait(dv, "pe")
            p.dvx(dv, dv.tensor_scalar(sR1[:, :], bankp(4, 0, 1, TT), 1.0 / D_MODEL,
                                       None, OP.mult))
            p.dvx(dv, dv.tensor_scalar(sR2[:, :], bankp(5, 0, 1, TT), 1.0 / D_MODEL,
                                       None, OP.mult))
            p.dvx(dv, dv.tensor_tensor(sR3[:, :], sR1[:, :], sR1[:, :], OP.mult))
            p.dvx(dv, dv.tensor_tensor(sR2[:, :], sR2[:, :], sR3[:, :], OP.subtract))
            p.wait(ac, "dv")
            p.acx(ac, ac.activation(out=sR3[:, :], in_=sR2[:, :], func=AF.Ln, bias=sEpsR[:, :]))
            p.acx(ac, ac.activation(out=sR2[:, :], in_=sR3[:, :], func=AF.Exp, scale=-0.5))
            p.wait(dv, "ac")
            p.dvx(dv, dv.scalar_tensor_tensor(sR3[:, :], sR1[:, :], -1.0, sR2[:, :],
                                              OP.mult, OP.mult))
            p.wait(pe, "dv")
            p.inc(pe.matmul(bank(6, TT), sOnes[:, :], sR2[:, :], start=True, stop=True), "pe")
            p.inc(pe.matmul(bank(7, TT), sOnes[:, :], sR3[:, :], start=True, stop=True), "pe")
            p.wait(dv, "pe", "st")
            for c in range(8):
                csl = slice(c * TT, (c + 1) * TT)
                p.dvx(dv, dv.tensor_tensor(sX3[:, csl], sRaw[:, csl], bank(6, TT), OP.mult))
                p.dvx(dv, dv.tensor_tensor(sX3[:, csl], sX3[:, csl], bank(7, TT), OP.add))
                p.dvx(dv, dv.tensor_copy(sX3b[:, csl], sX3[:, csl]))
            p.wait(sy, "dv")
            p.inc(sy.dma_start(out=x3f_r[:, :, tsl],
                               in_=sX3[:, :].rearrange("p (c t) -> p c t", c=8)), "st", 16)
            p.inc(sy.dma_start(out=x3b_r[:, :, tsl],
                               in_=sX3b[:, :].rearrange("p (c t) -> p c t", c=8)), "st", 16)
        p.wait(gp, "st")
        p.wait(gp, "ld")
        p.wait(gp, "gld")
    return nc


def prep_F_inputs(x2fs, phiqs, kvps_all, Wo, bo):
    consts = _consts()
    in_maps = []
    for cid in range(N_CORES):
        b = cid // 4
        m = dict(
            x2o=f32(x2fs[cid]),
            phiqi=np.ascontiguousarray(phiqs[cid]),
            woT=f32(np.asarray(Wo).T),
            bo=f32(np.asarray(bo).reshape(8, 128).T),
            **consts,
        )
        for j in range(4):
            m[f"kvp{j}"] = f32(kvps_all[4 * b + j])
        in_maps.append(m)
    return in_maps



QS = 1.0 / 127.0


def build_launch_G():
    nc = bass.Bass("TRN2", target_bir_lowering=False)
    T = TPC
    TT = 512
    NHID = 3072

    x3fi = nc.dram_tensor("x3fi", [D_MODEL, T], F32, kind="ExternalInput").ap()
    x3bi = nc.dram_tensor("x3bi", [D_MODEL, T], BF16, kind="ExternalInput").ap()
    wgT = nc.dram_tensor("wgT", [D_MODEL, NHID], I32, kind="ExternalInput").ap()
    wuT = nc.dram_tensor("wuT", [D_MODEL, NHID], I32, kind="ExternalInput").ap()
    wd3 = nc.dram_tensor("wd3", [8, NHID, 128], I32, kind="ExternalInput").ap()
    onesr, trilc, eyeb, eyef = _add_const_tensors(nc)
    outf = nc.dram_tensor("outf", [D_MODEL, T], F32, kind="ExternalOutput").ap()

    x3f_r = x3fi.rearrange("(c p) t -> p c t", p=128)
    x3b_r = x3bi.rearrange("(c p) t -> p c t", p=128)
    out_r = outf.rearrange("(c p) t -> p c t", p=128)

    with ExitStack() as st:
        sb = lambda name, shape, dt=F32: st.enter_context(nc.sbuf_tensor(name, shape, dt))
        sWg = sb("sWg", [128, 8 * NHID], BF16)
        sWu = sb("sWu", [128, 8 * NHID], BF16)
        sWdc = sb("sWdc", [128, 24 * 128], BF16)
        sWdc2 = sb("sWdc2", [128, 24 * 128], BF16)
        sXb = sb("sXb", [128, 8 * TT], BF16)
        sXf = sb("sXf", [128, 8 * TT])
        sGs = sb("sGs", [128, TT], BF16)
        sGs2 = sb("sGs2", [128, TT], BF16)
        sH1 = sb("sH1", [128, 24 * TT], BF16)
        sRaw = sb("sRaw", [128, 8 * TT])
        sSq = sb("sSq", [128, TT])
        sR1 = sb("sR1", [1, TT])
        sR2 = sb("sR2", [1, TT])
        sR3 = sb("sR3", [1, TT])
        sOut = sb("sOut", [128, 8 * TT])
        sOnes = sb("sOnes", [1, 128])
        sOnesCol = sb("sOnesCol", [128, 1])
        sEpsR = sb("sEpsR", [1, 1])

        ppool = st.enter_context(nc.psum_tensor("ppool", [128, 4096], F32))

        def bank(i, w=512, off=0):
            return ppool[:, i * 512 + off: i * 512 + off + w]

        def bankp(i, p0, prows, w, off=0):
            return ppool[p0:p0 + prows, i * 512 + off: i * 512 + off + w]

        p = P(nc, st, ["ld", "gld", "pe", "dv", "ac", "st"])
        sy, gp, pe, dv, ac = nc.sync, nc.gpsimd, nc.tensor, nc.vector, nc.scalar

        p.inc(sy.dma_start(out=sOnes[:, :], in_=onesr[:, :]), "ld", 16)
        p.inc(gp.dma_start(out=sWg[:, :].rearrange("p (c m) -> p c m", c=8),
                           in_=wgT.rearrange("(c p) m -> p c m", p=128)), "gld", 16)
        p.inc(gp.dma_start(out=sWu[:, :].rearrange("p (c m) -> p c m", c=8),
                           in_=wuT.rearrange("(c p) m -> p c m", p=128)), "gld", 16)
        p.wait(dv, "ld")
        p.dvx(dv, dv.memset(sOnesCol[:, :], 1.0))
        p.dvx(dv, dv.memset(sEpsR[:, :], LNEPS))
        p.wait(pe, "gld", "dv")
        dprev_pe = {}
        dprev_dv = {}

        for tt in range(T // TT):
            tsl = slice(tt * TT, (tt + 1) * TT)
            p.wait(sy, "pe", "dv", "ac")
            p.inc(sy.dma_start(out=sXb[:, :].rearrange("p (c t) -> p c t", c=8),
                               in_=x3b_r[:, :, tsl]), "ld", 16)
            p.inc(sy.dma_start(out=sXf[:, :].rearrange("p (c t) -> p c t", c=8),
                               in_=x3f_r[:, :, tsl]), "ld", 16)
            p.wait(pe, "ld")
            gcon_dv = [0] * 24
            gcon_ac = [0] * 24
            for m in range(24):
                gb, ub = (bank(0), bank(1)) if m % 2 == 0 else (bank(4), bank(5))
                if m >= 2:
                    p.wait_at(pe, "dv", gcon_dv[m - 2])
                    p.wait_at(pe, "ac", gcon_ac[m - 2])
                elif m == 0:
                    p.wait(pe, "dv", "ac")
                for k in range(8):
                    p.inc(pe.matmul(gb, sWg[:, k * NHID + m * 128: k * NHID + (m + 1) * 128],
                                    sXb[:, k * TT:(k + 1) * TT],
                                    start=(k == 0), stop=(k == 7)), "pe")
                for k in range(8):
                    p.inc(pe.matmul(ub, sWu[:, k * NHID + m * 128: k * NHID + (m + 1) * 128],
                                    sXb[:, k * TT:(k + 1) * TT],
                                    start=(k == 0), stop=(k == 7)), "pe")
                p.wait(ac, "pe")
                sgs = sGs if m % 2 == 0 else sGs2
                p.acx(ac, ac.activation(out=sgs[:, :], in_=gb, func=AF.Silu, scale=QS))
                gcon_ac[m] = p.at("ac")
                p.wait(dv, "ac")
                p.dvx(dv, dv.scalar_tensor_tensor(sH1[:, m * TT:(m + 1) * TT],
                                                  ub, QS, sgs[:, :],
                                                  OP.mult, OP.mult))
                gcon_dv[m] = p.at("dv")
            for dm in range(8):
                par = dm % 2
                wb = sWdc if par == 0 else sWdc2
                db = bank(2) if par == 0 else bank(6)
                if par in dprev_pe:
                    p.wait_at(gp, "pe", dprev_pe[par])
                p.inc(gp.dma_start(out=wb[:, :].rearrange("p (c m) -> p c m", c=24),
                                   in_=wd3[dm].rearrange("(c p) m -> p c m", p=128)), "gld", 16)
                if par in dprev_dv:
                    p.wait_at(pe, "dv", dprev_dv[par])
                if dm == 0:
                    p.wait(pe, "dv")
                p.wait(pe, "gld")
                for k in range(24):
                    p.inc(pe.matmul(db, wb[:, k * 128:(k + 1) * 128],
                                    sH1[:, k * TT:(k + 1) * TT],
                                    start=(k == 0), stop=(k == 23)), "pe")
                dprev_pe[par] = p.at("pe")
                p.wait(dv, "pe", "ld")
                p.dvx(dv, dv.scalar_tensor_tensor(sRaw[:, dm * TT:(dm + 1) * TT],
                                                  db, QS, sXf[:, dm * TT:(dm + 1) * TT],
                                                  OP.mult, OP.add))
                dprev_dv[par] = p.at("dv")
            p.wait(pe, "dv")
            for c in range(8):
                p.inc(pe.matmul(bankp(4, 0, 1, TT), sOnesCol[:, :],
                                sRaw[:, c * TT:(c + 1) * TT],
                                start=(c == 0), stop=(c == 7)), "pe")
            p.wait(ac, "dv")
            for c in range(8):
                if c > 0:
                    p.wait(ac, "pe")
                p.acx(ac, ac.activation(out=sSq[:, :], in_=sRaw[:, c * TT:(c + 1) * TT],
                                        func=AF.Square))
                p.wait(pe, "ac")
                p.inc(pe.matmul(bankp(5, 0, 1, TT), sOnesCol[:, :], sSq[:, :],
                                start=(c == 0), stop=(c == 7)), "pe")
            p.wait(dv, "pe")
            p.dvx(dv, dv.tensor_scalar(sR1[:, :], bankp(4, 0, 1, TT), 1.0 / D_MODEL,
                                       None, OP.mult))
            p.dvx(dv, dv.tensor_scalar(sR2[:, :], bankp(5, 0, 1, TT), 1.0 / D_MODEL,
                                       None, OP.mult))
            p.dvx(dv, dv.tensor_tensor(sR3[:, :], sR1[:, :], sR1[:, :], OP.mult))
            p.dvx(dv, dv.tensor_tensor(sR2[:, :], sR2[:, :], sR3[:, :], OP.subtract))
            p.wait(ac, "dv")
            p.acx(ac, ac.activation(out=sR3[:, :], in_=sR2[:, :], func=AF.Ln, bias=sEpsR[:, :]))
            p.acx(ac, ac.activation(out=sR2[:, :], in_=sR3[:, :], func=AF.Exp, scale=-0.5))
            p.wait(dv, "ac")
            p.dvx(dv, dv.scalar_tensor_tensor(sR3[:, :], sR1[:, :], -1.0, sR2[:, :],
                                              OP.mult, OP.mult))
            p.wait(pe, "dv")
            p.inc(pe.matmul(bank(6, TT), sOnes[:, :], sR2[:, :], start=True, stop=True), "pe")
            p.inc(pe.matmul(bank(7, TT), sOnes[:, :], sR3[:, :], start=True, stop=True), "pe")
            p.wait(dv, "pe", "st")
            for c in range(8):
                csl = slice(c * TT, (c + 1) * TT)
                p.dvx(dv, dv.tensor_tensor(sOut[:, csl], sRaw[:, csl], bank(6, TT), OP.mult))
                p.dvx(dv, dv.tensor_tensor(sOut[:, csl], sOut[:, csl], bank(7, TT), OP.add))
            p.wait(sy, "dv")
            p.inc(sy.dma_start(out=out_r[:, :, tsl],
                               in_=sOut[:, :].rearrange("p (c t) -> p c t", c=8)), "st", 16)
        p.wait(gp, "st")
        p.wait(gp, "ld")
    return nc


def prep_G_inputs(x3fs, x3bs, w_gate, w_up, w_down):
    consts = _consts()
    wgT = np.ascontiguousarray(np.asarray(w_gate).T.astype(np.int32))
    wuT = np.ascontiguousarray(np.asarray(w_up).T.astype(np.int32))
    wd = np.asarray(w_down)
    wd3 = np.ascontiguousarray(wd.reshape(8, 128, 3072).transpose(0, 2, 1).astype(np.int32))
    in_maps = []
    for cid in range(N_CORES):
        m = dict(
            x3fi=f32(x3fs[cid]),
            x3bi=np.ascontiguousarray(x3bs[cid]),
            wgT=wgT, wuT=wuT, wd3=wd3,
            **consts,
        )
        in_maps.append(m)
    return in_maps



def kernel(**inputs):
    run = lambda nc, maps: run_bass_kernel_spmd(nc, maps, core_ids=list(range(N_CORES))).results

    resA = run(build_launch_A(),
               prep_A_inputs(inputs["L0_attn_x"], inputs["L0_attn_norm_w"],
                             inputs["L0_attn_in_proj_w"], inputs["L0_attn_conv_w"],
                             inputs["L0_attn_conv_b"], inputs["L0_attn_A_log"],
                             inputs["L0_attn_D"], inputs["L0_attn_dt_bias"],
                             inputs["L0_attn_out_proj_w"]))
    pouts = [resA[c]["pout"] for c in range(N_CORES)]

    resB = run(build_launch_B(), prep_B_inputs(inputs["L0_attn_x"], pouts, inputs["L0_ffn_wg"]))
    x1fs = [resB[c]["x1f"] for c in range(N_CORES)]
    x1bs = [resB[c]["x1b"] for c in range(N_CORES)]
    scores = np.concatenate([resB[c]["sct"] for c in range(N_CORES)], axis=1).T

    resC = run(build_launch_C(),
               prep_C_inputs(x1bs, scores, inputs["L0_ffn_w1"], inputs["L0_ffn_w3"],
                             inputs["L0_ffn_w2"]))
    eos_full = [resC[e]["eo"] for e in range(N_CORES)]

    resD = run(build_launch_D(), prep_D_inputs(x1fs, eos_full))
    x2fs = [resD[c]["x2f"] for c in range(N_CORES)]
    x2bs = [resD[c]["x2b"] for c in range(N_CORES)]

    resE = run(build_launch_E(),
               prep_E_inputs(x2bs, inputs["L1_attn_Wq"], inputs["L1_attn_Wk"],
                             inputs["L1_attn_Wv"], inputs["L1_attn_proj"]))
    phiqs = [resE[c]["phiq"] for c in range(N_CORES)]
    kvps = [resE[c]["kvp"] for c in range(N_CORES)]

    resF = run(build_launch_F(),
               prep_F_inputs(x2fs, phiqs, kvps, inputs["L1_attn_Wo"], inputs["L1_attn_bo"]))
    x3fs = [resF[c]["x3f"] for c in range(N_CORES)]
    x3bs = [resF[c]["x3b"] for c in range(N_CORES)]

    resG = run(build_launch_G(),
               prep_G_inputs(x3fs, x3bs, inputs["L1_ffn_w_gate"], inputs["L1_ffn_w_up"],
                             inputs["L1_ffn_w_down"]))
    out = np.zeros((2, 8192, 1024), np.float32)
    for cid in range(N_CORES):
        b, q = cid // 4, cid % 4
        out[b, 2048 * q: 2048 * q + 2048] = resG[cid]["outf"].T
    return out


# version 03
# speedup vs baseline: 1.6834x; 1.1442x over previous
from contextlib import ExitStack

import numpy as np
import ml_dtypes

import concourse.bass as bass
import concourse.mybir as mybir
from concourse.alu_op_type import AluOpType
from concourse.bass_utils import run_bass_kernel_spmd

F32 = mybir.dt.float32
BF16 = mybir.dt.bfloat16
I32 = mybir.dt.int32
U32 = mybir.dt.uint32
AF = mybir.ActivationFunctionType
OP = AluOpType

BF = ml_dtypes.bfloat16

D_MODEL = 1024
NB = 2
T_FULL = 8192
N_CORES = 8
TPC = 2048

K_CONV = 4
MEPS = 1e-6
LNEPS = 1e-5


class P:

    def __init__(self, nc, stack, names):
        self.nc = nc
        self.sems = {}
        self.cnt = {}
        for name in names:
            self.sems[name] = stack.enter_context(nc.semaphore(name))
            self.cnt[name] = 0

    def inc(self, instr, name, n=1):
        instr.then_inc(self.sems[name], n)
        self.cnt[name] += n
        return instr

    def wait(self, eng, *names):
        for name in names:
            eng.wait_ge(self.sems[name], self.cnt[name])

    def acx(self, eng, instr):
        self.inc(instr, "ac")
        eng.drain()
        return instr

    def dvx(self, eng, instr):
        self.inc(instr, "dv")
        eng.drain()
        return instr

    def at(self, name):
        return self.cnt[name]

    def wait_at(self, eng, name, val):
        if val > 0:
            eng.wait_ge(self.sems[name], val)


def _consts():
    return dict(
        onesr=np.ones((1, 128), np.float32),
        trilc=np.triu(np.ones((128, 128), np.float32)),
        eyeb=np.eye(128, dtype=BF),
        eyef=np.eye(128, dtype=np.float32),
    )


def _add_const_tensors(nc):
    onesr = nc.dram_tensor("onesr", [1, 128], F32, kind="ExternalInput").ap()
    trilc = nc.dram_tensor("trilc", [128, 128], F32, kind="ExternalInput").ap()
    eyeb = nc.dram_tensor("eyeb", [128, 128], BF16, kind="ExternalInput").ap()
    eyef = nc.dram_tensor("eyef", [128, 128], F32, kind="ExternalInput").ap()
    return onesr, trilc, eyeb, eyef



def build_launch_A(ntile=None, dbg=False):
    nc = bass.Bass("TRN2", target_bir_lowering=False)
    TT = 512
    NTILE = ntile or (T_FULL // TT)
    NCH = 4
    HALO = 3
    NCK = TT // 128

    xb = nc.dram_tensor("xb", [D_MODEL, T_FULL], F32, kind="ExternalInput").ap()
    wzx = nc.dram_tensor("wzx", [D_MODEL, 1024], F32, kind="ExternalInput").ap()
    wbc = nc.dram_tensor("wbc", [D_MODEL, 256], F32, kind="ExternalInput").ap()
    wdt = nc.dram_tensor("wdt", [D_MODEL, 8], F32, kind="ExternalInput").ap()
    cw = nc.dram_tensor("cw", [512, K_CONV], F32, kind="ExternalInput").ap()
    cb = nc.dram_tensor("cb", [128, 4], F32, kind="ExternalInput").ap()
    alog = nc.dram_tensor("alog", [8, 1], F32, kind="ExternalInput").ap()
    dcol = nc.dram_tensor("dcol", [128, 4], F32, kind="ExternalInput").ap()
    dtb = nc.dram_tensor("dtb", [8, 1], F32, kind="ExternalInput").ap()
    nw = nc.dram_tensor("nw", [128, 8], F32, kind="ExternalInput").ap()
    wout = nc.dram_tensor("wout", [512, D_MODEL], F32, kind="ExternalInput").ap()
    selc = nc.dram_tensor("selc", [8, 1024], F32, kind="ExternalInput").ap()
    onesr, trilc, eyeb, eyef = _add_const_tensors(nc)
    pout = nc.dram_tensor("pout", [D_MODEL, T_FULL], F32, kind="ExternalOutput").ap()
    dbg_names = ["sH", "sHbf", "sDrow", "sRow8", "sDAc", "sCdb", "sCp", "sMaskT", "sU", "sUp",
                 "sColT", "sY", "sClrel", "sWs", "sDtsp", "sXhT", "sBsm", "sRow", "sMi", "sEm"]
    dbg_out = {}
    if dbg:
        shapes = {"sH": [128, 512], "sHbf": [128, 512], "sDrow": [1, 512], "sRow8": [1, 8],
                  "sDAc": [8, 1], "sCdb": [128, 128], "sCp": [128, 128], "sMaskT": [128, 128],
                  "sU": [128, 64], "sUp": [128, 64], "sColT": [128, 24], "sY": [128, 4 * 512],
                  "sClrel": [8, 512], "sWs": [8, 512], "sDtsp": [8, 512], "sXhT": [128, 512],
                  "sBsm": [128, 128], "sRow": [1, 128], "sMi": [128, 128], "sEm": [128, 128]}
        for n in dbg_names:
            dbg_out[n] = nc.dram_tensor("dbg_" + n, shapes[n], F32, kind="ExternalOutput").ap()

    xb_r = xb.rearrange("(c p) t -> p c t", p=128)
    pout_r = pout.rearrange("(c p) t -> p c t", p=128)

    with ExitStack() as st:
        sb = lambda name, shape, dt=F32: st.enter_context(nc.sbuf_tensor(name, shape, dt))
        sWzx = sb("sWzx", [128, 8 * 1024], BF16)
        sWbc = sb("sWbc", [128, 8 * 256], BF16)
        sWdt = sb("sWdt", [128, 8 * 8], BF16)
        sWout = sb("sWout", [128, 4 * 1024], BF16)
        sCw = sb("sCw", [128, NCH * K_CONV])
        sCb = sb("sCb", [128, NCH])
        sAlog = sb("sAlog", [8, 1])
        snegA = sb("snegA", [8, 1])
        sDcol = sb("sDcol", [128, 4])
        sDtb = sb("sDtb", [8, 1])
        sNw = sb("sNw", [128, 8])
        sOnes = sb("sOnes", [1, 128])
        sOnesCol = sb("sOnesCol", [128, 1])
        sTril = sb("sTril", [128, 128])
        sEyeB = sb("sEyeB", [128, 128], BF16)
        sEyeF = sb("sEyeF", [128, 128])
        sEpsR = sb("sEpsR", [1, 1])
        sH = sb("sH", [128, 512])
        sHbf = sb("sHbf", [128, 512], BF16)
        sClCar = sb("sClCar", [8, 1])
        sXraw = sb("sXraw", [128, 8 * TT])
        sSq = sb("sSq", [128, TT])
        sRrms = sb("sRrms", [1, TT])
        sRrms2 = sb("sRrms2", [1, TT])
        sTmpR3 = sb("sTmpR3", [8, TT])
        sXn = sb("sXn", [128, 8 * TT], BF16)
        sZ = sb("sZ", [128, NCH * TT], BF16)
        sSiluZ = sb("sSiluZ", [128, NCH * TT], BF16)
        sXs = sb("sXs", [128, NCH * (TT + HALO)], BF16)
        sCacc0 = sb("sCacc0", [128, TT])
        sCacc1 = sb("sCacc1", [128, TT])
        sXh = sb("sXh", [128, NCH * TT], BF16)
        sXhT = sb("sXhT", [128, NCH * 128], BF16)
        sB = sb("sB", [128, TT], BF16)
        sC = sb("sC", [128, TT], BF16)
        sBsm = sb("sBsm", [128, 128], BF16)
        sDtr = sb("sDtr", [8, TT])
        sTmpR = sb("sTmpR", [8, TT])
        sTmpR2 = sb("sTmpR2", [8, TT])
        sClrel = sb("sClrel", [8, TT])
        sWs = sb("sWs", [8, TT])
        sDtsp = sb("sDtsp", [8, TT])
        sCl = sb("sCl", [8, TT])
        sColT = sb("sColT", [128, 24])
        sRow = sb("sRow", [1, 128])
        sRow8 = sb("sRow8", [1, 8])
        sDrow = sb("sDrow", [1, 512])
        sDAc = sb("sDAc", [8, 1])
        sSelc = sb("sSelc", [8, 1024])
        sTrilG = sb("sTrilG", [128, 128], BF16)
        sMiA = sb("sMiA", [128, 1024])
        sEmA = sb("sEmA", [128, 1024], BF16)
        sCdbA = sb("sCdbA", [128, 1024], BF16)
        sMaskA = sb("sMaskA", [128, 1024], BF16)
        sCpA = sb("sCpA", [128, 1024], BF16)
        sUA = sb("sUA", [128, 512], BF16)
        sUpA = sb("sUpA", [128, 512], BF16)
        sDxh = sb("sDxh", [128, NCH * TT], BF16)
        sHtmp = sb("sHtmp", [128, 512])
        sY = sb("sY", [128, NCH * TT], BF16)
        sYevT = sb("sYevT", [128, 512])
        sPoutT = sb("sPoutT", [128, 512])

        ppool = st.enter_context(nc.psum_tensor("ppool", [128, 4096], F32))

        def bank(i, w=512, off=0):
            return ppool[:, i * 512 + off: i * 512 + off + w]

        def bankp(i, p0, prows, w, off=0):
            return ppool[p0:p0 + prows, i * 512 + off: i * 512 + off + w]

        p = P(nc, st, ["ld", "gld", "pe", "dv", "ac", "st"])
        sy, gp, pe, dv, ac = nc.sync, nc.gpsimd, nc.tensor, nc.vector, nc.scalar

        p.inc(sy.dma_start(out=sCw[:, :].rearrange("p (c k) -> p c k", c=NCH),
                           in_=cw.rearrange("(c p) k -> p c k", p=128)), "ld", 16)
        p.inc(sy.dma_start(out=sCb[:, :], in_=cb[:, :]), "ld", 16)
        p.inc(sy.dma_start(out=sAlog[:, :], in_=alog[:, :]), "ld", 16)
        p.inc(sy.dma_start(out=sDcol[:, :], in_=dcol[:, :]), "ld", 16)
        p.inc(sy.dma_start(out=sDtb[:, :], in_=dtb[:, :]), "ld", 16)
        p.inc(sy.dma_start(out=sNw[:, :], in_=nw[:, :]), "ld", 16)
        p.inc(sy.dma_start(out=sOnes[:, :], in_=onesr[:, :]), "ld", 16)
        p.inc(sy.dma_start(out=sTril[:, :], in_=trilc[:, :]), "ld", 16)
        p.inc(sy.dma_start(out=sEyeB[:, :], in_=eyeb[:, :]), "ld", 16)
        p.inc(sy.dma_start(out=sEyeF[:, :], in_=eyef[:, :]), "ld", 16)
        p.inc(sy.dma_start(out=sSelc[:, :], in_=selc[:, :]), "ld", 16)
        p.inc(gp.dma_start(out=sWzx[:, :].rearrange("p (c m) -> p c m", c=8),
                           in_=wzx.rearrange("(c p) m -> p c m", p=128)), "gld", 16)
        p.inc(gp.dma_start(out=sWbc[:, :].rearrange("p (c m) -> p c m", c=8),
                           in_=wbc.rearrange("(c p) m -> p c m", p=128)), "gld", 16)
        p.inc(gp.dma_start(out=sWdt[:, :].rearrange("p (c m) -> p c m", c=8),
                           in_=wdt.rearrange("(c p) m -> p c m", p=128)), "gld", 16)
        p.inc(gp.dma_start(out=sWout[:, :].rearrange("p (c m) -> p c m", c=4),
                           in_=wout.rearrange("(c p) m -> p c m", p=128)), "gld", 16)

        p.wait(dv, "ld", "gld")
        p.dvx(dv, dv.memset(sH[:, :], 0.0))
        p.dvx(dv, dv.memset(sHbf[:, :], 0.0))
        p.dvx(dv, dv.memset(sClCar[:, :], 0.0))
        p.dvx(dv, dv.memset(sXs[:, :], 0.0))
        p.dvx(dv, dv.memset(sOnesCol[:, :], 1.0))
        p.dvx(dv, dv.memset(sEpsR[:, :], MEPS))
        p.wait(ac, "dv")
        p.acx(ac, ac.activation(out=sDAc[:, :], in_=sAlog[:, :], func=AF.Exp))
        p.wait(dv, "ac")
        p.dvx(dv, dv.tensor_scalar(snegA[:, :], sDAc[:, :], -1.0, None, OP.mult))

        for tt in range(NTILE):
            tsl = slice(tt * TT, (tt + 1) * TT)
            p.wait(sy, "pe", "ac", "dv")
            p.inc(sy.dma_start(out=sXraw[:, :].rearrange("p (c t) -> p c t", c=8),
                               in_=xb_r[:, :, tsl]), "ld", 16)
            if tt > 0:
                for m in range(NCH):
                    base = m * (TT + HALO)
                    p.dvx(dv, dv.tensor_copy(sXs[:, base: base + HALO],
                                         sXs[:, base + TT: base + TT + HALO]))
            p.wait(ac, "ld")
            for c in range(8):
                if c > 0:
                    p.wait(ac, "pe")
                p.acx(ac, ac.activation(out=sSq[:, :], in_=sXraw[:, c * TT:(c + 1) * TT],
                                    func=AF.Square))
                p.wait(pe, "ac")
                p.inc(pe.matmul(bankp(0, 0, 1, TT), sOnesCol[:, :], sSq[:, :],
                                start=(c == 0), stop=(c == 7)), "pe")
            p.wait(ac, "pe")
            p.acx(ac, ac.activation(out=sRrms2[:, :], in_=bankp(0, 0, 1, TT), func=AF.Ln,
                                scale=1.0 / D_MODEL, bias=sEpsR[:, :]))
            p.acx(ac, ac.activation(out=sRrms[:, :], in_=sRrms2[:, :], func=AF.Exp,
                                scale=-0.5))
            p.wait(pe, "ac")
            p.inc(pe.matmul(bank(1, TT), sOnes[:, :], sRrms[:, :], start=True, stop=True), "pe")
            p.wait(dv, "pe", "ld")
            for c in range(8):
                p.dvx(dv, dv.scalar_tensor_tensor(sXn[:, c * TT:(c + 1) * TT],
                                              sXraw[:, c * TT:(c + 1) * TT],
                                              sNw[:, c:c + 1], bank(1, TT),
                                              OP.mult, OP.mult))
            p.wait(pe, "dv", "ac")
            for m in range(4):
                for k in range(8):
                    p.inc(pe.matmul(bank(m), sWzx[:, k * 1024 + m * 128: k * 1024 + (m + 1) * 128],
                                    sXn[:, k * TT:(k + 1) * TT],
                                    start=(k == 0), stop=(k == 7)), "pe")
            for m in range(4):
                for k in range(8):
                    p.inc(pe.matmul(bank(4 + m),
                                    sWzx[:, k * 1024 + 512 + m * 128: k * 1024 + 512 + (m + 1) * 128],
                                    sXn[:, k * TT:(k + 1) * TT],
                                    start=(k == 0), stop=(k == 7)), "pe")
            p.wait(dv, "pe")
            for m in range(4):
                p.dvx(dv, dv.tensor_copy(sZ[:, m * TT:(m + 1) * TT], bank(m)))
            for m in range(4):
                p.dvx(dv, dv.tensor_copy(sXs[:, m * (TT + HALO) + HALO: (m + 1) * (TT + HALO)],
                                     bank(4 + m)))
            p.wait(ac, "dv")
            for m in range(4):
                p.acx(ac, ac.activation(out=sSiluZ[:, m * TT:(m + 1) * TT],
                                    in_=sZ[:, m * TT:(m + 1) * TT], func=AF.Silu))
            p.wait(pe, "dv")
            for k in range(8):
                p.inc(pe.matmul(bank(0), sWbc[:, k * 256: k * 256 + 128],
                                sXn[:, k * TT:(k + 1) * TT], start=(k == 0), stop=(k == 7)), "pe")
            for k in range(8):
                p.inc(pe.matmul(bank(1), sWbc[:, k * 256 + 128: (k + 1) * 256],
                                sXn[:, k * TT:(k + 1) * TT], start=(k == 0), stop=(k == 7)), "pe")
            for k in range(8):
                p.inc(pe.matmul(bankp(2, 0, 8, TT), sWdt[:, k * 8:(k + 1) * 8],
                                sXn[:, k * TT:(k + 1) * TT], start=(k == 0), stop=(k == 7)), "pe")
            p.wait(dv, "pe")
            p.dvx(dv, dv.tensor_copy(sB[:, :], bank(0)))
            p.dvx(dv, dv.tensor_copy(sC[:, :], bank(1)))
            p.dvx(dv, dv.tensor_copy(sDtr[:, :], bankp(2, 0, 8, TT)))

            for m in range(NCH):
                base = m * (TT + HALO)
                p.wait(dv, "ac")
                p.dvx(dv, dv.tensor_scalar(sCacc0[:, :], sXs[:, base: base + TT],
                                       sCw[:, m * K_CONV: m * K_CONV + 1],
                                       sCb[:, m:m + 1], OP.mult, OP.add))
                accs = [sCacc0, sCacc1]
                for k in range(1, K_CONV):
                    src, dst = accs[(k - 1) % 2], accs[k % 2]
                    p.dvx(dv, dv.scalar_tensor_tensor(dst[:, :], sXs[:, base + k: base + k + TT],
                                                  sCw[:, m * K_CONV + k: m * K_CONV + k + 1],
                                                  src[:, :], OP.mult, OP.add))
                final = accs[(K_CONV - 1) % 2]
                p.wait(ac, "dv")
                p.acx(ac, ac.activation(out=sXh[:, m * TT:(m + 1) * TT], in_=final[:, :],
                                    func=AF.Silu))
            p.wait(dv, "ac")
            for m in range(NCH):
                p.dvx(dv, dv.tensor_scalar(sDxh[:, m * TT:(m + 1) * TT],
                                           sXh[:, m * TT:(m + 1) * TT],
                                           sDcol[:, m:m + 1], None, OP.mult))
            p.dvx(dv, dv.tensor_scalar(sTmpR[:, :], sDtr[:, :], sDtb[:, :], None, OP.add))
            p.wait(ac, "dv")
            p.acx(ac, ac.activation(out=sTmpR2[:, :], in_=sTmpR[:, :], func=AF.Abs))
            p.acx(ac, ac.activation(out=sTmpR3[:, :], in_=sTmpR2[:, :], func=AF.Exp, scale=-1.0))
            p.acx(ac, ac.activation(out=sTmpR2[:, :], in_=sTmpR3[:, :], func=AF.Ln, bias=1.0))
            p.wait(dv, "ac")
            p.dvx(dv, dv.tensor_scalar(sTmpR3[:, :], sTmpR[:, :], 0.0, None, OP.max))
            p.dvx(dv, dv.tensor_tensor(sDtsp[:, :], sTmpR3[:, :], sTmpR2[:, :], OP.add))
            p.dvx(dv, dv.tensor_scalar(sTmpR[:, :], sDtsp[:, :], snegA[:, :], None, OP.mult))
            p.dvx(dv, dv.tensor_tensor_scan(sCl[:, :], sTmpR[:, :], sTmpR[:, :], sClCar[:, :],
                                        OP.add, OP.bypass))
            for c in range(NCK):
                prevc = sClCar[:, :] if c == 0 else sCl[:, c * 128 - 1: c * 128]
                csl = slice(c * 128, (c + 1) * 128)
                p.dvx(dv, dv.tensor_scalar(sClrel[:, csl], sCl[:, csl], prevc, None,
                                       OP.subtract))
                endc = sClrel[:, c * 128 + 127: c * 128 + 128]
                p.dvx(dv, dv.tensor_scalar(sTmpR2[:, csl], sClrel[:, csl], endc, None,
                                       OP.subtract))
            p.dvx(dv, dv.tensor_copy(sClCar[:, :], sCl[:, TT - 1: TT]))
            p.wait(ac, "dv")
            p.acx(ac, ac.activation(out=sTmpR3[:, :], in_=sTmpR2[:, :], func=AF.Exp, scale=-1.0))
            p.wait(dv, "ac")
            p.dvx(dv, dv.tensor_tensor(sWs[:, :], sTmpR3[:, :], sDtsp[:, :], OP.mult))

            for c in range(NCK):
                csl = slice(c * 128, (c + 1) * 128)
                p.wait(pe, "dv")
                p.inc(pe.transpose(bankp(3, 0, 128, 8, off=0), sClrel[:, csl], sEyeF[0:8, 0:8]), "pe")
                p.inc(pe.transpose(bankp(3, 0, 128, 8, off=8), sWs[:, csl], sEyeF[0:8, 0:8]), "pe")
                p.inc(pe.transpose(bankp(3, 0, 128, 8, off=16), sDtsp[:, csl], sEyeF[0:8, 0:8]), "pe")
                p.wait(dv, "pe")
                p.dvx(dv, dv.tensor_copy(sColT[:, :], bankp(3, 0, 128, 24)))
                p.wait(ac, "dv")
                p.acx(ac, ac.activation(out=sDAc[:, :], in_=sClrel[:, c * 128 + 127: c * 128 + 128],
                                        func=AF.Exp))
                p.wait(pe, "dv")
                p.inc(pe.matmul(bankp(4, 0, 128, 128), sB[:, csl], sC[:, csl],
                                start=True, stop=True), "pe")
                p.inc(pe.transpose(bankp(4, 0, 128, 128, off=128).bitcast(BF16)[:, 0:128],
                                   sB[:, csl], sEyeB[:, :]), "pe")
                for m in range(NCH):
                    p.inc(pe.transpose(
                        bankp(4, 0, 128, 64, off=256 + m * 64).bitcast(BF16)[:, 0:128],
                        sXh[:, m * TT + c * 128: m * TT + (c + 1) * 128], sEyeB[:, :]), "pe")
                p.wait(dv, "pe")
                p.dvx(dv, dv.tensor_tensor(sTrilG[:, :], bankp(4, 0, 128, 128), sTril[:, :],
                                           OP.mult))
                p.dvx(dv, dv.tensor_copy(sBsm[:, :],
                                         bankp(4, 0, 128, 128, off=128).bitcast(BF16)[:, 0:128]))
                for m in range(NCH):
                    p.dvx(dv, dv.tensor_copy(
                        sXhT[:, m * 128:(m + 1) * 128],
                        bankp(4, 0, 128, 64, off=256 + m * 64).bitcast(BF16)[:, 0:128]))
                p.wait(pe, "dv", "ac")
                clb = ppool[:, 5 * 512: 5 * 512 + 1024]
                for h in range(8):
                    p.inc(pe.matmul(ppool[:, 5 * 512 + h * 128: 5 * 512 + (h + 1) * 128],
                                    sSelc[:, h * 128:(h + 1) * 128], sClrel[:, csl],
                                    start=True, stop=True), "pe")
                p.wait(dv, "pe")
                clcol_exp = sColT[:, 0:8].unsqueeze(2).broadcast_to([128, 8, 128])
                p.dvx(dv, dv.tensor_tensor(sMiA[:, :].rearrange("p (h t) -> p h t", h=8),
                                           clb.rearrange("p (h t) -> p h t", h=8),
                                           clcol_exp, OP.subtract))
                p.dvx(dv, dv.tensor_scalar(sMiA[:, :], sMiA[:, :], 0.0, None, OP.min))
                p.wait(ac, "dv")
                p.acx(ac, ac.activation(out=sEmA[:, :], in_=sMiA[:, :], func=AF.Exp))
                p.acx(ac, ac.activation(out=sCdbA[:, :], in_=clb, func=AF.Exp))
                p.wait(dv, "ac")
                trilg_exp = sTrilG[:, :].unsqueeze(1).broadcast_to([128, 8, 128])
                p.dvx(dv, dv.tensor_tensor(sMaskA[:, :].rearrange("p (h t) -> p h t", h=8),
                                           sEmA[:, :].rearrange("p (h t) -> p h t", h=8),
                                           trilg_exp, OP.mult))
                c_exp = sC[:, csl].unsqueeze(1).broadcast_to([128, 8, 128])
                p.dvx(dv, dv.tensor_tensor(sCpA[:, :].rearrange("p (h t) -> p h t", h=8),
                                           sCdbA[:, :].rearrange("p (h t) -> p h t", h=8),
                                           c_exp, OP.mult))
                dt_exp = sColT[:, 16:24].unsqueeze(2).broadcast_to([128, 8, 64])
                p.dvx(dv, dv.tensor_tensor(sUA[:, :].rearrange("p (h q) -> p h q", h=8),
                                           sXhT[:, :].rearrange("p (h q) -> p h q", h=8),
                                           dt_exp, OP.mult))
                ws_exp = sColT[:, 8:16].unsqueeze(2).broadcast_to([128, 8, 64])
                p.dvx(dv, dv.tensor_tensor(sUpA[:, :].rearrange("p (h q) -> p h q", h=8),
                                           sXhT[:, :].rearrange("p (h q) -> p h q", h=8),
                                           ws_exp, OP.mult))
                p.wait(pe, "dv")
                for h in range(8):
                    ybank = 7 if h % 2 == 0 else 2
                    rows = slice((h % 2) * 64, (h % 2) * 64 + 64)
                    yp = ppool[rows, ybank * 512 + (h // 2) * 128: ybank * 512 + (h // 2) * 128 + 128]
                    first = h in (0, 1)
                    last = h in (6, 7)
                    p.inc(pe.matmul(yp, sUA[:, h * 64:(h + 1) * 64],
                                    sMaskA[:, h * 128:(h + 1) * 128],
                                    start=first, stop=False), "pe")
                    p.inc(pe.matmul(yp, sHbf[:, h * 64:(h + 1) * 64],
                                    sCpA[:, h * 128:(h + 1) * 128],
                                    start=False, stop=last), "pe")
                    p.inc(pe.matmul(bank(0)[:, h * 64:(h + 1) * 64], sBsm[:, :],
                                    sUpA[:, h * 64:(h + 1) * 64],
                                    start=(h == 0), stop=(h == 7)), "pe")
                p.wait(dv, "pe")
                for par, ybank in ((0, 7), (1, 2)):
                    rows = slice(par * 64, par * 64 + 64)
                    ypsv = ppool[rows, ybank * 512: ybank * 512 + 512]
                    dxh_v = sDxh[:, :].rearrange("p (m t) -> p m t", m=NCH)[rows, :, csl]
                    silu_v = sSiluZ[:, :].rearrange("p (m t) -> p m t", m=NCH)[rows, :, csl]
                    y_v = sY[:, :].rearrange("p (m t) -> p m t", m=NCH)[rows, :, csl]
                    p.dvx(dv, dv.tensor_tensor(sYevT[rows, :].rearrange("p (m t) -> p m t", m=4),
                                               ypsv.rearrange("p (m t) -> p m t", m=4),
                                               dxh_v, OP.add))
                    p.dvx(dv, dv.tensor_tensor(y_v, sYevT[rows, :].rearrange("p (m t) -> p m t", m=4),
                                               silu_v, OP.mult))
                p.wait(pe, "ac", "dv")
                p.inc(pe.transpose(bankp(3, 0, 1, 8, off=256), sDAc[:, :], sEyeF[0:8, 0:8]), "pe")
                p.wait(dv, "pe")
                p.dvx(dv, dv.tensor_copy(sRow8[:, :], bankp(3, 0, 1, 8, off=256)))
                p.dvx(dv, dv.tensor_copy(sDrow[:, :].rearrange("q (a b) -> q a b", a=8),
                                         sRow8[:, :].unsqueeze(2).broadcast_to([1, 8, 64])))
                p.wait(pe, "dv")
                p.inc(pe.matmul(bank(1), sOnes[:, :], sDrow[:, :], start=True, stop=True), "pe")
                p.wait(dv, "pe")
                p.dvx(dv, dv.tensor_tensor(sHtmp[:, :], sH[:, :], bank(1), OP.mult))
                p.dvx(dv, dv.tensor_tensor(sH[:, :], sHtmp[:, :], bank(0), OP.add))
                p.dvx(dv, dv.tensor_copy(sHbf[:, :], sH[:, :]))

            p.wait(pe, "dv")
            for m in range(8):
                for k in range(4):
                    p.inc(pe.matmul(bank(m), sWout[:, k * 1024 + m * 128: k * 1024 + (m + 1) * 128],
                                    sY[:, k * TT:(k + 1) * TT],
                                    start=(k == 0), stop=(k == 3)), "pe")
            p.wait(dv, "pe")
            for m in range(8):
                p.dvx(dv, dv.tensor_copy(sPoutT[:, :], bank(m)))
                p.wait(sy, "dv")
                p.inc(sy.dma_start(out=pout_r[:, m, tsl], in_=sPoutT[:, :]), "st", 16)
                p.wait(dv, "st")

        if dbg:
            local = dict(sH=sH, sHbf=sHbf, sDrow=sDrow, sRow8=sRow8, sDAc=sDAc, sCdb=sCdb,
                         sCp=sCp, sMaskT=sMaskT, sU=sU, sUp=sUp, sColT=sColT, sY=sY,
                         sClrel=sClrel, sWs=sWs, sDtsp=sDtsp, sXhT=sXhT, sBsm=sBsm, sRow=sRow,
                         sMi=sMi, sEm=sEm)
            p.wait(gp, "dv", "ac", "pe")
            for n in dbg_names:
                p.inc(gp.dma_start(out=dbg_out[n][:, :], in_=local[n][:, :]), "st", 16)
        p.wait(gp, "st")
        p.wait(gp, "ld")
    return nc


def prep_A_inputs(L0_attn_x, norm_w, in_proj_w, conv_w, conv_b, A_log, D, dt_bias, out_proj_w):
    consts = _consts()
    in_maps = []
    for cid in range(N_CORES):
        b, g = cid // 4, cid % 4
        hsl = slice(8 * g, 8 * g + 8)
        chsl = slice(512 * g, 512 * g + 512)
        wz = in_proj_w[chsl]
        wx = in_proj_w[2048 + 512 * g: 2048 + 512 * g + 512]
        wbc = in_proj_w[4096:4352]
        wdtr = in_proj_w[4352 + 8 * g: 4352 + 8 * g + 8]
        m = dict(
            xb=f32(L0_attn_x[b].T),
            wzx=f32(np.concatenate([wz, wx], 0).T),
            wbc=f32(wbc.T),
            wdt=f32(wdtr.T),
            cw=f32(conv_w[chsl]),
            cb=f32(conv_b[chsl].reshape(4, 128).T),
            alog=f32(A_log[hsl][:, None]),
            dcol=f32(np.repeat(np.asarray(D)[hsl], 64).reshape(4, 128).T),
            dtb=f32(dt_bias[hsl][:, None]),
            nw=f32(np.asarray(norm_w).reshape(8, 128).T),
            wout=f32(out_proj_w[:, chsl].T),
            selc=f32(np.repeat(np.eye(8, dtype=np.float32), 128, axis=1)),
            **consts,
        )
        in_maps.append(m)
    return in_maps


def f32(x):
    return np.ascontiguousarray(np.asarray(x), dtype=np.float32)






def build_launch_B(dbg=False):
    nc = bass.Bass("TRN2", target_bir_lowering=False)
    TT = 512
    NTILE = TPC // TT

    xo = nc.dram_tensor("xo", [D_MODEL, TPC], F32, kind="ExternalInput").ap()
    pps = [nc.dram_tensor(f"pp{j}", [D_MODEL, TPC], F32, kind="ExternalInput").ap()
           for j in range(4)]
    wg = nc.dram_tensor("wg", [D_MODEL, 8], F32, kind="ExternalInput").ap()
    onesr, trilc, eyeb, eyef = _add_const_tensors(nc)
    x1f = nc.dram_tensor("x1f", [D_MODEL, TPC], F32, kind="ExternalOutput").ap()
    x1b = nc.dram_tensor("x1b", [D_MODEL, TPC], BF16, kind="ExternalOutput").ap()
    sct = nc.dram_tensor("sct", [8, TPC], F32, kind="ExternalOutput").ap()
    dbgB = {}
    if dbg:
        for n, shp in [("sRaw", [128, 4096]), ("sR1", [1, 512]), ("sR2", [1, 512]),
                       ("sR3", [1, 512]), ("sAcc", [128, 4096]), ("sXo", [128, 4096]),
                       ("sSq", [128, 512])]:
            dbgB[n] = nc.dram_tensor("dbg_" + n, shp, F32, kind="ExternalOutput").ap()

    xo_r = xo.rearrange("(c p) t -> p c t", p=128)
    pp_r = [q.rearrange("(c p) t -> p c t", p=128) for q in pps]
    x1f_r = x1f.rearrange("(c p) t -> p c t", p=128)
    x1b_r = x1b.rearrange("(c p) t -> p c t", p=128)

    with ExitStack() as st:
        sb = lambda name, shape, dt=F32: st.enter_context(nc.sbuf_tensor(name, shape, dt))
        sAcc = sb("sAcc", [128, 8 * TT])
        sP1 = sb("sP1", [128, 8 * TT])
        sXo = sb("sXo", [128, 8 * TT])
        sRaw = sb("sRaw", [128, 8 * TT])
        sSq = sb("sSq", [128, TT])
        sR1 = sb("sR1", [1, TT])
        sR2 = sb("sR2", [1, TT])
        sR3 = sb("sR3", [1, TT])
        sX1 = sb("sX1", [128, 8 * TT])
        sX1b = sb("sX1b", [128, 8 * TT], BF16)
        sWg = sb("sWg", [128, 8 * 8])
        sOnes = sb("sOnes", [1, 128])
        sOnesCol = sb("sOnesCol", [128, 1])
        sEyeF = sb("sEyeF", [128, 128])
        sEpsR = sb("sEpsR", [1, 1])
        sMx = sb("sMx", [128, 1])
        sSc = sb("sSc", [128, 8])
        sScE = sb("sScE", [128, 8])
        sScT = sb("sScT", [8, TT])
        sRec = sb("sRec", [128, 1])

        ppool = st.enter_context(nc.psum_tensor("ppool", [128, 4096], F32))

        def bank(i, w=512, off=0):
            return ppool[:, i * 512 + off: i * 512 + off + w]

        def bankp(i, p0, prows, w, off=0):
            return ppool[p0:p0 + prows, i * 512 + off: i * 512 + off + w]

        p = P(nc, st, ["ld", "pe", "dv", "ac", "st"])
        sy, gp, pe, dv, ac = nc.sync, nc.gpsimd, nc.tensor, nc.vector, nc.scalar

        p.inc(sy.dma_start(out=sOnes[:, :], in_=onesr[:, :]), "ld", 16)
        p.inc(sy.dma_start(out=sEyeF[:, :], in_=eyef[:, :]), "ld", 16)
        p.inc(sy.dma_start(out=sWg[:, :].rearrange("p (c m) -> p c m", c=8),
                           in_=wg.rearrange("(c p) m -> p c m", p=128)), "ld", 16)
        p.wait(dv, "ld")
        p.dvx(dv, dv.memset(sOnesCol[:, :], 1.0))
        p.dvx(dv, dv.memset(sEpsR[:, :], LNEPS))

        for tt in range(NTILE):
            tsl = slice(tt * TT, (tt + 1) * TT)
            p.wait(sy, "pe", "dv", "ac")
            p.inc(sy.dma_start(out=sXo[:, :].rearrange("p (c t) -> p c t", c=8),
                               in_=xo_r[:, :, tsl]), "ld", 16)
            p.inc(sy.dma_start(out=sAcc[:, :].rearrange("p (c t) -> p c t", c=8),
                               in_=pp_r[0][:, :, tsl]), "ld", 16)
            for j in range(1, 4):
                if j > 1:
                    p.wait(sy, "dv")
                p.inc(sy.dma_start(out=sP1[:, :].rearrange("p (c t) -> p c t", c=8),
                                   in_=pp_r[j][:, :, tsl]), "ld", 16)
                p.wait(dv, "ld")
                p.dvx(dv, dv.tensor_tensor(sAcc[:, :], sAcc[:, :], sP1[:, :], OP.add))
            p.wait(dv, "ld")
            p.dvx(dv, dv.scalar_tensor_tensor(sRaw[:, :], sXo[:, :], 2.0, sAcc[:, :],
                                              OP.mult, OP.add))
            p.wait(pe, "dv")
            for c in range(8):
                p.inc(pe.matmul(bankp(0, 0, 1, TT), sOnesCol[:, :],
                                sRaw[:, c * TT:(c + 1) * TT],
                                start=(c == 0), stop=(c == 7)), "pe")
            p.wait(ac, "dv")
            for c in range(8):
                if c > 0:
                    p.wait(ac, "pe")
                p.acx(ac, ac.activation(out=sSq[:, :], in_=sRaw[:, c * TT:(c + 1) * TT],
                                        func=AF.Square))
                p.wait(pe, "ac")
                p.inc(pe.matmul(bankp(1, 0, 1, TT), sOnesCol[:, :], sSq[:, :],
                                start=(c == 0), stop=(c == 7)), "pe")
            p.wait(dv, "pe")
            p.dvx(dv, dv.tensor_scalar(sR1[:, :], bankp(0, 0, 1, TT), 1.0 / D_MODEL,
                                       None, OP.mult))
            p.dvx(dv, dv.tensor_scalar(sR2[:, :], bankp(1, 0, 1, TT), 1.0 / D_MODEL,
                                       None, OP.mult))
            p.dvx(dv, dv.tensor_tensor(sR3[:, :], sR1[:, :], sR1[:, :], OP.mult))
            p.dvx(dv, dv.tensor_tensor(sR2[:, :], sR2[:, :], sR3[:, :], OP.subtract))
            p.wait(ac, "dv")
            p.acx(ac, ac.activation(out=sR3[:, :], in_=sR2[:, :], func=AF.Ln, bias=sEpsR[:, :]))
            p.acx(ac, ac.activation(out=sR2[:, :], in_=sR3[:, :], func=AF.Exp, scale=-0.5))
            p.wait(dv, "ac")
            p.dvx(dv, dv.scalar_tensor_tensor(sR3[:, :], sR1[:, :], -1.0, sR2[:, :],
                                              OP.mult, OP.mult))
            p.wait(pe, "dv")
            p.inc(pe.matmul(bank(2, TT), sOnes[:, :], sR2[:, :], start=True, stop=True), "pe")
            p.inc(pe.matmul(bank(3, TT), sOnes[:, :], sR3[:, :], start=True, stop=True), "pe")
            p.wait(dv, "pe", "st")
            for c in range(8):
                csl = slice(c * TT, (c + 1) * TT)
                p.dvx(dv, dv.tensor_tensor(sX1[:, csl], sRaw[:, csl], bank(2, TT), OP.mult))
                p.dvx(dv, dv.tensor_tensor(sX1[:, csl], sX1[:, csl], bank(3, TT), OP.add))
                p.dvx(dv, dv.tensor_copy(sX1b[:, csl], sX1[:, csl]))
            p.wait(sy, "dv")
            p.inc(sy.dma_start(out=x1f_r[:, :, tsl],
                               in_=sX1[:, :].rearrange("p (c t) -> p c t", c=8)), "st", 16)
            p.inc(sy.dma_start(out=x1b_r[:, :, tsl],
                               in_=sX1b[:, :].rearrange("p (c t) -> p c t", c=8)), "st", 16)
            for tg in range(TT // 128):
                p.wait(pe, "dv", "ac")
                for k in range(8):
                    p.inc(pe.matmul(bankp(4, 0, 128, 8), sX1[:, k * TT + tg * 128: k * TT + (tg + 1) * 128],
                                    sWg[:, k * 8:(k + 1) * 8],
                                    start=(k == 0), stop=(k == 7)), "pe")
                p.wait(dv, "pe")
                p.dvx(dv, dv.reduce_max(sMx[:, :], bankp(4, 0, 128, 8), axis=mybir.AxisListType.X))
                p.dvx(dv, dv.tensor_scalar(sMx[:, :], sMx[:, :], -1.0, None, OP.mult))
                p.wait(ac, "dv")
                p.acx(ac, ac.activation(out=sScE[:, :], in_=bankp(4, 0, 128, 8), func=AF.Exp,
                                        bias=sMx[:, :]))
                p.wait(dv, "ac")
                p.dvx(dv, dv.reduce_sum(sRec[:, :], sScE[:, :], axis=mybir.AxisListType.X))
                p.dvx(dv, dv.reciprocal(sRec[:, :], sRec[:, :]))
                p.dvx(dv, dv.tensor_scalar(sSc[:, :], sScE[:, :], sRec[:, :], None, OP.mult))
                p.wait(pe, "dv")
                p.inc(pe.transpose(bankp(5, 0, 8, 128), sSc[:, :], sEyeF[:, :]), "pe")
                p.wait(dv, "pe")
                if tg == 0:
                    p.wait(dv, "st")
                p.dvx(dv, dv.tensor_copy(sScT[0:8, tg * 128:(tg + 1) * 128], bankp(5, 0, 8, 128)))
            p.wait(sy, "dv")
            p.inc(sy.dma_start(out=sct[:, tsl], in_=sScT[0:8, :]), "st", 16)
        if dbg:
            local = dict(sRaw=sRaw, sR1=sR1, sR2=sR2, sR3=sR3, sAcc=sAcc, sXo=sXo, sSq=sSq)
            p.wait(gp, "dv", "ac", "pe")
            for n, ap in dbgB.items():
                p.inc(gp.dma_start(out=ap[:, :], in_=local[n][:, :]), "st", 16)
        p.wait(gp, "st")
        p.wait(gp, "ld")
    return nc


def prep_B_inputs(x_full, poutsA, wg):
    consts = _consts()
    in_maps = []
    for cid in range(N_CORES):
        b, q = cid // 4, cid % 4
        tsl = slice(2048 * q, 2048 * q + 2048)
        m = dict(
            xo=f32(x_full[b].T[:, tsl]),
            wg=f32(wg),
            **consts,
        )
        for j in range(4):
            m[f"pp{j}"] = f32(poutsA[4 * b + j][:, tsl])
        in_maps.append(m)
    return in_maps



def build_launch_C():
    nc = bass.Bass("TRN2", target_bir_lowering=False)
    TT = 512
    NTOK = 16384
    NTILE = NTOK // TT
    NITER = 42
    CAP = 4096.0

    x1all = nc.dram_tensor("x1all", [D_MODEL, NTOK], BF16, kind="ExternalInput").ap()
    srow = nc.dram_tensor("srow", [128, 128], F32, kind="ExternalInput").ap()
    w1 = nc.dram_tensor("w1", [D_MODEL, 2048], F32, kind="ExternalInput").ap()
    w3 = nc.dram_tensor("w3", [D_MODEL, 2048], F32, kind="ExternalInput").ap()
    w2 = nc.dram_tensor("w2", [2048, D_MODEL], F32, kind="ExternalInput").ap()
    triu1 = nc.dram_tensor("triu1", [128, 128], F32, kind="ExternalInput").ap()
    onesr, trilc, eyeb, eyef = _add_const_tensors(nc)
    eo = nc.dram_tensor("eo", [D_MODEL, NTOK], BF16, kind="ExternalOutput").ap()
    gateD = nc.dram_tensor("gateD", [128, 128], F32, kind="Internal").ap()

    x1_r = x1all.rearrange("(c p) t -> p c t", p=128)
    eo_r = eo.rearrange("(c p) t -> p c t", p=128)
    gateD_f = gateD.rearrange("p f -> (p f)")

    with ExitStack() as st:
        sb = lambda name, shape, dt=F32: st.enter_context(nc.sbuf_tensor(name, shape, dt))
        sW1 = sb("sW1", [128, 8 * 2048], BF16)
        sW3 = sb("sW3", [128, 8 * 2048], BF16)
        sW2 = sb("sW2", [128, 16 * 1024], BF16)
        sGrid = sb("sGrid", [128, 128])
        sLo = sb("sLo", [128, 1])
        sHi = sb("sHi", [128, 1])
        sMid = sb("sMid", [128, 1])
        sCnt = sb("sCnt", [128, 1])
        sSel = sb("sSel", [128, 1])
        sT1 = sb("sT1", [128, 1])
        sT2 = sb("sT2", [128, 1])
        sR11 = sb("sR11", [1, 1])
        sTrash = sb("sTrash", [128, 128], BF16)
        sEq = sb("sEq", [128, 128])
        sRk = sb("sRk", [128, 128])
        sRk2 = sb("sRk2", [128, 128])
        sMask = sb("sMask", [128, 128])
        sGate = sb("sGate", [128, 128])
        sTriu1 = sb("sTriu1", [128, 128])
        sOnes = sb("sOnes", [1, 128])
        sOnesCol = sb("sOnesCol", [128, 1])
        sGrow = sb("sGrow", [1, TT])
        sGb = sb("sGb", [128, TT], BF16)
        sXt = sb("sXt", [128, 8 * TT], BF16)
        sRt = sb("sRt", [128, 8 * TT], BF16)
        sGs = sb("sGs", [128, TT], BF16)
        sGs2 = sb("sGs2", [128, TT], BF16)
        sH1 = sb("sH1", [128, 16 * TT], BF16)
        sEo = sb("sEo", [128, 8 * TT], BF16)

        ppool = st.enter_context(nc.psum_tensor("ppool", [128, 4096], F32))

        def bank(i, w=512, off=0):
            return ppool[:, i * 512 + off: i * 512 + off + w]

        def bankp(i, p0, prows, w, off=0):
            return ppool[p0:p0 + prows, i * 512 + off: i * 512 + off + w]

        p = P(nc, st, ["ld", "gld", "pe", "dv", "ac", "st"])
        sy, gp, pe, dv, ac = nc.sync, nc.gpsimd, nc.tensor, nc.vector, nc.scalar

        p.inc(sy.dma_start(out=sGrid[:, :], in_=srow[:, :]), "ld", 16)
        p.inc(sy.dma_start(out=sTriu1[:, :], in_=triu1[:, :]), "ld", 16)
        p.inc(sy.dma_start(out=sOnes[:, :], in_=onesr[:, :]), "ld", 16)
        p.inc(gp.dma_start(out=sW1[:, :].rearrange("p (c m) -> p c m", c=8),
                           in_=w1.rearrange("(c p) m -> p c m", p=128)), "gld", 16)
        p.inc(gp.dma_start(out=sW3[:, :].rearrange("p (c m) -> p c m", c=8),
                           in_=w3.rearrange("(c p) m -> p c m", p=128)), "gld", 16)
        p.inc(gp.dma_start(out=sW2[:, :].rearrange("p (c m) -> p c m", c=16),
                           in_=w2.rearrange("(c p) m -> p c m", p=128)), "gld", 16)

        p.wait(dv, "ld")
        p.dvx(dv, dv.memset(sOnesCol[:, :], 1.0))
        p.dvx(dv, dv.memset(sLo[:, :], 0.0))
        p.dvx(dv, dv.memset(sHi[:, :], 2.0))

        for it in range(NITER):
            p.dvx(dv, dv.tensor_scalar(sMid[:, :], sLo[:, :], sHi[:, :], 0.5,
                                       OP.add, OP.mult))
            p.dvx(dv, dv.tensor_scalar(sTrash[:, :], sGrid[:, :], sMid[:, :], 0.0,
                                       OP.is_ge, OP.add, accum_out=sCnt[:, :]))
            p.wait(pe, "dv")
            p.inc(pe.matmul(bankp(0, 0, 1, 1), sCnt[:, :], sOnesCol[:, :],
                            start=True, stop=True), "pe")
            p.wait(dv, "pe")
            p.dvx(dv, dv.tensor_copy(sR11[:, :], bankp(0, 0, 1, 1)))
            p.wait(pe, "dv")
            p.inc(pe.matmul(bankp(1, 0, 128, 1), sOnes[:, :], sR11[:, :],
                            start=True, stop=True), "pe")
            p.wait(dv, "pe")
            p.dvx(dv, dv.tensor_scalar(sSel[:, :], bankp(1, 0, 128, 1), CAP, None, OP.is_ge))
            p.dvx(dv, dv.tensor_tensor(sT1[:, :], sMid[:, :], sLo[:, :], OP.subtract))
            p.dvx(dv, dv.tensor_tensor(sT1[:, :], sT1[:, :], sSel[:, :], OP.mult))
            p.dvx(dv, dv.tensor_tensor(sLo[:, :], sLo[:, :], sT1[:, :], OP.add))
            p.dvx(dv, dv.tensor_tensor(sT2[:, :], sHi[:, :], sMid[:, :], OP.subtract))
            p.dvx(dv, dv.tensor_tensor(sT2[:, :], sT2[:, :], sSel[:, :], OP.mult))
            p.dvx(dv, dv.tensor_tensor(sHi[:, :], sMid[:, :], sT2[:, :], OP.add))
        p.dvx(dv, dv.tensor_scalar(sMask[:, :], sGrid[:, :], sLo[:, :], 0.0,
                                   OP.is_gt, OP.add, accum_out=sCnt[:, :]))
        p.wait(pe, "dv")
        p.inc(pe.matmul(bankp(0, 0, 1, 1), sCnt[:, :], sOnesCol[:, :],
                        start=True, stop=True), "pe")
        p.wait(dv, "pe")
        p.dvx(dv, dv.tensor_scalar(sR11[:, :], bankp(0, 0, 1, 1), -1.0, CAP,
                                   OP.mult, OP.add))
        p.wait(pe, "dv")
        p.inc(pe.matmul(bankp(1, 0, 128, 1), sOnes[:, :], sR11[:, :],
                        start=True, stop=True), "pe")
        p.wait(dv, "pe")
        p.dvx(dv, dv.tensor_copy(sT1[:, :], bankp(1, 0, 128, 1)))
        p.dvx(dv, dv.tensor_scalar(sEq[:, :], sGrid[:, :], sLo[:, :], 0.0,
                                   OP.is_equal, OP.add, accum_out=sCnt[:, :]))
        p.dvx(dv, dv.tensor_tensor_scan(sRk[:, :], sEq[:, :], sEq[:, :], 0.0,
                                        OP.add, OP.bypass))
        p.dvx(dv, dv.tensor_tensor(sRk[:, :], sRk[:, :], sEq[:, :], OP.subtract))
        p.wait(pe, "dv")
        p.inc(pe.matmul(bankp(2, 0, 128, 1), sTriu1[:, :], sCnt[:, :],
                        start=True, stop=True), "pe")
        p.wait(dv, "pe")
        p.dvx(dv, dv.tensor_scalar(sRk2[:, :], sRk[:, :], bankp(2, 0, 128, 1), None, OP.add))
        p.dvx(dv, dv.tensor_scalar(sRk2[:, :], sRk2[:, :], sT1[:, :], None, OP.is_lt))
        p.dvx(dv, dv.tensor_tensor(sRk2[:, :], sRk2[:, :], sEq[:, :], OP.mult))
        p.dvx(dv, dv.tensor_tensor(sMask[:, :], sMask[:, :], sRk2[:, :], OP.add))
        p.dvx(dv, dv.tensor_tensor(sGate[:, :], sMask[:, :], sGrid[:, :], OP.mult))
        p.wait(sy, "dv")
        p.inc(sy.dma_start(out=gateD[:, :], in_=sGate[:, :]), "st", 16)
        p.wait(sy, "st")

        p.wait(dv, "gld")
        for tt in range(NTILE):
            tsl = slice(tt * TT, (tt + 1) * TT)
            p.wait(sy, "pe", "dv", "ac")
            p.inc(sy.dma_start(out=sXt[:, :].rearrange("p (c t) -> p c t", c=8),
                               in_=x1_r[:, :, tsl]), "ld", 16)
            p.inc(sy.dma_start(out=sGrow[:, :], in_=gateD_f[tt * TT:(tt + 1) * TT].unsqueeze(0)),
                  "ld", 16)
            p.wait(pe, "ld", "dv")
            p.inc(pe.matmul(bank(3, TT), sOnes[:, :], sGrow[:, :], start=True, stop=True), "pe")
            p.wait(dv, "pe", "ld")
            p.dvx(dv, dv.tensor_copy(sGb[:, :], bank(3, TT)))
            for c in range(8):
                csl = slice(c * TT, (c + 1) * TT)
                p.dvx(dv, dv.tensor_tensor(sRt[:, csl], sXt[:, csl], sGb[:, :], OP.mult))
            consumed_dv = [0] * 16
            consumed_ac = [0] * 16
            for m in range(16):
                gb, ub = (bank(0), bank(1)) if m % 2 == 0 else (bank(4), bank(5))
                if m >= 2:
                    p.wait_at(pe, "dv", consumed_dv[m - 2])
                    p.wait_at(pe, "ac", consumed_ac[m - 2])
                elif m == 0:
                    p.wait(pe, "dv", "ac")
                for k in range(8):
                    p.inc(pe.matmul(gb, sW1[:, k * 2048 + m * 128: k * 2048 + (m + 1) * 128],
                                    sRt[:, k * TT:(k + 1) * TT],
                                    start=(k == 0), stop=(k == 7)), "pe")
                for k in range(8):
                    p.inc(pe.matmul(ub, sW3[:, k * 2048 + m * 128: k * 2048 + (m + 1) * 128],
                                    sRt[:, k * TT:(k + 1) * TT],
                                    start=(k == 0), stop=(k == 7)), "pe")
                p.wait(ac, "pe")
                sgs = sGs if m % 2 == 0 else sGs2
                p.acx(ac, ac.activation(out=sgs[:, :], in_=gb, func=AF.Silu))
                consumed_ac[m] = p.at("ac")
                p.wait(dv, "ac")
                p.dvx(dv, dv.tensor_tensor(sH1[:, m * TT:(m + 1) * TT], sgs[:, :],
                                           ub, OP.mult))
                consumed_dv[m] = p.at("dv")
            dcon = [0] * 8
            for dm in range(8):
                db = bank(2) if dm % 2 == 0 else bank(6)
                if dm >= 2:
                    p.wait_at(pe, "dv", dcon[dm - 2])
                else:
                    p.wait(pe, "dv")
                for k in range(16):
                    p.inc(pe.matmul(db, sW2[:, k * 1024 + dm * 128: k * 1024 + (dm + 1) * 128],
                                    sH1[:, k * TT:(k + 1) * TT],
                                    start=(k == 0), stop=(k == 15)), "pe")
                p.wait(dv, "pe")
                if dm == 0:
                    p.wait(dv, "st")
                p.dvx(dv, dv.tensor_copy(sEo[:, dm * TT:(dm + 1) * TT], db))
                dcon[dm] = p.at("dv")
            p.wait(sy, "dv")
            p.inc(sy.dma_start(out=eo_r[:, :, tsl],
                               in_=sEo[:, :].rearrange("p (c t) -> p c t", c=8)), "st", 16)
        p.wait(gp, "st")
        p.wait(gp, "ld")
    return nc


def prep_C_inputs(x1bs, scores, w1, w3, w2):
    consts = _consts()
    x1all = np.ascontiguousarray(np.concatenate(x1bs, axis=1))
    in_maps = []
    for e in range(N_CORES):
        m = dict(
            x1all=x1all,
            srow=f32(np.ascontiguousarray(scores[:, e]).reshape(128, 128)),
            w1=f32(w1[e]),
            w3=f32(w3[e]),
            w2=f32(w2[e]),
            triu1=np.triu(np.ones((128, 128), np.float32), 1),
            **consts,
        )
        in_maps.append(m)
    return in_maps



def build_launch_D():
    nc = bass.Bass("TRN2", target_bir_lowering=False)
    TT = 512
    NTILE = TPC // TT

    x1o = nc.dram_tensor("x1o", [D_MODEL, TPC], F32, kind="ExternalInput").ap()
    eos = [nc.dram_tensor(f"eo{j}", [D_MODEL, TPC], BF16, kind="ExternalInput").ap()
           for j in range(8)]
    onesr, trilc, eyeb, eyef = _add_const_tensors(nc)
    x2f = nc.dram_tensor("x2f", [D_MODEL, TPC], F32, kind="ExternalOutput").ap()
    x2b = nc.dram_tensor("x2b", [D_MODEL, TPC], BF16, kind="ExternalOutput").ap()

    x1_r = x1o.rearrange("(c p) t -> p c t", p=128)
    eo_r = [q.rearrange("(c p) t -> p c t", p=128) for q in eos]
    x2f_r = x2f.rearrange("(c p) t -> p c t", p=128)
    x2b_r = x2b.rearrange("(c p) t -> p c t", p=128)

    with ExitStack() as st:
        sb = lambda name, shape, dt=F32: st.enter_context(nc.sbuf_tensor(name, shape, dt))
        sAcc = sb("sAcc", [128, 8 * TT])
        sP1 = sb("sP1", [128, 8 * TT], BF16)
        sRaw = sb("sRaw", [128, 8 * TT])
        sSq = sb("sSq", [128, TT])
        sR1 = sb("sR1", [1, TT])
        sR2 = sb("sR2", [1, TT])
        sR3 = sb("sR3", [1, TT])
        sX2 = sb("sX2", [128, 8 * TT])
        sX2b = sb("sX2b", [128, 8 * TT], BF16)
        sOnes = sb("sOnes", [1, 128])
        sOnesCol = sb("sOnesCol", [128, 1])
        sEpsR = sb("sEpsR", [1, 1])

        ppool = st.enter_context(nc.psum_tensor("ppool", [128, 4096], F32))

        def bank(i, w=512, off=0):
            return ppool[:, i * 512 + off: i * 512 + off + w]

        def bankp(i, p0, prows, w, off=0):
            return ppool[p0:p0 + prows, i * 512 + off: i * 512 + off + w]

        p = P(nc, st, ["ld", "pe", "dv", "ac", "st"])
        sy, gp, pe, dv, ac = nc.sync, nc.gpsimd, nc.tensor, nc.vector, nc.scalar

        p.inc(sy.dma_start(out=sOnes[:, :], in_=onesr[:, :]), "ld", 16)
        p.wait(dv, "ld")
        p.dvx(dv, dv.memset(sOnesCol[:, :], 1.0))
        p.dvx(dv, dv.memset(sEpsR[:, :], LNEPS))

        for tt in range(NTILE):
            tsl = slice(tt * TT, (tt + 1) * TT)
            p.wait(sy, "pe", "dv", "ac")
            p.inc(sy.dma_start(out=sAcc[:, :].rearrange("p (c t) -> p c t", c=8),
                               in_=x1_r[:, :, tsl]), "ld", 16)
            for j in range(8):
                if j > 0:
                    p.wait(sy, "dv")
                p.inc(sy.dma_start(out=sP1[:, :].rearrange("p (c t) -> p c t", c=8),
                                   in_=eo_r[j][:, :, tsl]), "ld", 16)
                p.wait(dv, "ld")
                p.dvx(dv, dv.tensor_tensor(sAcc[:, :], sAcc[:, :], sP1[:, :], OP.add))
            p.dvx(dv, dv.tensor_copy(sRaw[:, :], sAcc[:, :]))
            p.wait(pe, "dv")
            for c in range(8):
                p.inc(pe.matmul(bankp(0, 0, 1, TT), sOnesCol[:, :],
                                sRaw[:, c * TT:(c + 1) * TT],
                                start=(c == 0), stop=(c == 7)), "pe")
            p.wait(ac, "dv")
            for c in range(8):
                if c > 0:
                    p.wait(ac, "pe")
                p.acx(ac, ac.activation(out=sSq[:, :], in_=sRaw[:, c * TT:(c + 1) * TT],
                                        func=AF.Square))
                p.wait(pe, "ac")
                p.inc(pe.matmul(bankp(1, 0, 1, TT), sOnesCol[:, :], sSq[:, :],
                                start=(c == 0), stop=(c == 7)), "pe")
            p.wait(dv, "pe")
            p.dvx(dv, dv.tensor_scalar(sR1[:, :], bankp(0, 0, 1, TT), 1.0 / D_MODEL,
                                       None, OP.mult))
            p.dvx(dv, dv.tensor_scalar(sR2[:, :], bankp(1, 0, 1, TT), 1.0 / D_MODEL,
                                       None, OP.mult))
            p.dvx(dv, dv.tensor_tensor(sR3[:, :], sR1[:, :], sR1[:, :], OP.mult))
            p.dvx(dv, dv.tensor_tensor(sR2[:, :], sR2[:, :], sR3[:, :], OP.subtract))
            p.wait(ac, "dv")
            p.acx(ac, ac.activation(out=sR3[:, :], in_=sR2[:, :], func=AF.Ln, bias=sEpsR[:, :]))
            p.acx(ac, ac.activation(out=sR2[:, :], in_=sR3[:, :], func=AF.Exp, scale=-0.5))
            p.wait(dv, "ac")
            p.dvx(dv, dv.scalar_tensor_tensor(sR3[:, :], sR1[:, :], -1.0, sR2[:, :],
                                              OP.mult, OP.mult))
            p.wait(pe, "dv")
            p.inc(pe.matmul(bank(2, TT), sOnes[:, :], sR2[:, :], start=True, stop=True), "pe")
            p.inc(pe.matmul(bank(3, TT), sOnes[:, :], sR3[:, :], start=True, stop=True), "pe")
            p.wait(dv, "pe", "st")
            for c in range(8):
                csl = slice(c * TT, (c + 1) * TT)
                p.dvx(dv, dv.tensor_tensor(sX2[:, csl], sRaw[:, csl], bank(2, TT), OP.mult))
                p.dvx(dv, dv.tensor_tensor(sX2[:, csl], sX2[:, csl], bank(3, TT), OP.add))
                p.dvx(dv, dv.tensor_copy(sX2b[:, csl], sX2[:, csl]))
            p.wait(sy, "dv")
            p.inc(sy.dma_start(out=x2f_r[:, :, tsl],
                               in_=sX2[:, :].rearrange("p (c t) -> p c t", c=8)), "st", 16)
            p.inc(sy.dma_start(out=x2b_r[:, :, tsl],
                               in_=sX2b[:, :].rearrange("p (c t) -> p c t", c=8)), "st", 16)
        p.wait(gp, "st")
        p.wait(gp, "ld")
    return nc


def prep_D_inputs(x1fs, eos_full):
    consts = _consts()
    in_maps = []
    for cid in range(N_CORES):
        tsl = slice(2048 * cid, 2048 * cid + 2048)
        m = dict(x1o=f32(x1fs[cid]), **consts)
        for j in range(8):
            m[f"eo{j}"] = np.ascontiguousarray(eos_full[j][:, tsl])
        in_maps.append(m)
    return in_maps



LNC16 = 2.772588722239781
EPS16 = 1e-6 / 16.0


def build_launch_E():
    nc = bass.Bass("TRN2", target_bir_lowering=False)
    T = TPC
    NH = 16

    x2bi = nc.dram_tensor("x2bi", [D_MODEL, T], BF16, kind="ExternalInput").ap()
    wqT = nc.dram_tensor("wqT", [D_MODEL, D_MODEL], F32, kind="ExternalInput").ap()
    wkT = nc.dram_tensor("wkT", [D_MODEL, D_MODEL], F32, kind="ExternalInput").ap()
    wvT = nc.dram_tensor("wvT", [D_MODEL, D_MODEL], F32, kind="ExternalInput").ap()
    projT = nc.dram_tensor("projT", [128, NH * 256], F32, kind="ExternalInput").ap()
    onesr, trilc, eyeb, eyef = _add_const_tensors(nc)
    phiq = nc.dram_tensor("phiq", [T, NH * 256], BF16, kind="ExternalOutput").ap()
    kvp = nc.dram_tensor("kvp", [NH, 2, 128, 65], F32, kind="ExternalOutput").ap()

    x2_r = x2bi.rearrange("(c p) t -> p c t", p=128)
    phiq_r = phiq.rearrange("(g p) m -> p g m", p=128)

    with ExitStack() as st:
        sb = lambda name, shape, dt=F32: st.enter_context(nc.sbuf_tensor(name, shape, dt))
        sX2 = sb("sX2", [128, 8 * T], BF16)
        sVt = sb("sVt", [128, 16 * 1024], BF16)
        sWq = sb("sWq", [128, 8 * 1024], BF16)
        sWk = sb("sWk", [128, 8 * 1024], BF16)
        sProj = sb("sProj", [128, NH * 256], BF16)
        sQh = sb("sQh", [128, T], BF16)
        sKh = sb("sKh", [128, T], BF16)
        sQsq = sb("sQsq", [128, T])
        sPhiQh = sb("sPhiQh", [128, 16 * 256], BF16)
        sMx4 = sb("sMx4", [128, 4])
        sB4 = sb("sB4", [128, 4])
        sXwS = sb("sXwS", [128, 1024])
        sPhi4 = sb("sPhi4", [128, 1024], BF16)
        sKVo = sb("sKVo", [128, 130])
        sOnesCol = sb("sOnesCol", [128, 1])
        sOnesColB = sb("sOnesColB", [128, 1], BF16)

        ppool = st.enter_context(nc.psum_tensor("ppool", [128, 4096], F32))

        def bank(i, w=512, off=0):
            return ppool[:, i * 512 + off: i * 512 + off + w]

        def bankp(i, p0, prows, w, off=0):
            return ppool[p0:p0 + prows, i * 512 + off: i * 512 + off + w]

        p = P(nc, st, ["ld", "gld", "pe", "dv", "ac", "st"])
        sy, gp, pe, dv, ac = nc.sync, nc.gpsimd, nc.tensor, nc.vector, nc.scalar

        p.inc(sy.dma_start(out=sX2[:, :].rearrange("p (c t) -> p c t", c=8),
                           in_=x2_r[:, :, :]), "ld", 16)
        p.inc(gp.dma_start(out=sProj[:, :], in_=projT[:, :]), "gld", 16)
        p.inc(gp.dma_start(out=sWq[:, :].rearrange("p (c m) -> p c m", c=8),
                           in_=wvT.rearrange("(c p) m -> p c m", p=128)), "gld", 16)
        p.wait(dv, "ld")
        p.dvx(dv, dv.memset(sOnesCol[:, :], 1.0))
        p.dvx(dv, dv.memset(sOnesColB[:, :], 1.0))
        p.wait(pe, "ld", "gld", "dv")
        for tg in range(16):
            p.wait(pe, "dv")
            for n in range(2):
                for k in range(8):
                    p.inc(pe.matmul(bank(n), sX2[:, k * T + tg * 128: k * T + (tg + 1) * 128],
                                    sWq[:, k * 1024 + n * 512:(k * 1024) + (n + 1) * 512],
                                    start=(k == 0), stop=(k == 7)), "pe")
            p.wait(dv, "pe")
            for n in range(2):
                p.dvx(dv, dv.tensor_copy(sVt[:, tg * 1024 + n * 512: tg * 1024 + (n + 1) * 512],
                                         bank(n)))
        p.wait(gp, "dv")
        p.inc(gp.dma_start(out=sWq[:, :].rearrange("p (c m) -> p c m", c=8),
                           in_=wqT.rearrange("(c p) m -> p c m", p=128)), "gld", 16)
        p.inc(gp.dma_start(out=sWk[:, :].rearrange("p (c m) -> p c m", c=8),
                           in_=wkT.rearrange("(c p) m -> p c m", p=128)), "gld", 16)
        p.wait(pe, "gld")

        for hp in range(NH // 2):
            for (buf, w) in ((sQh, sWq), (sKh, sWk)):
                for n in range(4):
                    p.wait(pe, "dv", "ac")
                    for k in range(8):
                        p.inc(pe.matmul(bankp(0, 0, 128, 512),
                                        w[:, k * 1024 + hp * 128: k * 1024 + (hp + 1) * 128],
                                        sX2[:, k * T + n * 512: k * T + (n + 1) * 512],
                                        start=(k == 0), stop=(k == 7)), "pe")
                    p.wait(dv, "pe")
                    p.dvx(dv, dv.tensor_scalar(buf[:, n * 512:(n + 1) * 512],
                                               bankp(0, 0, 128, 512), 0.125, None, OP.mult))
            for sub in range(2):
                h = hp * 2 + sub
                hof = h * 64
                rsl = slice(sub * 64, sub * 64 + 64)
                for side in range(2):
                    qb = sQh if side == 0 else sKh
                    p.wait(ac, "dv", "pe")
                    p.acx(ac, ac.activation(out=sQsq[rsl, :], in_=qb[rsl, :], func=AF.Square))
                    if side == 0:
                        p.wait(dv, "st")
                    for g in range(4):
                        p.wait(pe, "ac", "dv")
                        for j in range(4):
                            tg = g * 4 + j
                            p.inc(pe.matmul(bankp(4, 0, 128, 256, off=j * 256),
                                            qb[rsl, tg * 128:(tg + 1) * 128],
                                            sProj[rsl, h * 256:(h + 1) * 256],
                                            start=True, stop=True), "pe")
                            p.inc(pe.matmul(bankp(6, 0, 128, 1, off=j),
                                            sQsq[rsl, tg * 128:(tg + 1) * 128],
                                            sOnesCol[rsl, :], start=True, stop=True), "pe")
                        p.wait(dv, "pe")
                        xwp = ppool[:, 4 * 512: 4 * 512 + 1024]
                        p.dvx(dv, dv.tensor_reduce(sMx4[:, :],
                                                   xwp.rearrange("p (j m) -> p j m", j=4),
                                                   axis=mybir.AxisListType.X, op=OP.max))
                        p.inc(dv.scalar_tensor_tensor(sB4[:, :], bankp(6, 0, 128, 4), 0.5,
                                                      sMx4[:, :], OP.mult, OP.add), "dv")
                        dv.drain()
                        p.dvx(dv, dv.tensor_scalar(sB4[:, :], sB4[:, :], LNC16, -1.0,
                                                   OP.add, OP.mult))
                        b4e = sB4[:, :].unsqueeze(2).broadcast_to([128, 4, 256])
                        p.dvx(dv, dv.tensor_tensor(sXwS[:, :].rearrange("p (j m) -> p j m", j=4),
                                                   xwp.rearrange("p (j m) -> p j m", j=4),
                                                   b4e, OP.add))
                        p.wait(ac, "dv")
                        p.acx(ac, ac.activation(out=sPhi4[:, :], in_=sXwS[:, :], func=AF.Exp))
                        p.wait(dv, "ac")
                        if side == 0:
                            p.dvx(dv, dv.tensor_scalar(sPhiQh[:, g * 1024:(g + 1) * 1024],
                                                       sPhi4[:, :], EPS16, None, OP.add))
                        else:
                            p.dvx(dv, dv.tensor_scalar(sPhi4[:, :], sPhi4[:, :], EPS16,
                                                       None, OP.add))
                            p.wait(pe, "dv")
                            for j in range(4):
                                tg = g * 4 + j
                                for mc in range(2):
                                    p.inc(pe.matmul(bankp(mc, 0, 128, 64),
                                                    sPhi4[:, j * 256 + mc * 128: j * 256 + (mc + 1) * 128],
                                                    sVt[:, tg * 1024 + hof: tg * 1024 + hof + 64],
                                                    start=(tg == 0), stop=(tg == 15)), "pe")
                                    p.inc(pe.matmul(bankp(2 + mc, 0, 128, 1),
                                                    sPhi4[:, j * 256 + mc * 128: j * 256 + (mc + 1) * 128],
                                                    sOnesColB[:, :], start=(tg == 0), stop=(tg == 15)), "pe")
                    if side == 0:
                        p.wait(sy, "dv", "st")
                        p.inc(sy.dma_start(out=phiq_r[:, :, h * 256:(h + 1) * 256],
                                           in_=sPhiQh[:, :].rearrange("p (g m) -> p g m", g=16)),
                              "st", 16)
                p.wait(dv, "pe", "st")
                p.dvx(dv, dv.tensor_copy(sKVo[:, 0:64], bankp(0, 0, 128, 64)))
                p.dvx(dv, dv.tensor_copy(sKVo[:, 64:65], bankp(2, 0, 128, 1)))
                p.dvx(dv, dv.tensor_copy(sKVo[:, 65:129], bankp(1, 0, 128, 64)))
                p.dvx(dv, dv.tensor_copy(sKVo[:, 129:130], bankp(3, 0, 128, 1)))
                p.wait(sy, "dv")
                p.inc(sy.dma_start(out=kvp[h].rearrange("k p m -> p k m"),
                                   in_=sKVo[:, :].rearrange("p (k m) -> p k m", k=2)), "st", 16)
        p.wait(gp, "st")
        p.wait(gp, "ld")
        p.wait(gp, "gld")
    return nc


def prep_E_inputs(x2bs, Wq, Wk, Wv, proj):
    consts = _consts()
    in_maps = []
    for cid in range(N_CORES):
        m = dict(
            x2bi=np.ascontiguousarray(x2bs[cid]),
            wqT=f32(np.asarray(Wq).T),
            wkT=f32(np.asarray(Wk).T),
            wvT=f32(np.asarray(Wv).T),
            projT=f32(np.tile(np.transpose(np.asarray(proj), (2, 0, 1)), (2, 1, 1)).reshape(128, -1)),
            **consts,
        )
        in_maps.append(m)
    return in_maps



def build_launch_F():
    nc = bass.Bass("TRN2", target_bir_lowering=False)
    T = TPC
    NH = 16
    TT = 512

    x2o = nc.dram_tensor("x2o", [D_MODEL, T], F32, kind="ExternalInput").ap()
    phiqi = nc.dram_tensor("phiqi", [T, NH * 256], BF16, kind="ExternalInput").ap()
    kvps = [nc.dram_tensor(f"kvp{j}", [NH, 2, 128, 65], F32, kind="ExternalInput").ap()
            for j in range(4)]
    woT = nc.dram_tensor("woT", [D_MODEL, D_MODEL], F32, kind="ExternalInput").ap()
    bo = nc.dram_tensor("bo", [128, 8], F32, kind="ExternalInput").ap()
    onesr, trilc, eyeb, eyef = _add_const_tensors(nc)
    x3f = nc.dram_tensor("x3f", [D_MODEL, T], F32, kind="ExternalOutput").ap()
    x3b = nc.dram_tensor("x3b", [D_MODEL, T], BF16, kind="ExternalOutput").ap()

    x2_r = x2o.rearrange("(c p) t -> p c t", p=128)
    phiq_r = phiqi.rearrange("(g p) m -> p g m", p=128)
    x3f_r = x3f.rearrange("(c p) t -> p c t", p=128)
    x3b_r = x3b.rearrange("(c p) t -> p c t", p=128)

    with ExitStack() as st:
        sb = lambda name, shape, dt=F32: st.enter_context(nc.sbuf_tensor(name, shape, dt))
        sKV = sb("sKV", [128, NH * 2 * 65])
        sKV2 = sb("sKV2", [128, NH * 2 * 65])
        sKVb = sb("sKVb", [128, NH * 2 * 65], BF16)
        sKrow = sb("sKrow", [1, 256])
        sKb = sb("sKb", [128, 256])
        sPhiF = sb("sPhiF", [128, 1024], BF16)
        sPq4 = sb("sPq4", [128, 1024], BF16)
        sPqm4 = sb("sPqm4", [128, 1024], BF16)
        sT4 = sb("sT4", [128, 1024])
        sDen4 = sb("sDen4", [128, 4])
        sMerged = sb("sMerged", [128, 8 * T], BF16)
        sWo = sb("sWo", [128, 8 * 1024], BF16)
        sBo = sb("sBo", [128, 8])
        sXo = sb("sXo", [128, 8 * TT])
        sRaw = sb("sRaw", [128, 8 * TT])
        sSq = sb("sSq", [128, TT])
        sR1 = sb("sR1", [1, TT])
        sR2 = sb("sR2", [1, TT])
        sR3 = sb("sR3", [1, TT])
        sX3 = sb("sX3", [128, 8 * TT])
        sX3b = sb("sX3b", [128, 8 * TT], BF16)
        sOnes = sb("sOnes", [1, 128])
        sOnesCol = sb("sOnesCol", [128, 1])
        sEyeF = sb("sEyeF", [128, 128])
        sEyeB = sb("sEyeB", [128, 128], BF16)
        sEpsR = sb("sEpsR", [1, 1])

        ppool = st.enter_context(nc.psum_tensor("ppool", [128, 4096], F32))

        def bank(i, w=512, off=0):
            return ppool[:, i * 512 + off: i * 512 + off + w]

        def bankp(i, p0, prows, w, off=0):
            return ppool[p0:p0 + prows, i * 512 + off: i * 512 + off + w]

        p = P(nc, st, ["ld", "gld", "pe", "dv", "ac", "st"])
        sy, gp, pe, dv, ac = nc.sync, nc.gpsimd, nc.tensor, nc.vector, nc.scalar

        p.inc(sy.dma_start(out=sOnes[:, :], in_=onesr[:, :]), "ld", 16)
        p.inc(sy.dma_start(out=sEyeF[:, :], in_=eyef[:, :]), "ld", 16)
        p.inc(sy.dma_start(out=sEyeB[:, :], in_=eyeb[:, :]), "ld", 16)
        p.inc(sy.dma_start(out=sBo[:, :], in_=bo[:, :]), "ld", 16)
        p.inc(sy.dma_start(out=sKV[:, :].rearrange("p (h k m) -> p h k m", h=NH, k=2),
                           in_=kvps[0].rearrange("h k p m -> p h k m")), "ld", 16)
        p.inc(gp.dma_start(out=sWo[:, :].rearrange("p (c m) -> p c m", c=8),
                           in_=woT.rearrange("(c p) m -> p c m", p=128)), "gld", 16)
        p.wait(dv, "ld")
        p.dvx(dv, dv.memset(sOnesCol[:, :], 1.0))
        p.dvx(dv, dv.memset(sEpsR[:, :], LNEPS))
        for j in range(1, 4):
            if j > 1:
                p.wait(sy, "dv")
            p.inc(sy.dma_start(out=sKV2[:, :].rearrange("p (h k m) -> p h k m", h=NH, k=2),
                               in_=kvps[j].rearrange("h k p m -> p h k m")), "ld", 16)
            p.wait(dv, "ld")
            p.dvx(dv, dv.tensor_tensor(sKV[:, :], sKV[:, :], sKV2[:, :], OP.add))
        p.dvx(dv, dv.tensor_copy(sKVb[:, :], sKV[:, :]))

        for h in range(NH):
            hb = h * 130
            p.wait(pe, "dv", "ac")
            p.inc(pe.transpose(bankp(5, 0, 1, 128), sKV[:, hb + 64: hb + 65], sEyeF[:, :]), "pe")
            p.inc(pe.transpose(bankp(5, 0, 1, 128, off=128), sKV[:, hb + 129: hb + 130],
                               sEyeF[:, :]), "pe")
            p.wait(dv, "pe")
            p.dvx(dv, dv.tensor_copy(sKrow[:, 0:128], bankp(5, 0, 1, 128)))
            p.dvx(dv, dv.tensor_copy(sKrow[:, 128:256], bankp(5, 0, 1, 128, off=128)))
            p.wait(pe, "dv")
            p.inc(pe.matmul(bankp(6, 0, 128, 256), sOnes[:, :], sKrow[:, :],
                            start=True, stop=True), "pe")
            p.wait(dv, "pe")
            p.dvx(dv, dv.tensor_copy(sKb[:, :], bankp(6, 0, 128, 256)))
            for g in range(4):
                p.wait(sy, "dv", "pe")
                p.inc(sy.dma_start(out=sPhiF[:, :].rearrange("p (j m) -> p j m", j=4),
                                   in_=phiq_r[:, g * 4:(g + 1) * 4, h * 256:(h + 1) * 256]),
                      "ld", 16)
                p.wait(dv, "ld")
                ksb_e = sKb[:, :].unsqueeze(1).broadcast_to([128, 4, 256])
                p.dvx(dv, dv.tensor_tensor(sT4[:, :].rearrange("p (j m) -> p j m", j=4),
                                           sPhiF[:, :].rearrange("p (j m) -> p j m", j=4),
                                           ksb_e, OP.mult))
                p.dvx(dv, dv.tensor_reduce(sDen4[:, :],
                                           sT4[:, :].rearrange("p (j m) -> p j m", j=4),
                                           axis=mybir.AxisListType.X, op=OP.add))
                p.dvx(dv, dv.tensor_scalar(sDen4[:, :], sDen4[:, :], 1e-6, None, OP.add))
                p.dvx(dv, dv.reciprocal(sDen4[:, :], sDen4[:, :]))
                rd_e = sDen4[:, :].unsqueeze(2).broadcast_to([128, 4, 256])
                p.dvx(dv, dv.tensor_tensor(sPq4[:, :].rearrange("p (j m) -> p j m", j=4),
                                           sPhiF[:, :].rearrange("p (j m) -> p j m", j=4),
                                           rd_e, OP.mult))
                p.wait(pe, "dv")
                for q in range(8):
                    p.inc(pe.transpose(
                        bankp(5, 0, 128, 64, off=q * 64).bitcast(BF16)[:, 0:128],
                        sPq4[:, q * 128:(q + 1) * 128], sEyeB[:, :]), "pe")
                p.wait(dv, "pe")
                p.dvx(dv, dv.tensor_copy(sPqm4[:, :],
                                         bankp(5, 0, 128, 512).bitcast(BF16)[:, 0:1024]))
                p.wait(pe, "dv")
                rows = slice((h % 2) * 64, (h % 2) * 64 + 64)
                for j in range(4):
                    for mc in range(2):
                        p.inc(pe.matmul(
                            ppool[rows, 7 * 512 + j * 128: 7 * 512 + (j + 1) * 128],
                            sKVb[:, hb + mc * 65: hb + mc * 65 + 64],
                            sPqm4[:, j * 256 + mc * 128: j * 256 + (mc + 1) * 128],
                            start=(mc == 0), stop=(mc == 1)), "pe")
                p.wait(dv, "pe")
                p.dvx(dv, dv.tensor_copy(
                    sMerged[rows, (h // 2) * T + g * 512:(h // 2) * T + (g + 1) * 512],
                    ppool[rows, 7 * 512: 7 * 512 + 512]))

        for tt in range(T // TT):
            tsl = slice(tt * TT, (tt + 1) * TT)
            p.wait(sy, "pe", "dv", "ac")
            p.inc(sy.dma_start(out=sXo[:, :].rearrange("p (c t) -> p c t", c=8),
                               in_=x2_r[:, :, tsl]), "ld", 16)
            p.wait(pe, "dv", "ac")
            for m in range(8):
                for k in range(8):
                    p.inc(pe.matmul(bank(m % 4), sWo[:, k * 1024 + m * 128:(k * 1024) + (m + 1) * 128],
                                    sMerged[:, k * T + tt * TT: k * T + (tt + 1) * TT],
                                    start=(k == 0), stop=(k == 7)), "pe")
                p.wait(dv, "pe", "ld")
                p.dvx(dv, dv.scalar_tensor_tensor(sRaw[:, m * TT:(m + 1) * TT],
                                                  bank(m % 4), sBo[:, m:m + 1],
                                                  sXo[:, m * TT:(m + 1) * TT],
                                                  OP.add, OP.add))
                p.wait(pe, "dv")
            p.wait(pe, "dv")
            for c in range(8):
                p.inc(pe.matmul(bankp(4, 0, 1, TT), sOnesCol[:, :],
                                sRaw[:, c * TT:(c + 1) * TT],
                                start=(c == 0), stop=(c == 7)), "pe")
            p.wait(ac, "dv")
            for c in range(8):
                if c > 0:
                    p.wait(ac, "pe")
                p.acx(ac, ac.activation(out=sSq[:, :], in_=sRaw[:, c * TT:(c + 1) * TT],
                                        func=AF.Square))
                p.wait(pe, "ac")
                p.inc(pe.matmul(bankp(5, 0, 1, TT), sOnesCol[:, :], sSq[:, :],
                                start=(c == 0), stop=(c == 7)), "pe")
            p.wait(dv, "pe")
            p.dvx(dv, dv.tensor_scalar(sR1[:, :], bankp(4, 0, 1, TT), 1.0 / D_MODEL,
                                       None, OP.mult))
            p.dvx(dv, dv.tensor_scalar(sR2[:, :], bankp(5, 0, 1, TT), 1.0 / D_MODEL,
                                       None, OP.mult))
            p.dvx(dv, dv.tensor_tensor(sR3[:, :], sR1[:, :], sR1[:, :], OP.mult))
            p.dvx(dv, dv.tensor_tensor(sR2[:, :], sR2[:, :], sR3[:, :], OP.subtract))
            p.wait(ac, "dv")
            p.acx(ac, ac.activation(out=sR3[:, :], in_=sR2[:, :], func=AF.Ln, bias=sEpsR[:, :]))
            p.acx(ac, ac.activation(out=sR2[:, :], in_=sR3[:, :], func=AF.Exp, scale=-0.5))
            p.wait(dv, "ac")
            p.dvx(dv, dv.scalar_tensor_tensor(sR3[:, :], sR1[:, :], -1.0, sR2[:, :],
                                              OP.mult, OP.mult))
            p.wait(pe, "dv")
            p.inc(pe.matmul(bank(6, TT), sOnes[:, :], sR2[:, :], start=True, stop=True), "pe")
            p.inc(pe.matmul(bank(7, TT), sOnes[:, :], sR3[:, :], start=True, stop=True), "pe")
            p.wait(dv, "pe", "st")
            for c in range(8):
                csl = slice(c * TT, (c + 1) * TT)
                p.dvx(dv, dv.tensor_tensor(sX3[:, csl], sRaw[:, csl], bank(6, TT), OP.mult))
                p.dvx(dv, dv.tensor_tensor(sX3[:, csl], sX3[:, csl], bank(7, TT), OP.add))
                p.dvx(dv, dv.tensor_copy(sX3b[:, csl], sX3[:, csl]))
            p.wait(sy, "dv")
            p.inc(sy.dma_start(out=x3f_r[:, :, tsl],
                               in_=sX3[:, :].rearrange("p (c t) -> p c t", c=8)), "st", 16)
            p.inc(sy.dma_start(out=x3b_r[:, :, tsl],
                               in_=sX3b[:, :].rearrange("p (c t) -> p c t", c=8)), "st", 16)
        p.wait(gp, "st")
        p.wait(gp, "ld")
        p.wait(gp, "gld")
    return nc


def prep_F_inputs(x2fs, phiqs, kvps_all, Wo, bo):
    consts = _consts()
    in_maps = []
    for cid in range(N_CORES):
        b = cid // 4
        m = dict(
            x2o=f32(x2fs[cid]),
            phiqi=np.ascontiguousarray(phiqs[cid]),
            woT=f32(np.asarray(Wo).T),
            bo=f32(np.asarray(bo).reshape(8, 128).T),
            **consts,
        )
        for j in range(4):
            m[f"kvp{j}"] = f32(kvps_all[4 * b + j])
        in_maps.append(m)
    return in_maps



QS = 1.0 / 127.0


def build_launch_G():
    nc = bass.Bass("TRN2", target_bir_lowering=False)
    T = TPC
    TT = 512
    NHID = 3072

    x3fi = nc.dram_tensor("x3fi", [D_MODEL, T], F32, kind="ExternalInput").ap()
    x3bi = nc.dram_tensor("x3bi", [D_MODEL, T], BF16, kind="ExternalInput").ap()
    wgT = nc.dram_tensor("wgT", [D_MODEL, NHID], I32, kind="ExternalInput").ap()
    wuT = nc.dram_tensor("wuT", [D_MODEL, NHID], I32, kind="ExternalInput").ap()
    wd3 = nc.dram_tensor("wd3", [8, NHID, 128], I32, kind="ExternalInput").ap()
    onesr, trilc, eyeb, eyef = _add_const_tensors(nc)
    outf = nc.dram_tensor("outf", [D_MODEL, T], F32, kind="ExternalOutput").ap()

    x3f_r = x3fi.rearrange("(c p) t -> p c t", p=128)
    x3b_r = x3bi.rearrange("(c p) t -> p c t", p=128)
    out_r = outf.rearrange("(c p) t -> p c t", p=128)

    with ExitStack() as st:
        sb = lambda name, shape, dt=F32: st.enter_context(nc.sbuf_tensor(name, shape, dt))
        sWg = sb("sWg", [128, 8 * NHID], BF16)
        sWu = sb("sWu", [128, 8 * NHID], BF16)
        sWdc = sb("sWdc", [128, 24 * 128], BF16)
        sWdc2 = sb("sWdc2", [128, 24 * 128], BF16)
        sXb = sb("sXb", [128, 8 * TT], BF16)
        sXf = sb("sXf", [128, 8 * TT])
        sGs = sb("sGs", [128, TT], BF16)
        sGs2 = sb("sGs2", [128, TT], BF16)
        sH1 = sb("sH1", [128, 24 * TT], BF16)
        sRaw = sb("sRaw", [128, 8 * TT])
        sSq = sb("sSq", [128, TT])
        sR1 = sb("sR1", [1, TT])
        sR2 = sb("sR2", [1, TT])
        sR3 = sb("sR3", [1, TT])
        sOut = sb("sOut", [128, 8 * TT])
        sOnes = sb("sOnes", [1, 128])
        sOnesCol = sb("sOnesCol", [128, 1])
        sEpsR = sb("sEpsR", [1, 1])

        ppool = st.enter_context(nc.psum_tensor("ppool", [128, 4096], F32))

        def bank(i, w=512, off=0):
            return ppool[:, i * 512 + off: i * 512 + off + w]

        def bankp(i, p0, prows, w, off=0):
            return ppool[p0:p0 + prows, i * 512 + off: i * 512 + off + w]

        p = P(nc, st, ["ld", "gld", "pe", "dv", "ac", "st"])
        sy, gp, pe, dv, ac = nc.sync, nc.gpsimd, nc.tensor, nc.vector, nc.scalar

        p.inc(sy.dma_start(out=sOnes[:, :], in_=onesr[:, :]), "ld", 16)
        p.inc(gp.dma_start(out=sWg[:, :].rearrange("p (c m) -> p c m", c=8),
                           in_=wgT.rearrange("(c p) m -> p c m", p=128)), "gld", 16)
        p.inc(gp.dma_start(out=sWu[:, :].rearrange("p (c m) -> p c m", c=8),
                           in_=wuT.rearrange("(c p) m -> p c m", p=128)), "gld", 16)
        p.wait(dv, "ld")
        p.dvx(dv, dv.memset(sOnesCol[:, :], 1.0))
        p.dvx(dv, dv.memset(sEpsR[:, :], LNEPS))
        p.wait(pe, "gld", "dv")
        dprev_pe = {}
        dprev_dv = {}

        for tt in range(T // TT):
            tsl = slice(tt * TT, (tt + 1) * TT)
            p.wait(sy, "pe", "dv", "ac")
            p.inc(sy.dma_start(out=sXb[:, :].rearrange("p (c t) -> p c t", c=8),
                               in_=x3b_r[:, :, tsl]), "ld", 16)
            p.inc(sy.dma_start(out=sXf[:, :].rearrange("p (c t) -> p c t", c=8),
                               in_=x3f_r[:, :, tsl]), "ld", 16)
            p.wait(pe, "ld")
            gcon_dv = [0] * 24
            gcon_ac = [0] * 24
            for m in range(24):
                gb, ub = (bank(0), bank(1)) if m % 2 == 0 else (bank(4), bank(5))
                if m >= 2:
                    p.wait_at(pe, "dv", gcon_dv[m - 2])
                    p.wait_at(pe, "ac", gcon_ac[m - 2])
                elif m == 0:
                    p.wait(pe, "dv", "ac")
                for k in range(8):
                    p.inc(pe.matmul(gb, sWg[:, k * NHID + m * 128: k * NHID + (m + 1) * 128],
                                    sXb[:, k * TT:(k + 1) * TT],
                                    start=(k == 0), stop=(k == 7)), "pe")
                for k in range(8):
                    p.inc(pe.matmul(ub, sWu[:, k * NHID + m * 128: k * NHID + (m + 1) * 128],
                                    sXb[:, k * TT:(k + 1) * TT],
                                    start=(k == 0), stop=(k == 7)), "pe")
                p.wait(ac, "pe")
                sgs = sGs if m % 2 == 0 else sGs2
                p.acx(ac, ac.activation(out=sgs[:, :], in_=gb, func=AF.Silu, scale=QS))
                gcon_ac[m] = p.at("ac")
                p.wait(dv, "ac")
                p.dvx(dv, dv.scalar_tensor_tensor(sH1[:, m * TT:(m + 1) * TT],
                                                  ub, QS, sgs[:, :],
                                                  OP.mult, OP.mult))
                gcon_dv[m] = p.at("dv")
            for dm in range(8):
                par = dm % 2
                wb = sWdc if par == 0 else sWdc2
                db = bank(2) if par == 0 else bank(6)
                if par in dprev_pe:
                    p.wait_at(gp, "pe", dprev_pe[par])
                p.inc(gp.dma_start(out=wb[:, :].rearrange("p (c m) -> p c m", c=24),
                                   in_=wd3[dm].rearrange("(c p) m -> p c m", p=128)), "gld", 16)
                if par in dprev_dv:
                    p.wait_at(pe, "dv", dprev_dv[par])
                if dm == 0:
                    p.wait(pe, "dv")
                p.wait(pe, "gld")
                for k in range(24):
                    p.inc(pe.matmul(db, wb[:, k * 128:(k + 1) * 128],
                                    sH1[:, k * TT:(k + 1) * TT],
                                    start=(k == 0), stop=(k == 23)), "pe")
                dprev_pe[par] = p.at("pe")
                p.wait(dv, "pe", "ld")
                p.dvx(dv, dv.scalar_tensor_tensor(sRaw[:, dm * TT:(dm + 1) * TT],
                                                  db, QS, sXf[:, dm * TT:(dm + 1) * TT],
                                                  OP.mult, OP.add))
                dprev_dv[par] = p.at("dv")
            p.wait(pe, "dv")
            for c in range(8):
                p.inc(pe.matmul(bankp(4, 0, 1, TT), sOnesCol[:, :],
                                sRaw[:, c * TT:(c + 1) * TT],
                                start=(c == 0), stop=(c == 7)), "pe")
            p.wait(ac, "dv")
            for c in range(8):
                if c > 0:
                    p.wait(ac, "pe")
                p.acx(ac, ac.activation(out=sSq[:, :], in_=sRaw[:, c * TT:(c + 1) * TT],
                                        func=AF.Square))
                p.wait(pe, "ac")
                p.inc(pe.matmul(bankp(5, 0, 1, TT), sOnesCol[:, :], sSq[:, :],
                                start=(c == 0), stop=(c == 7)), "pe")
            p.wait(dv, "pe")
            p.dvx(dv, dv.tensor_scalar(sR1[:, :], bankp(4, 0, 1, TT), 1.0 / D_MODEL,
                                       None, OP.mult))
            p.dvx(dv, dv.tensor_scalar(sR2[:, :], bankp(5, 0, 1, TT), 1.0 / D_MODEL,
                                       None, OP.mult))
            p.dvx(dv, dv.tensor_tensor(sR3[:, :], sR1[:, :], sR1[:, :], OP.mult))
            p.dvx(dv, dv.tensor_tensor(sR2[:, :], sR2[:, :], sR3[:, :], OP.subtract))
            p.wait(ac, "dv")
            p.acx(ac, ac.activation(out=sR3[:, :], in_=sR2[:, :], func=AF.Ln, bias=sEpsR[:, :]))
            p.acx(ac, ac.activation(out=sR2[:, :], in_=sR3[:, :], func=AF.Exp, scale=-0.5))
            p.wait(dv, "ac")
            p.dvx(dv, dv.scalar_tensor_tensor(sR3[:, :], sR1[:, :], -1.0, sR2[:, :],
                                              OP.mult, OP.mult))
            p.wait(pe, "dv")
            p.inc(pe.matmul(bank(6, TT), sOnes[:, :], sR2[:, :], start=True, stop=True), "pe")
            p.inc(pe.matmul(bank(7, TT), sOnes[:, :], sR3[:, :], start=True, stop=True), "pe")
            p.wait(dv, "pe", "st")
            for c in range(8):
                csl = slice(c * TT, (c + 1) * TT)
                p.dvx(dv, dv.tensor_tensor(sOut[:, csl], sRaw[:, csl], bank(6, TT), OP.mult))
                p.dvx(dv, dv.tensor_tensor(sOut[:, csl], sOut[:, csl], bank(7, TT), OP.add))
            p.wait(sy, "dv")
            p.inc(sy.dma_start(out=out_r[:, :, tsl],
                               in_=sOut[:, :].rearrange("p (c t) -> p c t", c=8)), "st", 16)
        p.wait(gp, "st")
        p.wait(gp, "ld")
    return nc


def prep_G_inputs(x3fs, x3bs, w_gate, w_up, w_down):
    consts = _consts()
    wgT = np.ascontiguousarray(np.asarray(w_gate).T.astype(np.int32))
    wuT = np.ascontiguousarray(np.asarray(w_up).T.astype(np.int32))
    wd = np.asarray(w_down)
    wd3 = np.ascontiguousarray(wd.reshape(8, 128, 3072).transpose(0, 2, 1).astype(np.int32))
    in_maps = []
    for cid in range(N_CORES):
        m = dict(
            x3fi=f32(x3fs[cid]),
            x3bi=np.ascontiguousarray(x3bs[cid]),
            wgT=wgT, wuT=wuT, wd3=wd3,
            **consts,
        )
        in_maps.append(m)
    return in_maps



def kernel(**inputs):
    run = lambda nc, maps: run_bass_kernel_spmd(nc, maps, core_ids=list(range(N_CORES))).results

    resA = run(build_launch_A(),
               prep_A_inputs(inputs["L0_attn_x"], inputs["L0_attn_norm_w"],
                             inputs["L0_attn_in_proj_w"], inputs["L0_attn_conv_w"],
                             inputs["L0_attn_conv_b"], inputs["L0_attn_A_log"],
                             inputs["L0_attn_D"], inputs["L0_attn_dt_bias"],
                             inputs["L0_attn_out_proj_w"]))
    pouts = [resA[c]["pout"] for c in range(N_CORES)]

    resB = run(build_launch_B(), prep_B_inputs(inputs["L0_attn_x"], pouts, inputs["L0_ffn_wg"]))
    x1fs = [resB[c]["x1f"] for c in range(N_CORES)]
    x1bs = [resB[c]["x1b"] for c in range(N_CORES)]
    scores = np.concatenate([resB[c]["sct"] for c in range(N_CORES)], axis=1).T

    resC = run(build_launch_C2(),
               prep_C2_inputs(x1bs, scores, inputs["L0_ffn_w1"], inputs["L0_ffn_w3"],
                              inputs["L0_ffn_w2"]))
    eobs = [resC[e]["eob"] for e in range(N_CORES)]
    idxgs = [resC[e]["idxg"] for e in range(N_CORES)]

    resD = run(build_launch_D2(), prep_D2_inputs(x1fs, eobs, idxgs))
    x2fs = [resD[c]["x2f"] for c in range(N_CORES)]
    x2bs = [resD[c]["x2b"] for c in range(N_CORES)]

    resE = run(build_launch_E(),
               prep_E_inputs(x2bs, inputs["L1_attn_Wq"], inputs["L1_attn_Wk"],
                             inputs["L1_attn_Wv"], inputs["L1_attn_proj"]))
    phiqs = [resE[c]["phiq"] for c in range(N_CORES)]
    kvps = [resE[c]["kvp"] for c in range(N_CORES)]

    resF = run(build_launch_F(),
               prep_F_inputs(x2fs, phiqs, kvps, inputs["L1_attn_Wo"], inputs["L1_attn_bo"]))
    x3fs = [resF[c]["x3f"] for c in range(N_CORES)]
    x3bs = [resF[c]["x3b"] for c in range(N_CORES)]

    resG = run(build_launch_G(),
               prep_G_inputs(x3fs, x3bs, inputs["L1_ffn_w_gate"], inputs["L1_ffn_w_up"],
                             inputs["L1_ffn_w_down"]))
    out = np.zeros((2, 8192, 1024), np.float32)
    for cid in range(N_CORES):
        b, q = cid // 4, cid % 4
        out[b, 2048 * q: 2048 * q + 2048] = resG[cid]["outf"].T
    return out



def build_launch_C2():
    nc = bass.Bass("TRN2", target_bir_lowering=False)
    NITER = 42
    CAP = 4096.0
    BIG = 60000.0

    x1rt = nc.dram_tensor("x1rt", [16384, 1024], BF16, kind="ExternalInput").ap()
    srow = nc.dram_tensor("srow", [128, 128], F32, kind="ExternalInput").ap()
    w1 = nc.dram_tensor("w1", [D_MODEL, 2048], F32, kind="ExternalInput").ap()
    w3 = nc.dram_tensor("w3", [D_MODEL, 2048], F32, kind="ExternalInput").ap()
    w2 = nc.dram_tensor("w2", [2048, D_MODEL], F32, kind="ExternalInput").ap()
    triu1 = nc.dram_tensor("triu1", [128, 128], F32, kind="ExternalInput").ap()
    onesr, trilc, eyeb, eyef = _add_const_tensors(nc)
    eob = nc.dram_tensor("eob", [4096, 1024], BF16, kind="ExternalOutput").ap()
    idxg = nc.dram_tensor("idxg", [128, 128], U32, kind="ExternalOutput").ap()
    xbuf = nc.dram_tensor("xbuf", [4096, 1032], BF16, kind="Internal").ap()

    with ExitStack() as st:
        sb = lambda name, shape, dt=F32: st.enter_context(nc.sbuf_tensor(name, shape, dt))
        sW1 = sb("sW1", [128, 8 * 2048], BF16)
        sW3 = sb("sW3", [128, 8 * 2048], BF16)
        sW2 = sb("sW2", [128, 16 * 1024], BF16)
        sGrid = sb("sGrid", [128, 128])
        sLo = sb("sLo", [128, 1]); sHi = sb("sHi", [128, 1]); sMid = sb("sMid", [128, 1])
        sCnt = sb("sCnt", [128, 1]); sSel = sb("sSel", [128, 1])
        sT1 = sb("sT1", [128, 1]); sT2 = sb("sT2", [128, 1])
        sR11 = sb("sR11", [1, 1])
        sTrash = sb("sTrash", [128, 128], BF16)
        sEq = sb("sEq", [128, 128]); sRk = sb("sRk", [128, 128]); sRk2 = sb("sRk2", [128, 128])
        sMask = sb("sMask", [128, 128]); sGate = sb("sGate", [128, 128])
        sRank = sb("sRank", [128, 128]); sIdxF = sb("sIdxF", [128, 128])
        sIdxU = sb("sIdxU", [128, 128], U32)
        sTriu1 = sb("sTriu1", [128, 128])
        sOnes = sb("sOnes", [1, 128]); sOnesCol = sb("sOnesCol", [128, 1])
        sEyeB = sb("sEyeB", [128, 128], BF16)
        sStg = [sb(f"sStg{i}", [128, 1032], BF16) for i in range(2)]
        sXr4 = sb("sXr4", [128, 4 * 1032], BF16)
        sScf = sb("sScf", [128, 4])
        sRt = sb("sRt", [128, 8 * 512], BF16)
        sGs = sb("sGs", [128, 512], BF16)
        sGs2 = sb("sGs2", [128, 512], BF16)
        sH1 = sb("sH1", [128, 16 * 512], BF16)
        sEoF = sb("sEoF", [128, 8 * 512], BF16)
        sEoT = sb("sEoT", [128, 4 * 1024], BF16)

        ppool = st.enter_context(nc.psum_tensor("ppool", [128, 4096], F32))

        def bank(i, w=512, off=0):
            return ppool[:, i * 512 + off: i * 512 + off + w]

        def bankp(i, p0, prows, w, off=0):
            return ppool[p0:p0 + prows, i * 512 + off: i * 512 + off + w]

        p = P(nc, st, ["ld", "gld", "pe", "dv", "ac", "st", "sc"])
        sy, gp, pe, dv, ac = nc.sync, nc.gpsimd, nc.tensor, nc.vector, nc.scalar

        p.inc(sy.dma_start(out=sGrid[:, :], in_=srow[:, :]), "ld", 16)
        p.inc(sy.dma_start(out=sTriu1[:, :], in_=triu1[:, :]), "ld", 16)
        p.inc(sy.dma_start(out=sOnes[:, :], in_=onesr[:, :]), "ld", 16)
        p.inc(sy.dma_start(out=sEyeB[:, :], in_=eyeb[:, :]), "ld", 16)
        p.inc(gp.dma_start(out=sW1[:, :].rearrange("p (c m) -> p c m", c=8),
                           in_=w1.rearrange("(c p) m -> p c m", p=128)), "gld", 16)
        p.inc(gp.dma_start(out=sW3[:, :].rearrange("p (c m) -> p c m", c=8),
                           in_=w3.rearrange("(c p) m -> p c m", p=128)), "gld", 16)
        p.inc(gp.dma_start(out=sW2[:, :].rearrange("p (c m) -> p c m", c=16),
                           in_=w2.rearrange("(c p) m -> p c m", p=128)), "gld", 16)

        p.wait(dv, "ld")
        p.dvx(dv, dv.memset(sOnesCol[:, :], 1.0))
        p.dvx(dv, dv.memset(sLo[:, :], 0.0))
        p.dvx(dv, dv.memset(sHi[:, :], 2.0))
        p.dvx(dv, dv.memset(sStg[0][:, 1025:1032], 0.0))
        p.dvx(dv, dv.memset(sStg[1][:, 1025:1032], 0.0))

        for it in range(NITER):
            p.dvx(dv, dv.tensor_scalar(sMid[:, :], sLo[:, :], sHi[:, :], 0.5,
                                       OP.add, OP.mult))
            p.dvx(dv, dv.tensor_scalar(sTrash[:, :], sGrid[:, :], sMid[:, :], 0.0,
                                       OP.is_ge, OP.add, accum_out=sCnt[:, :]))
            p.wait(pe, "dv")
            p.inc(pe.matmul(bankp(0, 0, 1, 1), sCnt[:, :], sOnesCol[:, :],
                            start=True, stop=True), "pe")
            p.wait(dv, "pe")
            p.dvx(dv, dv.tensor_copy(sR11[:, :], bankp(0, 0, 1, 1)))
            p.wait(pe, "dv")
            p.inc(pe.matmul(bankp(1, 0, 128, 1), sOnes[:, :], sR11[:, :],
                            start=True, stop=True), "pe")
            p.wait(dv, "pe")
            p.dvx(dv, dv.tensor_scalar(sSel[:, :], bankp(1, 0, 128, 1), CAP, None, OP.is_ge))
            p.dvx(dv, dv.tensor_tensor(sT1[:, :], sMid[:, :], sLo[:, :], OP.subtract))
            p.dvx(dv, dv.tensor_tensor(sT1[:, :], sT1[:, :], sSel[:, :], OP.mult))
            p.dvx(dv, dv.tensor_tensor(sLo[:, :], sLo[:, :], sT1[:, :], OP.add))
            p.dvx(dv, dv.tensor_tensor(sT2[:, :], sHi[:, :], sMid[:, :], OP.subtract))
            p.dvx(dv, dv.tensor_tensor(sT2[:, :], sT2[:, :], sSel[:, :], OP.mult))
            p.dvx(dv, dv.tensor_tensor(sHi[:, :], sMid[:, :], sT2[:, :], OP.add))
        p.dvx(dv, dv.tensor_scalar(sMask[:, :], sGrid[:, :], sLo[:, :], 0.0,
                                   OP.is_gt, OP.add, accum_out=sCnt[:, :]))
        p.wait(pe, "dv")
        p.inc(pe.matmul(bankp(0, 0, 1, 1), sCnt[:, :], sOnesCol[:, :],
                        start=True, stop=True), "pe")
        p.wait(dv, "pe")
        p.dvx(dv, dv.tensor_scalar(sR11[:, :], bankp(0, 0, 1, 1), -1.0, CAP,
                                   OP.mult, OP.add))
        p.wait(pe, "dv")
        p.inc(pe.matmul(bankp(1, 0, 128, 1), sOnes[:, :], sR11[:, :],
                        start=True, stop=True), "pe")
        p.wait(dv, "pe")
        p.dvx(dv, dv.tensor_copy(sT1[:, :], bankp(1, 0, 128, 1)))
        p.dvx(dv, dv.tensor_scalar(sEq[:, :], sGrid[:, :], sLo[:, :], 0.0,
                                   OP.is_equal, OP.add, accum_out=sCnt[:, :]))
        p.dvx(dv, dv.tensor_tensor_scan(sRk[:, :], sEq[:, :], sEq[:, :], 0.0,
                                        OP.add, OP.bypass))
        p.dvx(dv, dv.tensor_tensor(sRk[:, :], sRk[:, :], sEq[:, :], OP.subtract))
        p.wait(pe, "dv")
        p.inc(pe.matmul(bankp(2, 0, 128, 1), sTriu1[:, :], sCnt[:, :],
                        start=True, stop=True), "pe")
        p.wait(dv, "pe")
        p.dvx(dv, dv.tensor_scalar(sRk2[:, :], sRk[:, :], bankp(2, 0, 128, 1), None, OP.add))
        p.dvx(dv, dv.tensor_scalar(sRk2[:, :], sRk2[:, :], sT1[:, :], None, OP.is_lt))
        p.dvx(dv, dv.tensor_tensor(sRk2[:, :], sRk2[:, :], sEq[:, :], OP.mult))
        p.dvx(dv, dv.tensor_tensor(sMask[:, :], sMask[:, :], sRk2[:, :], OP.add))
        p.dvx(dv, dv.tensor_tensor(sGate[:, :], sMask[:, :], sGrid[:, :], OP.mult))
        p.dvx(dv, dv.tensor_scalar(sTrash[:, :], sMask[:, :], 1.0, 0.0,
                                   OP.mult, OP.add, accum_out=sCnt[:, :]))
        p.dvx(dv, dv.tensor_tensor_scan(sRk[:, :], sMask[:, :], sMask[:, :], 0.0,
                                        OP.add, OP.bypass))
        p.dvx(dv, dv.tensor_tensor(sRk[:, :], sRk[:, :], sMask[:, :], OP.subtract))
        p.wait(pe, "dv")
        p.inc(pe.matmul(bankp(2, 0, 128, 1), sTriu1[:, :], sCnt[:, :],
                        start=True, stop=True), "pe")
        p.wait(dv, "pe")
        p.dvx(dv, dv.tensor_scalar(sRank[:, :], sRk[:, :], bankp(2, 0, 128, 1), None, OP.add))
        p.dvx(dv, dv.tensor_tensor(sIdxF[:, :], sRank[:, :], sMask[:, :], OP.mult))
        p.dvx(dv, dv.tensor_scalar(sRk2[:, :], sMask[:, :], -BIG, BIG, OP.mult, OP.add))
        p.dvx(dv, dv.tensor_tensor(sIdxF[:, :], sIdxF[:, :], sRk2[:, :], OP.add))
        p.dvx(dv, dv.tensor_copy(sIdxU[:, :], sIdxF[:, :]))
        p.wait(sy, "dv")
        p.inc(sy.dma_start(out=idxg[:, :], in_=sIdxU[:, :]), "st", 16)

        bcreg = gp.to_reg(4095)
        stg_busy = [0, 0]
        for i in range(128):
            par = i % 2
            stg = sStg[par]
            if i >= 2:
                p.wait_at(sy, "sc", stg_busy[par])
            p.inc(sy.dma_start(out=stg[:, 0:1024],
                               in_=x1rt[i * 128:(i + 1) * 128, :]), "ld", 16)
            p.wait(dv, "ld")
            p.dvx(dv, dv.tensor_copy(stg[:, 1024:1025], sGate[:, i:i + 1]))
            p.wait(gp, "dv")
            p.inc(gp.indirect_dma_start(
                out=xbuf[:, :], out_offset=bass.IndirectOffsetOnAxis(ap=sIdxU[:, i:i + 1], axis=0),
                in_=stg[:, 0:1032], in_offset=None,
                bounds_check=bcreg, oob_is_err=False), "sc", 16)
            stg_busy[par] = p.at("sc")
        p.wait(gp, "sc")
        p.wait(sy, "sc")

        p.wait(dv, "gld")
        for til in range(8):
            p.wait(sy, "pe", "dv", "ac")
            p.inc(sy.dma_start(out=sXr4[:, :].rearrange("p (j m) -> p j m", j=4),
                               in_=xbuf.rearrange("(t p) m -> p t m", p=128)[:, til * 4:(til + 1) * 4, :]),
                  "ld", 16)
            p.wait(dv, "ld")
            for j in range(4):
                p.dvx(dv, dv.tensor_copy(sScf[:, j:j + 1],
                                         sXr4[:, j * 1032 + 1024: j * 1032 + 1025]))
            for j in range(4):
                p.dvx(dv, dv.tensor_scalar(sXr4[:, j * 1032: j * 1032 + 1024],
                                           sXr4[:, j * 1032: j * 1032 + 1024],
                                           sScf[:, j:j + 1],
                                           None, OP.mult))
            for j in range(4):
                tb = 6 + (j % 2)
                p.wait(pe, "dv")
                for cchunk in range(8):
                    p.inc(pe.transpose(
                        bankp(tb, 0, 128, 64, off=cchunk * 64).bitcast(BF16)[:, 0:128],
                        sXr4[:, j * 1032 + cchunk * 128: j * 1032 + (cchunk + 1) * 128],
                        sEyeB[:, :]), "pe")
                p.wait(dv, "pe")
                p.dvx(dv, dv.tensor_copy(
                    sRt[:, :].rearrange("p (c t) -> p c t", c=8)[:, :, j * 128:(j + 1) * 128],
                    bankp(tb, 0, 128, 512).bitcast(BF16)[:, 0:1024].rearrange(
                        "p (c t) -> p c t", c=8)))
            consumed_dv = [0] * 16
            consumed_ac = [0] * 16
            for m in range(16):
                gb, ub = (bank(0), bank(1)) if m % 2 == 0 else (bank(4), bank(5))
                if m >= 2:
                    p.wait_at(pe, "dv", consumed_dv[m - 2])
                    p.wait_at(pe, "ac", consumed_ac[m - 2])
                elif m == 0:
                    p.wait(pe, "dv", "ac")
                for k in range(8):
                    p.inc(pe.matmul(gb, sW1[:, k * 2048 + m * 128: k * 2048 + (m + 1) * 128],
                                    sRt[:, k * 512:(k + 1) * 512],
                                    start=(k == 0), stop=(k == 7)), "pe")
                for k in range(8):
                    p.inc(pe.matmul(ub, sW3[:, k * 2048 + m * 128: k * 2048 + (m + 1) * 128],
                                    sRt[:, k * 512:(k + 1) * 512],
                                    start=(k == 0), stop=(k == 7)), "pe")
                p.wait(ac, "pe")
                sgs = sGs if m % 2 == 0 else sGs2
                p.acx(ac, ac.activation(out=sgs[:, :], in_=gb, func=AF.Silu))
                consumed_ac[m] = p.at("ac")
                p.wait(dv, "ac")
                p.dvx(dv, dv.tensor_tensor(sH1[:, m * 512:(m + 1) * 512], sgs[:, :],
                                           ub, OP.mult))
                consumed_dv[m] = p.at("dv")
            dcon = [0] * 8
            for dm in range(8):
                db = bank(2) if dm % 2 == 0 else bank(3)
                if dm >= 2:
                    p.wait_at(pe, "dv", dcon[dm - 2])
                else:
                    p.wait(pe, "dv")
                for k in range(16):
                    p.inc(pe.matmul(db, sW2[:, k * 1024 + dm * 128: k * 1024 + (dm + 1) * 128],
                                    sH1[:, k * 512:(k + 1) * 512],
                                    start=(k == 0), stop=(k == 15)), "pe")
                p.wait(dv, "pe")
                p.dvx(dv, dv.tensor_copy(sEoF[:, dm * 512:(dm + 1) * 512], db))
                dcon[dm] = p.at("dv")
            for j in range(4):
                tb = 6 + (j % 2)
                p.wait(pe, "dv")
                for dm in range(8):
                    p.inc(pe.transpose(
                        bankp(tb, 0, 128, 64, off=dm * 64).bitcast(BF16)[:, 0:128],
                        sEoF[:, dm * 512 + j * 128: dm * 512 + (j + 1) * 128],
                        sEyeB[:, :]), "pe")
                p.wait(dv, "pe", "st")
                p.dvx(dv, dv.tensor_copy(
                    sEoT[:, j * 1024:(j + 1) * 1024],
                    bankp(tb, 0, 128, 512).bitcast(BF16)[:, 0:1024]))
            p.wait(sy, "dv")
            p.inc(sy.dma_start(out=eob.rearrange("(t p) m -> p t m", p=128)[:, til * 4:(til + 1) * 4, :],
                               in_=sEoT[:, :].rearrange("p (j m) -> p j m", j=4)), "st", 16)
        p.wait(gp, "st")
        p.wait(gp, "ld")
    return nc


def prep_C2_inputs(x1bs, scores, w1, w3, w2):
    consts = _consts()
    x1all = np.ascontiguousarray(np.concatenate(x1bs, axis=1).T)
    in_maps = []
    for e in range(N_CORES):
        g = np.ascontiguousarray(scores[:, e]).reshape(128, 128).T
        m = dict(
            x1rt=x1all,
            srow=f32(g),
            w1=f32(w1[e]), w3=f32(w3[e]), w2=f32(w2[e]),
            triu1=np.triu(np.ones((128, 128), np.float32), 1),
            **consts,
        )
        in_maps.append(m)
    return in_maps



def build_launch_D2():
    nc = bass.Bass("TRN2", target_bir_lowering=False)
    TT = 512
    NTILE = TPC // TT

    x1o = nc.dram_tensor("x1o", [D_MODEL, TPC], F32, kind="ExternalInput").ap()
    eobs = [nc.dram_tensor(f"eob{j}", [4096, 1024], BF16, kind="ExternalInput").ap()
            for j in range(8)]
    idxs = [nc.dram_tensor(f"idx{j}", [128, 16], U32, kind="ExternalInput").ap()
            for j in range(8)]
    onesr, trilc, eyeb, eyef = _add_const_tensors(nc)
    x2f = nc.dram_tensor("x2f", [D_MODEL, TPC], F32, kind="ExternalOutput").ap()
    x2b = nc.dram_tensor("x2b", [D_MODEL, TPC], BF16, kind="ExternalOutput").ap()

    x1_r = x1o.rearrange("(c p) t -> p c t", p=128)
    x2f_r = x2f.rearrange("(c p) t -> p c t", p=128)
    x2b_r = x2b.rearrange("(c p) t -> p c t", p=128)

    with ExitStack() as st:
        sb = lambda name, shape, dt=F32: st.enter_context(nc.sbuf_tensor(name, shape, dt))
        sAcc = sb("sAcc", [128, 8 * TT])
        sAt = sb("sAt", [128, 1024])
        sRaw = sb("sRaw", [128, 8 * TT])
        sSq = sb("sSq", [128, TT])
        sR1 = sb("sR1", [1, TT]); sR2 = sb("sR2", [1, TT]); sR3 = sb("sR3", [1, TT])
        sX2 = sb("sX2", [128, 8 * TT])
        sX2b = sb("sX2b", [128, 8 * TT], BF16)
        sIdx = [sb(f"sIdx{j}", [128, 16], U32) for j in range(8)]
        sOnes = sb("sOnes", [1, 128]); sOnesCol = sb("sOnesCol", [128, 1])
        sEyeF = sb("sEyeF", [128, 128])
        sEpsR = sb("sEpsR", [1, 1])

        ppool = st.enter_context(nc.psum_tensor("ppool", [128, 4096], F32))

        def bank(i, w=512, off=0):
            return ppool[:, i * 512 + off: i * 512 + off + w]

        def bankp(i, p0, prows, w, off=0):
            return ppool[p0:p0 + prows, i * 512 + off: i * 512 + off + w]

        p = P(nc, st, ["ld", "gth", "pe", "dv", "ac", "st"])
        sy, gp, pe, dv, ac = nc.sync, nc.gpsimd, nc.tensor, nc.vector, nc.scalar
        bcreg = gp.to_reg(4095)

        p.inc(sy.dma_start(out=sOnes[:, :], in_=onesr[:, :]), "ld", 16)
        p.inc(sy.dma_start(out=sEyeF[:, :], in_=eyef[:, :]), "ld", 16)
        for j in range(8):
            p.inc(sy.dma_start(out=sIdx[j][:, :], in_=idxs[j][:, :]), "ld", 16)
        p.wait(dv, "ld")
        p.dvx(dv, dv.memset(sOnesCol[:, :], 1.0))
        p.dvx(dv, dv.memset(sEpsR[:, :], LNEPS))

        for tt in range(NTILE):
            tsl = slice(tt * TT, (tt + 1) * TT)
            p.wait(sy, "pe", "dv", "ac")
            p.inc(sy.dma_start(out=sRaw[:, :].rearrange("p (c t) -> p c t", c=8),
                               in_=x1_r[:, :, tsl]), "ld", 16)
            for grp in range(4):
                col = tt * 4 + grp
                p.wait(dv, "gth", "pe")
                p.dvx(dv, dv.memset(sAt[:, :], 0.0))
                p.wait(gp, "dv")
                for e in range(8):
                    p.inc(gp.indirect_dma_start(
                        out=sAt[:, :], out_offset=None,
                        in_=eobs[e][:, :],
                        in_offset=bass.IndirectOffsetOnAxis(ap=sIdx[e][:, col:col + 1], axis=0),
                        bounds_check=bcreg, oob_is_err=False,
                        compute_op=OP.add), "gth", 16)
                    p.wait(gp, "gth")
                p.wait(pe, "gth", "dv")
                for c in range(8):
                    p.inc(pe.transpose(bankp(c // 4, 0, 128, 128, off=(c % 4) * 128),
                                       sAt[:, c * 128:(c + 1) * 128], sEyeF[:, :]), "pe")
                p.wait(dv, "pe")
                p.dvx(dv, dv.tensor_copy(
                    sAcc[:, :].rearrange("p (c t) -> p c t", c=8)[:, :, grp * 128:(grp + 1) * 128],
                    ppool[:, 0:1024].rearrange("p (c t) -> p c t", c=8)))
            p.wait(dv, "ld")
            p.dvx(dv, dv.tensor_tensor(sRaw[:, :], sRaw[:, :], sAcc[:, :], OP.add))
            p.wait(pe, "dv")
            for c in range(8):
                p.inc(pe.matmul(bankp(4, 0, 1, TT), sOnesCol[:, :],
                                sRaw[:, c * TT:(c + 1) * TT],
                                start=(c == 0), stop=(c == 7)), "pe")
            p.wait(ac, "dv")
            for c in range(8):
                if c > 0:
                    p.wait(ac, "pe")
                p.acx(ac, ac.activation(out=sSq[:, :], in_=sRaw[:, c * TT:(c + 1) * TT],
                                        func=AF.Square))
                p.wait(pe, "ac")
                p.inc(pe.matmul(bankp(5, 0, 1, TT), sOnesCol[:, :], sSq[:, :],
                                start=(c == 0), stop=(c == 7)), "pe")
            p.wait(dv, "pe")
            p.dvx(dv, dv.tensor_scalar(sR1[:, :], bankp(4, 0, 1, TT), 1.0 / D_MODEL,
                                       None, OP.mult))
            p.dvx(dv, dv.tensor_scalar(sR2[:, :], bankp(5, 0, 1, TT), 1.0 / D_MODEL,
                                       None, OP.mult))
            p.dvx(dv, dv.tensor_tensor(sR3[:, :], sR1[:, :], sR1[:, :], OP.mult))
            p.dvx(dv, dv.tensor_tensor(sR2[:, :], sR2[:, :], sR3[:, :], OP.subtract))
            p.wait(ac, "dv")
            p.acx(ac, ac.activation(out=sR3[:, :], in_=sR2[:, :], func=AF.Ln, bias=sEpsR[:, :]))
            p.acx(ac, ac.activation(out=sR2[:, :], in_=sR3[:, :], func=AF.Exp, scale=-0.5))
            p.wait(dv, "ac")
            p.dvx(dv, dv.scalar_tensor_tensor(sR3[:, :], sR1[:, :], -1.0, sR2[:, :],
                                              OP.mult, OP.mult))
            p.wait(pe, "dv")
            p.inc(pe.matmul(bank(6, TT), sOnes[:, :], sR2[:, :], start=True, stop=True), "pe")
            p.inc(pe.matmul(bank(7, TT), sOnes[:, :], sR3[:, :], start=True, stop=True), "pe")
            p.wait(dv, "pe", "st")
            for c in range(8):
                csl = slice(c * TT, (c + 1) * TT)
                p.dvx(dv, dv.tensor_tensor(sX2[:, csl], sRaw[:, csl], bank(6, TT), OP.mult))
                p.dvx(dv, dv.tensor_tensor(sX2[:, csl], sX2[:, csl], bank(7, TT), OP.add))
                p.dvx(dv, dv.tensor_copy(sX2b[:, csl], sX2[:, csl]))
            p.wait(sy, "dv")
            p.inc(sy.dma_start(out=x2f_r[:, :, tsl],
                               in_=sX2[:, :].rearrange("p (c t) -> p c t", c=8)), "st", 16)
            p.inc(sy.dma_start(out=x2b_r[:, :, tsl],
                               in_=sX2b[:, :].rearrange("p (c t) -> p c t", c=8)), "st", 16)
        p.wait(gp, "st")
        p.wait(gp, "ld")
        p.wait(gp, "gth")
    return nc


def prep_D2_inputs(x1fs, eobs_all, idxgs_all):
    consts = _consts()
    in_maps = []
    for cid in range(N_CORES):
        m = dict(x1o=f32(x1fs[cid]), **consts)
        for j in range(8):
            m[f"eob{j}"] = np.ascontiguousarray(eobs_all[j])
            m[f"idx{j}"] = np.ascontiguousarray(idxgs_all[j][:, cid * 16:(cid + 1) * 16])
        in_maps.append(m)
    return in_maps


# version 04
# speedup vs baseline: 1.7553x; 1.0427x over previous
from contextlib import ExitStack

import numpy as np
import ml_dtypes

import concourse.bass as bass
import concourse.mybir as mybir
from concourse.alu_op_type import AluOpType
from concourse.bass_utils import run_bass_kernel_spmd

F32 = mybir.dt.float32
BF16 = mybir.dt.bfloat16
I32 = mybir.dt.int32
U32 = mybir.dt.uint32
AF = mybir.ActivationFunctionType
OP = AluOpType

BF = ml_dtypes.bfloat16

D_MODEL = 1024
NB = 2
T_FULL = 8192
N_CORES = 8
TPC = 2048

K_CONV = 4
MEPS = 1e-6
LNEPS = 1e-5


class P:

    def __init__(self, nc, stack, names):
        self.nc = nc
        self.sems = {}
        self.cnt = {}
        for name in names:
            self.sems[name] = stack.enter_context(nc.semaphore(name))
            self.cnt[name] = 0

    def inc(self, instr, name, n=1):
        instr.then_inc(self.sems[name], n)
        self.cnt[name] += n
        return instr

    def wait(self, eng, *names):
        for name in names:
            eng.wait_ge(self.sems[name], self.cnt[name])

    def acx(self, eng, instr):
        self.inc(instr, "ac")
        eng.drain()
        return instr

    def dvx(self, eng, instr):
        self.inc(instr, "dv")
        eng.drain()
        return instr

    def at(self, name):
        return self.cnt[name]

    def wait_at(self, eng, name, val):
        if val > 0:
            eng.wait_ge(self.sems[name], val)


def _consts():
    return dict(
        onesr=np.ones((1, 128), np.float32),
        trilc=np.triu(np.ones((128, 128), np.float32)),
        eyeb=np.eye(128, dtype=BF),
        eyef=np.eye(128, dtype=np.float32),
    )


def _add_const_tensors(nc):
    onesr = nc.dram_tensor("onesr", [1, 128], F32, kind="ExternalInput").ap()
    trilc = nc.dram_tensor("trilc", [128, 128], F32, kind="ExternalInput").ap()
    eyeb = nc.dram_tensor("eyeb", [128, 128], BF16, kind="ExternalInput").ap()
    eyef = nc.dram_tensor("eyef", [128, 128], F32, kind="ExternalInput").ap()
    return onesr, trilc, eyeb, eyef



def build_launch_A(ntile=None, dbg=False):
    nc = bass.Bass("TRN2", target_bir_lowering=False)
    TT = 512
    NTILE = ntile or (T_FULL // TT)
    NCH = 4
    HALO = 3
    NCK = TT // 128

    xb = nc.dram_tensor("xb", [D_MODEL, T_FULL], F32, kind="ExternalInput").ap()
    wzx = nc.dram_tensor("wzx", [D_MODEL, 1024], F32, kind="ExternalInput").ap()
    wbc = nc.dram_tensor("wbc", [D_MODEL, 256], F32, kind="ExternalInput").ap()
    wdt = nc.dram_tensor("wdt", [D_MODEL, 8], F32, kind="ExternalInput").ap()
    cw = nc.dram_tensor("cw", [512, K_CONV], F32, kind="ExternalInput").ap()
    cb = nc.dram_tensor("cb", [128, 4], F32, kind="ExternalInput").ap()
    alog = nc.dram_tensor("alog", [8, 1], F32, kind="ExternalInput").ap()
    dcol = nc.dram_tensor("dcol", [128, 4], F32, kind="ExternalInput").ap()
    dtb = nc.dram_tensor("dtb", [8, 1], F32, kind="ExternalInput").ap()
    nw = nc.dram_tensor("nw", [128, 8], F32, kind="ExternalInput").ap()
    wout = nc.dram_tensor("wout", [512, D_MODEL], F32, kind="ExternalInput").ap()
    selc = nc.dram_tensor("selc", [8, 1024], F32, kind="ExternalInput").ap()
    onesr, trilc, eyeb, eyef = _add_const_tensors(nc)
    pout = nc.dram_tensor("pout", [D_MODEL, T_FULL], F32, kind="ExternalOutput").ap()
    dbg_names = ["sH", "sHbf", "sDrow", "sRow8", "sDAc", "sCdb", "sCp", "sMaskT", "sU", "sUp",
                 "sColT", "sY", "sClrel", "sWs", "sDtsp", "sXhT", "sBsm", "sRow", "sMi", "sEm"]
    dbg_out = {}
    if dbg:
        shapes = {"sH": [128, 512], "sHbf": [128, 512], "sDrow": [1, 512], "sRow8": [1, 8],
                  "sDAc": [8, 1], "sCdb": [128, 128], "sCp": [128, 128], "sMaskT": [128, 128],
                  "sU": [128, 64], "sUp": [128, 64], "sColT": [128, 24], "sY": [128, 4 * 512],
                  "sClrel": [8, 512], "sWs": [8, 512], "sDtsp": [8, 512], "sXhT": [128, 512],
                  "sBsm": [128, 128], "sRow": [1, 128], "sMi": [128, 128], "sEm": [128, 128]}
        for n in dbg_names:
            dbg_out[n] = nc.dram_tensor("dbg_" + n, shapes[n], F32, kind="ExternalOutput").ap()

    xb_r = xb.rearrange("(c p) t -> p c t", p=128)
    pout_r = pout.rearrange("(c p) t -> p c t", p=128)

    with ExitStack() as st:
        sb = lambda name, shape, dt=F32: st.enter_context(nc.sbuf_tensor(name, shape, dt))
        sWzx = sb("sWzx", [128, 8 * 1024], BF16)
        sWbc = sb("sWbc", [128, 8 * 256], BF16)
        sWdt = sb("sWdt", [128, 8 * 8], BF16)
        sWout = sb("sWout", [128, 4 * 1024], BF16)
        sCw = sb("sCw", [128, NCH * K_CONV])
        sCb = sb("sCb", [128, NCH])
        sAlog = sb("sAlog", [8, 1])
        snegA = sb("snegA", [8, 1])
        sDcol = sb("sDcol", [128, 4])
        sDtb = sb("sDtb", [8, 1])
        sNw = sb("sNw", [128, 8])
        sOnes = sb("sOnes", [1, 128])
        sOnesCol = sb("sOnesCol", [128, 1])
        sTril = sb("sTril", [128, 128])
        sEyeB = sb("sEyeB", [128, 128], BF16)
        sEyeF = sb("sEyeF", [128, 128])
        sEpsR = sb("sEpsR", [1, 1])
        sH = sb("sH", [128, 512])
        sHbf = sb("sHbf", [128, 512], BF16)
        sClCar = sb("sClCar", [8, 1])
        sXraw = sb("sXraw", [128, 8 * TT])
        sSq = sb("sSq", [128, 8 * TT])
        sRrms = sb("sRrms", [1, TT])
        sRrms2 = sb("sRrms2", [1, TT])
        sTmpR3 = sb("sTmpR3", [8, TT])
        sXn = sb("sXn", [128, 8 * TT], BF16)
        sZ = sb("sZ", [128, NCH * TT], BF16)
        sSiluZ = sb("sSiluZ", [128, NCH * TT], BF16)
        sXs = sb("sXs", [128, NCH * (TT + HALO)], BF16)
        sCacc0 = sb("sCacc0", [128, TT], BF16)
        sCacc1 = sb("sCacc1", [128, TT], BF16)
        sXh = sb("sXh", [128, NCH * TT], BF16)
        sXhT = sb("sXhT", [128, NCH * 128], BF16)
        sB = sb("sB", [128, TT], BF16)
        sC = sb("sC", [128, TT], BF16)
        sBsm = sb("sBsm", [128, 128], BF16)
        sDtr = sb("sDtr", [8, TT])
        sTmpR = sb("sTmpR", [8, TT])
        sTmpR2 = sb("sTmpR2", [8, TT])
        sClrel = sb("sClrel", [8, TT])
        sWs = sb("sWs", [8, TT])
        sDtsp = sb("sDtsp", [8, TT])
        sCl = sb("sCl", [8, TT])
        sColT = sb("sColT", [128, 24])
        sRow = sb("sRow", [1, 128])
        sRow8 = sb("sRow8", [1, 8])
        sDrow = sb("sDrow", [1, 512])
        sDAc = sb("sDAc", [8, 1])
        sSelc = sb("sSelc", [8, 1024])
        sTrilG = sb("sTrilG", [128, 128], BF16)
        sMiA = sb("sMiA", [128, 1024])
        sEmA = sb("sEmA", [128, 1024], BF16)
        sCdbA = sb("sCdbA", [128, 1024], BF16)
        sDAf = sb("sDAf", [128, 8])
        sMaskA = sb("sMaskA", [128, 1024], BF16)
        sCpA = sb("sCpA", [128, 1024], BF16)
        sUA = sb("sUA", [128, 512], BF16)
        sUpA = sb("sUpA", [128, 512], BF16)
        sDxh = sb("sDxh", [128, NCH * TT], BF16)
        sHtmp = sb("sHtmp", [128, 512])
        sY = sb("sY", [128, NCH * TT], BF16)
        sYevT = sb("sYevT", [128, 512])
        sPoutT = sb("sPoutT", [128, 512])

        ppool = st.enter_context(nc.psum_tensor("ppool", [128, 4096], F32))

        def bank(i, w=512, off=0):
            return ppool[:, i * 512 + off: i * 512 + off + w]

        def bankp(i, p0, prows, w, off=0):
            return ppool[p0:p0 + prows, i * 512 + off: i * 512 + off + w]

        p = P(nc, st, ["ld", "gld", "pe", "dv", "ac", "st"])
        sy, gp, pe, dv, ac = nc.sync, nc.gpsimd, nc.tensor, nc.vector, nc.scalar

        p.inc(sy.dma_start(out=sCw[:, :].rearrange("p (c k) -> p c k", c=NCH),
                           in_=cw.rearrange("(c p) k -> p c k", p=128)), "ld", 16)
        p.inc(sy.dma_start(out=sCb[:, :], in_=cb[:, :]), "ld", 16)
        p.inc(sy.dma_start(out=sAlog[:, :], in_=alog[:, :]), "ld", 16)
        p.inc(sy.dma_start(out=sDcol[:, :], in_=dcol[:, :]), "ld", 16)
        p.inc(sy.dma_start(out=sDtb[:, :], in_=dtb[:, :]), "ld", 16)
        p.inc(sy.dma_start(out=sNw[:, :], in_=nw[:, :]), "ld", 16)
        p.inc(sy.dma_start(out=sOnes[:, :], in_=onesr[:, :]), "ld", 16)
        p.inc(sy.dma_start(out=sTril[:, :], in_=trilc[:, :]), "ld", 16)
        p.inc(sy.dma_start(out=sEyeB[:, :], in_=eyeb[:, :]), "ld", 16)
        p.inc(sy.dma_start(out=sEyeF[:, :], in_=eyef[:, :]), "ld", 16)
        p.inc(sy.dma_start(out=sSelc[:, :], in_=selc[:, :]), "ld", 16)
        p.inc(gp.dma_start(out=sWzx[:, :].rearrange("p (c m) -> p c m", c=8),
                           in_=wzx.rearrange("(c p) m -> p c m", p=128)), "gld", 16)
        p.inc(gp.dma_start(out=sWbc[:, :].rearrange("p (c m) -> p c m", c=8),
                           in_=wbc.rearrange("(c p) m -> p c m", p=128)), "gld", 16)
        p.inc(gp.dma_start(out=sWdt[:, :].rearrange("p (c m) -> p c m", c=8),
                           in_=wdt.rearrange("(c p) m -> p c m", p=128)), "gld", 16)
        p.inc(gp.dma_start(out=sWout[:, :].rearrange("p (c m) -> p c m", c=4),
                           in_=wout.rearrange("(c p) m -> p c m", p=128)), "gld", 16)

        p.wait(dv, "ld", "gld")
        p.dvx(dv, dv.memset(sH[:, :], 0.0))
        p.dvx(dv, dv.memset(sHbf[:, :], 0.0))
        p.dvx(dv, dv.memset(sClCar[:, :], 0.0))
        p.dvx(dv, dv.memset(sXs[:, :], 0.0))
        p.dvx(dv, dv.memset(sOnesCol[:, :], 1.0))
        p.dvx(dv, dv.memset(sEpsR[:, :], MEPS))
        p.wait(ac, "dv")
        p.acx(ac, ac.activation(out=sDAc[:, :], in_=sAlog[:, :], func=AF.Exp))
        p.wait(dv, "ac")
        p.dvx(dv, dv.tensor_scalar(snegA[:, :], sDAc[:, :], -1.0, None, OP.mult))

        for tt in range(NTILE):
            tsl = slice(tt * TT, (tt + 1) * TT)
            p.wait(sy, "pe", "ac", "dv")
            p.inc(sy.dma_start(out=sXraw[:, :].rearrange("p (c t) -> p c t", c=8),
                               in_=xb_r[:, :, tsl]), "ld", 16)
            if tt > 0:
                for m in range(NCH):
                    base = m * (TT + HALO)
                    p.dvx(dv, dv.tensor_copy(sXs[:, base: base + HALO],
                                         sXs[:, base + TT: base + TT + HALO]))
            p.wait(ac, "ld", "pe")
            p.acx(ac, ac.activation(out=sSq[:, :], in_=sXraw[:, :], func=AF.Square))
            p.wait(pe, "ac")
            for c in range(8):
                p.inc(pe.matmul(bankp(0, 0, 1, TT), sOnesCol[:, :],
                                sSq[:, c * TT:(c + 1) * TT],
                                start=(c == 0), stop=(c == 7)), "pe")
            p.wait(ac, "pe")
            p.acx(ac, ac.activation(out=sRrms2[:, :], in_=bankp(0, 0, 1, TT), func=AF.Ln,
                                scale=1.0 / D_MODEL, bias=sEpsR[:, :]))
            p.acx(ac, ac.activation(out=sRrms[:, :], in_=sRrms2[:, :], func=AF.Exp,
                                scale=-0.5))
            p.wait(pe, "ac")
            p.inc(pe.matmul(bank(1, TT), sOnes[:, :], sRrms[:, :], start=True, stop=True), "pe")
            p.wait(dv, "pe", "ld")
            for c in range(8):
                p.dvx(dv, dv.scalar_tensor_tensor(sXn[:, c * TT:(c + 1) * TT],
                                              sXraw[:, c * TT:(c + 1) * TT],
                                              sNw[:, c:c + 1], bank(1, TT),
                                              OP.mult, OP.mult))
            p.wait(pe, "dv", "ac")
            for m in range(4):
                for k in range(8):
                    p.inc(pe.matmul(bank(m), sWzx[:, k * 1024 + m * 128: k * 1024 + (m + 1) * 128],
                                    sXn[:, k * TT:(k + 1) * TT],
                                    start=(k == 0), stop=(k == 7)), "pe")
            for m in range(4):
                for k in range(8):
                    p.inc(pe.matmul(bank(4 + m),
                                    sWzx[:, k * 1024 + 512 + m * 128: k * 1024 + 512 + (m + 1) * 128],
                                    sXn[:, k * TT:(k + 1) * TT],
                                    start=(k == 0), stop=(k == 7)), "pe")
            p.wait(dv, "pe")
            for m in range(4):
                p.dvx(dv, dv.tensor_copy(sZ[:, m * TT:(m + 1) * TT], bank(m)))
            for m in range(4):
                p.dvx(dv, dv.tensor_copy(sXs[:, m * (TT + HALO) + HALO: (m + 1) * (TT + HALO)],
                                     bank(4 + m)))
            p.wait(ac, "dv")
            for m in range(4):
                p.acx(ac, ac.activation(out=sSiluZ[:, m * TT:(m + 1) * TT],
                                    in_=sZ[:, m * TT:(m + 1) * TT], func=AF.Silu))
            p.wait(pe, "dv")
            for k in range(8):
                p.inc(pe.matmul(bank(0), sWbc[:, k * 256: k * 256 + 128],
                                sXn[:, k * TT:(k + 1) * TT], start=(k == 0), stop=(k == 7)), "pe")
            for k in range(8):
                p.inc(pe.matmul(bank(1), sWbc[:, k * 256 + 128: (k + 1) * 256],
                                sXn[:, k * TT:(k + 1) * TT], start=(k == 0), stop=(k == 7)), "pe")
            for k in range(8):
                p.inc(pe.matmul(bankp(2, 0, 8, TT), sWdt[:, k * 8:(k + 1) * 8],
                                sXn[:, k * TT:(k + 1) * TT], start=(k == 0), stop=(k == 7)), "pe")
            p.wait(dv, "pe")
            p.dvx(dv, dv.tensor_copy(sB[:, :], bank(0)))
            p.dvx(dv, dv.tensor_copy(sC[:, :], bank(1)))
            p.dvx(dv, dv.tensor_copy(sDtr[:, :], bankp(2, 0, 8, TT)))

            for m in range(NCH):
                base = m * (TT + HALO)
                p.wait(dv, "ac")
                p.dvx(dv, dv.tensor_scalar(sCacc0[:, :], sXs[:, base: base + TT],
                                       sCw[:, m * K_CONV: m * K_CONV + 1],
                                       sCb[:, m:m + 1], OP.mult, OP.add))
                accs = [sCacc0, sCacc1]
                for k in range(1, K_CONV):
                    src, dst = accs[(k - 1) % 2], accs[k % 2]
                    p.dvx(dv, dv.scalar_tensor_tensor(dst[:, :], sXs[:, base + k: base + k + TT],
                                                  sCw[:, m * K_CONV + k: m * K_CONV + k + 1],
                                                  src[:, :], OP.mult, OP.add))
                final = accs[(K_CONV - 1) % 2]
                p.wait(ac, "dv")
                p.acx(ac, ac.activation(out=sXh[:, m * TT:(m + 1) * TT], in_=final[:, :],
                                    func=AF.Silu))
            p.wait(dv, "ac")
            for m in range(NCH):
                p.dvx(dv, dv.tensor_scalar(sDxh[:, m * TT:(m + 1) * TT],
                                           sXh[:, m * TT:(m + 1) * TT],
                                           sDcol[:, m:m + 1], None, OP.mult))
            p.dvx(dv, dv.tensor_scalar(sTmpR[:, :], sDtr[:, :], sDtb[:, :], None, OP.add))
            p.wait(ac, "dv")
            p.acx(ac, ac.activation(out=sTmpR2[:, :], in_=sTmpR[:, :], func=AF.Abs))
            p.acx(ac, ac.activation(out=sTmpR3[:, :], in_=sTmpR2[:, :], func=AF.Exp, scale=-1.0))
            p.acx(ac, ac.activation(out=sTmpR2[:, :], in_=sTmpR3[:, :], func=AF.Ln, bias=1.0))
            p.wait(dv, "ac")
            p.dvx(dv, dv.tensor_scalar(sTmpR3[:, :], sTmpR[:, :], 0.0, None, OP.max))
            p.dvx(dv, dv.tensor_tensor(sDtsp[:, :], sTmpR3[:, :], sTmpR2[:, :], OP.add))
            p.dvx(dv, dv.tensor_scalar(sTmpR[:, :], sDtsp[:, :], snegA[:, :], None, OP.mult))
            p.dvx(dv, dv.tensor_tensor_scan(sCl[:, :], sTmpR[:, :], sTmpR[:, :], sClCar[:, :],
                                        OP.add, OP.bypass))
            for c in range(NCK):
                prevc = sClCar[:, :] if c == 0 else sCl[:, c * 128 - 1: c * 128]
                csl = slice(c * 128, (c + 1) * 128)
                p.dvx(dv, dv.tensor_scalar(sClrel[:, csl], sCl[:, csl], prevc, None,
                                       OP.subtract))
                endc = sClrel[:, c * 128 + 127: c * 128 + 128]
                p.dvx(dv, dv.tensor_scalar(sTmpR2[:, csl], sClrel[:, csl], endc, None,
                                       OP.subtract))
            p.dvx(dv, dv.tensor_copy(sClCar[:, :], sCl[:, TT - 1: TT]))
            p.wait(ac, "dv")
            p.acx(ac, ac.activation(out=sTmpR3[:, :], in_=sTmpR2[:, :], func=AF.Exp, scale=-1.0))
            p.wait(dv, "ac")
            p.dvx(dv, dv.tensor_tensor(sWs[:, :], sTmpR3[:, :], sDtsp[:, :], OP.mult))

            for c in range(NCK):
                csl = slice(c * 128, (c + 1) * 128)
                p.wait(pe, "dv", "ac")
                clb = ppool[:, 5 * 512: 5 * 512 + 1024]
                p.inc(pe.transpose(bankp(3, 0, 128, 8, off=0), sClrel[:, csl], sEyeF[0:8, 0:8]), "pe")
                p.inc(pe.transpose(bankp(3, 0, 128, 8, off=8), sWs[:, csl], sEyeF[0:8, 0:8]), "pe")
                p.inc(pe.transpose(bankp(3, 0, 128, 8, off=16), sDtsp[:, csl], sEyeF[0:8, 0:8]), "pe")
                p.inc(pe.matmul(bankp(4, 0, 128, 128), sB[:, csl], sC[:, csl],
                                start=True, stop=True), "pe")
                p.inc(pe.transpose(bankp(4, 0, 128, 128, off=128).bitcast(BF16)[:, 0:128],
                                   sB[:, csl], sEyeB[:, :]), "pe")
                for m in range(NCH):
                    p.inc(pe.transpose(
                        bankp(4, 0, 128, 64, off=256 + m * 64).bitcast(BF16)[:, 0:128],
                        sXh[:, m * TT + c * 128: m * TT + (c + 1) * 128], sEyeB[:, :]), "pe")
                for h in range(8):
                    p.inc(pe.matmul(ppool[:, 5 * 512 + h * 128: 5 * 512 + (h + 1) * 128],
                                    sSelc[:, h * 128:(h + 1) * 128], sClrel[:, csl],
                                    start=True, stop=True), "pe")
                p.wait(dv, "pe")
                p.dvx(dv, dv.tensor_copy(sColT[:, :], bankp(3, 0, 128, 24)))
                p.dvx(dv, dv.tensor_tensor(sTrilG[:, :], bankp(4, 0, 128, 128), sTril[:, :],
                                           OP.mult))
                p.dvx(dv, dv.tensor_copy(sBsm[:, :],
                                         bankp(4, 0, 128, 128, off=128).bitcast(BF16)[:, 0:128]))
                p.dvx(dv, dv.tensor_copy(
                    sXhT[:, :].rearrange("p (m t) -> p m t", m=NCH),
                    ppool[:, 4 * 512 + 256: 4 * 512 + 512].bitcast(BF16)[:, 0:512].rearrange(
                        "p (m t) -> p m t", m=NCH)))
                clcol_exp = sColT[:, 0:8].unsqueeze(2).broadcast_to([128, 8, 128])
                p.dvx(dv, dv.tensor_tensor(sMiA[:, :].rearrange("p (h t) -> p h t", h=8),
                                           clb.rearrange("p (h t) -> p h t", h=8),
                                           clcol_exp, OP.subtract))
                p.dvx(dv, dv.tensor_scalar(sMiA[:, :], sMiA[:, :], 0.0, None, OP.min))
                p.wait(ac, "dv")
                p.acx(ac, ac.activation(out=sEmA[:, :], in_=sMiA[:, :], func=AF.Exp))
                p.acx(ac, ac.activation(out=sCdbA[:, :], in_=clb, func=AF.Exp))
                p.acx(ac, ac.activation(out=sDAf[:, :], in_=clb[:, 127:1024:128], func=AF.Exp))
                p.wait(dv, "ac")
                trilg_exp = sTrilG[:, :].unsqueeze(1).broadcast_to([128, 8, 128])
                p.dvx(dv, dv.tensor_tensor(sMaskA[:, :].rearrange("p (h t) -> p h t", h=8),
                                           sEmA[:, :].rearrange("p (h t) -> p h t", h=8),
                                           trilg_exp, OP.mult))
                c_exp = sC[:, csl].unsqueeze(1).broadcast_to([128, 8, 128])
                p.dvx(dv, dv.tensor_tensor(sCpA[:, :].rearrange("p (h t) -> p h t", h=8),
                                           sCdbA[:, :].rearrange("p (h t) -> p h t", h=8),
                                           c_exp, OP.mult))
                dt_exp = sColT[:, 16:24].unsqueeze(2).broadcast_to([128, 8, 64])
                p.dvx(dv, dv.tensor_tensor(sUA[:, :].rearrange("p (h q) -> p h q", h=8),
                                           sXhT[:, :].rearrange("p (h q) -> p h q", h=8),
                                           dt_exp, OP.mult))
                ws_exp = sColT[:, 8:16].unsqueeze(2).broadcast_to([128, 8, 64])
                p.dvx(dv, dv.tensor_tensor(sUpA[:, :].rearrange("p (h q) -> p h q", h=8),
                                           sXhT[:, :].rearrange("p (h q) -> p h q", h=8),
                                           ws_exp, OP.mult))
                p.wait(pe, "dv")
                for h in range(8):
                    ybank = 7 if h % 2 == 0 else 2
                    rows = slice((h % 2) * 64, (h % 2) * 64 + 64)
                    yp = ppool[rows, ybank * 512 + (h // 2) * 128: ybank * 512 + (h // 2) * 128 + 128]
                    first = h in (0, 1)
                    last = h in (6, 7)
                    p.inc(pe.matmul(yp, sUA[:, h * 64:(h + 1) * 64],
                                    sMaskA[:, h * 128:(h + 1) * 128],
                                    start=first, stop=False), "pe")
                    p.inc(pe.matmul(yp, sHbf[:, h * 64:(h + 1) * 64],
                                    sCpA[:, h * 128:(h + 1) * 128],
                                    start=False, stop=last), "pe")
                    p.inc(pe.matmul(bank(0)[:, h * 64:(h + 1) * 64], sBsm[:, :],
                                    sUpA[:, h * 64:(h + 1) * 64],
                                    start=(h == 0), stop=(h == 7)), "pe")
                p.wait(dv, "pe")
                for par, ybank in ((0, 7), (1, 2)):
                    rows = slice(par * 64, par * 64 + 64)
                    ypsv = ppool[rows, ybank * 512: ybank * 512 + 512]
                    dxh_v = sDxh[:, :].rearrange("p (m t) -> p m t", m=NCH)[rows, :, csl]
                    silu_v = sSiluZ[:, :].rearrange("p (m t) -> p m t", m=NCH)[rows, :, csl]
                    y_v = sY[:, :].rearrange("p (m t) -> p m t", m=NCH)[rows, :, csl]
                    p.dvx(dv, dv.tensor_tensor(sYevT[rows, :].rearrange("p (m t) -> p m t", m=4),
                                               ypsv.rearrange("p (m t) -> p m t", m=4),
                                               dxh_v, OP.add))
                    p.dvx(dv, dv.tensor_tensor(y_v, sYevT[rows, :].rearrange("p (m t) -> p m t", m=4),
                                               silu_v, OP.mult))
                dA_view = sDAf[:, 0:8].unsqueeze(2).broadcast_to([128, 8, 64])
                p.dvx(dv, dv.tensor_tensor(sHtmp[:, :].rearrange("p (h q) -> p h q", h=8),
                                           sH[:, :].rearrange("p (h q) -> p h q", h=8),
                                           dA_view, OP.mult))
                p.dvx(dv, dv.tensor_tensor(sH[:, :], sHtmp[:, :], bank(0), OP.add))
                p.dvx(dv, dv.tensor_copy(sHbf[:, :], sH[:, :]))

            p.wait(pe, "dv")
            for m in range(8):
                for k in range(4):
                    p.inc(pe.matmul(bank(m), sWout[:, k * 1024 + m * 128: k * 1024 + (m + 1) * 128],
                                    sY[:, k * TT:(k + 1) * TT],
                                    start=(k == 0), stop=(k == 3)), "pe")
            p.wait(dv, "pe")
            for m in range(8):
                p.dvx(dv, dv.tensor_copy(sPoutT[:, :], bank(m)))
                p.wait(sy, "dv")
                p.inc(sy.dma_start(out=pout_r[:, m, tsl], in_=sPoutT[:, :]), "st", 16)
                p.wait(dv, "st")

        if dbg:
            local = dict(sH=sH, sHbf=sHbf, sDrow=sDrow, sRow8=sRow8, sDAc=sDAc, sCdb=sCdb,
                         sCp=sCp, sMaskT=sMaskT, sU=sU, sUp=sUp, sColT=sColT, sY=sY,
                         sClrel=sClrel, sWs=sWs, sDtsp=sDtsp, sXhT=sXhT, sBsm=sBsm, sRow=sRow,
                         sMi=sMi, sEm=sEm)
            p.wait(gp, "dv", "ac", "pe")
            for n in dbg_names:
                p.inc(gp.dma_start(out=dbg_out[n][:, :], in_=local[n][:, :]), "st", 16)
        p.wait(gp, "st")
        p.wait(gp, "ld")
    return nc


def prep_A_inputs(L0_attn_x, norm_w, in_proj_w, conv_w, conv_b, A_log, D, dt_bias, out_proj_w):
    consts = _consts()
    in_maps = []
    for cid in range(N_CORES):
        b, g = cid // 4, cid % 4
        hsl = slice(8 * g, 8 * g + 8)
        chsl = slice(512 * g, 512 * g + 512)
        wz = in_proj_w[chsl]
        wx = in_proj_w[2048 + 512 * g: 2048 + 512 * g + 512]
        wbc = in_proj_w[4096:4352]
        wdtr = in_proj_w[4352 + 8 * g: 4352 + 8 * g + 8]
        m = dict(
            xb=f32(L0_attn_x[b].T),
            wzx=f32(np.concatenate([wz, wx], 0).T),
            wbc=f32(wbc.T),
            wdt=f32(wdtr.T),
            cw=f32(conv_w[chsl]),
            cb=f32(conv_b[chsl].reshape(4, 128).T),
            alog=f32(A_log[hsl][:, None]),
            dcol=f32(np.repeat(np.asarray(D)[hsl], 64).reshape(4, 128).T),
            dtb=f32(dt_bias[hsl][:, None]),
            nw=f32(np.asarray(norm_w).reshape(8, 128).T),
            wout=f32(out_proj_w[:, chsl].T),
            selc=f32(np.repeat(np.eye(8, dtype=np.float32), 128, axis=1)),
            **consts,
        )
        in_maps.append(m)
    return in_maps


def f32(x):
    return np.ascontiguousarray(np.asarray(x), dtype=np.float32)






def build_launch_B(dbg=False):
    nc = bass.Bass("TRN2", target_bir_lowering=False)
    TT = 512
    NTILE = TPC // TT

    xo = nc.dram_tensor("xo", [D_MODEL, TPC], F32, kind="ExternalInput").ap()
    pps = [nc.dram_tensor(f"pp{j}", [D_MODEL, TPC], F32, kind="ExternalInput").ap()
           for j in range(4)]
    wg = nc.dram_tensor("wg", [D_MODEL, 8], F32, kind="ExternalInput").ap()
    onesr, trilc, eyeb, eyef = _add_const_tensors(nc)
    x1f = nc.dram_tensor("x1f", [D_MODEL, TPC], F32, kind="ExternalOutput").ap()
    x1b = nc.dram_tensor("x1b", [D_MODEL, TPC], BF16, kind="ExternalOutput").ap()
    sct = nc.dram_tensor("sct", [8, TPC], F32, kind="ExternalOutput").ap()
    dbgB = {}
    if dbg:
        for n, shp in [("sRaw", [128, 4096]), ("sR1", [1, 512]), ("sR2", [1, 512]),
                       ("sR3", [1, 512]), ("sAcc", [128, 4096]), ("sXo", [128, 4096]),
                       ("sSq", [128, 512])]:
            dbgB[n] = nc.dram_tensor("dbg_" + n, shp, F32, kind="ExternalOutput").ap()

    xo_r = xo.rearrange("(c p) t -> p c t", p=128)
    pp_r = [q.rearrange("(c p) t -> p c t", p=128) for q in pps]
    x1f_r = x1f.rearrange("(c p) t -> p c t", p=128)
    x1b_r = x1b.rearrange("(c p) t -> p c t", p=128)

    with ExitStack() as st:
        sb = lambda name, shape, dt=F32: st.enter_context(nc.sbuf_tensor(name, shape, dt))
        sAcc = sb("sAcc", [128, 8 * TT])
        sP1 = sb("sP1", [128, 8 * TT])
        sXo = sb("sXo", [128, 8 * TT])
        sRaw = sb("sRaw", [128, 8 * TT])
        sSq = sb("sSq", [128, TT])
        sR1 = sb("sR1", [1, TT])
        sR2 = sb("sR2", [1, TT])
        sR3 = sb("sR3", [1, TT])
        sX1 = sb("sX1", [128, 8 * TT])
        sX1b = sb("sX1b", [128, 8 * TT], BF16)
        sWg = sb("sWg", [128, 8 * 8])
        sOnes = sb("sOnes", [1, 128])
        sOnesCol = sb("sOnesCol", [128, 1])
        sEyeF = sb("sEyeF", [128, 128])
        sEpsR = sb("sEpsR", [1, 1])
        sMx = sb("sMx", [128, 1])
        sSc = sb("sSc", [128, 8])
        sScE = sb("sScE", [128, 8])
        sScT = sb("sScT", [8, TT])
        sRec = sb("sRec", [128, 1])

        ppool = st.enter_context(nc.psum_tensor("ppool", [128, 4096], F32))

        def bank(i, w=512, off=0):
            return ppool[:, i * 512 + off: i * 512 + off + w]

        def bankp(i, p0, prows, w, off=0):
            return ppool[p0:p0 + prows, i * 512 + off: i * 512 + off + w]

        p = P(nc, st, ["ld", "pe", "dv", "ac", "st"])
        sy, gp, pe, dv, ac = nc.sync, nc.gpsimd, nc.tensor, nc.vector, nc.scalar

        p.inc(sy.dma_start(out=sOnes[:, :], in_=onesr[:, :]), "ld", 16)
        p.inc(sy.dma_start(out=sEyeF[:, :], in_=eyef[:, :]), "ld", 16)
        p.inc(sy.dma_start(out=sWg[:, :].rearrange("p (c m) -> p c m", c=8),
                           in_=wg.rearrange("(c p) m -> p c m", p=128)), "ld", 16)
        p.wait(dv, "ld")
        p.dvx(dv, dv.memset(sOnesCol[:, :], 1.0))
        p.dvx(dv, dv.memset(sEpsR[:, :], LNEPS))

        for tt in range(NTILE):
            tsl = slice(tt * TT, (tt + 1) * TT)
            p.wait(sy, "pe", "dv", "ac")
            p.inc(sy.dma_start(out=sXo[:, :].rearrange("p (c t) -> p c t", c=8),
                               in_=xo_r[:, :, tsl]), "ld", 16)
            p.inc(sy.dma_start(out=sAcc[:, :].rearrange("p (c t) -> p c t", c=8),
                               in_=pp_r[0][:, :, tsl]), "ld", 16)
            for j in range(1, 4):
                if j > 1:
                    p.wait(sy, "dv")
                p.inc(sy.dma_start(out=sP1[:, :].rearrange("p (c t) -> p c t", c=8),
                                   in_=pp_r[j][:, :, tsl]), "ld", 16)
                p.wait(dv, "ld")
                p.dvx(dv, dv.tensor_tensor(sAcc[:, :], sAcc[:, :], sP1[:, :], OP.add))
            p.wait(dv, "ld")
            p.dvx(dv, dv.scalar_tensor_tensor(sRaw[:, :], sXo[:, :], 2.0, sAcc[:, :],
                                              OP.mult, OP.add))
            p.wait(pe, "dv")
            for c in range(8):
                p.inc(pe.matmul(bankp(0, 0, 1, TT), sOnesCol[:, :],
                                sRaw[:, c * TT:(c + 1) * TT],
                                start=(c == 0), stop=(c == 7)), "pe")
            p.wait(ac, "dv")
            for c in range(8):
                if c > 0:
                    p.wait(ac, "pe")
                p.acx(ac, ac.activation(out=sSq[:, :], in_=sRaw[:, c * TT:(c + 1) * TT],
                                        func=AF.Square))
                p.wait(pe, "ac")
                p.inc(pe.matmul(bankp(1, 0, 1, TT), sOnesCol[:, :], sSq[:, :],
                                start=(c == 0), stop=(c == 7)), "pe")
            p.wait(dv, "pe")
            p.dvx(dv, dv.tensor_scalar(sR1[:, :], bankp(0, 0, 1, TT), 1.0 / D_MODEL,
                                       None, OP.mult))
            p.dvx(dv, dv.tensor_scalar(sR2[:, :], bankp(1, 0, 1, TT), 1.0 / D_MODEL,
                                       None, OP.mult))
            p.dvx(dv, dv.tensor_tensor(sR3[:, :], sR1[:, :], sR1[:, :], OP.mult))
            p.dvx(dv, dv.tensor_tensor(sR2[:, :], sR2[:, :], sR3[:, :], OP.subtract))
            p.wait(ac, "dv")
            p.acx(ac, ac.activation(out=sR3[:, :], in_=sR2[:, :], func=AF.Ln, bias=sEpsR[:, :]))
            p.acx(ac, ac.activation(out=sR2[:, :], in_=sR3[:, :], func=AF.Exp, scale=-0.5))
            p.wait(dv, "ac")
            p.dvx(dv, dv.scalar_tensor_tensor(sR3[:, :], sR1[:, :], -1.0, sR2[:, :],
                                              OP.mult, OP.mult))
            p.wait(pe, "dv")
            p.inc(pe.matmul(bank(2, TT), sOnes[:, :], sR2[:, :], start=True, stop=True), "pe")
            p.inc(pe.matmul(bank(3, TT), sOnes[:, :], sR3[:, :], start=True, stop=True), "pe")
            p.wait(dv, "pe", "st")
            for c in range(8):
                csl = slice(c * TT, (c + 1) * TT)
                p.dvx(dv, dv.tensor_tensor(sX1[:, csl], sRaw[:, csl], bank(2, TT), OP.mult))
                p.dvx(dv, dv.tensor_tensor(sX1[:, csl], sX1[:, csl], bank(3, TT), OP.add))
                p.dvx(dv, dv.tensor_copy(sX1b[:, csl], sX1[:, csl]))
            p.wait(sy, "dv")
            p.inc(sy.dma_start(out=x1f_r[:, :, tsl],
                               in_=sX1[:, :].rearrange("p (c t) -> p c t", c=8)), "st", 16)
            p.inc(sy.dma_start(out=x1b_r[:, :, tsl],
                               in_=sX1b[:, :].rearrange("p (c t) -> p c t", c=8)), "st", 16)
            for tg in range(TT // 128):
                p.wait(pe, "dv", "ac")
                for k in range(8):
                    p.inc(pe.matmul(bankp(4, 0, 128, 8), sX1[:, k * TT + tg * 128: k * TT + (tg + 1) * 128],
                                    sWg[:, k * 8:(k + 1) * 8],
                                    start=(k == 0), stop=(k == 7)), "pe")
                p.wait(dv, "pe")
                p.dvx(dv, dv.reduce_max(sMx[:, :], bankp(4, 0, 128, 8), axis=mybir.AxisListType.X))
                p.dvx(dv, dv.tensor_scalar(sMx[:, :], sMx[:, :], -1.0, None, OP.mult))
                p.wait(ac, "dv")
                p.acx(ac, ac.activation(out=sScE[:, :], in_=bankp(4, 0, 128, 8), func=AF.Exp,
                                        bias=sMx[:, :]))
                p.wait(dv, "ac")
                p.dvx(dv, dv.reduce_sum(sRec[:, :], sScE[:, :], axis=mybir.AxisListType.X))
                p.dvx(dv, dv.reciprocal(sRec[:, :], sRec[:, :]))
                p.dvx(dv, dv.tensor_scalar(sSc[:, :], sScE[:, :], sRec[:, :], None, OP.mult))
                p.wait(pe, "dv")
                p.inc(pe.transpose(bankp(5, 0, 8, 128), sSc[:, :], sEyeF[:, :]), "pe")
                p.wait(dv, "pe")
                if tg == 0:
                    p.wait(dv, "st")
                p.dvx(dv, dv.tensor_copy(sScT[0:8, tg * 128:(tg + 1) * 128], bankp(5, 0, 8, 128)))
            p.wait(sy, "dv")
            p.inc(sy.dma_start(out=sct[:, tsl], in_=sScT[0:8, :]), "st", 16)
        if dbg:
            local = dict(sRaw=sRaw, sR1=sR1, sR2=sR2, sR3=sR3, sAcc=sAcc, sXo=sXo, sSq=sSq)
            p.wait(gp, "dv", "ac", "pe")
            for n, ap in dbgB.items():
                p.inc(gp.dma_start(out=ap[:, :], in_=local[n][:, :]), "st", 16)
        p.wait(gp, "st")
        p.wait(gp, "ld")
    return nc


def prep_B_inputs(x_full, poutsA, wg):
    consts = _consts()
    in_maps = []
    for cid in range(N_CORES):
        b, q = cid // 4, cid % 4
        tsl = slice(2048 * q, 2048 * q + 2048)
        m = dict(
            xo=f32(x_full[b].T[:, tsl]),
            wg=f32(wg),
            **consts,
        )
        for j in range(4):
            m[f"pp{j}"] = f32(poutsA[4 * b + j][:, tsl])
        in_maps.append(m)
    return in_maps



def build_launch_C():
    nc = bass.Bass("TRN2", target_bir_lowering=False)
    TT = 512
    NTOK = 16384
    NTILE = NTOK // TT
    NITER = 42
    CAP = 4096.0

    x1all = nc.dram_tensor("x1all", [D_MODEL, NTOK], BF16, kind="ExternalInput").ap()
    srow = nc.dram_tensor("srow", [128, 128], F32, kind="ExternalInput").ap()
    w1 = nc.dram_tensor("w1", [D_MODEL, 2048], F32, kind="ExternalInput").ap()
    w3 = nc.dram_tensor("w3", [D_MODEL, 2048], F32, kind="ExternalInput").ap()
    w2 = nc.dram_tensor("w2", [2048, D_MODEL], F32, kind="ExternalInput").ap()
    triu1 = nc.dram_tensor("triu1", [128, 128], F32, kind="ExternalInput").ap()
    onesr, trilc, eyeb, eyef = _add_const_tensors(nc)
    eo = nc.dram_tensor("eo", [D_MODEL, NTOK], BF16, kind="ExternalOutput").ap()
    gateD = nc.dram_tensor("gateD", [128, 128], F32, kind="Internal").ap()

    x1_r = x1all.rearrange("(c p) t -> p c t", p=128)
    eo_r = eo.rearrange("(c p) t -> p c t", p=128)
    gateD_f = gateD.rearrange("p f -> (p f)")

    with ExitStack() as st:
        sb = lambda name, shape, dt=F32: st.enter_context(nc.sbuf_tensor(name, shape, dt))
        sW1 = sb("sW1", [128, 8 * 2048], BF16)
        sW3 = sb("sW3", [128, 8 * 2048], BF16)
        sW2 = sb("sW2", [128, 16 * 1024], BF16)
        sGrid = sb("sGrid", [128, 128])
        sLo = sb("sLo", [128, 1])
        sHi = sb("sHi", [128, 1])
        sMid = sb("sMid", [128, 1])
        sCnt = sb("sCnt", [128, 1])
        sSel = sb("sSel", [128, 1])
        sT1 = sb("sT1", [128, 1])
        sT2 = sb("sT2", [128, 1])
        sR11 = sb("sR11", [1, 1])
        sTrash = sb("sTrash", [128, 128], BF16)
        sEq = sb("sEq", [128, 128])
        sRk = sb("sRk", [128, 128])
        sRk2 = sb("sRk2", [128, 128])
        sMask = sb("sMask", [128, 128])
        sGate = sb("sGate", [128, 128])
        sTriu1 = sb("sTriu1", [128, 128])
        sOnes = sb("sOnes", [1, 128])
        sOnesCol = sb("sOnesCol", [128, 1])
        sGrow = sb("sGrow", [1, TT])
        sGb = sb("sGb", [128, TT], BF16)
        sXt = sb("sXt", [128, 8 * TT], BF16)
        sRt = sb("sRt", [128, 8 * TT], BF16)
        sGs = sb("sGs", [128, TT], BF16)
        sGs2 = sb("sGs2", [128, TT], BF16)
        sH1 = sb("sH1", [128, 16 * TT], BF16)
        sEo = sb("sEo", [128, 8 * TT], BF16)

        ppool = st.enter_context(nc.psum_tensor("ppool", [128, 4096], F32))

        def bank(i, w=512, off=0):
            return ppool[:, i * 512 + off: i * 512 + off + w]

        def bankp(i, p0, prows, w, off=0):
            return ppool[p0:p0 + prows, i * 512 + off: i * 512 + off + w]

        p = P(nc, st, ["ld", "gld", "pe", "dv", "ac", "st"])
        sy, gp, pe, dv, ac = nc.sync, nc.gpsimd, nc.tensor, nc.vector, nc.scalar

        p.inc(sy.dma_start(out=sGrid[:, :], in_=srow[:, :]), "ld", 16)
        p.inc(sy.dma_start(out=sTriu1[:, :], in_=triu1[:, :]), "ld", 16)
        p.inc(sy.dma_start(out=sOnes[:, :], in_=onesr[:, :]), "ld", 16)
        p.inc(gp.dma_start(out=sW1[:, :].rearrange("p (c m) -> p c m", c=8),
                           in_=w1.rearrange("(c p) m -> p c m", p=128)), "gld", 16)
        p.inc(gp.dma_start(out=sW3[:, :].rearrange("p (c m) -> p c m", c=8),
                           in_=w3.rearrange("(c p) m -> p c m", p=128)), "gld", 16)
        p.inc(gp.dma_start(out=sW2[:, :].rearrange("p (c m) -> p c m", c=16),
                           in_=w2.rearrange("(c p) m -> p c m", p=128)), "gld", 16)

        p.wait(dv, "ld")
        p.dvx(dv, dv.memset(sOnesCol[:, :], 1.0))
        p.dvx(dv, dv.memset(sLo[:, :], 0.0))
        p.dvx(dv, dv.memset(sHi[:, :], 2.0))

        for it in range(NITER):
            p.dvx(dv, dv.tensor_scalar(sMid[:, :], sLo[:, :], sHi[:, :], 0.5,
                                       OP.add, OP.mult))
            p.dvx(dv, dv.tensor_scalar(sTrash[:, :], sGrid[:, :], sMid[:, :], 0.0,
                                       OP.is_ge, OP.add, accum_out=sCnt[:, :]))
            p.wait(pe, "dv")
            p.inc(pe.matmul(bankp(0, 0, 1, 1), sCnt[:, :], sOnesCol[:, :],
                            start=True, stop=True), "pe")
            p.wait(dv, "pe")
            p.dvx(dv, dv.tensor_copy(sR11[:, :], bankp(0, 0, 1, 1)))
            p.wait(pe, "dv")
            p.inc(pe.matmul(bankp(1, 0, 128, 1), sOnes[:, :], sR11[:, :],
                            start=True, stop=True), "pe")
            p.wait(dv, "pe")
            p.dvx(dv, dv.tensor_scalar(sSel[:, :], bankp(1, 0, 128, 1), CAP, None, OP.is_ge))
            p.dvx(dv, dv.tensor_tensor(sT1[:, :], sMid[:, :], sLo[:, :], OP.subtract))
            p.dvx(dv, dv.tensor_tensor(sT1[:, :], sT1[:, :], sSel[:, :], OP.mult))
            p.dvx(dv, dv.tensor_tensor(sLo[:, :], sLo[:, :], sT1[:, :], OP.add))
            p.dvx(dv, dv.tensor_tensor(sT2[:, :], sHi[:, :], sMid[:, :], OP.subtract))
            p.dvx(dv, dv.tensor_tensor(sT2[:, :], sT2[:, :], sSel[:, :], OP.mult))
            p.dvx(dv, dv.tensor_tensor(sHi[:, :], sMid[:, :], sT2[:, :], OP.add))
        p.dvx(dv, dv.tensor_scalar(sMask[:, :], sGrid[:, :], sLo[:, :], 0.0,
                                   OP.is_gt, OP.add, accum_out=sCnt[:, :]))
        p.wait(pe, "dv")
        p.inc(pe.matmul(bankp(0, 0, 1, 1), sCnt[:, :], sOnesCol[:, :],
                        start=True, stop=True), "pe")
        p.wait(dv, "pe")
        p.dvx(dv, dv.tensor_scalar(sR11[:, :], bankp(0, 0, 1, 1), -1.0, CAP,
                                   OP.mult, OP.add))
        p.wait(pe, "dv")
        p.inc(pe.matmul(bankp(1, 0, 128, 1), sOnes[:, :], sR11[:, :],
                        start=True, stop=True), "pe")
        p.wait(dv, "pe")
        p.dvx(dv, dv.tensor_copy(sT1[:, :], bankp(1, 0, 128, 1)))
        p.dvx(dv, dv.tensor_scalar(sEq[:, :], sGrid[:, :], sLo[:, :], 0.0,
                                   OP.is_equal, OP.add, accum_out=sCnt[:, :]))
        p.dvx(dv, dv.tensor_tensor_scan(sRk[:, :], sEq[:, :], sEq[:, :], 0.0,
                                        OP.add, OP.bypass))
        p.dvx(dv, dv.tensor_tensor(sRk[:, :], sRk[:, :], sEq[:, :], OP.subtract))
        p.wait(pe, "dv")
        p.inc(pe.matmul(bankp(2, 0, 128, 1), sTriu1[:, :], sCnt[:, :],
                        start=True, stop=True), "pe")
        p.wait(dv, "pe")
        p.dvx(dv, dv.tensor_scalar(sRk2[:, :], sRk[:, :], bankp(2, 0, 128, 1), None, OP.add))
        p.dvx(dv, dv.tensor_scalar(sRk2[:, :], sRk2[:, :], sT1[:, :], None, OP.is_lt))
        p.dvx(dv, dv.tensor_tensor(sRk2[:, :], sRk2[:, :], sEq[:, :], OP.mult))
        p.dvx(dv, dv.tensor_tensor(sMask[:, :], sMask[:, :], sRk2[:, :], OP.add))
        p.dvx(dv, dv.tensor_tensor(sGate[:, :], sMask[:, :], sGrid[:, :], OP.mult))
        p.wait(sy, "dv")
        p.inc(sy.dma_start(out=gateD[:, :], in_=sGate[:, :]), "st", 16)
        p.wait(sy, "st")

        p.wait(dv, "gld")
        for tt in range(NTILE):
            tsl = slice(tt * TT, (tt + 1) * TT)
            p.wait(sy, "pe", "dv", "ac")
            p.inc(sy.dma_start(out=sXt[:, :].rearrange("p (c t) -> p c t", c=8),
                               in_=x1_r[:, :, tsl]), "ld", 16)
            p.inc(sy.dma_start(out=sGrow[:, :], in_=gateD_f[tt * TT:(tt + 1) * TT].unsqueeze(0)),
                  "ld", 16)
            p.wait(pe, "ld", "dv")
            p.inc(pe.matmul(bank(3, TT), sOnes[:, :], sGrow[:, :], start=True, stop=True), "pe")
            p.wait(dv, "pe", "ld")
            p.dvx(dv, dv.tensor_copy(sGb[:, :], bank(3, TT)))
            for c in range(8):
                csl = slice(c * TT, (c + 1) * TT)
                p.dvx(dv, dv.tensor_tensor(sRt[:, csl], sXt[:, csl], sGb[:, :], OP.mult))
            consumed_dv = [0] * 16
            consumed_ac = [0] * 16
            for m in range(16):
                gb, ub = (bank(0), bank(1)) if m % 2 == 0 else (bank(4), bank(5))
                if m >= 2:
                    p.wait_at(pe, "dv", consumed_dv[m - 2])
                    p.wait_at(pe, "ac", consumed_ac[m - 2])
                elif m == 0:
                    p.wait(pe, "dv", "ac")
                for k in range(8):
                    p.inc(pe.matmul(gb, sW1[:, k * 2048 + m * 128: k * 2048 + (m + 1) * 128],
                                    sRt[:, k * TT:(k + 1) * TT],
                                    start=(k == 0), stop=(k == 7)), "pe")
                for k in range(8):
                    p.inc(pe.matmul(ub, sW3[:, k * 2048 + m * 128: k * 2048 + (m + 1) * 128],
                                    sRt[:, k * TT:(k + 1) * TT],
                                    start=(k == 0), stop=(k == 7)), "pe")
                p.wait(ac, "pe")
                sgs = sGs if m % 2 == 0 else sGs2
                p.acx(ac, ac.activation(out=sgs[:, :], in_=gb, func=AF.Silu))
                consumed_ac[m] = p.at("ac")
                p.wait(dv, "ac")
                p.dvx(dv, dv.tensor_tensor(sH1[:, m * TT:(m + 1) * TT], sgs[:, :],
                                           ub, OP.mult))
                consumed_dv[m] = p.at("dv")
            dcon = [0] * 8
            for dm in range(8):
                db = bank(2) if dm % 2 == 0 else bank(6)
                if dm >= 2:
                    p.wait_at(pe, "dv", dcon[dm - 2])
                else:
                    p.wait(pe, "dv")
                for k in range(16):
                    p.inc(pe.matmul(db, sW2[:, k * 1024 + dm * 128: k * 1024 + (dm + 1) * 128],
                                    sH1[:, k * TT:(k + 1) * TT],
                                    start=(k == 0), stop=(k == 15)), "pe")
                p.wait(dv, "pe")
                if dm == 0:
                    p.wait(dv, "st")
                p.dvx(dv, dv.tensor_copy(sEo[:, dm * TT:(dm + 1) * TT], db))
                dcon[dm] = p.at("dv")
            p.wait(sy, "dv")
            p.inc(sy.dma_start(out=eo_r[:, :, tsl],
                               in_=sEo[:, :].rearrange("p (c t) -> p c t", c=8)), "st", 16)
        p.wait(gp, "st")
        p.wait(gp, "ld")
    return nc


def prep_C_inputs(x1bs, scores, w1, w3, w2):
    consts = _consts()
    x1all = np.ascontiguousarray(np.concatenate(x1bs, axis=1))
    in_maps = []
    for e in range(N_CORES):
        m = dict(
            x1all=x1all,
            srow=f32(np.ascontiguousarray(scores[:, e]).reshape(128, 128)),
            w1=f32(w1[e]),
            w3=f32(w3[e]),
            w2=f32(w2[e]),
            triu1=np.triu(np.ones((128, 128), np.float32), 1),
            **consts,
        )
        in_maps.append(m)
    return in_maps



def build_launch_D():
    nc = bass.Bass("TRN2", target_bir_lowering=False)
    TT = 512
    NTILE = TPC // TT

    x1o = nc.dram_tensor("x1o", [D_MODEL, TPC], F32, kind="ExternalInput").ap()
    eos = [nc.dram_tensor(f"eo{j}", [D_MODEL, TPC], BF16, kind="ExternalInput").ap()
           for j in range(8)]
    onesr, trilc, eyeb, eyef = _add_const_tensors(nc)
    x2f = nc.dram_tensor("x2f", [D_MODEL, TPC], F32, kind="ExternalOutput").ap()
    x2b = nc.dram_tensor("x2b", [D_MODEL, TPC], BF16, kind="ExternalOutput").ap()

    x1_r = x1o.rearrange("(c p) t -> p c t", p=128)
    eo_r = [q.rearrange("(c p) t -> p c t", p=128) for q in eos]
    x2f_r = x2f.rearrange("(c p) t -> p c t", p=128)
    x2b_r = x2b.rearrange("(c p) t -> p c t", p=128)

    with ExitStack() as st:
        sb = lambda name, shape, dt=F32: st.enter_context(nc.sbuf_tensor(name, shape, dt))
        sAcc = sb("sAcc", [128, 8 * TT])
        sP1 = sb("sP1", [128, 8 * TT], BF16)
        sRaw = sb("sRaw", [128, 8 * TT])
        sSq = sb("sSq", [128, TT])
        sR1 = sb("sR1", [1, TT])
        sR2 = sb("sR2", [1, TT])
        sR3 = sb("sR3", [1, TT])
        sX2 = sb("sX2", [128, 8 * TT])
        sX2b = sb("sX2b", [128, 8 * TT], BF16)
        sOnes = sb("sOnes", [1, 128])
        sOnesCol = sb("sOnesCol", [128, 1])
        sEpsR = sb("sEpsR", [1, 1])

        ppool = st.enter_context(nc.psum_tensor("ppool", [128, 4096], F32))

        def bank(i, w=512, off=0):
            return ppool[:, i * 512 + off: i * 512 + off + w]

        def bankp(i, p0, prows, w, off=0):
            return ppool[p0:p0 + prows, i * 512 + off: i * 512 + off + w]

        p = P(nc, st, ["ld", "pe", "dv", "ac", "st"])
        sy, gp, pe, dv, ac = nc.sync, nc.gpsimd, nc.tensor, nc.vector, nc.scalar

        p.inc(sy.dma_start(out=sOnes[:, :], in_=onesr[:, :]), "ld", 16)
        p.wait(dv, "ld")
        p.dvx(dv, dv.memset(sOnesCol[:, :], 1.0))
        p.dvx(dv, dv.memset(sEpsR[:, :], LNEPS))

        for tt in range(NTILE):
            tsl = slice(tt * TT, (tt + 1) * TT)
            p.wait(sy, "pe", "dv", "ac")
            p.inc(sy.dma_start(out=sAcc[:, :].rearrange("p (c t) -> p c t", c=8),
                               in_=x1_r[:, :, tsl]), "ld", 16)
            for j in range(8):
                if j > 0:
                    p.wait(sy, "dv")
                p.inc(sy.dma_start(out=sP1[:, :].rearrange("p (c t) -> p c t", c=8),
                                   in_=eo_r[j][:, :, tsl]), "ld", 16)
                p.wait(dv, "ld")
                p.dvx(dv, dv.tensor_tensor(sAcc[:, :], sAcc[:, :], sP1[:, :], OP.add))
            p.dvx(dv, dv.tensor_copy(sRaw[:, :], sAcc[:, :]))
            p.wait(pe, "dv")
            for c in range(8):
                p.inc(pe.matmul(bankp(0, 0, 1, TT), sOnesCol[:, :],
                                sRaw[:, c * TT:(c + 1) * TT],
                                start=(c == 0), stop=(c == 7)), "pe")
            p.wait(ac, "dv")
            for c in range(8):
                if c > 0:
                    p.wait(ac, "pe")
                p.acx(ac, ac.activation(out=sSq[:, :], in_=sRaw[:, c * TT:(c + 1) * TT],
                                        func=AF.Square))
                p.wait(pe, "ac")
                p.inc(pe.matmul(bankp(1, 0, 1, TT), sOnesCol[:, :], sSq[:, :],
                                start=(c == 0), stop=(c == 7)), "pe")
            p.wait(dv, "pe")
            p.dvx(dv, dv.tensor_scalar(sR1[:, :], bankp(0, 0, 1, TT), 1.0 / D_MODEL,
                                       None, OP.mult))
            p.dvx(dv, dv.tensor_scalar(sR2[:, :], bankp(1, 0, 1, TT), 1.0 / D_MODEL,
                                       None, OP.mult))
            p.dvx(dv, dv.tensor_tensor(sR3[:, :], sR1[:, :], sR1[:, :], OP.mult))
            p.dvx(dv, dv.tensor_tensor(sR2[:, :], sR2[:, :], sR3[:, :], OP.subtract))
            p.wait(ac, "dv")
            p.acx(ac, ac.activation(out=sR3[:, :], in_=sR2[:, :], func=AF.Ln, bias=sEpsR[:, :]))
            p.acx(ac, ac.activation(out=sR2[:, :], in_=sR3[:, :], func=AF.Exp, scale=-0.5))
            p.wait(dv, "ac")
            p.dvx(dv, dv.scalar_tensor_tensor(sR3[:, :], sR1[:, :], -1.0, sR2[:, :],
                                              OP.mult, OP.mult))
            p.wait(pe, "dv")
            p.inc(pe.matmul(bank(2, TT), sOnes[:, :], sR2[:, :], start=True, stop=True), "pe")
            p.inc(pe.matmul(bank(3, TT), sOnes[:, :], sR3[:, :], start=True, stop=True), "pe")
            p.wait(dv, "pe", "st")
            for c in range(8):
                csl = slice(c * TT, (c + 1) * TT)
                p.dvx(dv, dv.tensor_tensor(sX2[:, csl], sRaw[:, csl], bank(2, TT), OP.mult))
                p.dvx(dv, dv.tensor_tensor(sX2[:, csl], sX2[:, csl], bank(3, TT), OP.add))
                p.dvx(dv, dv.tensor_copy(sX2b[:, csl], sX2[:, csl]))
            p.wait(sy, "dv")
            p.inc(sy.dma_start(out=x2f_r[:, :, tsl],
                               in_=sX2[:, :].rearrange("p (c t) -> p c t", c=8)), "st", 16)
            p.inc(sy.dma_start(out=x2b_r[:, :, tsl],
                               in_=sX2b[:, :].rearrange("p (c t) -> p c t", c=8)), "st", 16)
        p.wait(gp, "st")
        p.wait(gp, "ld")
    return nc


def prep_D_inputs(x1fs, eos_full):
    consts = _consts()
    in_maps = []
    for cid in range(N_CORES):
        tsl = slice(2048 * cid, 2048 * cid + 2048)
        m = dict(x1o=f32(x1fs[cid]), **consts)
        for j in range(8):
            m[f"eo{j}"] = np.ascontiguousarray(eos_full[j][:, tsl])
        in_maps.append(m)
    return in_maps



LNC16 = 2.772588722239781
EPS16 = 1e-6 / 16.0


def build_launch_E():
    nc = bass.Bass("TRN2", target_bir_lowering=False)
    T = TPC
    NH = 16

    x2bi = nc.dram_tensor("x2bi", [D_MODEL, T], BF16, kind="ExternalInput").ap()
    wqT = nc.dram_tensor("wqT", [D_MODEL, D_MODEL], F32, kind="ExternalInput").ap()
    wkT = nc.dram_tensor("wkT", [D_MODEL, D_MODEL], F32, kind="ExternalInput").ap()
    wvT = nc.dram_tensor("wvT", [D_MODEL, D_MODEL], F32, kind="ExternalInput").ap()
    projT = nc.dram_tensor("projT", [128, NH * 256], F32, kind="ExternalInput").ap()
    onesr, trilc, eyeb, eyef = _add_const_tensors(nc)
    phiq = nc.dram_tensor("phiq", [T, NH * 256], BF16, kind="ExternalOutput").ap()
    kvp = nc.dram_tensor("kvp", [NH, 2, 128, 65], F32, kind="ExternalOutput").ap()

    x2_r = x2bi.rearrange("(c p) t -> p c t", p=128)
    phiq_r = phiq.rearrange("(g p) m -> p g m", p=128)

    with ExitStack() as st:
        sb = lambda name, shape, dt=F32: st.enter_context(nc.sbuf_tensor(name, shape, dt))
        sX2 = sb("sX2", [128, 8 * T], BF16)
        sVt = sb("sVt", [128, 16 * 1024], BF16)
        sWq = sb("sWq", [128, 8 * 1024], BF16)
        sWk = sb("sWk", [128, 8 * 1024], BF16)
        sProj = sb("sProj", [128, NH * 256], BF16)
        sQh = sb("sQh", [128, T], BF16)
        sKh = sb("sKh", [128, T], BF16)
        sQsq = sb("sQsq", [128, T])
        sPhiQh = sb("sPhiQh", [128, 16 * 256], BF16)
        sMx4 = sb("sMx4", [128, 4])
        sB4 = sb("sB4", [128, 4])
        sXwS = sb("sXwS", [128, 1024])
        sPhi4 = sb("sPhi4", [128, 1024], BF16)
        sKVo = sb("sKVo", [128, 130])
        sOnesCol = sb("sOnesCol", [128, 1])
        sOnesColB = sb("sOnesColB", [128, 1], BF16)

        ppool = st.enter_context(nc.psum_tensor("ppool", [128, 4096], F32))

        def bank(i, w=512, off=0):
            return ppool[:, i * 512 + off: i * 512 + off + w]

        def bankp(i, p0, prows, w, off=0):
            return ppool[p0:p0 + prows, i * 512 + off: i * 512 + off + w]

        p = P(nc, st, ["ld", "gld", "pe", "dv", "ac", "st"])
        sy, gp, pe, dv, ac = nc.sync, nc.gpsimd, nc.tensor, nc.vector, nc.scalar

        p.inc(sy.dma_start(out=sX2[:, :].rearrange("p (c t) -> p c t", c=8),
                           in_=x2_r[:, :, :]), "ld", 16)
        p.inc(gp.dma_start(out=sProj[:, :], in_=projT[:, :]), "gld", 16)
        p.inc(gp.dma_start(out=sWq[:, :].rearrange("p (c m) -> p c m", c=8),
                           in_=wvT.rearrange("(c p) m -> p c m", p=128)), "gld", 16)
        p.wait(dv, "ld")
        p.dvx(dv, dv.memset(sOnesCol[:, :], 1.0))
        p.dvx(dv, dv.memset(sOnesColB[:, :], 1.0))
        p.wait(pe, "ld", "gld", "dv")
        for tg in range(16):
            p.wait(pe, "dv")
            for n in range(2):
                for k in range(8):
                    p.inc(pe.matmul(bank(n), sX2[:, k * T + tg * 128: k * T + (tg + 1) * 128],
                                    sWq[:, k * 1024 + n * 512:(k * 1024) + (n + 1) * 512],
                                    start=(k == 0), stop=(k == 7)), "pe")
            p.wait(dv, "pe")
            for n in range(2):
                p.dvx(dv, dv.tensor_copy(sVt[:, tg * 1024 + n * 512: tg * 1024 + (n + 1) * 512],
                                         bank(n)))
        p.wait(gp, "dv")
        p.inc(gp.dma_start(out=sWq[:, :].rearrange("p (c m) -> p c m", c=8),
                           in_=wqT.rearrange("(c p) m -> p c m", p=128)), "gld", 16)
        p.inc(gp.dma_start(out=sWk[:, :].rearrange("p (c m) -> p c m", c=8),
                           in_=wkT.rearrange("(c p) m -> p c m", p=128)), "gld", 16)
        p.wait(pe, "gld")

        for hp in range(NH // 2):
            for (buf, w) in ((sQh, sWq), (sKh, sWk)):
                for n in range(4):
                    p.wait(pe, "dv", "ac")
                    for k in range(8):
                        p.inc(pe.matmul(bankp(0, 0, 128, 512),
                                        w[:, k * 1024 + hp * 128: k * 1024 + (hp + 1) * 128],
                                        sX2[:, k * T + n * 512: k * T + (n + 1) * 512],
                                        start=(k == 0), stop=(k == 7)), "pe")
                    p.wait(dv, "pe")
                    p.dvx(dv, dv.tensor_scalar(buf[:, n * 512:(n + 1) * 512],
                                               bankp(0, 0, 128, 512), 0.125, None, OP.mult))
            for sub in range(2):
                h = hp * 2 + sub
                hof = h * 64
                rsl = slice(sub * 64, sub * 64 + 64)
                for side in range(2):
                    qb = sQh if side == 0 else sKh
                    p.wait(ac, "dv", "pe")
                    p.acx(ac, ac.activation(out=sQsq[rsl, :], in_=qb[rsl, :], func=AF.Square))
                    if side == 0:
                        p.wait(dv, "st")
                    for g in range(4):
                        p.wait(pe, "ac", "dv")
                        for j in range(4):
                            tg = g * 4 + j
                            p.inc(pe.matmul(bankp(4, 0, 128, 256, off=j * 256),
                                            qb[rsl, tg * 128:(tg + 1) * 128],
                                            sProj[rsl, h * 256:(h + 1) * 256],
                                            start=True, stop=True), "pe")
                            p.inc(pe.matmul(bankp(6, 0, 128, 1, off=j),
                                            sQsq[rsl, tg * 128:(tg + 1) * 128],
                                            sOnesCol[rsl, :], start=True, stop=True), "pe")
                        p.wait(dv, "pe")
                        xwp = ppool[:, 4 * 512: 4 * 512 + 1024]
                        p.dvx(dv, dv.tensor_reduce(sMx4[:, :],
                                                   xwp.rearrange("p (j m) -> p j m", j=4),
                                                   axis=mybir.AxisListType.X, op=OP.max))
                        p.inc(dv.scalar_tensor_tensor(sB4[:, :], bankp(6, 0, 128, 4), 0.5,
                                                      sMx4[:, :], OP.mult, OP.add), "dv")
                        dv.drain()
                        p.dvx(dv, dv.tensor_scalar(sB4[:, :], sB4[:, :], LNC16, -1.0,
                                                   OP.add, OP.mult))
                        b4e = sB4[:, :].unsqueeze(2).broadcast_to([128, 4, 256])
                        p.dvx(dv, dv.tensor_tensor(sXwS[:, :].rearrange("p (j m) -> p j m", j=4),
                                                   xwp.rearrange("p (j m) -> p j m", j=4),
                                                   b4e, OP.add))
                        p.wait(ac, "dv")
                        p.acx(ac, ac.activation(out=sPhi4[:, :], in_=sXwS[:, :], func=AF.Exp))
                        p.wait(dv, "ac")
                        if side == 0:
                            p.dvx(dv, dv.tensor_scalar(sPhiQh[:, g * 1024:(g + 1) * 1024],
                                                       sPhi4[:, :], EPS16, None, OP.add))
                        else:
                            p.dvx(dv, dv.tensor_scalar(sPhi4[:, :], sPhi4[:, :], EPS16,
                                                       None, OP.add))
                            p.wait(pe, "dv")
                            for j in range(4):
                                tg = g * 4 + j
                                for mc in range(2):
                                    p.inc(pe.matmul(bankp(mc, 0, 128, 64),
                                                    sPhi4[:, j * 256 + mc * 128: j * 256 + (mc + 1) * 128],
                                                    sVt[:, tg * 1024 + hof: tg * 1024 + hof + 64],
                                                    start=(tg == 0), stop=(tg == 15)), "pe")
                                    p.inc(pe.matmul(bankp(2 + mc, 0, 128, 1),
                                                    sPhi4[:, j * 256 + mc * 128: j * 256 + (mc + 1) * 128],
                                                    sOnesColB[:, :], start=(tg == 0), stop=(tg == 15)), "pe")
                    if side == 0:
                        p.wait(sy, "dv", "st")
                        p.inc(sy.dma_start(out=phiq_r[:, :, h * 256:(h + 1) * 256],
                                           in_=sPhiQh[:, :].rearrange("p (g m) -> p g m", g=16)),
                              "st", 16)
                p.wait(dv, "pe", "st")
                p.dvx(dv, dv.tensor_copy(sKVo[:, 0:64], bankp(0, 0, 128, 64)))
                p.dvx(dv, dv.tensor_copy(sKVo[:, 64:65], bankp(2, 0, 128, 1)))
                p.dvx(dv, dv.tensor_copy(sKVo[:, 65:129], bankp(1, 0, 128, 64)))
                p.dvx(dv, dv.tensor_copy(sKVo[:, 129:130], bankp(3, 0, 128, 1)))
                p.wait(sy, "dv")
                p.inc(sy.dma_start(out=kvp[h].rearrange("k p m -> p k m"),
                                   in_=sKVo[:, :].rearrange("p (k m) -> p k m", k=2)), "st", 16)
        p.wait(gp, "st")
        p.wait(gp, "ld")
        p.wait(gp, "gld")
    return nc


def prep_E_inputs(x2bs, Wq, Wk, Wv, proj):
    consts = _consts()
    in_maps = []
    for cid in range(N_CORES):
        m = dict(
            x2bi=np.ascontiguousarray(x2bs[cid]),
            wqT=f32(np.asarray(Wq).T),
            wkT=f32(np.asarray(Wk).T),
            wvT=f32(np.asarray(Wv).T),
            projT=f32(np.tile(np.transpose(np.asarray(proj), (2, 0, 1)), (2, 1, 1)).reshape(128, -1)),
            **consts,
        )
        in_maps.append(m)
    return in_maps



def build_launch_F():
    nc = bass.Bass("TRN2", target_bir_lowering=False)
    T = TPC
    NH = 16
    TT = 512

    x2o = nc.dram_tensor("x2o", [D_MODEL, T], F32, kind="ExternalInput").ap()
    phiqi = nc.dram_tensor("phiqi", [T, NH * 256], BF16, kind="ExternalInput").ap()
    kvps = [nc.dram_tensor(f"kvp{j}", [NH, 2, 128, 65], F32, kind="ExternalInput").ap()
            for j in range(4)]
    woT = nc.dram_tensor("woT", [D_MODEL, D_MODEL], F32, kind="ExternalInput").ap()
    bo = nc.dram_tensor("bo", [128, 8], F32, kind="ExternalInput").ap()
    onesr, trilc, eyeb, eyef = _add_const_tensors(nc)
    x3f = nc.dram_tensor("x3f", [D_MODEL, T], F32, kind="ExternalOutput").ap()
    x3b = nc.dram_tensor("x3b", [D_MODEL, T], BF16, kind="ExternalOutput").ap()

    x2_r = x2o.rearrange("(c p) t -> p c t", p=128)
    phiq_r = phiqi.rearrange("(g p) m -> p g m", p=128)
    x3f_r = x3f.rearrange("(c p) t -> p c t", p=128)
    x3b_r = x3b.rearrange("(c p) t -> p c t", p=128)

    with ExitStack() as st:
        sb = lambda name, shape, dt=F32: st.enter_context(nc.sbuf_tensor(name, shape, dt))
        sKV = sb("sKV", [128, NH * 2 * 65])
        sKV2 = sb("sKV2", [128, NH * 2 * 65])
        sKVb = sb("sKVb", [128, NH * 2 * 65], BF16)
        sKrow = sb("sKrow", [1, 256])
        sKb = sb("sKb", [128, 256])
        sPhiF = sb("sPhiF", [128, 1024], BF16)
        sPq4 = sb("sPq4", [128, 1024], BF16)
        sPqm4 = sb("sPqm4", [128, 1024], BF16)
        sT4 = sb("sT4", [128, 1024])
        sDen4 = sb("sDen4", [128, 4])
        sMerged = sb("sMerged", [128, 8 * T], BF16)
        sWo = sb("sWo", [128, 8 * 1024], BF16)
        sBo = sb("sBo", [128, 8])
        sXo = sb("sXo", [128, 8 * TT])
        sRaw = sb("sRaw", [128, 8 * TT])
        sSq = sb("sSq", [128, TT])
        sR1 = sb("sR1", [1, TT])
        sR2 = sb("sR2", [1, TT])
        sR3 = sb("sR3", [1, TT])
        sX3 = sb("sX3", [128, 8 * TT])
        sX3b = sb("sX3b", [128, 8 * TT], BF16)
        sOnes = sb("sOnes", [1, 128])
        sOnesCol = sb("sOnesCol", [128, 1])
        sEyeF = sb("sEyeF", [128, 128])
        sEyeB = sb("sEyeB", [128, 128], BF16)
        sEpsR = sb("sEpsR", [1, 1])

        ppool = st.enter_context(nc.psum_tensor("ppool", [128, 4096], F32))

        def bank(i, w=512, off=0):
            return ppool[:, i * 512 + off: i * 512 + off + w]

        def bankp(i, p0, prows, w, off=0):
            return ppool[p0:p0 + prows, i * 512 + off: i * 512 + off + w]

        p = P(nc, st, ["ld", "gld", "pe", "dv", "ac", "st"])
        sy, gp, pe, dv, ac = nc.sync, nc.gpsimd, nc.tensor, nc.vector, nc.scalar

        p.inc(sy.dma_start(out=sOnes[:, :], in_=onesr[:, :]), "ld", 16)
        p.inc(sy.dma_start(out=sEyeF[:, :], in_=eyef[:, :]), "ld", 16)
        p.inc(sy.dma_start(out=sEyeB[:, :], in_=eyeb[:, :]), "ld", 16)
        p.inc(sy.dma_start(out=sBo[:, :], in_=bo[:, :]), "ld", 16)
        p.inc(sy.dma_start(out=sKV[:, :].rearrange("p (h k m) -> p h k m", h=NH, k=2),
                           in_=kvps[0].rearrange("h k p m -> p h k m")), "ld", 16)
        p.inc(gp.dma_start(out=sWo[:, :].rearrange("p (c m) -> p c m", c=8),
                           in_=woT.rearrange("(c p) m -> p c m", p=128)), "gld", 16)
        p.wait(dv, "ld")
        p.dvx(dv, dv.memset(sOnesCol[:, :], 1.0))
        p.dvx(dv, dv.memset(sEpsR[:, :], LNEPS))
        for j in range(1, 4):
            if j > 1:
                p.wait(sy, "dv")
            p.inc(sy.dma_start(out=sKV2[:, :].rearrange("p (h k m) -> p h k m", h=NH, k=2),
                               in_=kvps[j].rearrange("h k p m -> p h k m")), "ld", 16)
            p.wait(dv, "ld")
            p.dvx(dv, dv.tensor_tensor(sKV[:, :], sKV[:, :], sKV2[:, :], OP.add))
        p.dvx(dv, dv.tensor_copy(sKVb[:, :], sKV[:, :]))

        for h in range(NH):
            hb = h * 130
            p.wait(pe, "dv", "ac")
            p.inc(pe.transpose(bankp(5, 0, 1, 128), sKV[:, hb + 64: hb + 65], sEyeF[:, :]), "pe")
            p.inc(pe.transpose(bankp(5, 0, 1, 128, off=128), sKV[:, hb + 129: hb + 130],
                               sEyeF[:, :]), "pe")
            p.wait(dv, "pe")
            p.dvx(dv, dv.tensor_copy(sKrow[:, 0:128], bankp(5, 0, 1, 128)))
            p.dvx(dv, dv.tensor_copy(sKrow[:, 128:256], bankp(5, 0, 1, 128, off=128)))
            p.wait(pe, "dv")
            p.inc(pe.matmul(bankp(6, 0, 128, 256), sOnes[:, :], sKrow[:, :],
                            start=True, stop=True), "pe")
            p.wait(dv, "pe")
            p.dvx(dv, dv.tensor_copy(sKb[:, :], bankp(6, 0, 128, 256)))
            for g in range(4):
                p.wait(sy, "dv", "pe")
                p.inc(sy.dma_start(out=sPhiF[:, :].rearrange("p (j m) -> p j m", j=4),
                                   in_=phiq_r[:, g * 4:(g + 1) * 4, h * 256:(h + 1) * 256]),
                      "ld", 16)
                p.wait(dv, "ld")
                ksb_e = sKb[:, :].unsqueeze(1).broadcast_to([128, 4, 256])
                p.dvx(dv, dv.tensor_tensor(sT4[:, :].rearrange("p (j m) -> p j m", j=4),
                                           sPhiF[:, :].rearrange("p (j m) -> p j m", j=4),
                                           ksb_e, OP.mult))
                p.dvx(dv, dv.tensor_reduce(sDen4[:, :],
                                           sT4[:, :].rearrange("p (j m) -> p j m", j=4),
                                           axis=mybir.AxisListType.X, op=OP.add))
                p.dvx(dv, dv.tensor_scalar(sDen4[:, :], sDen4[:, :], 1e-6, None, OP.add))
                p.dvx(dv, dv.reciprocal(sDen4[:, :], sDen4[:, :]))
                rd_e = sDen4[:, :].unsqueeze(2).broadcast_to([128, 4, 256])
                p.dvx(dv, dv.tensor_tensor(sPq4[:, :].rearrange("p (j m) -> p j m", j=4),
                                           sPhiF[:, :].rearrange("p (j m) -> p j m", j=4),
                                           rd_e, OP.mult))
                p.wait(pe, "dv")
                for q in range(8):
                    p.inc(pe.transpose(
                        bankp(5, 0, 128, 64, off=q * 64).bitcast(BF16)[:, 0:128],
                        sPq4[:, q * 128:(q + 1) * 128], sEyeB[:, :]), "pe")
                p.wait(dv, "pe")
                p.dvx(dv, dv.tensor_copy(sPqm4[:, :],
                                         bankp(5, 0, 128, 512).bitcast(BF16)[:, 0:1024]))
                p.wait(pe, "dv")
                rows = slice((h % 2) * 64, (h % 2) * 64 + 64)
                for j in range(4):
                    for mc in range(2):
                        p.inc(pe.matmul(
                            ppool[rows, 7 * 512 + j * 128: 7 * 512 + (j + 1) * 128],
                            sKVb[:, hb + mc * 65: hb + mc * 65 + 64],
                            sPqm4[:, j * 256 + mc * 128: j * 256 + (mc + 1) * 128],
                            start=(mc == 0), stop=(mc == 1)), "pe")
                p.wait(dv, "pe")
                p.dvx(dv, dv.tensor_copy(
                    sMerged[rows, (h // 2) * T + g * 512:(h // 2) * T + (g + 1) * 512],
                    ppool[rows, 7 * 512: 7 * 512 + 512]))

        for tt in range(T // TT):
            tsl = slice(tt * TT, (tt + 1) * TT)
            p.wait(sy, "pe", "dv", "ac")
            p.inc(sy.dma_start(out=sXo[:, :].rearrange("p (c t) -> p c t", c=8),
                               in_=x2_r[:, :, tsl]), "ld", 16)
            p.wait(pe, "dv", "ac")
            for m in range(8):
                for k in range(8):
                    p.inc(pe.matmul(bank(m % 4), sWo[:, k * 1024 + m * 128:(k * 1024) + (m + 1) * 128],
                                    sMerged[:, k * T + tt * TT: k * T + (tt + 1) * TT],
                                    start=(k == 0), stop=(k == 7)), "pe")
                p.wait(dv, "pe", "ld")
                p.dvx(dv, dv.scalar_tensor_tensor(sRaw[:, m * TT:(m + 1) * TT],
                                                  bank(m % 4), sBo[:, m:m + 1],
                                                  sXo[:, m * TT:(m + 1) * TT],
                                                  OP.add, OP.add))
                p.wait(pe, "dv")
            p.wait(pe, "dv")
            for c in range(8):
                p.inc(pe.matmul(bankp(4, 0, 1, TT), sOnesCol[:, :],
                                sRaw[:, c * TT:(c + 1) * TT],
                                start=(c == 0), stop=(c == 7)), "pe")
            p.wait(ac, "dv")
            for c in range(8):
                if c > 0:
                    p.wait(ac, "pe")
                p.acx(ac, ac.activation(out=sSq[:, :], in_=sRaw[:, c * TT:(c + 1) * TT],
                                        func=AF.Square))
                p.wait(pe, "ac")
                p.inc(pe.matmul(bankp(5, 0, 1, TT), sOnesCol[:, :], sSq[:, :],
                                start=(c == 0), stop=(c == 7)), "pe")
            p.wait(dv, "pe")
            p.dvx(dv, dv.tensor_scalar(sR1[:, :], bankp(4, 0, 1, TT), 1.0 / D_MODEL,
                                       None, OP.mult))
            p.dvx(dv, dv.tensor_scalar(sR2[:, :], bankp(5, 0, 1, TT), 1.0 / D_MODEL,
                                       None, OP.mult))
            p.dvx(dv, dv.tensor_tensor(sR3[:, :], sR1[:, :], sR1[:, :], OP.mult))
            p.dvx(dv, dv.tensor_tensor(sR2[:, :], sR2[:, :], sR3[:, :], OP.subtract))
            p.wait(ac, "dv")
            p.acx(ac, ac.activation(out=sR3[:, :], in_=sR2[:, :], func=AF.Ln, bias=sEpsR[:, :]))
            p.acx(ac, ac.activation(out=sR2[:, :], in_=sR3[:, :], func=AF.Exp, scale=-0.5))
            p.wait(dv, "ac")
            p.dvx(dv, dv.scalar_tensor_tensor(sR3[:, :], sR1[:, :], -1.0, sR2[:, :],
                                              OP.mult, OP.mult))
            p.wait(pe, "dv")
            p.inc(pe.matmul(bank(6, TT), sOnes[:, :], sR2[:, :], start=True, stop=True), "pe")
            p.inc(pe.matmul(bank(7, TT), sOnes[:, :], sR3[:, :], start=True, stop=True), "pe")
            p.wait(dv, "pe", "st")
            for c in range(8):
                csl = slice(c * TT, (c + 1) * TT)
                p.dvx(dv, dv.tensor_tensor(sX3[:, csl], sRaw[:, csl], bank(6, TT), OP.mult))
                p.dvx(dv, dv.tensor_tensor(sX3[:, csl], sX3[:, csl], bank(7, TT), OP.add))
                p.dvx(dv, dv.tensor_copy(sX3b[:, csl], sX3[:, csl]))
            p.wait(sy, "dv")
            p.inc(sy.dma_start(out=x3f_r[:, :, tsl],
                               in_=sX3[:, :].rearrange("p (c t) -> p c t", c=8)), "st", 16)
            p.inc(sy.dma_start(out=x3b_r[:, :, tsl],
                               in_=sX3b[:, :].rearrange("p (c t) -> p c t", c=8)), "st", 16)
        p.wait(gp, "st")
        p.wait(gp, "ld")
        p.wait(gp, "gld")
    return nc


def prep_F_inputs(x2fs, phiqs, kvps_all, Wo, bo):
    consts = _consts()
    in_maps = []
    for cid in range(N_CORES):
        b = cid // 4
        m = dict(
            x2o=f32(x2fs[cid]),
            phiqi=np.ascontiguousarray(phiqs[cid]),
            woT=f32(np.asarray(Wo).T),
            bo=f32(np.asarray(bo).reshape(8, 128).T),
            **consts,
        )
        for j in range(4):
            m[f"kvp{j}"] = f32(kvps_all[4 * b + j])
        in_maps.append(m)
    return in_maps



QS = 1.0 / 127.0


def build_launch_G():
    nc = bass.Bass("TRN2", target_bir_lowering=False)
    T = TPC
    TT = 512
    NHID = 3072

    x3fi = nc.dram_tensor("x3fi", [D_MODEL, T], F32, kind="ExternalInput").ap()
    x3bi = nc.dram_tensor("x3bi", [D_MODEL, T], BF16, kind="ExternalInput").ap()
    wgT = nc.dram_tensor("wgT", [D_MODEL, NHID], I32, kind="ExternalInput").ap()
    wuT = nc.dram_tensor("wuT", [D_MODEL, NHID], I32, kind="ExternalInput").ap()
    wd3 = nc.dram_tensor("wd3", [8, NHID, 128], I32, kind="ExternalInput").ap()
    onesr, trilc, eyeb, eyef = _add_const_tensors(nc)
    outf = nc.dram_tensor("outf", [D_MODEL, T], F32, kind="ExternalOutput").ap()

    x3f_r = x3fi.rearrange("(c p) t -> p c t", p=128)
    x3b_r = x3bi.rearrange("(c p) t -> p c t", p=128)
    out_r = outf.rearrange("(c p) t -> p c t", p=128)

    with ExitStack() as st:
        sb = lambda name, shape, dt=F32: st.enter_context(nc.sbuf_tensor(name, shape, dt))
        sWg = sb("sWg", [128, 8 * NHID], BF16)
        sWu = sb("sWu", [128, 8 * NHID], BF16)
        sWdc = sb("sWdc", [128, 24 * 128], BF16)
        sWdc2 = sb("sWdc2", [128, 24 * 128], BF16)
        sXb = sb("sXb", [128, 8 * TT], BF16)
        sXf = sb("sXf", [128, 8 * TT])
        sGs = sb("sGs", [128, TT], BF16)
        sGs2 = sb("sGs2", [128, TT], BF16)
        sH1 = sb("sH1", [128, 24 * TT], BF16)
        sRaw = sb("sRaw", [128, 8 * TT])
        sSq = sb("sSq", [128, TT])
        sR1 = sb("sR1", [1, TT])
        sR2 = sb("sR2", [1, TT])
        sR3 = sb("sR3", [1, TT])
        sOut = sb("sOut", [128, 8 * TT])
        sOnes = sb("sOnes", [1, 128])
        sOnesCol = sb("sOnesCol", [128, 1])
        sEpsR = sb("sEpsR", [1, 1])

        ppool = st.enter_context(nc.psum_tensor("ppool", [128, 4096], F32))

        def bank(i, w=512, off=0):
            return ppool[:, i * 512 + off: i * 512 + off + w]

        def bankp(i, p0, prows, w, off=0):
            return ppool[p0:p0 + prows, i * 512 + off: i * 512 + off + w]

        p = P(nc, st, ["ld", "gld", "pe", "dv", "ac", "st"])
        sy, gp, pe, dv, ac = nc.sync, nc.gpsimd, nc.tensor, nc.vector, nc.scalar

        p.inc(sy.dma_start(out=sOnes[:, :], in_=onesr[:, :]), "ld", 16)
        p.inc(gp.dma_start(out=sWg[:, :].rearrange("p (c m) -> p c m", c=8),
                           in_=wgT.rearrange("(c p) m -> p c m", p=128)), "gld", 16)
        p.inc(gp.dma_start(out=sWu[:, :].rearrange("p (c m) -> p c m", c=8),
                           in_=wuT.rearrange("(c p) m -> p c m", p=128)), "gld", 16)
        p.wait(dv, "ld")
        p.dvx(dv, dv.memset(sOnesCol[:, :], 1.0))
        p.dvx(dv, dv.memset(sEpsR[:, :], LNEPS))
        p.wait(pe, "gld", "dv")
        dprev_pe = {}
        dprev_dv = {}

        for tt in range(T // TT):
            tsl = slice(tt * TT, (tt + 1) * TT)
            p.wait(sy, "pe", "dv", "ac")
            p.inc(sy.dma_start(out=sXb[:, :].rearrange("p (c t) -> p c t", c=8),
                               in_=x3b_r[:, :, tsl]), "ld", 16)
            p.inc(sy.dma_start(out=sXf[:, :].rearrange("p (c t) -> p c t", c=8),
                               in_=x3f_r[:, :, tsl]), "ld", 16)
            p.wait(pe, "ld")
            gcon_dv = [0] * 24
            gcon_ac = [0] * 24
            for m in range(24):
                gb, ub = (bank(0), bank(1)) if m % 2 == 0 else (bank(4), bank(5))
                if m >= 2:
                    p.wait_at(pe, "dv", gcon_dv[m - 2])
                    p.wait_at(pe, "ac", gcon_ac[m - 2])
                elif m == 0:
                    p.wait(pe, "dv", "ac")
                for k in range(8):
                    p.inc(pe.matmul(gb, sWg[:, k * NHID + m * 128: k * NHID + (m + 1) * 128],
                                    sXb[:, k * TT:(k + 1) * TT],
                                    start=(k == 0), stop=(k == 7)), "pe")
                for k in range(8):
                    p.inc(pe.matmul(ub, sWu[:, k * NHID + m * 128: k * NHID + (m + 1) * 128],
                                    sXb[:, k * TT:(k + 1) * TT],
                                    start=(k == 0), stop=(k == 7)), "pe")
                p.wait(ac, "pe")
                sgs = sGs if m % 2 == 0 else sGs2
                p.acx(ac, ac.activation(out=sgs[:, :], in_=gb, func=AF.Silu, scale=QS))
                gcon_ac[m] = p.at("ac")
                p.wait(dv, "ac")
                p.dvx(dv, dv.scalar_tensor_tensor(sH1[:, m * TT:(m + 1) * TT],
                                                  ub, QS, sgs[:, :],
                                                  OP.mult, OP.mult))
                gcon_dv[m] = p.at("dv")
            for dm in range(8):
                par = dm % 2
                wb = sWdc if par == 0 else sWdc2
                db = bank(2) if par == 0 else bank(6)
                if par in dprev_pe:
                    p.wait_at(gp, "pe", dprev_pe[par])
                p.inc(gp.dma_start(out=wb[:, :].rearrange("p (c m) -> p c m", c=24),
                                   in_=wd3[dm].rearrange("(c p) m -> p c m", p=128)), "gld", 16)
                if par in dprev_dv:
                    p.wait_at(pe, "dv", dprev_dv[par])
                if dm == 0:
                    p.wait(pe, "dv")
                p.wait(pe, "gld")
                for k in range(24):
                    p.inc(pe.matmul(db, wb[:, k * 128:(k + 1) * 128],
                                    sH1[:, k * TT:(k + 1) * TT],
                                    start=(k == 0), stop=(k == 23)), "pe")
                dprev_pe[par] = p.at("pe")
                p.wait(dv, "pe", "ld")
                p.dvx(dv, dv.scalar_tensor_tensor(sRaw[:, dm * TT:(dm + 1) * TT],
                                                  db, QS, sXf[:, dm * TT:(dm + 1) * TT],
                                                  OP.mult, OP.add))
                dprev_dv[par] = p.at("dv")
            p.wait(pe, "dv")
            for c in range(8):
                p.inc(pe.matmul(bankp(4, 0, 1, TT), sOnesCol[:, :],
                                sRaw[:, c * TT:(c + 1) * TT],
                                start=(c == 0), stop=(c == 7)), "pe")
            p.wait(ac, "dv")
            for c in range(8):
                if c > 0:
                    p.wait(ac, "pe")
                p.acx(ac, ac.activation(out=sSq[:, :], in_=sRaw[:, c * TT:(c + 1) * TT],
                                        func=AF.Square))
                p.wait(pe, "ac")
                p.inc(pe.matmul(bankp(5, 0, 1, TT), sOnesCol[:, :], sSq[:, :],
                                start=(c == 0), stop=(c == 7)), "pe")
            p.wait(dv, "pe")
            p.dvx(dv, dv.tensor_scalar(sR1[:, :], bankp(4, 0, 1, TT), 1.0 / D_MODEL,
                                       None, OP.mult))
            p.dvx(dv, dv.tensor_scalar(sR2[:, :], bankp(5, 0, 1, TT), 1.0 / D_MODEL,
                                       None, OP.mult))
            p.dvx(dv, dv.tensor_tensor(sR3[:, :], sR1[:, :], sR1[:, :], OP.mult))
            p.dvx(dv, dv.tensor_tensor(sR2[:, :], sR2[:, :], sR3[:, :], OP.subtract))
            p.wait(ac, "dv")
            p.acx(ac, ac.activation(out=sR3[:, :], in_=sR2[:, :], func=AF.Ln, bias=sEpsR[:, :]))
            p.acx(ac, ac.activation(out=sR2[:, :], in_=sR3[:, :], func=AF.Exp, scale=-0.5))
            p.wait(dv, "ac")
            p.dvx(dv, dv.scalar_tensor_tensor(sR3[:, :], sR1[:, :], -1.0, sR2[:, :],
                                              OP.mult, OP.mult))
            p.wait(pe, "dv")
            p.inc(pe.matmul(bank(6, TT), sOnes[:, :], sR2[:, :], start=True, stop=True), "pe")
            p.inc(pe.matmul(bank(7, TT), sOnes[:, :], sR3[:, :], start=True, stop=True), "pe")
            p.wait(dv, "pe", "st")
            for c in range(8):
                csl = slice(c * TT, (c + 1) * TT)
                p.dvx(dv, dv.tensor_tensor(sOut[:, csl], sRaw[:, csl], bank(6, TT), OP.mult))
                p.dvx(dv, dv.tensor_tensor(sOut[:, csl], sOut[:, csl], bank(7, TT), OP.add))
            p.wait(sy, "dv")
            p.inc(sy.dma_start(out=out_r[:, :, tsl],
                               in_=sOut[:, :].rearrange("p (c t) -> p c t", c=8)), "st", 16)
        p.wait(gp, "st")
        p.wait(gp, "ld")
    return nc


def prep_G_inputs(x3fs, x3bs, w_gate, w_up, w_down):
    consts = _consts()
    wgT = np.ascontiguousarray(np.asarray(w_gate).T.astype(np.int32))
    wuT = np.ascontiguousarray(np.asarray(w_up).T.astype(np.int32))
    wd = np.asarray(w_down)
    wd3 = np.ascontiguousarray(wd.reshape(8, 128, 3072).transpose(0, 2, 1).astype(np.int32))
    in_maps = []
    for cid in range(N_CORES):
        m = dict(
            x3fi=f32(x3fs[cid]),
            x3bi=np.ascontiguousarray(x3bs[cid]),
            wgT=wgT, wuT=wuT, wd3=wd3,
            **consts,
        )
        in_maps.append(m)
    return in_maps



def kernel(**inputs):
    run = lambda nc, maps: run_bass_kernel_spmd(nc, maps, core_ids=list(range(N_CORES))).results

    resA = run(build_launch_A(),
               prep_A_inputs(inputs["L0_attn_x"], inputs["L0_attn_norm_w"],
                             inputs["L0_attn_in_proj_w"], inputs["L0_attn_conv_w"],
                             inputs["L0_attn_conv_b"], inputs["L0_attn_A_log"],
                             inputs["L0_attn_D"], inputs["L0_attn_dt_bias"],
                             inputs["L0_attn_out_proj_w"]))
    pouts = [resA[c]["pout"] for c in range(N_CORES)]

    resB = run(build_launch_B(), prep_B_inputs(inputs["L0_attn_x"], pouts, inputs["L0_ffn_wg"]))
    x1fs = [resB[c]["x1f"] for c in range(N_CORES)]
    x1bs = [resB[c]["x1b"] for c in range(N_CORES)]
    scores = np.concatenate([resB[c]["sct"] for c in range(N_CORES)], axis=1).T

    resC = run(build_launch_C2(),
               prep_C2_inputs(x1bs, scores, inputs["L0_ffn_w1"], inputs["L0_ffn_w3"],
                              inputs["L0_ffn_w2"]))
    eobs = [resC[e]["eob"] for e in range(N_CORES)]
    idxgs = [resC[e]["idxg"] for e in range(N_CORES)]

    resD = run(build_launch_D2(), prep_D2_inputs(x1fs, eobs, idxgs))
    x2fs = [resD[c]["x2f"] for c in range(N_CORES)]
    x2bs = [resD[c]["x2b"] for c in range(N_CORES)]

    resE = run(build_launch_E(),
               prep_E_inputs(x2bs, inputs["L1_attn_Wq"], inputs["L1_attn_Wk"],
                             inputs["L1_attn_Wv"], inputs["L1_attn_proj"]))
    phiqs = [resE[c]["phiq"] for c in range(N_CORES)]
    kvps = [resE[c]["kvp"] for c in range(N_CORES)]

    resF = run(build_launch_F(),
               prep_F_inputs(x2fs, phiqs, kvps, inputs["L1_attn_Wo"], inputs["L1_attn_bo"]))
    x3fs = [resF[c]["x3f"] for c in range(N_CORES)]
    x3bs = [resF[c]["x3b"] for c in range(N_CORES)]

    resG = run(build_launch_G(),
               prep_G_inputs(x3fs, x3bs, inputs["L1_ffn_w_gate"], inputs["L1_ffn_w_up"],
                             inputs["L1_ffn_w_down"]))
    out = np.zeros((2, 8192, 1024), np.float32)
    for cid in range(N_CORES):
        b, q = cid // 4, cid % 4
        out[b, 2048 * q: 2048 * q + 2048] = resG[cid]["outf"].T
    return out



def build_launch_C2():
    nc = bass.Bass("TRN2", target_bir_lowering=False)
    NITER = 42
    CAP = 4096.0
    BIG = 60000.0

    x1rt = nc.dram_tensor("x1rt", [16384, 1024], BF16, kind="ExternalInput").ap()
    srow = nc.dram_tensor("srow", [128, 128], F32, kind="ExternalInput").ap()
    w1 = nc.dram_tensor("w1", [D_MODEL, 2048], F32, kind="ExternalInput").ap()
    w3 = nc.dram_tensor("w3", [D_MODEL, 2048], F32, kind="ExternalInput").ap()
    w2 = nc.dram_tensor("w2", [2048, D_MODEL], F32, kind="ExternalInput").ap()
    triu1 = nc.dram_tensor("triu1", [128, 128], F32, kind="ExternalInput").ap()
    onesr, trilc, eyeb, eyef = _add_const_tensors(nc)
    eob = nc.dram_tensor("eob", [4096, 1024], BF16, kind="ExternalOutput").ap()
    idxg = nc.dram_tensor("idxg", [128, 128], U32, kind="ExternalOutput").ap()
    xbuf = nc.dram_tensor("xbuf", [4096, 1032], BF16, kind="Internal").ap()

    with ExitStack() as st:
        sb = lambda name, shape, dt=F32: st.enter_context(nc.sbuf_tensor(name, shape, dt))
        sW1 = sb("sW1", [128, 8 * 2048], BF16)
        sW3 = sb("sW3", [128, 8 * 2048], BF16)
        sW2 = sb("sW2", [128, 16 * 1024], BF16)
        sGrid = sb("sGrid", [128, 128])
        sLo = sb("sLo", [128, 1]); sHi = sb("sHi", [128, 1]); sMid = sb("sMid", [128, 1])
        sCnt = sb("sCnt", [128, 1]); sSel = sb("sSel", [128, 1])
        sT1 = sb("sT1", [128, 1]); sT2 = sb("sT2", [128, 1])
        sR11 = sb("sR11", [1, 1])
        sTrash = sb("sTrash", [128, 128], BF16)
        sEq = sb("sEq", [128, 128]); sRk = sb("sRk", [128, 128]); sRk2 = sb("sRk2", [128, 128])
        sMask = sb("sMask", [128, 128]); sGate = sb("sGate", [128, 128])
        sRank = sb("sRank", [128, 128]); sIdxF = sb("sIdxF", [128, 128])
        sIdxU = sb("sIdxU", [128, 128], U32)
        sTriu1 = sb("sTriu1", [128, 128])
        sOnes = sb("sOnes", [1, 128]); sOnesCol = sb("sOnesCol", [128, 1])
        sEyeB = sb("sEyeB", [128, 128], BF16)
        sStg = [sb(f"sStg{i}", [128, 1032], BF16) for i in range(2)]
        sXr4 = sb("sXr4", [128, 4 * 1032], BF16)
        sScf = sb("sScf", [128, 4])
        sRt = sb("sRt", [128, 8 * 512], BF16)
        sGs = sb("sGs", [128, 512], BF16)
        sGs2 = sb("sGs2", [128, 512], BF16)
        sH1 = sb("sH1", [128, 16 * 512], BF16)
        sEoF = sb("sEoF", [128, 8 * 512], BF16)
        sEoT = sb("sEoT", [128, 4 * 1024], BF16)

        ppool = st.enter_context(nc.psum_tensor("ppool", [128, 4096], F32))

        def bank(i, w=512, off=0):
            return ppool[:, i * 512 + off: i * 512 + off + w]

        def bankp(i, p0, prows, w, off=0):
            return ppool[p0:p0 + prows, i * 512 + off: i * 512 + off + w]

        p = P(nc, st, ["ld", "gld", "pe", "dv", "ac", "st", "sc"])
        sy, gp, pe, dv, ac = nc.sync, nc.gpsimd, nc.tensor, nc.vector, nc.scalar

        p.inc(sy.dma_start(out=sGrid[:, :], in_=srow[:, :]), "ld", 16)
        p.inc(sy.dma_start(out=sTriu1[:, :], in_=triu1[:, :]), "ld", 16)
        p.inc(sy.dma_start(out=sOnes[:, :], in_=onesr[:, :]), "ld", 16)
        p.inc(sy.dma_start(out=sEyeB[:, :], in_=eyeb[:, :]), "ld", 16)
        p.inc(gp.dma_start(out=sW1[:, :].rearrange("p (c m) -> p c m", c=8),
                           in_=w1.rearrange("(c p) m -> p c m", p=128)), "gld", 16)
        p.inc(gp.dma_start(out=sW3[:, :].rearrange("p (c m) -> p c m", c=8),
                           in_=w3.rearrange("(c p) m -> p c m", p=128)), "gld", 16)
        p.inc(gp.dma_start(out=sW2[:, :].rearrange("p (c m) -> p c m", c=16),
                           in_=w2.rearrange("(c p) m -> p c m", p=128)), "gld", 16)

        p.wait(dv, "ld")
        p.dvx(dv, dv.memset(sOnesCol[:, :], 1.0))
        p.dvx(dv, dv.memset(sLo[:, :], 0.0))
        p.dvx(dv, dv.memset(sHi[:, :], 2.0))
        p.dvx(dv, dv.memset(sStg[0][:, 1025:1032], 0.0))
        p.dvx(dv, dv.memset(sStg[1][:, 1025:1032], 0.0))

        for it in range(NITER):
            p.dvx(dv, dv.tensor_scalar(sMid[:, :], sLo[:, :], sHi[:, :], 0.5,
                                       OP.add, OP.mult))
            p.dvx(dv, dv.tensor_scalar(sTrash[:, :], sGrid[:, :], sMid[:, :], 0.0,
                                       OP.is_ge, OP.add, accum_out=sCnt[:, :]))
            p.wait(pe, "dv")
            p.inc(pe.matmul(bankp(0, 0, 1, 1), sCnt[:, :], sOnesCol[:, :],
                            start=True, stop=True), "pe")
            p.wait(dv, "pe")
            p.dvx(dv, dv.tensor_copy(sR11[:, :], bankp(0, 0, 1, 1)))
            p.wait(pe, "dv")
            p.inc(pe.matmul(bankp(1, 0, 128, 1), sOnes[:, :], sR11[:, :],
                            start=True, stop=True), "pe")
            p.wait(dv, "pe")
            p.dvx(dv, dv.tensor_scalar(sSel[:, :], bankp(1, 0, 128, 1), CAP, None, OP.is_ge))
            p.dvx(dv, dv.tensor_tensor(sT1[:, :], sMid[:, :], sLo[:, :], OP.subtract))
            p.dvx(dv, dv.tensor_tensor(sT1[:, :], sT1[:, :], sSel[:, :], OP.mult))
            p.dvx(dv, dv.tensor_tensor(sLo[:, :], sLo[:, :], sT1[:, :], OP.add))
            p.dvx(dv, dv.tensor_tensor(sT2[:, :], sHi[:, :], sMid[:, :], OP.subtract))
            p.dvx(dv, dv.tensor_tensor(sT2[:, :], sT2[:, :], sSel[:, :], OP.mult))
            p.dvx(dv, dv.tensor_tensor(sHi[:, :], sMid[:, :], sT2[:, :], OP.add))
        p.dvx(dv, dv.tensor_scalar(sMask[:, :], sGrid[:, :], sLo[:, :], 0.0,
                                   OP.is_gt, OP.add, accum_out=sCnt[:, :]))
        p.wait(pe, "dv")
        p.inc(pe.matmul(bankp(0, 0, 1, 1), sCnt[:, :], sOnesCol[:, :],
                        start=True, stop=True), "pe")
        p.wait(dv, "pe")
        p.dvx(dv, dv.tensor_scalar(sR11[:, :], bankp(0, 0, 1, 1), -1.0, CAP,
                                   OP.mult, OP.add))
        p.wait(pe, "dv")
        p.inc(pe.matmul(bankp(1, 0, 128, 1), sOnes[:, :], sR11[:, :],
                        start=True, stop=True), "pe")
        p.wait(dv, "pe")
        p.dvx(dv, dv.tensor_copy(sT1[:, :], bankp(1, 0, 128, 1)))
        p.dvx(dv, dv.tensor_scalar(sEq[:, :], sGrid[:, :], sLo[:, :], 0.0,
                                   OP.is_equal, OP.add, accum_out=sCnt[:, :]))
        p.dvx(dv, dv.tensor_tensor_scan(sRk[:, :], sEq[:, :], sEq[:, :], 0.0,
                                        OP.add, OP.bypass))
        p.dvx(dv, dv.tensor_tensor(sRk[:, :], sRk[:, :], sEq[:, :], OP.subtract))
        p.wait(pe, "dv")
        p.inc(pe.matmul(bankp(2, 0, 128, 1), sTriu1[:, :], sCnt[:, :],
                        start=True, stop=True), "pe")
        p.wait(dv, "pe")
        p.dvx(dv, dv.tensor_scalar(sRk2[:, :], sRk[:, :], bankp(2, 0, 128, 1), None, OP.add))
        p.dvx(dv, dv.tensor_scalar(sRk2[:, :], sRk2[:, :], sT1[:, :], None, OP.is_lt))
        p.dvx(dv, dv.tensor_tensor(sRk2[:, :], sRk2[:, :], sEq[:, :], OP.mult))
        p.dvx(dv, dv.tensor_tensor(sMask[:, :], sMask[:, :], sRk2[:, :], OP.add))
        p.dvx(dv, dv.tensor_tensor(sGate[:, :], sMask[:, :], sGrid[:, :], OP.mult))
        p.dvx(dv, dv.tensor_scalar(sTrash[:, :], sMask[:, :], 1.0, 0.0,
                                   OP.mult, OP.add, accum_out=sCnt[:, :]))
        p.dvx(dv, dv.tensor_tensor_scan(sRk[:, :], sMask[:, :], sMask[:, :], 0.0,
                                        OP.add, OP.bypass))
        p.dvx(dv, dv.tensor_tensor(sRk[:, :], sRk[:, :], sMask[:, :], OP.subtract))
        p.wait(pe, "dv")
        p.inc(pe.matmul(bankp(2, 0, 128, 1), sTriu1[:, :], sCnt[:, :],
                        start=True, stop=True), "pe")
        p.wait(dv, "pe")
        p.dvx(dv, dv.tensor_scalar(sRank[:, :], sRk[:, :], bankp(2, 0, 128, 1), None, OP.add))
        p.dvx(dv, dv.tensor_tensor(sIdxF[:, :], sRank[:, :], sMask[:, :], OP.mult))
        p.dvx(dv, dv.tensor_scalar(sRk2[:, :], sMask[:, :], -BIG, BIG, OP.mult, OP.add))
        p.dvx(dv, dv.tensor_tensor(sIdxF[:, :], sIdxF[:, :], sRk2[:, :], OP.add))
        p.dvx(dv, dv.tensor_copy(sIdxU[:, :], sIdxF[:, :]))
        p.wait(sy, "dv")
        p.inc(sy.dma_start(out=idxg[:, :], in_=sIdxU[:, :]), "st", 16)

        bcreg = gp.to_reg(4095)
        stg_busy = [0, 0]
        for i in range(128):
            par = i % 2
            stg = sStg[par]
            if i >= 2:
                p.wait_at(sy, "sc", stg_busy[par])
            p.inc(sy.dma_start(out=stg[:, 0:1024],
                               in_=x1rt[i * 128:(i + 1) * 128, :]), "ld", 16)
            p.wait(dv, "ld")
            p.dvx(dv, dv.tensor_copy(stg[:, 1024:1025], sGate[:, i:i + 1]))
            p.wait(gp, "dv")
            p.inc(gp.indirect_dma_start(
                out=xbuf[:, :], out_offset=bass.IndirectOffsetOnAxis(ap=sIdxU[:, i:i + 1], axis=0),
                in_=stg[:, 0:1032], in_offset=None,
                bounds_check=bcreg, oob_is_err=False), "sc", 16)
            stg_busy[par] = p.at("sc")
        p.wait(gp, "sc")
        p.wait(sy, "sc")

        p.wait(dv, "gld")
        for til in range(8):
            p.wait(sy, "pe", "dv", "ac")
            p.inc(sy.dma_start(out=sXr4[:, :].rearrange("p (j m) -> p j m", j=4),
                               in_=xbuf.rearrange("(t p) m -> p t m", p=128)[:, til * 4:(til + 1) * 4, :]),
                  "ld", 16)
            p.wait(dv, "ld")
            for j in range(4):
                p.dvx(dv, dv.tensor_copy(sScf[:, j:j + 1],
                                         sXr4[:, j * 1032 + 1024: j * 1032 + 1025]))
            for j in range(4):
                p.dvx(dv, dv.tensor_scalar(sXr4[:, j * 1032: j * 1032 + 1024],
                                           sXr4[:, j * 1032: j * 1032 + 1024],
                                           sScf[:, j:j + 1],
                                           None, OP.mult))
            for j in range(4):
                tb = 6 + (j % 2)
                p.wait(pe, "dv")
                for cchunk in range(8):
                    p.inc(pe.transpose(
                        bankp(tb, 0, 128, 64, off=cchunk * 64).bitcast(BF16)[:, 0:128],
                        sXr4[:, j * 1032 + cchunk * 128: j * 1032 + (cchunk + 1) * 128],
                        sEyeB[:, :]), "pe")
                p.wait(dv, "pe")
                p.dvx(dv, dv.tensor_copy(
                    sRt[:, :].rearrange("p (c t) -> p c t", c=8)[:, :, j * 128:(j + 1) * 128],
                    bankp(tb, 0, 128, 512).bitcast(BF16)[:, 0:1024].rearrange(
                        "p (c t) -> p c t", c=8)))
            consumed_dv = [0] * 16
            consumed_ac = [0] * 16
            for m in range(16):
                gb, ub = (bank(0), bank(1)) if m % 2 == 0 else (bank(4), bank(5))
                if m >= 2:
                    p.wait_at(pe, "dv", consumed_dv[m - 2])
                    p.wait_at(pe, "ac", consumed_ac[m - 2])
                elif m == 0:
                    p.wait(pe, "dv", "ac")
                for k in range(8):
                    p.inc(pe.matmul(gb, sW1[:, k * 2048 + m * 128: k * 2048 + (m + 1) * 128],
                                    sRt[:, k * 512:(k + 1) * 512],
                                    start=(k == 0), stop=(k == 7)), "pe")
                for k in range(8):
                    p.inc(pe.matmul(ub, sW3[:, k * 2048 + m * 128: k * 2048 + (m + 1) * 128],
                                    sRt[:, k * 512:(k + 1) * 512],
                                    start=(k == 0), stop=(k == 7)), "pe")
                p.wait(ac, "pe")
                sgs = sGs if m % 2 == 0 else sGs2
                p.acx(ac, ac.activation(out=sgs[:, :], in_=gb, func=AF.Silu))
                consumed_ac[m] = p.at("ac")
                p.wait(dv, "ac")
                p.dvx(dv, dv.tensor_tensor(sH1[:, m * 512:(m + 1) * 512], sgs[:, :],
                                           ub, OP.mult))
                consumed_dv[m] = p.at("dv")
            dcon = [0] * 8
            for dm in range(8):
                db = bank(2) if dm % 2 == 0 else bank(3)
                if dm >= 2:
                    p.wait_at(pe, "dv", dcon[dm - 2])
                else:
                    p.wait(pe, "dv")
                for k in range(16):
                    p.inc(pe.matmul(db, sW2[:, k * 1024 + dm * 128: k * 1024 + (dm + 1) * 128],
                                    sH1[:, k * 512:(k + 1) * 512],
                                    start=(k == 0), stop=(k == 15)), "pe")
                p.wait(dv, "pe")
                p.dvx(dv, dv.tensor_copy(sEoF[:, dm * 512:(dm + 1) * 512], db))
                dcon[dm] = p.at("dv")
            for j in range(4):
                tb = 6 + (j % 2)
                p.wait(pe, "dv")
                for dm in range(8):
                    p.inc(pe.transpose(
                        bankp(tb, 0, 128, 64, off=dm * 64).bitcast(BF16)[:, 0:128],
                        sEoF[:, dm * 512 + j * 128: dm * 512 + (j + 1) * 128],
                        sEyeB[:, :]), "pe")
                p.wait(dv, "pe", "st")
                p.dvx(dv, dv.tensor_copy(
                    sEoT[:, j * 1024:(j + 1) * 1024],
                    bankp(tb, 0, 128, 512).bitcast(BF16)[:, 0:1024]))
            p.wait(sy, "dv")
            p.inc(sy.dma_start(out=eob.rearrange("(t p) m -> p t m", p=128)[:, til * 4:(til + 1) * 4, :],
                               in_=sEoT[:, :].rearrange("p (j m) -> p j m", j=4)), "st", 16)
        p.wait(gp, "st")
        p.wait(gp, "ld")
    return nc


def prep_C2_inputs(x1bs, scores, w1, w3, w2):
    consts = _consts()
    x1all = np.ascontiguousarray(np.concatenate(x1bs, axis=1).T)
    in_maps = []
    for e in range(N_CORES):
        g = np.ascontiguousarray(scores[:, e]).reshape(128, 128).T
        m = dict(
            x1rt=x1all,
            srow=f32(g),
            w1=f32(w1[e]), w3=f32(w3[e]), w2=f32(w2[e]),
            triu1=np.triu(np.ones((128, 128), np.float32), 1),
            **consts,
        )
        in_maps.append(m)
    return in_maps



def build_launch_D2():
    nc = bass.Bass("TRN2", target_bir_lowering=False)
    TT = 512
    NTILE = TPC // TT

    x1o = nc.dram_tensor("x1o", [D_MODEL, TPC], F32, kind="ExternalInput").ap()
    eobs = [nc.dram_tensor(f"eob{j}", [4096, 1024], BF16, kind="ExternalInput").ap()
            for j in range(8)]
    idxs = [nc.dram_tensor(f"idx{j}", [128, 16], U32, kind="ExternalInput").ap()
            for j in range(8)]
    onesr, trilc, eyeb, eyef = _add_const_tensors(nc)
    x2f = nc.dram_tensor("x2f", [D_MODEL, TPC], F32, kind="ExternalOutput").ap()
    x2b = nc.dram_tensor("x2b", [D_MODEL, TPC], BF16, kind="ExternalOutput").ap()

    x1_r = x1o.rearrange("(c p) t -> p c t", p=128)
    x2f_r = x2f.rearrange("(c p) t -> p c t", p=128)
    x2b_r = x2b.rearrange("(c p) t -> p c t", p=128)

    with ExitStack() as st:
        sb = lambda name, shape, dt=F32: st.enter_context(nc.sbuf_tensor(name, shape, dt))
        sAcc = sb("sAcc", [128, 8 * TT])
        sAt = sb("sAt", [128, 1024])
        sRaw = sb("sRaw", [128, 8 * TT])
        sSq = sb("sSq", [128, TT])
        sR1 = sb("sR1", [1, TT]); sR2 = sb("sR2", [1, TT]); sR3 = sb("sR3", [1, TT])
        sX2 = sb("sX2", [128, 8 * TT])
        sX2b = sb("sX2b", [128, 8 * TT], BF16)
        sIdx = [sb(f"sIdx{j}", [128, 16], U32) for j in range(8)]
        sOnes = sb("sOnes", [1, 128]); sOnesCol = sb("sOnesCol", [128, 1])
        sEyeF = sb("sEyeF", [128, 128])
        sEpsR = sb("sEpsR", [1, 1])

        ppool = st.enter_context(nc.psum_tensor("ppool", [128, 4096], F32))

        def bank(i, w=512, off=0):
            return ppool[:, i * 512 + off: i * 512 + off + w]

        def bankp(i, p0, prows, w, off=0):
            return ppool[p0:p0 + prows, i * 512 + off: i * 512 + off + w]

        p = P(nc, st, ["ld", "gth", "pe", "dv", "ac", "st"])
        sy, gp, pe, dv, ac = nc.sync, nc.gpsimd, nc.tensor, nc.vector, nc.scalar
        bcreg = gp.to_reg(4095)

        p.inc(sy.dma_start(out=sOnes[:, :], in_=onesr[:, :]), "ld", 16)
        p.inc(sy.dma_start(out=sEyeF[:, :], in_=eyef[:, :]), "ld", 16)
        for j in range(8):
            p.inc(sy.dma_start(out=sIdx[j][:, :], in_=idxs[j][:, :]), "ld", 16)
        p.wait(dv, "ld")
        p.dvx(dv, dv.memset(sOnesCol[:, :], 1.0))
        p.dvx(dv, dv.memset(sEpsR[:, :], LNEPS))

        for tt in range(NTILE):
            tsl = slice(tt * TT, (tt + 1) * TT)
            p.wait(sy, "pe", "dv", "ac")
            p.inc(sy.dma_start(out=sRaw[:, :].rearrange("p (c t) -> p c t", c=8),
                               in_=x1_r[:, :, tsl]), "ld", 16)
            for grp in range(4):
                col = tt * 4 + grp
                p.wait(dv, "gth", "pe")
                p.dvx(dv, dv.memset(sAt[:, :], 0.0))
                p.wait(gp, "dv")
                for e in range(8):
                    p.inc(gp.indirect_dma_start(
                        out=sAt[:, :], out_offset=None,
                        in_=eobs[e][:, :],
                        in_offset=bass.IndirectOffsetOnAxis(ap=sIdx[e][:, col:col + 1], axis=0),
                        bounds_check=bcreg, oob_is_err=False,
                        compute_op=OP.add), "gth", 16)
                    p.wait(gp, "gth")
                p.wait(pe, "gth", "dv")
                for c in range(8):
                    p.inc(pe.transpose(bankp(c // 4, 0, 128, 128, off=(c % 4) * 128),
                                       sAt[:, c * 128:(c + 1) * 128], sEyeF[:, :]), "pe")
                p.wait(dv, "pe")
                p.dvx(dv, dv.tensor_copy(
                    sAcc[:, :].rearrange("p (c t) -> p c t", c=8)[:, :, grp * 128:(grp + 1) * 128],
                    ppool[:, 0:1024].rearrange("p (c t) -> p c t", c=8)))
            p.wait(dv, "ld")
            p.dvx(dv, dv.tensor_tensor(sRaw[:, :], sRaw[:, :], sAcc[:, :], OP.add))
            p.wait(pe, "dv")
            for c in range(8):
                p.inc(pe.matmul(bankp(4, 0, 1, TT), sOnesCol[:, :],
                                sRaw[:, c * TT:(c + 1) * TT],
                                start=(c == 0), stop=(c == 7)), "pe")
            p.wait(ac, "dv")
            for c in range(8):
                if c > 0:
                    p.wait(ac, "pe")
                p.acx(ac, ac.activation(out=sSq[:, :], in_=sRaw[:, c * TT:(c + 1) * TT],
                                        func=AF.Square))
                p.wait(pe, "ac")
                p.inc(pe.matmul(bankp(5, 0, 1, TT), sOnesCol[:, :], sSq[:, :],
                                start=(c == 0), stop=(c == 7)), "pe")
            p.wait(dv, "pe")
            p.dvx(dv, dv.tensor_scalar(sR1[:, :], bankp(4, 0, 1, TT), 1.0 / D_MODEL,
                                       None, OP.mult))
            p.dvx(dv, dv.tensor_scalar(sR2[:, :], bankp(5, 0, 1, TT), 1.0 / D_MODEL,
                                       None, OP.mult))
            p.dvx(dv, dv.tensor_tensor(sR3[:, :], sR1[:, :], sR1[:, :], OP.mult))
            p.dvx(dv, dv.tensor_tensor(sR2[:, :], sR2[:, :], sR3[:, :], OP.subtract))
            p.wait(ac, "dv")
            p.acx(ac, ac.activation(out=sR3[:, :], in_=sR2[:, :], func=AF.Ln, bias=sEpsR[:, :]))
            p.acx(ac, ac.activation(out=sR2[:, :], in_=sR3[:, :], func=AF.Exp, scale=-0.5))
            p.wait(dv, "ac")
            p.dvx(dv, dv.scalar_tensor_tensor(sR3[:, :], sR1[:, :], -1.0, sR2[:, :],
                                              OP.mult, OP.mult))
            p.wait(pe, "dv")
            p.inc(pe.matmul(bank(6, TT), sOnes[:, :], sR2[:, :], start=True, stop=True), "pe")
            p.inc(pe.matmul(bank(7, TT), sOnes[:, :], sR3[:, :], start=True, stop=True), "pe")
            p.wait(dv, "pe", "st")
            for c in range(8):
                csl = slice(c * TT, (c + 1) * TT)
                p.dvx(dv, dv.tensor_tensor(sX2[:, csl], sRaw[:, csl], bank(6, TT), OP.mult))
                p.dvx(dv, dv.tensor_tensor(sX2[:, csl], sX2[:, csl], bank(7, TT), OP.add))
                p.dvx(dv, dv.tensor_copy(sX2b[:, csl], sX2[:, csl]))
            p.wait(sy, "dv")
            p.inc(sy.dma_start(out=x2f_r[:, :, tsl],
                               in_=sX2[:, :].rearrange("p (c t) -> p c t", c=8)), "st", 16)
            p.inc(sy.dma_start(out=x2b_r[:, :, tsl],
                               in_=sX2b[:, :].rearrange("p (c t) -> p c t", c=8)), "st", 16)
        p.wait(gp, "st")
        p.wait(gp, "ld")
        p.wait(gp, "gth")
    return nc


def prep_D2_inputs(x1fs, eobs_all, idxgs_all):
    consts = _consts()
    in_maps = []
    for cid in range(N_CORES):
        m = dict(x1o=f32(x1fs[cid]), **consts)
        for j in range(8):
            m[f"eob{j}"] = np.ascontiguousarray(eobs_all[j])
            m[f"idx{j}"] = np.ascontiguousarray(idxgs_all[j][:, cid * 16:(cid + 1) * 16])
        in_maps.append(m)
    return in_maps
